# version 01
# baseline (speedup 1.0000x reference)
import numpy as np
import ml_dtypes
from contextlib import ExitStack
import concourse.bass as bass
import concourse.mybir as mybir
from concourse.bass import IndirectOffsetOnAxis
from concourse.bass_utils import run_bass_kernel_spmd

F32 = mybir.dt.float32
BF16 = mybir.dt.bfloat16
U32 = mybir.dt.uint32
AF = mybir.ActivationFunctionType
ALU = mybir.AluOpType
AX = mybir.AxisListType

NCORES = 8
B = 2
S = 4096
T = B * S
D = 1024
H = 16
HD = 64
DH = 4096
MAGIC = 12582912.0

ENGINES = ("sync", "gpsimd", "tensor", "vector", "scalar")


def host_consts():
    bf = ml_dtypes.bfloat16
    c = {}
    c["ident"] = np.eye(128).astype(bf)
    c["onesb"] = np.ones((128, 512), bf)
    c["onesf"] = np.ones((128, 144), np.float32)
    c["zz"] = np.zeros((128, 544), bf)
    lup = (np.arange(128)[:, None] < np.arange(128)[None, :]).astype(np.float32)
    c["lup"] = lup.astype(bf)
    gs = []
    for nb in (16, 64, 256):
        pp = np.arange(64)
        table = (pp * 9973) % nb
        csize = 64 // nb if nb < 64 else 1
        g = (table[:, None] < table[None, :]).astype(np.float32) / csize
        gs.append(g)
    c["gmats"] = np.concatenate(gs, axis=1).astype(np.float32)
    masks = []
    for cs in (64, 16):
        kk = np.arange(128)[:, None] // cs
        qq = np.arange(128 + cs)[None, :] // cs
        m = ((qq - kk == 0) | (qq - kk == 1)).astype(np.float32)
        masks.append(m.astype(bf))
    c["mask1"] = masks[0]
    c["mask2"] = masks[1]
    c["identf"] = np.eye(128).astype(np.float32)
    return c


def shard_inputs(inputs):
    A = np.ascontiguousarray
    g = inputs
    base = {}
    for nm in ("Wq", "Wk", "Wv", "Wo"):
        base["l0_" + nm] = A(np.asarray(g[f"L0_attn_{nm}"], np.float32).T)
        base["l1_" + nm] = A(np.asarray(g[f"L1_attn_{nm}"], np.float32))
    for nm in ("bq", "bk", "bv", "bo"):
        v0 = np.asarray(g[f"L0_attn_{nm}"], np.float32)
        v1 = np.asarray(g[f"L1_attn_{nm}"], np.float32)
        base["l0_" + nm + "_row"] = A(v0.reshape(1, D))
        base["l0_" + nm + "_pp"] = A(v0.reshape(8, 128).T)
        base["l1_" + nm + "_row"] = A(v1.reshape(1, D))
        base["l1_" + nm + "_pp"] = A(v1.reshape(8, 128).T)
    base["f0_w1T"] = A(np.asarray(g["L0_ffn_w1_w"], np.float32).T)
    base["f0_w2T"] = A(np.asarray(g["L0_ffn_w2_w"], np.float32).T)
    base["f0_h1T"] = A(np.asarray(g["L0_ffn_h1_w"], np.float32).T)
    base["f0_h2T"] = A(np.asarray(g["L0_ffn_h2_w"], np.float32).T)
    base["f0_w3T"] = A(np.asarray(g["L0_ffn_w3_w"], np.float32).T)
    for nm, nch in (("w1_s", 32), ("w2_s", 32), ("h1_s", 32), ("h2_s", 32), ("w3_s", 8)):
        base["f0_" + nm] = A(np.asarray(g[f"L0_ffn_{nm}"], np.float32).reshape(nch, 128).T)
    pr = np.asarray(g["L1_attn_probes"], np.float32)
    base["l1_probes"] = A(pr.transpose(1, 2, 0, 3).reshape(H, HD, 192))
    base["f1_w1T"] = A(np.asarray(g["L1_ffn_w1"], np.float32).T)
    base["f1_w2T"] = A(np.asarray(g["L1_ffn_w2"], np.float32).T)
    cw = np.asarray(g["L1_ffn_conv_w"], np.float32)[:, 0, :]
    base["f1_convw"] = A(cw.reshape(32, 128, 3).transpose(1, 0, 2).reshape(128, 96))
    base["f1_convb"] = A(np.asarray(g["L1_ffn_conv_b"], np.float32).reshape(32, 128).T)
    base.update(host_consts())

    x = np.asarray(g["L0_attn_x"], np.float32)
    maps = []
    for c in range(NCORES):
        m = dict(base)
        m["x"] = A(x[2 * c:2 * c + 2].reshape(T, D))
        maps.append(m)
    return maps


class P:
    def __init__(self, nc, st):
        self.nc = nc
        self.ops = {e: [] for e in ENGINES}
        self.sems = {}
        self.cnt = {}
        for s in ("DS", "DG", "DA", "T", "V", "A", "G"):
            self.sems[s] = st.enter_context(nc.semaphore("sem_" + s))
            self.cnt[s] = 0

    def w(self, eng, sem, val):
        if val is None or val <= 0:
            return
        self.ops[eng].append(lambda e, s=self.sems[sem], v=val: e.wait_ge(s, v))

    def do(self, eng, fn, sem=None, inc=0):
        if sem is None:
            self.ops[eng].append(fn)
            return None
        self.cnt[sem] += inc
        def f(e, fn=fn, s=self.sems[sem], i=inc):
            fn(e).then_inc(s, i)
        self.ops[eng].append(f)
        return self.cnt[sem]

    def snap(self):
        return dict(self.cnt)

    def join(self, eng, snap, sems=None):
        for s in (sems if sems is not None else snap):
            self.w(eng, s, snap[s])

    def finalize(self, blk):
        for name in ENGINES:
            ops = self.ops[name]
            deco = getattr(blk, name)
            def body(e, ops=ops):
                for f in ops:
                    f(e)
            deco(body)


def emit_ingest(p, io):
    for i in range(16):
        p.do("gpsimd", lambda e, i=i: e.dma_start(
            out=io["xbf_tok"][i * 512:(i + 1) * 512, :],
            in_=io["x"][i * 512:(i + 1) * 512, :]), "DG", 16)


def emit_l0(p, io, sb, cst, ps, b):
    wq, wk, wv, wo = sb["wq"], sb["wk"], sb["wv"], sb["wo"]
    brow, at0, xsb, qkv = sb["brow"], sb["at0"], sb["xsb"], sb["qkv"]
    otok, stile, x1t = sb["otok"], sb["stile"], sb["x1t"]
    prod, sc16, att, acc, zr = sb["prod"], sb["sc16"], sb["att"], sb["acc"], sb["zr"]
    zs = sb["zs"]
    sqf = sb["sqf"]
    r_mu, r_m2, r_rs = sb["r_mu"][0:1, 0:256], sb["r_m2"][0:1, 0:256], sb["r_rs"][0:1, 0:256]
    bo_pp = sb["bo_pp"]
    onesb, onesf, ident = cst["onesb"], cst["onesfsb"], cst["ident"]

    p.w("gpsimd", "T", p.cnt["T"])
    for nm, dstv in (("l0_Wq", wq), ("l0_Wk", wk), ("l0_Wv", wv)):
        for c in range(8):
            p.do("gpsimd", lambda e, d=dstv[:, c * 1024:(c + 1) * 1024],
                 s=io[nm][c * 128:(c + 1) * 128, :]: e.dma_start(out=d, in_=s),
                 "DG", 16)
    if b == 0:
        for i, nm in enumerate(("l0_bq_row", "l0_bk_row", "l0_bv_row")):
            p.do("gpsimd", lambda e, d=brow[0:1, i * 1024:(i + 1) * 1024],
                 s=io[nm][0:1, :]: e.dma_start(out=d, in_=s), "DG", 16)
        p.do("sync", lambda e: e.dma_start(out=bo_pp, in_=io["l0_bo_pp"][:, :]), "DS", 16)
    wload = p.cnt["DG"]
    bld = p.cnt["DS"]

    evacA = p.cnt["A"]
    evacV = p.cnt["V"]
    tcnt = p.cnt["T"]
    dcnt = p.cnt["DA"]
    x1done = p.cnt["V"]
    wmap = (wq, wk, wv)
    p.w("tensor", "V", p.cnt["V"])
    p.w("vector", "T", p.cnt["T"])
    p.w("scalar", "T", p.cnt["T"])

    for tau in range(32):
        blkc = tau // 4
        if tau % 4 == 0:
            p.w("scalar", "T", tcnt)
            for c in range(8):
                rows = io["xbf_tok"][b * S + blkc * 512: b * S + blkc * 512 + 512,
                                     c * 128:(c + 1) * 128]
                dcnt = p.do("scalar", lambda e, rows=rows, c=c: e.dma_start(
                    out=xsb[:, c * 512:(c + 1) * 512], in_=rows, transpose=True),
                    "DA", 16)
        p.w("tensor", "DA", dcnt)
        p.w("tensor", "DG", wload)
        waves = [[(0, 0), (0, 1), (1, 0), (1, 1)], [(2, 0), (2, 1)]]
        for wi, wave in enumerate(waves):
            p.w("tensor", "A", evacA)
            for j, (i, fh) in enumerate(wave):
                pb = ps[j]
                for c in range(8):
                    lhsT = xsb[:, c * 512 + (tau % 4) * 128: c * 512 + (tau % 4) * 128 + 128]
                    rhs = wmap[i][:, c * 1024 + fh * 512: c * 1024 + fh * 512 + 512]
                    p.do("tensor", lambda e, pb=pb, lhsT=lhsT, rhs=rhs, c=c:
                         e.matmul(pb[:, :], lhsT, rhs, start=(c == 0), stop=False))
                tcnt = p.do("tensor", lambda e, pb=pb, i=i, fh=fh: e.matmul(
                    pb[:, :], onesb[0:1, 0:128],
                    brow[0:1, i * 1024 + fh * 512: i * 1024 + fh * 512 + 512],
                    start=False, stop=True), "T", 1)
            p.w("scalar", "T", tcnt)
            if wi == 0:
                p.w("scalar", "V", evacV)
            for j, (i, fh) in enumerate(wave):
                evacA = p.do("scalar", lambda e, i=i, fh=fh, j=j: e.activation(
                    qkv[:, i * 1024 + fh * 512: i * 1024 + fh * 512 + 512],
                    ps[j][:, :], AF.Identity), "A", 1)
        if b == 0 and tau == 0 and "dbg_qkv" in io:
            p.w("sync", "A", evacA)
            p.do("sync", lambda e: e.dma_start(out=io["dbg_qkv"][:, :], in_=qkv), "DS", 16)
            p.w("vector", "DS", p.cnt["DS"])
        p.w("vector", "A", evacA)
        q_t, k_t, v_t = qkv[:, 0:1024], qkv[:, 1024:2048], qkv[:, 2048:3072]
        for hq in range(H):
            qrep = q_t[:, hq * HD:(hq + 1) * HD].unsqueeze(1).broadcast_to([128, 16, HD])
            p.do("vector", lambda e, qrep=qrep: e.tensor_tensor(
                prod.rearrange("p (a b) -> p a b", a=16),
                k_t.rearrange("p (a b) -> p a b", a=16), qrep, ALU.mult))
            p.do("vector", lambda e, hq=hq: e.reduce_sum(
                sc16[:, hq * 16:(hq + 1) * 16],
                prod.rearrange("p (a b) -> p a b", a=16), axis=AX.X))
        vv = p.do("vector", lambda e: e.tensor_copy(prod[:, 0:256], sc16), "V", 1)
        p.w("scalar", "V", vv)
        aa = p.do("scalar", lambda e: e.activation(att[:, :], sc16[:, :], AF.Exp,
                                                   scale=0.125), "A", 1)
        if b == 0 and tau == 0 and "dbg_sc" in io:
            p.w("sync", "A", aa)
            p.do("sync", lambda e: e.dma_start(out=io["dbg_sc"][:, 0:256], in_=sc16), "DS", 16)
            p.do("sync", lambda e: e.dma_start(out=io["dbg_sc"][:, 256:512], in_=att), "DS", 16)
            p.w("vector", "DS", p.cnt["DS"])
        p.w("vector", "A", aa)
        rs = p.do("vector", lambda e: e.reduce_sum(
            zs[:, :], att.rearrange("p (a b) -> p a b", a=16), axis=AX.X), "V", 1)
        p.w("vector", "V", rs)
        rc = p.do("vector", lambda e: e.reciprocal(zr[:, :], zs[:, :]), "V", 1)
        p.w("vector", "V", rc)
        for hq in range(H):
            accv = acc[:, hq * HD:(hq + 1) * HD]
            p.do("vector", lambda e, accv=accv, hq=hq: e.tensor_scalar(
                accv, v_t[:, 0:HD], att[:, hq * 16:hq * 16 + 1], None, ALU.mult))
            for hk in range(1, 16):
                p.do("vector", lambda e, accv=accv, hk=hk, hq=hq: e.scalar_tensor_tensor(
                    accv, v_t[:, hk * HD:(hk + 1) * HD],
                    att[:, hq * 16 + hk:hq * 16 + hk + 1], accv, ALU.mult, ALU.add))
        if b == 0 and tau == 0 and "dbg_acc" in io:
            fence = p.do("vector", lambda e: e.tensor_copy(prod[:, 0:16], zr), "V", 1)
            p.w("sync", "V", fence)
            p.do("sync", lambda e: e.dma_start(out=io["dbg_acc"][:, 0:1024], in_=acc), "DS", 16)
            p.do("sync", lambda e: e.dma_start(out=io["dbg_acc"][:, 1024:1040], in_=zr), "DS", 16)
            p.w("vector", "DS", p.cnt["DS"])
        zrep = zr.unsqueeze(2).broadcast_to([128, 16, HD])
        evacV = p.do("vector", lambda e, zrep=zrep: e.tensor_tensor(
            otok.rearrange("p (a b) -> p a b", a=16),
            acc.rearrange("p (a b) -> p a b", a=16), zrep, ALU.mult), "V", 1)
        if b == 0 and tau == 0 and "dbg_otok" in io:
            p.w("sync", "V", evacV)
            p.do("sync", lambda e: e.dma_start(out=io["dbg_otok"][:, :], in_=otok), "DS", 16)
            p.w("tensor", "DS", p.cnt["DS"])
        p.w("tensor", "V", evacV)
        for c in range(8):
            pbf = ps["bfB"] if c < 4 else ps["bfA"]
            sem = ("T", 1) if c == 7 else (None, 0)
            tcnt2 = p.do("tensor", lambda e, c=c, pbf=pbf: e.transpose(
                pbf[:, (c % 4) * 128:(c % 4) * 128 + 128],
                otok[:, c * 128:(c + 1) * 128], ident[:, :]), sem[0], sem[1])
        p.w("vector", "T", tcnt2)
        ovA = at0.rearrange("p (c n) -> p c n", c=8)[:, 0:4, tau * 128:(tau + 1) * 128]
        ovB = at0.rearrange("p (c n) -> p c n", c=8)[:, 4:8, tau * 128:(tau + 1) * 128]
        p.do("vector", lambda e, ovA=ovA: e.tensor_copy(
            ovA, ps["bfB"].rearrange("p (c n) -> p c n", c=4)))
        evacV = p.do("vector", lambda e, ovB=ovB: e.tensor_copy(
            ovB, ps["bfA"].rearrange("p (c n) -> p c n", c=4)), "V", 1)
    tauT = p.cnt["T"]
    tauV = evacV
    if b == 0 and "dbg_at0" in io:
        p.w("sync", "V", tauV)
        for c in range(8):
            p.do("sync", lambda e, c=c: e.dma_start(
                out=io["dbg_at0"][c * 128:(c + 1) * 128, :],
                in_=at0[:, c * 4096:(c + 1) * 4096]), "DS", 16)
    p.w("gpsimd", "T", tauT)
    for pl in range(16):
        for half in range(2):
            p.do("gpsimd", lambda e, pl=pl, half=half: e.dma_start(
                out=wo[half * 64:half * 64 + 64, pl * 1024:(pl + 1) * 1024],
                in_=io["l0_Wo"][pl * 64:(pl + 1) * 64, :]), "DG", 16)
    wold = p.cnt["DG"]

    vv_bank = [tauV, tauV]
    dsc = p.cnt["DS"]
    for hq in range(H):
        p.w("scalar", "T", tauT)
        p.w("scalar", "V", evacV)
        for c in range(8):
            rows = io["xbf_tok"][b * S + hq * 256: b * S + hq * 256 + 256,
                                 c * 128:(c + 1) * 128]
            dcnt = p.do("scalar", lambda e, rows=rows, c=c: e.dma_start(
                out=xsb[:, c * 512:c * 512 + 256], in_=rows, transpose=True), "DA", 16)
        p.w("tensor", "V", tauV)
        p.w("tensor", "DG", wold)
        for fc in range(8):
            pb = ps[fc % 2]
            p.w("tensor", "V", vv_bank[fc % 2])
            for pl in range(16):
                lhsT = wo[(hq % 2) * 64:(hq % 2) * 64 + 64,
                          pl * 1024 + fc * 128: pl * 1024 + fc * 128 + 128]
                rhs = at0[(hq % 2) * 64:(hq % 2) * 64 + 64,
                          (hq // 2) * 4096:(hq // 2) * 4096 + 4096] \
                    .rearrange("p (n q) -> p q n", q=16)[:, pl, :]
                sem = ("T", 1) if pl == 15 else (None, 0)
                tcnt = p.do("tensor", lambda e, pb=pb, lhsT=lhsT, rhs=rhs, pl=pl:
                            e.matmul(pb[:, 0:256], lhsT, rhs,
                                     start=(pl == 0), stop=(pl == 15)), sem[0], sem[1])
            p.w("vector", "T", tcnt)
            p.w("vector", "DA", dcnt)
            p.w("vector", "DS", bld)
            sv = stile[:, fc * 256:(fc + 1) * 256]
            p.do("vector", lambda e, sv=sv, pb=pb, fc=fc: e.tensor_scalar(
                sv, pb[:, 0:256], bo_pp[:, fc:fc + 1], None, ALU.add))
            vv = p.do("vector", lambda e, sv=sv, fc=fc: e.tensor_tensor(
                sv, sv, xsb[:, fc * 512:fc * 512 + 256], ALU.add), "V", 1)
            vv_bank[fc % 2] = vv
            p.w("scalar", "V", vv)
            aa = p.do("scalar", lambda e, fc=fc, sv=sv: e.activation(
                sqf[:, fc * 256:(fc + 1) * 256], sv, AF.Square), "A", 1)
            p.w("tensor", "V", vv)
            p.w("tensor", "A", aa)
            p.do("tensor", lambda e, sv=sv, fc=fc: e.matmul(
                ps[2][0:1, 0:256], onesb[:, 0:1], sv,
                start=(fc == 0), stop=(fc == 7)))
            tst = p.do("tensor", lambda e, fc=fc: e.matmul(
                ps[2][0:1, 256:512], onesf[:, 0:1], sqf[:, fc * 256:(fc + 1) * 256],
                start=(fc == 0), stop=(fc == 7)), "T", 1)
        p.w("vector", "T", tst)
        p.do("vector", lambda e: e.tensor_scalar(r_mu, ps[2][0:1, 0:256], 1.0 / D, None, ALU.mult))
        p.do("vector", lambda e: e.tensor_scalar(r_m2, ps[2][0:1, 256:512], 1.0 / D, None, ALU.mult))
        p.do("vector", lambda e: e.tensor_tensor(r_rs, r_mu, r_mu, ALU.mult))
        p.do("vector", lambda e: e.tensor_tensor(r_m2, r_m2, r_rs, ALU.subtract))
        vv = p.do("vector", lambda e: e.tensor_scalar(r_m2, r_m2, 1e-5, None, ALU.add), "V", 1)
        p.w("scalar", "V", vv)
        p.do("scalar", lambda e: e.activation(r_rs, r_m2, AF.Ln))
        aa = p.do("scalar", lambda e: e.activation(r_rs, r_rs, AF.Exp, scale=-0.5), "A", 1)
        p.w("tensor", "A", aa)
        p.w("tensor", "V", x1done)
        p.do("tensor", lambda e: e.matmul(ps[3][:, 0:256], onesf[0:1, 0:128], r_mu,
                                          start=True, stop=True))
        tb = p.do("tensor", lambda e: e.matmul(ps[3][:, 256:512], onesf[0:1, 0:128], r_rs,
                                               start=True, stop=True), "T", 1)
        p.w("vector", "T", tb)
        p.w("vector", "DS", dsc)
        for fc in range(8):
            sv = stile[:, fc * 256:(fc + 1) * 256]
            ov = x1t[:, fc * 256:(fc + 1) * 256]
            p.do("vector", lambda e, sv=sv, ov=ov: e.tensor_tensor(
                ov, sv, ps[3][:, 0:256], ALU.subtract))
            x1done = p.do("vector", lambda e, ov=ov: e.tensor_tensor(
                ov, ov, ps[3][:, 256:512], ALU.mult), "V", 1)
        evacV = x1done
        p.w("sync", "V", x1done)
        for fc in range(8):
            dsc = p.do("sync", lambda e, fc=fc, hq=hq: e.dma_start(
                out=io["x1T"][fc * 128:(fc + 1) * 128,
                              b * S + hq * 256: b * S + hq * 256 + 256],
                in_=x1t[:, fc * 256:(fc + 1) * 256]), "DS", 16)




def emit_ln(p, sb, cst, ps, stile, sqf, nw, dst_fn):
    onesb, onesf = cst["onesb"], cst["onesfsb"]
    r_mu, r_m2, r_rs = sb["r_mu"][0:1, 0:nw], sb["r_m2"][0:1, 0:nw], sb["r_rs"][0:1, 0:nw]
    D_ = 1024
    tst = None
    for fc in range(8):
        sv = stile[:, fc * nw:(fc + 1) * nw]
        aa = p.do("scalar", lambda e, fc=fc, sv=sv: e.activation(
            sqf[:, fc * nw:(fc + 1) * nw], sv, AF.Square), "A", 1)
        p.w("tensor", "A", aa)
        p.do("tensor", lambda e, sv=sv, fc=fc: e.matmul(
            ps[2][0:1, 0:nw], onesb[:, 0:1], sv, start=(fc == 0), stop=(fc == 7)))
        tst = p.do("tensor", lambda e, fc=fc, nw=nw: e.matmul(
            ps[2][0:1, 256:256 + nw] if nw <= 256 else ps[2][0:1, 0:0],
            onesf[:, 0:1], sqf[:, fc * nw:(fc + 1) * nw],
            start=(fc == 0), stop=(fc == 7)), "T", 1)
    p.w("vector", "T", tst)
    p.do("vector", lambda e: e.tensor_scalar(r_mu, ps[2][0:1, 0:nw], 1.0 / D_, None, ALU.mult))
    p.do("vector", lambda e: e.tensor_scalar(r_m2, ps[2][0:1, 256:256 + nw], 1.0 / D_, None, ALU.mult))
    p.do("vector", lambda e: e.tensor_tensor(r_rs, r_mu, r_mu, ALU.mult))
    p.do("vector", lambda e: e.tensor_tensor(r_m2, r_m2, r_rs, ALU.subtract))
    vv = p.do("vector", lambda e: e.tensor_scalar(r_m2, r_m2, 1e-5, None, ALU.add), "V", 1)
    p.w("scalar", "V", vv)
    p.do("scalar", lambda e: e.activation(r_rs, r_m2, AF.Ln))
    aa = p.do("scalar", lambda e: e.activation(r_rs, r_rs, AF.Exp, scale=-0.5), "A", 1)
    p.w("tensor", "A", aa)
    p.do("tensor", lambda e: e.matmul(ps[3][:, 0:nw], onesf[0:1, 0:128], r_mu, start=True, stop=True))
    tb = p.do("tensor", lambda e: e.matmul(ps[3][:, 256:256 + nw], onesf[0:1, 0:128], r_rs, start=True, stop=True), "T", 1)
    p.w("vector", "T", tb)
    last = None
    for fc in range(8):
        sv = stile[:, fc * nw:(fc + 1) * nw]
        ov = sb["lnout"][:, fc * nw:(fc + 1) * nw]
        p.do("vector", lambda e, sv=sv, ov=ov, nw=nw: e.tensor_tensor(
            ov, sv, ps[3][:, 0:nw], ALU.subtract))
        last = p.do("vector", lambda e, ov=ov, nw=nw: e.tensor_tensor(
            ov, ov, ps[3][:, 256:256 + nw], ALU.mult), "V", 1)
        p.w("sync", "V", last)
        dst_fn(fc, ov)
    return last


def emit_ffn0(p, io, AR1, AR2, AR3, ARF, ARS, sb, cst, ps):
    nc = p.nc
    scal = sb["fscal"]
    for i, nm in enumerate(("f0_w1_s", "f0_w2_s", "f0_h1_s", "f0_h2_s")):
        p.do("sync", lambda e, i=i, nm=nm: e.dma_start(
            out=scal[:, i * 32:(i + 1) * 32], in_=io[nm][:, :]), "DS", 16)
    p.do("sync", lambda e: e.dma_start(out=scal[:, 128:136], in_=io["f0_w3_s"][:, :]), "DS", 16)
    sld = p.cnt["DS"]

    wA = AR1[:, 0:16384]
    xblk = AR3[:, 0:4096]
    hout = AR3[:, 4096:8192]
    silu_f = ARF[:, 0:512]
    guard_T = p.cnt["T"]
    guard_V = p.cnt["V"]
    guard_A = p.cnt["A"]
    p.w("gpsimd", "T", guard_T)
    p.w("sync", "V", guard_V)

    dcnt = p.cnt["DS"]
    ld = p.cnt["DG"]
    evacV = guard_V
    evacA = guard_A
    tcnt = guard_T
    hst = p.cnt["DS"]
    for js in range(4):
        p.w("gpsimd", "T", tcnt)
        for c in range(8):
            p.do("gpsimd", lambda e, c=c, js=js: e.dma_start(
                out=wA[:, c * 1024:(c + 1) * 1024],
                in_=io["f0_w1T"][c * 128:(c + 1) * 128, js * 1024:(js + 1) * 1024]), "DG", 16)
            ld = p.do("gpsimd", lambda e, c=c, js=js: e.dma_start(
                out=wA[:, 8192 + c * 1024:8192 + (c + 1) * 1024],
                in_=io["f0_w2T"][c * 128:(c + 1) * 128, js * 1024:(js + 1) * 1024]), "DG", 16)
        for tb in range(16):
            p.w("sync", "T", tcnt)
            xv = io["x1T"].rearrange("(c p) t -> p c t", p=128)[:, :, tb * 512:(tb + 1) * 512]
            dcnt = p.do("sync", lambda e, xv=xv: e.dma_start(
                out=xblk.rearrange("p (c t) -> p c t", c=8), in_=xv), "DS", 16)
            for jc in range(8):
                pb_u1, pb_u2 = ps[jc % 2 * 2], ps[jc % 2 * 2 + 1]
                p.w("tensor", "DS", dcnt)
                p.w("tensor", "DG", ld)
                p.w("tensor", "V", evacV)
                for c in range(8):
                    p.do("tensor", lambda e, pb_u1=pb_u1, c=c, jc=jc: e.matmul(
                        pb_u1[:, :], wA[:, c * 1024 + jc * 128: c * 1024 + jc * 128 + 128],
                        xblk[:, c * 512:(c + 1) * 512], start=(c == 0), stop=(c == 7)))
                for c in range(8):
                    tcnt = p.do("tensor", lambda e, pb_u2=pb_u2, c=c, jc=jc: e.matmul(
                        pb_u2[:, :], wA[:, 8192 + c * 1024 + jc * 128: 8192 + c * 1024 + jc * 128 + 128],
                        xblk[:, c * 512:(c + 1) * 512], start=(c == 0), stop=(c == 7)),
                        ("T" if c == 7 else None), (1 if c == 7 else 0))
                p.w("scalar", "T", tcnt)
                p.w("scalar", "DS", sld)
                p.w("scalar", "V", evacV)
                aa = p.do("scalar", lambda e, pb_u1=pb_u1, js=js, jc=jc: e.activation(
                    silu_f[:, :], pb_u1[:, :], AF.Silu,
                    scale=scal[:, js * 8 + jc:js * 8 + jc + 1]), "A", 1)
                p.w("vector", "A", aa)
                p.w("vector", "DS", hst)
                hv = hout[:, jc * 512:(jc + 1) * 512]
                evacV = p.do("vector", lambda e, pb_u2=pb_u2, hv=hv, js=js, jc=jc:
                             e.scalar_tensor_tensor(
                                 hv, pb_u2[:, :], scal[:, 32 + js * 8 + jc:32 + js * 8 + jc + 1],
                                 silu_f[:, :], ALU.mult, ALU.mult), "V", 1)
                p.w("sync", "V", evacV)
                hst = p.do("sync", lambda e, js=js, jc=jc, tb=tb, hv=hv: e.dma_start(
                    out=io["hA"][js * 1024 + jc * 128: js * 1024 + jc * 128 + 128,
                                 tb * 512:(tb + 1) * 512], in_=hv), "DS", 16)

    for stage, (wt, scol, src, dst) in enumerate((
            ("f0_h1T", 64, "hA", "hB"), ("f0_h2T", 96, "hB", "hA"))):
        wsl = AR2[:, 0:32768]
        hblk = AR1[:, 0:16384]
        gout = AR3[:, 4096:8192]
        gst = p.cnt["DS"]
        for js in range(4):
            p.w("gpsimd", "T", tcnt)
            for c in range(32):
                ld = p.do("gpsimd", lambda e, c=c, js=js, wt=wt: e.dma_start(
                    out=wsl[:, c * 1024:(c + 1) * 1024],
                    in_=io[wt][c * 128:(c + 1) * 128, js * 1024:(js + 1) * 1024]), "DG", 16)
            for tb in range(16):
                p.w("sync", "T", tcnt)
                hv2 = io[src].rearrange("(c p) t -> p c t", p=128)[:, :, tb * 512:(tb + 1) * 512]
                dcnt = p.do("sync", lambda e, hv2=hv2: e.dma_start(
                    out=hblk.rearrange("p (c t) -> p c t", c=32), in_=hv2), "DS", 16)
                for jc in range(8):
                    pb = ps[jc % 4]
                    p.w("tensor", "DS", dcnt)
                    p.w("tensor", "DG", ld)
                    p.w("tensor", "A", evacA)
                    for c in range(32):
                        tcnt = p.do("tensor", lambda e, pb=pb, c=c, jc=jc: e.matmul(
                            pb[:, :], wsl[:, c * 1024 + jc * 128: c * 1024 + jc * 128 + 128],
                            hblk[:, c * 512:(c + 1) * 512], start=(c == 0), stop=(c == 31)),
                            ("T" if c == 31 else None), (1 if c == 31 else 0))
                    p.w("scalar", "T", tcnt)
                    p.w("scalar", "DS", gst)
                    gv = gout[:, jc * 512:(jc + 1) * 512]
                    evacA = p.do("scalar", lambda e, pb=pb, gv=gv, js=js, jc=jc, scol=scol:
                                 e.activation(gv, pb[:, :], AF.Gelu,
                                              scale=scal[:, scol + js * 8 + jc:scol + js * 8 + jc + 1]),
                                 "A", 1)
                    p.w("sync", "A", evacA)
                    gst = p.do("sync", lambda e, js=js, jc=jc, tb=tb, gv=gv, dst=dst: e.dma_start(
                        out=io[dst][js * 1024 + jc * 128: js * 1024 + jc * 128 + 128,
                                    tb * 512:(tb + 1) * 512], in_=gv), "DS", 16)

    w3 = AR2[:, 0:32768]
    p.w("gpsimd", "T", tcnt)
    for c in range(32):
        ld = p.do("gpsimd", lambda e, c=c: e.dma_start(
            out=w3[:, c * 1024:(c + 1) * 1024],
            in_=io["f0_w3T"][c * 128:(c + 1) * 128, :]), "DG", 16)
    gblk = AR1[:, 0:16384]
    xres = AR3[:, 0:4096]
    stile = AR3[:, 4096:8192]
    sb["lnout"] = AR3[:, 8192:12288]
    sqf = ARF[:, 0:4096]
    x2st = p.cnt["DS"]
    lnV = p.cnt["V"]
    for tb in range(16):
        p.w("sync", "T", tcnt)
        gv2 = io["hA"].rearrange("(c p) t -> p c t", p=128)[:, :, tb * 512:(tb + 1) * 512]
        p.do("sync", lambda e, gv2=gv2: e.dma_start(
            out=gblk.rearrange("p (c t) -> p c t", c=32), in_=gv2), "DS", 16)
        xv2 = io["x1T"].rearrange("(c p) t -> p c t", p=128)[:, :, tb * 512:(tb + 1) * 512]
        dcnt = p.do("sync", lambda e, xv2=xv2: e.dma_start(
            out=xres.rearrange("p (c t) -> p c t", c=8), in_=xv2), "DS", 16)
        vlast = None
        for ec in range(8):
            pb = ps[ec % 2]
            p.w("tensor", "DS", dcnt)
            p.w("tensor", "DG", ld)
            p.w("tensor", "V", lnV)
            for c in range(32):
                tcnt = p.do("tensor", lambda e, pb=pb, c=c, ec=ec: e.matmul(
                    pb[:, :], w3[:, c * 1024 + ec * 128: c * 1024 + ec * 128 + 128],
                    gblk[:, c * 512:(c + 1) * 512], start=(c == 0), stop=(c == 31)),
                    ("T" if c == 31 else None), (1 if c == 31 else 0))
            p.w("vector", "T", tcnt)
            p.w("vector", "DS", x2st)
            sv = stile[:, ec * 512:(ec + 1) * 512]
            lnV = p.do("vector", lambda e, pb=pb, sv=sv, ec=ec: e.scalar_tensor_tensor(
                sv, pb[:, :], scal[:, 128 + ec:128 + ec + 1],
                xres[:, ec * 512:(ec + 1) * 512], ALU.mult, ALU.add), "V", 1)
        p.w("scalar", "V", lnV)
        x2st = emit_ffn_ln512(p, io, sb, cst, ps, stile, sqf, tb, "x2T")
        lnV = p.cnt["V"]


def emit_ffn_ln512(p, io, sb, cst, ps, stile, sqf, tb, dst):
    onesb, onesf = cst["onesb"], cst["onesfsb"]
    nw = 512
    r_mu, r_m2, r_rs = sb["r_mu"][0:1, 0:nw], sb["r_m2"][0:1, 0:nw], sb["r_rs"][0:1, 0:nw]
    tst = None
    for fc in range(8):
        sv = stile[:, fc * nw:(fc + 1) * nw]
        aa = p.do("scalar", lambda e, fc=fc, sv=sv: e.activation(
            sqf[:, fc * nw:(fc + 1) * nw] if fc < 8 else None, sv, AF.Square), "A", 1)
        p.w("tensor", "A", aa)
        p.do("tensor", lambda e, sv=sv, fc=fc: e.matmul(
            ps[2][0:1, 0:nw], onesb[:, 0:1], sv, start=(fc == 0), stop=(fc == 7)))
        tst = p.do("tensor", lambda e, fc=fc: e.matmul(
            ps[3][0:1, 0:nw], onesf[:, 0:1], sqf[:, fc * nw:(fc + 1) * nw],
            start=(fc == 0), stop=(fc == 7)), "T", 1)
    p.w("vector", "T", tst)
    p.do("vector", lambda e: e.tensor_scalar(r_mu, ps[2][0:1, 0:nw], 1.0 / 1024, None, ALU.mult))
    p.do("vector", lambda e: e.tensor_scalar(r_m2, ps[3][0:1, 0:nw], 1.0 / 1024, None, ALU.mult))
    p.do("vector", lambda e: e.tensor_tensor(r_rs, r_mu, r_mu, ALU.mult))
    p.do("vector", lambda e: e.tensor_tensor(r_m2, r_m2, r_rs, ALU.subtract))
    vv = p.do("vector", lambda e: e.tensor_scalar(r_m2, r_m2, 1e-5, None, ALU.add), "V", 1)
    p.w("scalar", "V", vv)
    p.do("scalar", lambda e: e.activation(r_rs, r_m2, AF.Ln))
    aa = p.do("scalar", lambda e: e.activation(r_rs, r_rs, AF.Exp, scale=-0.5), "A", 1)
    p.w("tensor", "A", aa)
    p.do("tensor", lambda e: e.matmul(ps[2][:, 0:nw], onesf[0:1, 0:128], r_mu, start=True, stop=True))
    tb2 = p.do("tensor", lambda e: e.matmul(ps[3][:, 0:nw], onesf[0:1, 0:128], r_rs, start=True, stop=True), "T", 1)
    p.w("vector", "T", tb2)
    stD = p.cnt["DS"]
    dual = (dst == "x2T" and "x2Tf" in io)
    for fc in range(8):
        sv = stile[:, fc * nw:(fc + 1) * nw]
        ov = sb["lnout"][:, fc * nw:(fc + 1) * nw]
        if dual:
            ovf = sb["lnoutf"]
            p.w("vector", "DS", stD)
            p.do("vector", lambda e, sv=sv, ovf=ovf: e.tensor_tensor(ovf, sv, ps[2][:, 0:nw], ALU.subtract))
            lastf = p.do("vector", lambda e, ovf=ovf: e.tensor_tensor(ovf, ovf, ps[3][:, 0:nw], ALU.mult), "V", 1)
            p.w("sync", "V", lastf)
            stD = p.do("sync", lambda e, fc=fc, tb=tb, ovf=ovf: e.dma_start(
                out=io["x2Tf"][fc * 128:(fc + 1) * 128, tb * 512:(tb + 1) * 512], in_=ovf), "DS", 16)
        p.do("vector", lambda e, sv=sv, ov=ov: e.tensor_tensor(ov, sv, ps[2][:, 0:nw], ALU.subtract))
        last = p.do("vector", lambda e, ov=ov: e.tensor_tensor(ov, ov, ps[3][:, 0:nw], ALU.mult), "V", 1)
        p.w("sync", "V", last)
        stD = p.do("sync", lambda e, fc=fc, tb=tb, ov=ov, dst=dst: e.dma_start(
            out=io[dst][fc * 128:(fc + 1) * 128, tb * 512:(tb + 1) * 512], in_=ov), "DS", 16)
    return stD




def emit_l1qkv(p, io, AR1, AR2, AR3, ARF, ARS, sb, cst, ps, b):
    onesb, onesf, lup, gmats = cst["onesb"], cst["onesfsb"], cst["lup"], cst["gmats"]
    probes = sb["probes"]
    x2blk = AR3[:, 0:4096]
    ev = AR3[:, 4096:5120]
    ohP = AR3[:, 5120:11264]
    ranks = sb["ranks"]
    qrtf = AR2[:, 0:32768].bitcast(F32)
    tptf = ARF[:, 0:32]
    sptf = ARF[:, 32:64]
    cnt1 = ARF[:, 64:65]
    gst1 = ARF[:, 65:66]
    wqf = ARF[:, 128:1152]
    rext = ARF[:, 128:640]
    projsb = ARF[:, 1152:4224]
    diag = ARF[:, 4224:6272]
    wq1 = sb["wq1"]; wk1 = sb["wk1"]; wv1 = sb["wv1"]; brow1 = sb["brow1"]

    if b == 0:
        p.w("gpsimd", "T", p.cnt["T"])
        for nm, dstv in (("l1_Wq", wq1), ("l1_Wk", wk1), ("l1_Wv", wv1)):
            for c in range(8):
                p.do("gpsimd", lambda e, d=dstv[:, c * 1024:(c + 1) * 1024],
                     s=io[nm][c * 128:(c + 1) * 128, :]: e.dma_start(out=d, in_=s),
                     "DG", 16)
        for i, nm in enumerate(("l1_bq_row", "l1_bk_row", "l1_bv_row")):
            p.do("gpsimd", lambda e, d=brow1[0:1, i * 1024:(i + 1) * 1024],
                 s=io[nm][0:1, :]: e.dma_start(out=d, in_=s), "DG", 16)
        for h in range(16):
            p.do("sync", lambda e, h=h: e.dma_start(
                out=probes[(h % 2) * 64:(h % 2) * 64 + 64, (h // 2) * 192:(h // 2) * 192 + 192],
                in_=io["l1_probes"][h, :, :]), "DS", 16)
        p.do("sync", lambda e: e.dma_start(out=sb["bqpp1"], in_=io["l1_bq_pp"][:, :]), "DS", 16)
    wload = p.cnt["DG"]
    sload = p.cnt["DS"]
    wmap = (wq1, wk1, wv1)

    evacA = p.cnt["A"]
    tcnt = p.cnt["T"]
    dcnt = p.cnt["DS"]
    est = p.cnt["DS"]
    p.w("tensor", "V", p.cnt["V"])
    p.w("tensor", "A", p.cnt["A"])
    for tau in range(32):
        blkc = tau // 4
        if tau % 4 == 0:
            p.w("sync", "T", tcnt)
            for c in range(8):
                dcnt = p.do("sync", lambda e, c=c, blkc=blkc: e.dma_start(
                    out=x2blk[:, c * 512:(c + 1) * 512],
                    in_=io["x2T"][c * 128:(c + 1) * 128,
                                  b * S + blkc * 512: b * S + blkc * 512 + 512]), "DS", 16)
        waves = [[(0, 0), (0, 1), (1, 0), (1, 1)], [(2, 0), (2, 1)]]
        for wi, wave in enumerate(waves):
            p.w("tensor", "A", evacA)
            p.w("tensor", "DS", dcnt)
            p.w("tensor", "DG", wload)
            for j, (i, fh) in enumerate(wave):
                pb = ps[j]
                for c in range(8):
                    lhsT = x2blk[:, c * 512 + (tau % 4) * 128: c * 512 + (tau % 4) * 128 + 128]
                    rhs = wmap[i][:, c * 1024 + fh * 512: c * 1024 + fh * 512 + 512]
                    p.do("tensor", lambda e, pb=pb, lhsT=lhsT, rhs=rhs, c=c:
                         e.matmul(pb[:, :], lhsT, rhs, start=(c == 0), stop=False))
                tcnt = p.do("tensor", lambda e, pb=pb, i=i, fh=fh: e.matmul(
                    pb[:, :], onesb[0:1, 0:128],
                    brow1[0:1, i * 1024 + fh * 512: i * 1024 + fh * 512 + 512],
                    start=False, stop=True), "T", 1)
            p.w("scalar", "T", tcnt)
            for j, (i, fh) in enumerate(wave):
                evv = ev[:, (j % 2) * 512:(j % 2) * 512 + 512]
                p.w("scalar", "DS", est)
                evacA = p.do("scalar", lambda e, j=j, evv=evv: e.activation(
                    evv, ps[j][:, :], AF.Identity), "A", 1)
                p.w("sync", "A", evacA)
                ov = io["qkvs"].rearrange("b h t r -> b h t r")[
                    b, fh * 8:(fh + 1) * 8, tau * 128:(tau + 1) * 128, i * 64:(i + 1) * 64]                     .rearrange("h t r -> t h r")
                est = p.do("sync", lambda e, ov=ov, evv=evv: e.dma_start(
                    out=ov, in_=evv.rearrange("p (h r) -> p h r", h=8)), "DS", 16)

    x2f = AR3[:, 0:8192].bitcast(F32)
    for hh2 in range(2):
        p.w("tensor", "V", p.cnt["V"])
        p.w("sync", "V", p.cnt["V"])
        tq = p.cnt["T"]
        for fcl in range(4):
            fc = hh2 * 4 + fcl
            p.w("sync", "T", tcnt)
            wqld = p.do("sync", lambda e, fc=fc: e.dma_start(
                out=wqf.rearrange("p (c n) -> p c n", n=128),
                in_=io["l1_Wq"].rearrange("(c p) f -> p c f", p=128)[:, :, fc * 128:(fc + 1) * 128]), "DS", 16)
            for tb8 in range(8):
                p.w("sync", "T", tcnt)
                for c in range(8):
                    dcnt = p.do("sync", lambda e, c=c, tb8=tb8: e.dma_start(
                        out=x2f[:, c * 512:(c + 1) * 512],
                        in_=io["x2Tf"][c * 128:(c + 1) * 128,
                                       b * S + tb8 * 512: b * S + tb8 * 512 + 512]), "DS", 16)
                pb = ps[tb8 % 2]
                p.w("tensor", "DS", dcnt)
                p.w("tensor", "DS", wqld)
                p.w("tensor", "A", evacA)
                for c in range(8):
                    tcnt = p.do("tensor", lambda e, pb=pb, c=c: e.matmul(
                        pb[:, :], wqf[:, c * 128:(c + 1) * 128],
                        x2f[:, c * 512:(c + 1) * 512], start=(c == 0), stop=(c == 7)),
                        ("T" if c == 7 else None), (1 if c == 7 else 0))
                p.w("scalar", "T", tcnt)
                p.w("scalar", "DS", sload)
                evacA = p.do("scalar", lambda e, pb=pb, fc=fc, fcl=fcl, tb8=tb8: e.activation(
                    qrtf[:, fcl * 4096 + tb8 * 512: fcl * 4096 + tb8 * 512 + 512],
                    pb[:, :], AF.Identity, bias=sb["bqpp1"][:, fc:fc + 1]), "A", 1)
        qdone = evacA
        if b == 0 and hh2 == 0 and "dbg_qrt" in io:
            p.w("sync", "A", qdone)
            p.do("sync", lambda e: e.dma_start(out=io["dbg_qrt"][:, :], in_=qrtf), "DS", 16)
            p.w("tensor", "DS", p.cnt["DS"])
        for fcl in range(4):
            fc = hh2 * 4 + fcl
            for hh in range(2):
                h = fc * 2 + hh
                m96 = ARF[:, 6272:6320]
                pc16 = ARF[:, 6320:6576]
                p.w("tensor", "A", qdone)
                p.w("tensor", "DS", sload)
                ohdone = p.cnt["V"]
                for half in range(2):
                    p.w("tensor", "V", ohdone)
                    pr = p.cnt["A"]
                    for ta in range(16):
                        tau = half * 16 + ta
                        pb = ps[ta % 2]
                        p.w("tensor", "A", pr if ta < 2 else evacA)
                        tcnt = p.do("tensor", lambda e, pb=pb, tau=tau, fcl=fcl, hh=hh, fc=fc:
                                    e.matmul(pb[:, 0:192],
                                             qrtf[hh * 64:hh * 64 + 64,
                                                  fcl * 4096 + tau * 128: fcl * 4096 + tau * 128 + 128],
                                             probes[hh * 64:hh * 64 + 64,
                                                    fc * 192:(fc + 1) * 192],
                                             start=True, stop=True), "T", 1)
                        p.w("scalar", "T", tcnt)
                        evacA = p.do("scalar", lambda e, pb=pb, ta=ta: e.activation(
                            projsb[:, ta * 192:(ta + 1) * 192], pb[:, 0:192],
                            AF.Identity), "A", 1)
                    p.w("vector", "A", evacA)
                    ohh = ohP.rearrange("p (g x) -> p g x", x=192)[:, half * 16:half * 16 + 16, :]
                    p.do("vector", lambda e: e.reduce_max(
                        m96, projsb.rearrange("p (g a) -> p g a", a=64), axis=AX.X))
                    p.do("vector", lambda e: e.reduce_max(
                        pc16.rearrange("p (g c) -> p g c", c=16),
                        projsb.rearrange("p (g x) -> p g x", x=192)[:, :, 0:64]
                        .rearrange("p g (a c) -> p g a c", a=4).transpose([0, 1, 3, 2]),
                        axis=AX.X))
                    fce = p.do("vector", lambda e: e.tensor_copy(rext[:, 0:48], m96), "V", 1)
                    p.w("vector", "V", fce)
                    m0 = m96.rearrange("p (g l) -> p g l", l=3)
                    p.do("vector", lambda e, ohh=ohh: e.tensor_tensor(
                        ohh[:, :, 0:64].rearrange("p g (a c) -> p g a c", a=4),
                        pc16.rearrange("p (g c) -> p g c", c=16).unsqueeze(2).broadcast_to([128, 16, 4, 16]),
                        m0[:, :, 0:1].unsqueeze(3).broadcast_to([128, 16, 4, 16]),
                        ALU.is_ge))
                    for lvl in (1, 2):
                        p.do("vector", lambda e, lvl=lvl, ohh=ohh: e.tensor_tensor(
                            ohh[:, :, lvl * 64:(lvl + 1) * 64],
                            projsb.rearrange("p (g x) -> p g x", x=192)[:, :, lvl * 64:(lvl + 1) * 64],
                            m0[:, :, lvl:lvl + 1].broadcast_to([128, 16, 64]),
                            ALU.is_ge))
                    ohdone = p.do("vector", lambda e: e.tensor_copy(rext[:, 0:192], ohP[:, 0:192]), "V", 1)
                    if b == 0 and h == 0 and "dbg_oh" in io:
                        p.w("sync", "V", ohdone)
                        p.do("sync", lambda e, half=half: e.dma_start(
                            out=io["dbg_proj"][:, half * 3072:(half + 1) * 3072], in_=projsb), "DS", 16)
                        p.w("vector", "DS", p.cnt["DS"])
                        p.w("scalar", "DS", p.cnt["DS"])
                if b == 0 and h == 0 and "dbg_oh" in io:
                    p.w("sync", "V", ohdone)
                    p.do("sync", lambda e: e.dma_start(out=io["dbg_oh"][:, 0:6144], in_=ohP), "DS", 16)
                    p.w("vector", "DS", p.cnt["DS"])
                for lvl in range(3):
                    csize = 4 if lvl == 0 else 1
                    ohv = ohP.rearrange("p (g x) -> p g x", x=192)[:, :, lvl * 64:(lvl + 1) * 64]
                    p.w("tensor", "V", ohdone)
                    for tau in range(32):
                        tcnt = p.do("tensor", lambda e, tau=tau, ohv=ohv: e.matmul(
                            ps[3][0:64, tau:tau + 1], ohv[:, tau, :], onesb[:, 0:1],
                            start=True, stop=True), "T", 1)
                    p.w("vector", "T", tcnt)
                    vv = p.do("vector", lambda e: e.tensor_copy(tptf[0:64, :], ps[3][0:64, 0:32]), "V", 1)
                    p.w("vector", "V", vv)
                    sc = p.do("vector", lambda e: e.tensor_tensor_scan(
                        sptf[0:64, :], onesf[0:64, 0:32], tptf[0:64, :], 0.0,
                        ALU.mult, ALU.add), "V", 1)
                    p.w("vector", "V", sc)
                    vv = p.do("vector", lambda e: e.tensor_copy(cnt1[0:64, :], sptf[0:64, 31:32]), "V", 1)
                    p.w("vector", "V", vv)
                    p.do("vector", lambda e: e.tensor_tensor(
                        sptf[0:64, :], sptf[0:64, :], tptf[0:64, :], ALU.subtract))
                    vv = p.do("vector", lambda e: e.tensor_copy(rext[0:64, 0:32], sptf[0:64, :]), "V", 1)
                    p.w("tensor", "V", vv)
                    tg = p.do("tensor", lambda e, lvl=lvl: e.matmul(
                        ps[3][0:64, 40:41], gmats[:, lvl * 64:(lvl + 1) * 64], cnt1[0:64, :],
                        start=True, stop=True), "T", 1)
                    p.w("vector", "T", tg)
                    vv = p.do("vector", lambda e: e.tensor_copy(gst1[0:64, :], ps[3][0:64, 40:41]), "V", 1)
                    p.w("vector", "V", vv)
                    vv = p.do("vector", lambda e: e.tensor_scalar(
                        sptf[0:64, :], sptf[0:64, :], gst1[0:64, 0:1], None, ALU.add), "V", 1)
                    p.w("vector", "V", vv)
                    dv = p.do("vector", lambda e: e.tensor_tensor(
                        diag.rearrange("p (g x) -> p g x", x=64)[0:64, :, :],
                        cst["identf"][0:64, 0:64].unsqueeze(1).broadcast_to([64, 32, 64]),
                        sptf[0:64, :].unsqueeze(2).broadcast_to([64, 32, 64]),
                        ALU.mult), "V", 1)
                    p.w("tensor", "V", dv)
                    for tau in range(32):
                        pa = ps[tau // 8]
                        po = (tau % 8) * 64
                        p.do("tensor", lambda e, pa=pa, po=po, tau=tau, ohv=ohv: e.matmul(
                            pa[:, po:po + 64], lup[:, :], ohv[:, tau, :],
                            start=True, stop=False))
                        tcnt = p.do("tensor", lambda e, pa=pa, po=po, tau=tau: e.matmul(
                            pa[:, po:po + 64], onesf[0:64, 0:128],
                            diag.rearrange("p (g x) -> p g x", x=64)[0:64, tau, :],
                            start=False, stop=True), "T", 1)
                    p.w("vector", "T", tcnt)
                    rsb = ranks.bitcast(F32)[:, ((b * 16 + h) * 3 + lvl) * 32:
                                             ((b * 16 + h) * 3 + lvl) * 32 + 32]
                    for bank in range(4):
                        p.do("vector", lambda e, bank=bank, ohv=ohv: e.tensor_tensor(
                            rext.rearrange("p (g x) -> p g x", x=64),
                            ps[bank][:, :].rearrange("p (g x) -> p g x", x=64),
                            ohv[:, bank * 8:(bank + 1) * 8, :],
                            ALU.mult))
                        rr = p.do("vector", lambda e, bank=bank, rsb=rsb: e.reduce_sum(
                            rsb[:, bank * 8:(bank + 1) * 8],
                            rext.rearrange("p (g x) -> p g x", x=64), axis=AX.X), "V", 1)
                    p.w("vector", "V", rr)
                    if b == 0 and h == 0 and lvl == 0 and "dbg_cnt" in io:
                        fz = p.do("vector", lambda e: e.tensor_copy(rext[:, 33:65], tptf[:, :]), "V", 1)
                        p.w("sync", "V", fz)
                        p.do("sync", lambda e: e.dma_start(out=io["dbg_cnt"][:, 0:32], in_=tptf[:, :]), "DS", 16)
                        p.do("sync", lambda e: e.dma_start(out=io["dbg_cnt"][:, 32:64], in_=sptf[:, :]), "DS", 16)
                        p.do("sync", lambda e: e.dma_start(out=io["dbg_cnt"][:, 64:65], in_=cnt1[:, :], allow_slow_non_contiguous=True), "DS", 16)
                        p.do("sync", lambda e: e.dma_start(out=io["dbg_cnt"][:, 65:66], in_=gst1[:, :], allow_slow_non_contiguous=True), "DS", 16)
                        p.do("sync", lambda e, rsb=rsb: e.dma_start(out=io["dbg_cnt"][:, 66:98], in_=rsb), "DS", 16)
                        p.w("vector", "DS", p.cnt["DS"])
                    fr = p.do("vector", lambda e, rsb=rsb, csize=csize: e.tensor_scalar(
                        rext[:, 0:32], rsb, 1.0 / csize, 4095.0, ALU.mult, ALU.min), "V", 1)
                    p.w("vector", "V", fr)
                    ru = ranks[:, ((b * 16 + h) * 3 + lvl) * 32: ((b * 16 + h) * 3 + lvl) * 32 + 32]
                    ohdone = p.do("vector", lambda e, ru=ru: e.tensor_copy(ru, rext[:, 0:32]), "V", 1)


def emit_dump_ranks(p, io, sb):
    p.w("sync", "V", p.cnt["V"])
    p.do("sync", lambda e: e.dma_start(out=io["dbg_ranks"][:, :], in_=sb["ranks"][:, :]), "DS", 16)




def lvl_spans(lvl):
    cs = (256, 64, 16)[lvl]
    spans = []
    for kb in range(32):
        if lvl == 0:
            g = kb // 2
            a, bnd = g * 256, min(g * 256 + 512, 4096)
        else:
            a, bnd = kb * 128, min(kb * 128 + 128 + cs, 4096)
        spans.append((a, bnd))
    return spans


def emit_l1attn(p, io, AR1, AR2, AR3, ARF, ARS, sb, cst, ps, b):
    ranks = sb["ranks"]
    onesb, ident = cst["onesb"], cst["ident"]
    qkvrows = AR2[:, 0:8192]
    qkT = AR1[:, 0:4096]
    kvT = AR1[:, 4096:8192]
    v65 = AR1[:, 8192:10272]
    expT = AR1[:, 10272:11296]
    out_tok = AR1[:, 11296:13344]
    outsb = AR1[:, 13344:13856]
    acc = AR1[:, 13856:17952]
    rz = ARS[:, 160:168]

    p.w("gpsimd", "V", p.cnt["V"])
    p.w("gpsimd", "T", p.cnt["T"])
    if b == 0:
        p.do("gpsimd", lambda e: e.memset(v65[:, :], 1.0), "G", 1)
    mst = p.cnt["G"]

    ld_qkv = p.cnt["DG"]
    vld = [p.cnt["DS"]]
    ebuse = [p.cnt["T"], p.cnt["T"]]
    sc_done = p.cnt["DG"]
    tr_done = p.cnt["DA"]
    g_done = p.cnt["DG"]
    ot_st = p.cnt["DS"]
    acst = p.cnt["DS"]
    evacV = p.cnt["V"]
    evacA = p.cnt["A"]
    tcnt = p.cnt["T"]
    p.w("tensor", "V", p.cnt["V"])

    for h in range(16):
        u = (b * 16 + h) * 3
        p.w("gpsimd", "DA", tr_done)
        src = io["qkvs"][b, h, :, :].rearrange("(c p) r -> p c r", p=128)
        ld_qkv = p.do("gpsimd", lambda e, src=src: e.dma_start(
            out=qkvrows.rearrange("p (c r) -> p c r", r=256), in_=src), "DG", 16)
        for lvl in range(3):
            p.w("gpsimd", "DG", ld_qkv)
            p.w("gpsimd", "DA", tr_done)
            p.w("gpsimd", "DS", vld[0])
            for tau in range(32):
                sc_done = p.do("gpsimd", lambda e, tau=tau, u=u, lvl=lvl: e.indirect_dma_start(
                    out=io["qkv_sorted"][:, :],
                    out_offset=IndirectOffsetOnAxis(
                        ranks[:, (u + lvl) * 32 + tau:(u + lvl) * 32 + tau + 1], axis=0),
                    in_=qkvrows[:, tau * 256:(tau + 1) * 256], in_offset=None), "DG", 16)
            p.w("scalar", "DG", sc_done)
            p.w("scalar", "T", tcnt)
            for w in range(8):
                p.do("scalar", lambda e, w=w: e.dma_start(
                    out=qkT[:, w * 512:(w + 1) * 512],
                    in_=io["qkv_sorted"][w * 512:(w + 1) * 512, 0:128],
                    transpose=True), "DA", 16)
                tr_done = p.do("scalar", lambda e, w=w: e.dma_start(
                    out=kvT[:, w * 512:(w + 1) * 512],
                    in_=io["qkv_sorted"][w * 512:(w + 1) * 512, 64:192],
                    transpose=True), "DA", 16)
            p.w("sync", "DG", sc_done)
            p.w("sync", "G", mst)
            p.w("sync", "T", tcnt)
            vld[0] = p.do("sync", lambda e: e.dma_start(
                out=v65.rearrange("p (c r) -> p c r", r=65)[:, :, 0:64],
                in_=io["qkv_sorted"][:, 128:192].rearrange("(c p) r -> p c r", p=128)), "DS", 16)
            if b == 0 and h == 0 and lvl == 0 and "dbg_qsort" in io:
                p.w("sync", "DG", sc_done)
                p.do("sync", lambda e: e.dma_start(out=io["dbg_qsort"][:, :], in_=io["qkv_sorted"][:, :]), "DS", 16)
                p.w("scalar", "DA", 0)
                dqs = p.cnt["DS"]
                p.w("scalar", "DS", dqs)
            spans = lvl_spans(lvl)
            mask = cst["mask1"] if lvl == 1 else (cst["mask2"] if lvl == 2 else None)
            base_evacV = p.cnt["V"]
            win_state = {}
            win_open = [False] * 8
            closers = [max(kb for kb in range(32)
                           if spans[kb][0] < (w + 1) * 512 and spans[kb][1] > w * 512)
                       for w in range(8)]
            widx = 0
            for kb in range(32):
                a, bnd = spans[kb]
                pbs = ps[kb % 2]
                p.w("tensor", "DA", tr_done)
                p.w("tensor", "DS", vld[0])
                p.w("tensor", "A", evacA)
                tcnt = p.do("tensor", lambda e, pbs=pbs, kb=kb, a=a, bnd=bnd: e.matmul(
                    pbs[:, 0:bnd - a], kvT[0:64, kb * 128:(kb + 1) * 128],
                    qkT[0:64, a:bnd], start=True, stop=True), "T", 1)
                p.w("scalar", "T", tcnt)
                p.w("scalar", "T", ebuse[kb % 2])
                eb = expT[:, (kb % 2) * 512:(kb % 2) * 512 + 512]
                evacA = p.do("scalar", lambda e, pbs=pbs, eb=eb, a=a, bnd=bnd: e.activation(
                    eb[:, 0:bnd - a], pbs[:, 0:bnd - a], AF.Exp, scale=0.125), "A", 1)
                if b == 0 and h == 0 and lvl == 0 and kb in (0, 1) and "dbg_win" in io:
                    p.w("sync", "A", evacA)
                    p.do("sync", lambda e, eb=eb, kb=kb, a=a, bnd=bnd: e.dma_start(
                        out=io["dbg_exp"][128 * kb:128 * kb + 128, 0:bnd - a], in_=eb[:, 0:bnd - a]), "DS", 16)
                    p.w("tensor", "DS", p.cnt["DS"])
                    p.w("scalar", "DS", p.cnt["DS"])
                if mask is not None:
                    p.w("vector", "A", evacA)
                    evacV = p.do("vector", lambda e, eb=eb, mask=mask, a=a, bnd=bnd: e.tensor_tensor(
                        eb[:, 0:bnd - a], eb[:, 0:bnd - a], mask[:, 0:bnd - a], ALU.mult), "V", 1)
                    ebsrc = eb
                    p.w("tensor", "V", evacV)
                else:
                    ebsrc = eb
                    p.w("tensor", "A", evacA)
                pos = a
                while pos < bnd:
                    w = pos // 512
                    pe = min(bnd, (w + 1) * 512)
                    pw = ps[2 + (w % 2)]
                    if not win_open[w]:
                        win_open[w] = True
                        p.w("tensor", "V", win_state.get(w - 2, base_evacV))
                        p.do("tensor", lambda e, pw=pw: e.matmul(
                            pw[0:65, :], cst["zz"][0:1, 0:65], cst["zz"][0:1, 0:512],
                            start=True, stop=False))
                    last_piece = (pe >= bnd)
                    tret = p.do("tensor", lambda e, pw=pw, kb=kb, pos=pos, pe=pe,
                                a=a, w=w, ebsrc=ebsrc: e.matmul(
                        pw[0:65, pos - w * 512:pe - w * 512],
                        v65[:, kb * 65:(kb + 1) * 65], ebsrc[:, pos - a:pe - a],
                        start=False, stop=False),
                        "T" if last_piece else None, 1 if last_piece else 0)
                    if tret is not None:
                        ebuse[kb % 2] = tret
                    pos = pe
                for w in range(8):
                    if win_open[w] and closers[w] == kb:
                        pw = ps[2 + (w % 2)]
                        tcl = p.do("tensor", lambda e, pw=pw: e.matmul(
                            pw[0:65, 0:1], cst["zz"][0:1, 0:65], cst["zz"][0:1, 0:1],
                            start=False, stop=True), "T", 1)
                        p.w("vector", "T", tcl)
                        p.w("vector", "T", tcnt)
                        cpv = p.do("vector", lambda e, pw=pw: e.tensor_copy(
                            outsb[0:65, 0:512], pw[0:65, :]), "V", 1)
                        if b == 0 and h == 0 and lvl == 0 and w == 0 and "dbg_win" in io:
                            p.w("sync", "V", cpv)
                            p.do("sync", lambda e: e.dma_start(out=io["dbg_win"][0:65, :], in_=outsb[0:65, 0:512]), "DS", 16)
                            p.w("tensor", "DS", p.cnt["DS"])
                        p.w("tensor", "V", cpv)
                        pbf = ps["bfA"] if (w % 2) == 0 else ps["bfB"]
                        for tt in range(4):
                            sm = ("T", 1) if tt == 3 else (None, 0)
                            ttc = p.do("tensor", lambda e, tt=tt, pbf=pbf: e.transpose(
                                pbf[:, tt * 128:tt * 128 + 65],
                                outsb[0:65, tt * 128:(tt + 1) * 128],
                                ident[0:65, 0:65]), sm[0], sm[1])
                        p.w("vector", "T", ttc)
                        for tt in range(4):
                            rc = p.do("vector", lambda e, tt=tt, pbf=pbf: e.reciprocal(
                                rz[:, tt:tt + 1], pbf[:, tt * 128 + 64:tt * 128 + 65]), "V", 1)
                            p.w("vector", "V", rc)
                            evacV = p.do("vector", lambda e, tt=tt, w=w, pbf=pbf: e.tensor_scalar(
                                out_tok[:, (w * 4 + tt) * 64:(w * 4 + tt) * 64 + 64],
                                pbf[:, tt * 128:tt * 128 + 64],
                                rz[:, tt:tt + 1], 1.0 / 3.0, ALU.mult, ALU.mult), "V", 1)
                        win_state[w] = evacV
            tcnt = p.cnt["T"]
            if b == 0 and h == 0 and lvl == 0 and "dbg_qsort" in io:
                p.w("sync", "DA", tr_done)
                p.do("sync", lambda e: e.dma_start(out=io["dbg_qkt"][:, 0:4096], in_=qkT), "DS", 16)
                p.do("sync", lambda e: e.dma_start(out=io["dbg_qkt"][:, 4096:8192], in_=kvT), "DS", 16)
                p.w("sync", "V", evacV)
                p.do("sync", lambda e: e.dma_start(out=io["dbg_otok2"][:, :], in_=out_tok), "DS", 16)
            p.w("sync", "V", evacV)
            ot_st = p.do("sync", lambda e: e.dma_start(
                out=io["outsorted"].rearrange("(c p) r -> p c r", p=128),
                in_=out_tok.rearrange("p (c r) -> p c r", r=64)), "DS", 16)
            p.w("gpsimd", "DS", ot_st)
            if lvl == 0:
                p.w("gpsimd", "DS", acst)
            for tau in range(32):
                kw = dict(compute_op=ALU.add, oob_is_err=True) if lvl > 0 else {}
                g_done = p.do("gpsimd", lambda e, tau=tau, u=u, lvl=lvl, h=h, kw=kw:
                              e.indirect_dma_start(
                    out=acc[:, tau * 128 + (h % 2) * 64: tau * 128 + (h % 2) * 64 + 64],
                    out_offset=None,
                    in_=io["outsorted"][:, :],
                    in_offset=IndirectOffsetOnAxis(
                        ranks[:, (u + lvl) * 32 + tau:(u + lvl) * 32 + tau + 1], axis=0),
                    **kw), "DG", 16)
            if b == 0 and h == 0 and lvl == 0 and "dbg_acc3" in io:
                p.w("sync", "DG", g_done)
                p.do("sync", lambda e: e.dma_start(out=io["dbg_acc3"][:, :], in_=acc), "DS", 16)
            p.w("sync", "DG", g_done)
        if h == 1 and b == 0 and "dbg_acc2" in io:
            p.w("sync", "DG", g_done)
            p.do("sync", lambda e: e.dma_start(out=io["dbg_acc2"][:, :], in_=acc), "DS", 16)
        if h % 2 == 1:
            p.w("sync", "DG", g_done)
            acst = p.do("sync", lambda e, b=b, h=h: e.dma_start(
                out=io["attn1tok"][b, h // 2, :, :].rearrange("(c p) r -> p c r", p=128),
                in_=acc.rearrange("p (c r) -> p c r", r=128)), "DS", 16)


def emit_l1out(p, io, AR1, AR2, AR3, ARF, ARS, sb, cst, ps, b):
    wo1 = AR1[:, 18432:26624]
    attf = AR3[:, 0:4096]
    xres = AR3[:, 4096:8192]
    stile = AR3[:, 8192:12288]
    sqf = ARF[:, 0:4096]
    bo1 = sb["bo1pp"]
    if b == 0:
        p.w("gpsimd", "T", p.cnt["T"])
        for c in range(8):
            p.do("gpsimd", lambda e, c=c: e.dma_start(
                out=wo1[:, c * 1024:(c + 1) * 1024],
                in_=io["l1_Wo"][c * 128:(c + 1) * 128, :]), "DG", 16)
        p.do("sync", lambda e: e.dma_start(out=bo1, in_=io["l1_bo_pp"][:, :]), "DS", 16)
    wld = p.cnt["DG"]
    bld = p.cnt["DS"]
    tcnt = p.cnt["T"]
    lnV = p.cnt["V"]
    x3st = p.cnt["DS"]
    p.w("tensor", "V", p.cnt["V"])
    for tb8 in range(8):
        p.w("scalar", "T", tcnt)
        p.w("scalar", "DS", x3st)
        for pr in range(8):
            dcnt = p.do("scalar", lambda e, pr=pr, tb8=tb8, b=b: e.dma_start(
                out=attf[:, pr * 512:(pr + 1) * 512],
                in_=io["attn1tok"][b, pr, tb8 * 512:(tb8 + 1) * 512, :],
                transpose=True), "DA", 16)
        p.w("sync", "T", tcnt)
        for c in range(8):
            rcnt = p.do("sync", lambda e, c=c, tb8=tb8, b=b: e.dma_start(
                out=xres[:, c * 512:(c + 1) * 512],
                in_=io["x2T"][c * 128:(c + 1) * 128,
                              b * S + tb8 * 512: b * S + tb8 * 512 + 512]), "DS", 16)
        for fc in range(8):
            pb = ps[fc % 2]
            p.w("tensor", "DA", dcnt)
            p.w("tensor", "DS", rcnt)
            p.w("tensor", "DG", wld)
            p.w("tensor", "V", lnV)
            for ec in range(8):
                tcnt = p.do("tensor", lambda e, pb=pb, ec=ec, fc=fc: e.matmul(
                    pb[:, :], wo1[:, ec * 1024 + fc * 128: ec * 1024 + fc * 128 + 128],
                    attf[:, ec * 512:(ec + 1) * 512], start=(ec == 0), stop=(ec == 7)),
                    "T" if ec == 7 else None, 1 if ec == 7 else 0)
            p.w("vector", "T", tcnt)
            p.w("vector", "DS", x3st)
            sv = stile[:, fc * 512:(fc + 1) * 512]
            p.do("vector", lambda e, pb=pb, sv=sv, fc=fc: e.tensor_scalar(
                sv, pb[:, :], bo1[:, fc:fc + 1], None, ALU.add))
            lnV = p.do("vector", lambda e, sv=sv, fc=fc: e.tensor_tensor(
                sv, sv, xres[:, fc * 512:(fc + 1) * 512], ALU.add), "V", 1)
        p.w("scalar", "V", lnV)
        sb["lnout"] = AR3[:, 0:4096]
        x3st = emit_ffn_ln512(p, io, sb, cst, ps, stile, sqf, b * 8 + tb8, "x3T")
        lnV = p.cnt["V"]




def emit_wquant(p, io, ARF, AR3, ARS, sb, cst, ps, src, dst, nch, msc_col):
    onesf = cst["onesfsb"]
    wt = ARF[:, 0:4096]
    q = AR3[:, 0:4096]
    wabs = sb["wabs"]
    msc = sb["msc"]
    wsc = sb["wsc"]
    N = io[src].shape()[1] if False else 4096 if nch == 8 else 1024
    vv = p.cnt["V"]
    dcnt = p.cnt["DS"]
    for c in range(nch):
        p.w("sync", "V", vv)
        dcnt = p.do("sync", lambda e, c=c: e.dma_start(
            out=wt[:, 0:N], in_=io[src][c * 128:(c + 1) * 128, :]), "DS", 16)
        p.w("vector", "DS", dcnt)
        rv = p.do("vector", lambda e: e.reduce_sum(
            wabs[:, 1:2], wt[:, 0:N], axis=AX.X, apply_absolute_value=True), "V", 1)
        p.w("vector", "V", rv)
        if c == 0:
            vv = p.do("vector", lambda e: e.tensor_copy(wabs[:, 0:1], wabs[:, 1:2]), "V", 1)
        else:
            vv = p.do("vector", lambda e: e.tensor_tensor(
                wabs[:, 0:1], wabs[:, 0:1], wabs[:, 1:2], ALU.add), "V", 1)
    p.w("tensor", "V", vv)
    tm = p.do("tensor", lambda e: e.matmul(ps[2][0:1, 0:1], onesf[:, 0:1],
                                           wabs[:, 0:1], start=True, stop=True), "T", 1)
    p.w("vector", "T", tm)
    p.do("vector", lambda e: e.tensor_scalar(
        msc[0:1, msc_col:msc_col + 1], ps[2][0:1, 0:1], 1.0 / (nch * 128 * N),
        1e-5, ALU.mult, ALU.max))
    vv = p.do("vector", lambda e: e.engine_nop(), "V", 1)
    p.w("vector", "V", vv)
    rc = p.do("vector", lambda e: e.reciprocal(
        msc[0:1, msc_col + 2:msc_col + 3], msc[0:1, msc_col:msc_col + 1]), "V", 1)
    p.w("vector", "V", rc)
    p.w("tensor", "V", rc)
    tb = p.do("tensor", lambda e: e.matmul(ps[3][:, 0:1], onesf[0:1, 0:128],
                                           msc[0:1, msc_col + 2:msc_col + 3],
                                           start=True, stop=True), "T", 1)
    p.w("vector", "T", tb)
    cb = p.do("vector", lambda e: e.tensor_copy(wsc[:, 0:1], ps[3][:, 0:1]), "V", 1)
    p.w("vector", "V", cb)
    st = p.cnt["DS"]
    for c in range(nch):
        p.w("sync", "V", vv)
        dcnt = p.do("sync", lambda e, c=c: e.dma_start(
            out=wt[:, 0:N], in_=io[src][c * 128:(c + 1) * 128, :]), "DS", 16)
        p.w("vector", "DS", dcnt)
        p.w("vector", "DS", st)
        p.do("vector", lambda e: e.tensor_scalar(
            wt[:, 0:N], wt[:, 0:N], wsc[:, 0:1], None, ALU.mult))
        p.do("vector", lambda e: e.tensor_scalar(
            wt[:, 0:N], wt[:, 0:N], MAGIC, -MAGIC, ALU.add, ALU.add))
        vv = p.do("vector", lambda e: e.tensor_scalar(
            q[:, 0:N], wt[:, 0:N], 1.0, -1.0, ALU.min, ALU.max), "V", 1)
        p.w("sync", "V", vv)
        st = p.do("sync", lambda e, c=c: e.dma_start(
            out=io[dst][c * 128:(c + 1) * 128, :], in_=q[:, 0:N]), "DS", 16)


def emit_ffn1(p, io, AR1, AR2, AR3, ARF, ARS, sb, cst, ps):
    onesb, onesf = cst["onesb"], cst["onesfsb"]
    msc = sb["msc"]
    emit_wquant(p, io, ARF, AR3, ARS, sb, cst, ps, "f1_w1T", "wq1q", 8, 0)
    emit_wquant(p, io, ARF, AR3, ARS, sb, cst, ps, "f1_w2T", "wq2q", 32, 4)
    cw = sb["convw"]; cb2 = sb["convb"]
    p.do("sync", lambda e: e.dma_start(out=cw, in_=io["f1_convw"][:, :]), "DS", 16)
    p.do("sync", lambda e: e.dma_start(out=cb2, in_=io["f1_convb"][:, :]), "DS", 16)
    cld = p.cnt["DS"]

    p.w("sync", "DA", p.cnt["DA"])
    p.w("scalar", "DA", p.cnt["DA"])
    xblk = AR3[:, 0:4096]
    xq = AR3[:, 4096:8192]
    wsl = AR1[:, 0:8192]
    m1 = ARF[:, 0:512]
    m2 = ARF[:, 512:1024]
    scb = ARF[:, 1024:1536]
    dqb = ARF[:, 1536:2048]
    hrow = ARF[0:1, 2048:2560]
    ev = AR3[:, 8192:9216]

    tcnt = p.cnt["T"]
    evacV = p.cnt["V"]
    dh = p.cnt["DS"]
    ld = p.cnt["DG"]
    lastdq = [p.cnt["V"], p.cnt["V"]]
    p.w("tensor", "V", p.cnt["V"])
    p.w("tensor", "A", p.cnt["A"])
    for tb in range(16):
        p.w("sync", "T", tcnt)
        for c in range(8):
            dcnt = p.do("sync", lambda e, c=c, tb=tb: e.dma_start(
                out=xblk[:, c * 512:(c + 1) * 512],
                in_=io["x3T"][c * 128:(c + 1) * 128, tb * 512:(tb + 1) * 512]), "DS", 16)
        p.w("vector", "DS", dcnt)
        p.w("vector", "V", evacV)
        p.do("vector", lambda e: e.scalar_tensor_tensor(
            m1[:, :], xblk[:, 0:512], -1.0, xblk[:, 0:512], ALU.mult, ALU.max))
        for c in range(1, 8):
            p.do("vector", lambda e, c=c: e.tensor_max(
                m1[:, :], m1[:, :], xblk[:, c * 512:(c + 1) * 512]))
            p.do("vector", lambda e, c=c: e.scalar_tensor_tensor(
                m1[:, :], xblk[:, c * 512:(c + 1) * 512], -1.0, m1[:, :],
                ALU.mult, ALU.max))
        vv = p.do("vector", lambda e: e.engine_nop(), "V", 1)
        nh = 64
        while nh >= 1:
            vv = p.do("vector", lambda e, nh=nh: e.tensor_copy(
                scb[0:min(2 * nh, 128), 0:512], m1[0:min(2 * nh, 128), :]), "V", 1)
            p.w("sync", "V", vv)
            sh = p.do("sync", lambda e, nh=nh: e.dma_start(
                out=m2[0:max(nh, 1), :], in_=m1[nh:2 * nh, :]), "DS", 16)
            p.w("vector", "DS", sh)
            vv = p.do("vector", lambda e, nh=nh: e.tensor_max(
                m1[0:nh, :], m1[0:nh, :], m2[0:nh, :]), "V", 1)
            p.w("vector", "V", vv)
            nh //= 2
        p.do("vector", lambda e: e.tensor_scalar(
            m1[0:1, :], m1[0:1, :], 1e-5, None, ALU.max))
        vv = p.do("vector", lambda e: e.engine_nop(), "V", 1)
        p.w("vector", "V", vv)
        rc = p.do("vector", lambda e: e.reciprocal(m2[0:1, :], m1[0:1, :]), "V", 1)
        p.w("vector", "V", rc)
        p.do("vector", lambda e: e.tensor_scalar(m2[0:1, :], m2[0:1, :], 127.0, None, ALU.mult))
        p.do("vector", lambda e: e.tensor_scalar(
            hrow, m1[0:1, :], 1.0 / 127.0, msc[0:1, 0:1], ALU.mult, ALU.mult))
        vv = p.do("vector", lambda e: e.engine_nop(), "V", 1)
        if tb == 0 and "dbg_q1" in io:
            p.w("sync", "V", vv)
            p.do("sync", lambda e: e.dma_start(out=io["dbg_q1"][0:1, 0:512], in_=m1[0:1, :]), "DS", 16)
            p.do("sync", lambda e: e.dma_start(out=io["dbg_q1"][1:2, 0:512], in_=m2[0:1, :]), "DS", 16)
            p.do("sync", lambda e: e.dma_start(out=io["dbg_q1"][2:3, 0:512], in_=hrow), "DS", 16)
            p.w("vector", "DS", p.cnt["DS"])
        p.w("tensor", "V", vv)
        p.do("tensor", lambda e: e.matmul(ps[2][:, 0:512], onesf[0:1, 0:128],
                                          m2[0:1, :], start=True, stop=True))
        tbx = p.do("tensor", lambda e: e.matmul(ps[3][:, 0:512], onesf[0:1, 0:128],
                                                hrow, start=True, stop=True), "T", 1)
        p.w("vector", "T", tbx)
        p.do("vector", lambda e: e.tensor_copy(scb[:, :], ps[2][:, 0:512]))
        vv = p.do("vector", lambda e: e.tensor_copy(dqb[:, :], ps[3][:, 0:512]), "V", 1)
        p.w("vector", "V", vv)
        for c in range(8):
            p.do("vector", lambda e, c=c: e.tensor_tensor(
                xq[:, c * 512:(c + 1) * 512].bitcast(F32)[:, 0:256] if False else
                m1[:, 0:512], xblk[:, c * 512:(c + 1) * 512], scb[:, :], ALU.mult))
            vv = p.do("vector", lambda e, c=c: e.tensor_scalar(
                xq[:, c * 512:(c + 1) * 512], m1[:, 0:512], MAGIC, -MAGIC,
                ALU.add, ALU.add), "V", 1)
        if tb == 0 and "dbg_q1" in io:
            p.w("sync", "V", vv)
            p.do("sync", lambda e: e.dma_start(out=io["dbg_xq"][:, :], in_=xq), "DS", 16)
            p.w("gpsimd", "DS", p.cnt["DS"])
        for js in range(4):
            p.w("gpsimd", "T", tcnt)
            p.w("gpsimd", "V", vv)
            for c in range(8):
                ld = p.do("gpsimd", lambda e, c=c, js=js: e.dma_start(
                    out=wsl[:, c * 1024:(c + 1) * 1024],
                    in_=io["wq1q"][c * 128:(c + 1) * 128, js * 1024:(js + 1) * 1024]), "DG", 16)
            for jc in range(8):
                pb = ps[jc % 2]
                p.w("tensor", "DG", ld)
                p.w("tensor", "V", vv)
                p.w("tensor", "V", lastdq[jc % 2])
                for c in range(8):
                    tcnt = p.do("tensor", lambda e, pb=pb, c=c, jc=jc: e.matmul(
                        pb[:, :], wsl[:, c * 1024 + jc * 128: c * 1024 + jc * 128 + 128],
                        xq[:, c * 512:(c + 1) * 512], start=(c == 0), stop=(c == 7)),
                        "T" if c == 7 else None, 1 if c == 7 else 0)
                p.w("vector", "T", tcnt)
                p.w("vector", "DS", dh)
                p.do("vector", lambda e, pb=pb: e.tensor_tensor(
                    ev[:, 0:512], pb[:, :], dqb[:, :], ALU.mult))
                evacV = p.do("vector", lambda e: e.engine_nop(), "V", 1)
                lastdq[jc % 2] = evacV
                p.w("sync", "V", evacV)
                dh = p.do("sync", lambda e, js=js, jc=jc, tb=tb: e.dma_start(
                    out=io["hA"][js * 1024 + jc * 128: js * 1024 + jc * 128 + 128,
                                 tb * 512:(tb + 1) * 512], in_=ev[:, 0:512]), "DS", 16)

    hc = AR2[:, 0:16384]
    xq2 = AR2[:, 16384:32768]
    whalf = AR1[:, 8192:24576]
    halo = AR3[:, 9216:9216 + 516]
    x3res = AR3[:, 4096:8192]
    stile = AR3[:, 0:4096]
    lnf = ARF[:, 2560:3072]
    sqf = ARF[:, 3072:3584]
    outb = ARF[:, 3584:4096]
    r_mu, r_m2, r_rs = sb["r_mu"][0:1, 0:512], sb["r_m2"][0:1, 0:512], sb["r_rs"][0:1, 0:512]
    tcnt = p.cnt["T"]
    evacV = p.cnt["V"]
    ost = p.cnt["DS"]
    prevOvv = p.cnt["V"]
    lnV = p.cnt["V"]
    for tb in range(16):
        seq_edge_l = (tb % 8 == 0)
        seq_edge_r = (tb % 8 == 7)
        p.w("scalar", "DS", cld)
        for jc in range(32):
            p.w("sync", "V", evacV if jc == 0 else p.cnt["V"] - 0)
            lo = tb * 512 - (0 if seq_edge_l else 1)
            hi = tb * 512 + 512 + (0 if seq_edge_r else 1)
            off = 1 if seq_edge_l else 0
            hld = p.do("sync", lambda e, jc=jc, lo=lo, hi=hi, off=off: e.dma_start(
                out=halo[:, off:off + (hi - lo)],
                in_=io["hA"][jc * 128:(jc + 1) * 128, lo:hi]), "DS", 16)
            if seq_edge_l:
                p.do("vector", lambda e: e.memset(halo[:, 0:1], 0.0))
            if seq_edge_r:
                p.do("vector", lambda e: e.memset(halo[:, 513:514], 0.0))
            p.w("vector", "DS", hld)
            p.do("vector", lambda e, jc=jc: e.tensor_scalar(
                m1[:, :], halo[:, 1:513], cw[:, jc * 3 + 1:jc * 3 + 2], None, ALU.mult))
            p.do("vector", lambda e, jc=jc: e.scalar_tensor_tensor(
                m1[:, :], halo[:, 0:512], cw[:, jc * 3 + 0:jc * 3 + 1], m1[:, :],
                ALU.mult, ALU.add))
            cvv = p.do("vector", lambda e, jc=jc: e.scalar_tensor_tensor(
                m1[:, :], halo[:, 2:514], cw[:, jc * 3 + 2:jc * 3 + 3], m1[:, :],
                ALU.mult, ALU.add), "V", 1)
            p.w("scalar", "V", cvv)
            aa = p.do("scalar", lambda e, jc=jc: e.activation(
                hc[:, jc * 512:(jc + 1) * 512], m1[:, :], AF.Silu,
                bias=cb2[:, jc:jc + 1]), "A", 1)
        p.w("vector", "A", aa)
        p.do("vector", lambda e: e.scalar_tensor_tensor(
            m1[:, :], hc[:, 0:512], -1.0, hc[:, 0:512], ALU.mult, ALU.max))
        for c in range(1, 32):
            p.do("vector", lambda e, c=c: e.tensor_max(
                m1[:, :], m1[:, :], hc[:, c * 512:(c + 1) * 512]))
            p.do("vector", lambda e, c=c: e.scalar_tensor_tensor(
                m1[:, :], hc[:, c * 512:(c + 1) * 512], -1.0, m1[:, :],
                ALU.mult, ALU.max))
        vv = p.do("vector", lambda e: e.engine_nop(), "V", 1)
        nh = 64
        while nh >= 1:
            vv = p.do("vector", lambda e, nh=nh: e.tensor_copy(
                scb[0:min(2 * nh, 128), 0:512], m1[0:min(2 * nh, 128), :]), "V", 1)
            p.w("sync", "V", vv)
            sh = p.do("sync", lambda e, nh=nh: e.dma_start(
                out=m2[0:max(nh, 1), :], in_=m1[nh:2 * nh, :]), "DS", 16)
            p.w("vector", "DS", sh)
            vv = p.do("vector", lambda e, nh=nh: e.tensor_max(
                m1[0:nh, :], m1[0:nh, :], m2[0:nh, :]), "V", 1)
            p.w("vector", "V", vv)
            nh //= 2
        p.do("vector", lambda e: e.tensor_scalar(
            m1[0:1, :], m1[0:1, :], 1e-5, None, ALU.max))
        vv = p.do("vector", lambda e: e.engine_nop(), "V", 1)
        p.w("vector", "V", vv)
        rc = p.do("vector", lambda e: e.reciprocal(m2[0:1, :], m1[0:1, :]), "V", 1)
        p.w("vector", "V", rc)
        p.do("vector", lambda e: e.tensor_scalar(m2[0:1, :], m2[0:1, :], 127.0, None, ALU.mult))
        p.do("vector", lambda e: e.tensor_scalar(
            hrow, m1[0:1, :], 1.0 / 127.0, msc[0:1, 4:5], ALU.mult, ALU.mult))
        vv = p.do("vector", lambda e: e.engine_nop(), "V", 1)
        p.w("tensor", "V", vv)
        p.do("tensor", lambda e: e.matmul(ps[2][:, 0:512], onesf[0:1, 0:128],
                                          m2[0:1, :], start=True, stop=True))
        tbx = p.do("tensor", lambda e: e.matmul(ps[3][:, 0:512], onesf[0:1, 0:128],
                                                hrow, start=True, stop=True), "T", 1)
        p.w("vector", "T", tbx)
        p.do("vector", lambda e: e.tensor_copy(scb[:, :], ps[2][:, 0:512]))
        vv = p.do("vector", lambda e: e.tensor_copy(dqb[:, :], ps[3][:, 0:512]), "V", 1)
        p.w("vector", "V", vv)
        for c in range(32):
            p.do("vector", lambda e, c=c: e.tensor_tensor(
                m1[:, 0:512], hc[:, c * 512:(c + 1) * 512], scb[:, :], ALU.mult))
            vv = p.do("vector", lambda e, c=c: e.tensor_scalar(
                xq2[:, c * 512:(c + 1) * 512], m1[:, 0:512], MAGIC, -MAGIC,
                ALU.add, ALU.add), "V", 1)
        p.w("sync", "V", vv)
        for c in range(8):
            rld = p.do("sync", lambda e, c=c, tb=tb: e.dma_start(
                out=x3res[:, c * 512:(c + 1) * 512],
                in_=io["x3T"][c * 128:(c + 1) * 128, tb * 512:(tb + 1) * 512]), "DS", 16)
        lnV = max(lnV if tb > 0 else p.cnt["V"], prevOvv)
        for half in range(2):
            p.w("gpsimd", "T", tcnt)
            for c in range(32):
                ld = p.do("gpsimd", lambda e, c=c, half=half: e.dma_start(
                    out=whalf[:, c * 512:(c + 1) * 512],
                    in_=io["wq2q"][c * 128:(c + 1) * 128,
                                   half * 512:(half + 1) * 512]), "DG", 16)
            for el in range(4):
                ec = half * 4 + el
                pb = ps[el % 2]
                p.w("tensor", "DG", ld)
                p.w("tensor", "V", vv)
                p.w("tensor", "DS", rld)
                p.w("tensor", "V", lnV)
                for c in range(32):
                    tcnt = p.do("tensor", lambda e, pb=pb, c=c, el=el: e.matmul(
                        pb[:, :], whalf[:, c * 512 + el * 128: c * 512 + el * 128 + 128],
                        xq2[:, c * 512:(c + 1) * 512], start=(c == 0), stop=(c == 31)),
                        "T" if c == 31 else None, 1 if c == 31 else 0)
                p.w("vector", "T", tcnt)
                sv = stile[:, ec * 512:(ec + 1) * 512]
                p.do("vector", lambda e, pb=pb, sv=sv: e.tensor_tensor(
                    m1[:, :], pb[:, :], dqb[:, :], ALU.mult))
                lnV = p.do("vector", lambda e, sv=sv, ec=ec: e.tensor_tensor(
                    sv, m1[:, :], x3res[:, ec * 512:(ec + 1) * 512], ALU.add), "V", 1)
        tst = p.cnt["T"]
        for fc in range(8):
            sv = stile[:, fc * 512:(fc + 1) * 512]
            p.w("scalar", "V", lnV)
            p.w("scalar", "T", tst)
            aa = p.do("scalar", lambda e, fc=fc, sv=sv: e.activation(
                sqf[:, 0:512], sv, AF.Square), "A", 1)
            p.w("tensor", "A", aa)
            p.w("tensor", "V", lnV)
            p.do("tensor", lambda e, sv=sv, fc=fc: e.matmul(
                ps[2][0:1, 0:512], onesb[:, 0:1], sv, start=(fc == 0), stop=(fc == 7)))
            tst = p.do("tensor", lambda e, fc=fc: e.matmul(
                ps[3][0:1, 0:512], onesf[:, 0:1], sqf[:, 0:512],
                start=(fc == 0), stop=(fc == 7)), "T", 1)
        p.w("vector", "T", tst)
        p.do("vector", lambda e: e.tensor_scalar(r_mu, ps[2][0:1, 0:512], 1.0 / D, None, ALU.mult))
        p.do("vector", lambda e: e.tensor_scalar(r_m2, ps[3][0:1, 0:512], 1.0 / D, None, ALU.mult))
        p.do("vector", lambda e: e.tensor_tensor(r_rs, r_mu, r_mu, ALU.mult))
        p.do("vector", lambda e: e.tensor_tensor(r_m2, r_m2, r_rs, ALU.subtract))
        vv = p.do("vector", lambda e: e.tensor_scalar(r_m2, r_m2, 1e-5, None, ALU.add), "V", 1)
        p.w("scalar", "V", vv)
        p.do("scalar", lambda e: e.activation(r_rs, r_m2, AF.Ln))
        aa = p.do("scalar", lambda e: e.activation(r_rs, r_rs, AF.Exp, scale=-0.5), "A", 1)
        p.w("tensor", "A", aa)
        p.do("tensor", lambda e: e.matmul(ps[2][:, 0:512], onesf[0:1, 0:128], r_mu, start=True, stop=True))
        tb2 = p.do("tensor", lambda e: e.matmul(ps[3][:, 0:512], onesf[0:1, 0:128], r_rs, start=True, stop=True), "T", 1)
        p.w("vector", "T", tb2)
        for fc in range(8):
            sv = stile[:, fc * 512:(fc + 1) * 512]
            p.w("vector", "DS", ost)
            p.do("vector", lambda e, sv=sv: e.tensor_tensor(lnf[:, :], sv, ps[2][:, 0:512], ALU.subtract))
            lvv = p.do("vector", lambda e: e.tensor_tensor(lnf[:, :], lnf[:, :], ps[3][:, 0:512], ALU.mult), "V", 1)
            p.w("tensor", "V", lvv)
            trcp = [prevOvv, prevOvv]
            for tt in range(4):
                p.w("tensor", "V", trcp[tt % 2])
                tr = p.do("tensor", lambda e, tt=tt: e.transpose(
                    ps[0 if tt % 2 == 0 else 1][:, 0:128],
                    lnf[:, tt * 128:(tt + 1) * 128], cst["identf"][:, :]), "T", 1)
                p.w("vector", "T", tr)
                ovv = p.do("vector", lambda e, tt=tt: e.tensor_copy(
                    outb[:, tt * 128:(tt + 1) * 128], ps[0 if tt % 2 == 0 else 1][:, 0:128]), "V", 1)
                trcp[tt % 2] = ovv
            prevOvv = ovv
            p.w("sync", "V", ovv)
            for tt in range(4):
                ost = p.do("sync", lambda e, fc=fc, tb=tb, tt=tt: e.dma_start(
                    out=io["out"][tb * 512 + tt * 128: tb * 512 + tt * 128 + 128,
                                  fc * 128:(fc + 1) * 128],
                    in_=outb[:, tt * 128:(tt + 1) * 128]), "DS", 16)


def build_program(taps=(), upto="all"):
    nc = bass.Bass("TRN2", target_bir_lowering=False)
    io = {}
    def inp(name, shape, dt):
        io[name] = nc.dram_tensor(name, shape, dt, kind="ExternalInput").ap()
    def internal(name, shape, dt):
        kind = "ExternalOutput" if name in taps else "Internal"
        io[name] = nc.dram_tensor(name, shape, dt, kind=kind).ap()

    inp("x", [T, D], F32)
    for nm in ("Wq", "Wk", "Wv", "Wo"):
        inp("l0_" + nm, [D, D], F32)
        inp("l1_" + nm, [D, D], F32)
    for nm in ("bq", "bk", "bv", "bo"):
        inp(f"l0_{nm}_row", [1, D], F32); inp(f"l0_{nm}_pp", [128, 8], F32)
        inp(f"l1_{nm}_row", [1, D], F32); inp(f"l1_{nm}_pp", [128, 8], F32)
    inp("f0_w1T", [D, DH], F32); inp("f0_w2T", [D, DH], F32)
    inp("f0_h1T", [DH, DH], F32); inp("f0_h2T", [DH, DH], F32)
    inp("f0_w3T", [DH, D], F32)
    for nm, nch in (("w1_s", 32), ("w2_s", 32), ("h1_s", 32), ("h2_s", 32), ("w3_s", 8)):
        inp("f0_" + nm, [128, nch], F32)
    inp("l1_probes", [H, HD, 192], F32)
    inp("f1_w1T", [D, DH], F32); inp("f1_w2T", [DH, D], F32)
    inp("f1_convw", [128, 96], F32); inp("f1_convb", [128, 32], F32)
    inp("ident", [128, 128], BF16); inp("onesb", [128, 512], BF16)
    inp("onesf", [128, 144], F32); inp("zz", [128, 544], BF16)
    inp("lup", [128, 128], BF16); inp("gmats", [64, 192], F32)
    inp("mask1", [128, 192], BF16); inp("mask2", [128, 144], BF16)
    inp("identf", [128, 128], F32)

    if "dbg_at0" in taps:
        internal("dbg_at0", [D, S], BF16)
    if "dbg_otok" in taps:
        internal("dbg_otok", [128, D], BF16)
    if "dbg_qkv" in taps:
        internal("dbg_qkv", [128, 3 * D], BF16)
    if "dbg_sc" in taps:
        internal("dbg_sc", [128, 512], F32)
    if "dbg_acc" in taps:
        internal("dbg_acc", [128, 1040], F32)
    if "dbg_ranks" in taps:
        internal("dbg_ranks", [128, 3072], U32)
    if "dbg_oh" in taps:
        internal("dbg_oh", [128, 6144], BF16)
        internal("dbg_proj", [128, 6144], F32)
    if "dbg_cnt" in taps:
        internal("dbg_cnt", [128, 98], F32)
    if "dbg_qrt" in taps:
        internal("dbg_qrt", [128, 16384], F32)
    if "dbg_qsort" in taps:
        internal("dbg_qsort", [S, 256], BF16)
    if "dbg_win" in taps:
        internal("dbg_win", [65, 512], BF16)
        internal("dbg_exp", [256, 512], BF16)
    if "dbg_acc2" in taps:
        internal("dbg_acc2", [128, 4096], BF16)
    if "dbg_acc3" in taps:
        internal("dbg_acc3", [128, 4096], BF16)
    if "dbg_q1" in taps:
        internal("dbg_q1", [4, 512], F32)
        internal("dbg_xq", [128, 4096], BF16)
        internal("dbg_qkt", [128, 8192], BF16)
        internal("dbg_otok2", [128, 2048], BF16)
    internal("xbf_tok", [T, D], BF16)
    internal("x1T", [D, T], BF16)
    internal("x2T", [D, T], BF16)
    internal("x3T", [D, T], BF16)
    internal("hA", [DH, T], BF16)
    internal("qkvs", [B, H, S, 256], BF16)
    internal("qkv_sorted", [S, 256], BF16)
    internal("outsorted", [S, 64], BF16)
    internal("attn1tok", [B, 8, S, 128], BF16)
    internal("wq1q", [D, DH], BF16)
    internal("x2Tf", [D, T], F32)
    internal("wq2q", [DH, D], BF16)
    internal("hB", [DH, T], BF16)
    io["out"] = nc.dram_tensor("out", [T, D], F32, kind="ExternalOutput").ap()

    with ExitStack() as st:
        AR1 = st.enter_context(nc.sbuf_tensor("AR1", [128, 28672], BF16))
        AR2 = st.enter_context(nc.sbuf_tensor("AR2", [128, 32768], BF16))
        AR3 = st.enter_context(nc.sbuf_tensor("AR3", [128, 12288], BF16))
        ARF = st.enter_context(nc.sbuf_tensor("ARF", [128, 6656], F32))
        ARS = st.enter_context(nc.sbuf_tensor("ARS", [128, 1024], F32))
        PRB = st.enter_context(nc.sbuf_tensor("PRB", [128, 1536], F32))
        RNK = st.enter_context(nc.sbuf_tensor("RNK", [128, 3072], U32))
        cst = {}
        for nm, shape, dt in (("ident", [128, 128], BF16), ("onesb", [128, 512], BF16),
                              ("onesfsb", [128, 144], F32), ("zz", [128, 544], BF16),
                              ("lup", [128, 128], BF16), ("gmats", [64, 192], F32),
                              ("mask1", [128, 192], BF16), ("mask2", [128, 144], BF16),
                              ("identf", [128, 128], F32)):
            cst[nm] = st.enter_context(nc.sbuf_tensor("c_" + nm, shape, dt))
        ps = {}
        for i in range(4):
            ps[i] = st.enter_context(nc.psum_tensor(f"ps{i}", [128, 512], F32))
        ps["bfA"] = st.enter_context(nc.psum_tensor("psbfA", [128, 512], BF16))
        ps["bfB"] = st.enter_context(nc.psum_tensor("psbfB", [128, 512], BF16))
        blk = st.enter_context(nc.Block())
        p = P(nc, st)

        sb = dict(
            wq=AR1[:, 0:8192], wk=AR1[:, 8192:16384], wv=AR1[:, 16384:24576],
            wo=AR1[:, 0:16384], brow=AR1[:, 24576:27648],
            at0=AR2[:, 0:32768],
            xsb=AR3[:, 0:4096], qkv=AR3[:, 4096:7168], otok=AR3[:, 7168:8192],
            stile=AR3[:, 8192:10240], x1t=AR3[:, 10240:12288],
            prod=ARF[:, 0:1024], sc16=ARF[:, 1024:1280], att=ARF[:, 1280:1536],
            acc=ARF[:, 1536:2560], zr=ARF[:, 2560:2576], zs=ARF[:, 6160:6176],
            sqf=ARF[:, 2576:4624],
            r_mu=ARF[0:1, 4624:5136], r_m2=ARF[0:1, 5136:5648],
            r_rs=ARF[0:1, 5648:6160],
            bo_pp=ARS[:, 0:8], fscal=ARS[:, 8:144], bqpp1=ARS[:, 144:152],
            bo1pp=ARS[:, 168:176], wabs=ARS[:, 176:178], msc=ARS[:, 178:186],
            lnoutf=ARS[:, 320:832],
            wsc=ARS[:, 186:187], convw=ARS[:, 192:288], convb=ARS[:, 288:320],
            probes=PRB, ranks=RNK,
            wq1=AR1[:, 0:8192], wk1=AR1[:, 8192:16384], wv1=AR1[:, 16384:24576],
            brow1=AR1[:, 24576:27648],
        )

        src_map = {"ident": "ident", "onesb": "onesb", "onesfsb": "onesf",
                   "zz": "zz", "lup": "lup", "gmats": "gmats",
                   "mask1": "mask1", "mask2": "mask2", "identf": "identf"}
        for nm, t in cst.items():
            p.do("sync", lambda e, t=t, s=io[src_map[nm]]:
                 e.dma_start(out=t[:, :], in_=s[:, :]), "DS", 16)
        cload = p.snap()
        for e in ("tensor", "vector", "scalar", "gpsimd"):
            p.join(e, cload, ["DS"])

        emit_ingest(p, io)
        ing = p.snap()
        p.join("scalar", ing, ["DG"])
        p.join("tensor", ing, ["DG"])

        emit_l0(p, io, sb, cst, ps, 0)
        emit_l0(p, io, sb, cst, ps, 1)
        if upto in ("ffn0", "l1", "all"):
            emit_ffn0(p, io, AR1, AR2, AR3, ARF, ARS, sb, cst, ps)
        if upto in ("l1q", "l1a", "l1o", "all"):
            emit_l1qkv(p, io, AR1, AR2, AR3, ARF, ARS, sb, cst, ps, 0)
            emit_l1qkv(p, io, AR1, AR2, AR3, ARF, ARS, sb, cst, ps, 1)
            if "dbg_ranks" in io:
                emit_dump_ranks(p, io, sb)
        if upto in ("l1a", "l1o", "all"):
            emit_l1attn(p, io, AR1, AR2, AR3, ARF, ARS, sb, cst, ps, 0)
            if upto in ("l1o", "all"):
                emit_l1out(p, io, AR1, AR2, AR3, ARF, ARS, sb, cst, ps, 0)
            emit_l1attn(p, io, AR1, AR2, AR3, ARF, ARS, sb, cst, ps, 1)
            if upto in ("l1o", "all"):
                emit_l1out(p, io, AR1, AR2, AR3, ARF, ARS, sb, cst, ps, 1)
        if upto == "all":
            emit_ffn1(p, io, AR1, AR2, AR3, ARF, ARS, sb, cst, ps)
        else:
            fin = p.snap()
            p.join("gpsimd", fin, ["DS", "V", "T"])
            for i in range(8):
                p.do("gpsimd", lambda e, i=i: e.dma_start(
                    out=io["out"].rearrange("(a t) d -> a t d", a=8)[i, :, :],
                    in_=io["x1T"][:, i * 1024:(i + 1) * 1024]), "DG", 16)

        p.w("sync", "DS", p.cnt["DS"])
        p.w("gpsimd", "DG", p.cnt["DG"])
        p.finalize(blk)
    return nc


def kernel(**inputs):
    maps = shard_inputs(inputs)
    nc = build_program()
    res = run_bass_kernel_spmd(nc, maps, core_ids=list(range(NCORES)))
    out = np.concatenate([np.asarray(r["out"], np.float32).reshape(B, S, D)
                          for r in res.results], axis=0)
    return out


# version 02
# speedup vs baseline: 1.1586x; 1.1586x over previous
import numpy as np
import ml_dtypes
from contextlib import ExitStack
import concourse.bass as bass
import concourse.mybir as mybir
from concourse.bass import IndirectOffsetOnAxis
from concourse.bass_utils import run_bass_kernel_spmd

F32 = mybir.dt.float32
BF16 = mybir.dt.bfloat16
U32 = mybir.dt.uint32
AF = mybir.ActivationFunctionType
ALU = mybir.AluOpType
AX = mybir.AxisListType

NCORES = 8
B = 2
S = 4096
T = B * S
D = 1024
H = 16
HD = 64
DH = 4096
MAGIC = 12582912.0

ENGINES = ("sync", "gpsimd", "tensor", "vector", "scalar")


def host_consts():
    bf = ml_dtypes.bfloat16
    c = {}
    c["ident"] = np.eye(128).astype(bf)
    c["onesb"] = np.ones((128, 512), bf)
    c["onesf"] = np.ones((128, 144), np.float32)
    c["zz"] = np.zeros((128, 544), bf)
    lup = (np.arange(128)[:, None] < np.arange(128)[None, :]).astype(np.float32)
    c["lup"] = lup.astype(bf)
    gs = []
    for nb in (16, 64, 256):
        pp = np.arange(64)
        table = (pp * 9973) % nb
        csize = 64 // nb if nb < 64 else 1
        g = (table[:, None] < table[None, :]).astype(np.float32) / csize
        gs.append(g)
    c["gmats"] = np.concatenate(gs, axis=1).astype(np.float32)
    masks = []
    for cs in (64, 16):
        kk = np.arange(128)[:, None] // cs
        qq = np.arange(128 + cs)[None, :] // cs
        m = ((qq - kk == 0) | (qq - kk == 1)).astype(np.float32)
        masks.append(m.astype(bf))
    c["mask1"] = masks[0]
    c["mask2"] = masks[1]
    c["identf"] = np.eye(128).astype(np.float32)
    return c


def shard_inputs(inputs):
    A = np.ascontiguousarray
    g = inputs
    base = {}
    for nm in ("Wq", "Wk", "Wv", "Wo"):
        base["l0_" + nm] = A(np.asarray(g[f"L0_attn_{nm}"], np.float32).T)
        base["l1_" + nm] = A(np.asarray(g[f"L1_attn_{nm}"], np.float32))
    for nm in ("bq", "bk", "bv", "bo"):
        v0 = np.asarray(g[f"L0_attn_{nm}"], np.float32)
        v1 = np.asarray(g[f"L1_attn_{nm}"], np.float32)
        base["l0_" + nm + "_row"] = A(v0.reshape(1, D))
        base["l0_" + nm + "_pp"] = A(v0.reshape(8, 128).T)
        base["l1_" + nm + "_row"] = A(v1.reshape(1, D))
        base["l1_" + nm + "_pp"] = A(v1.reshape(8, 128).T)
    base["f0_w1T"] = A(np.asarray(g["L0_ffn_w1_w"], np.float32).T)
    base["f0_w2T"] = A(np.asarray(g["L0_ffn_w2_w"], np.float32).T)
    base["f0_h1T"] = A(np.asarray(g["L0_ffn_h1_w"], np.float32).T)
    base["f0_h2T"] = A(np.asarray(g["L0_ffn_h2_w"], np.float32).T)
    base["f0_w3T"] = A(np.asarray(g["L0_ffn_w3_w"], np.float32).T)
    for nm, nch in (("w1_s", 32), ("w2_s", 32), ("h1_s", 32), ("h2_s", 32), ("w3_s", 8)):
        base["f0_" + nm] = A(np.asarray(g[f"L0_ffn_{nm}"], np.float32).reshape(nch, 128).T)
    pr = np.asarray(g["L1_attn_probes"], np.float32)
    base["l1_probes"] = A(pr.transpose(1, 2, 0, 3).reshape(H, HD, 192))
    base["f1_w1T"] = A(np.asarray(g["L1_ffn_w1"], np.float32).T)
    base["f1_w2T"] = A(np.asarray(g["L1_ffn_w2"], np.float32).T)
    cw = np.asarray(g["L1_ffn_conv_w"], np.float32)[:, 0, :]
    base["f1_convw"] = A(cw.reshape(32, 128, 3).transpose(1, 0, 2).reshape(128, 96))
    base["f1_convb"] = A(np.asarray(g["L1_ffn_conv_b"], np.float32).reshape(32, 128).T)
    base.update(host_consts())

    x = np.asarray(g["L0_attn_x"], np.float32)
    maps = []
    for c in range(NCORES):
        m = dict(base)
        m["x"] = A(x[2 * c:2 * c + 2].reshape(T, D))
        maps.append(m)
    return maps


class P:
    def __init__(self, nc, st):
        self.nc = nc
        self.ops = {e: [] for e in ENGINES}
        self.sems = {}
        self.cnt = {}
        for s in ("DS", "DG", "DA", "T", "V", "A", "G"):
            self.sems[s] = st.enter_context(nc.semaphore("sem_" + s))
            self.cnt[s] = 0

    def w(self, eng, sem, val):
        if val is None or val <= 0:
            return
        self.ops[eng].append(lambda e, s=self.sems[sem], v=val: e.wait_ge(s, v))

    def do(self, eng, fn, sem=None, inc=0):
        if sem is None:
            self.ops[eng].append(fn)
            return None
        self.cnt[sem] += inc
        def f(e, fn=fn, s=self.sems[sem], i=inc):
            fn(e).then_inc(s, i)
        self.ops[eng].append(f)
        return self.cnt[sem]

    def snap(self):
        return dict(self.cnt)

    def join(self, eng, snap, sems=None):
        for s in (sems if sems is not None else snap):
            self.w(eng, s, snap[s])

    def finalize(self, blk):
        for name in ENGINES:
            ops = self.ops[name]
            deco = getattr(blk, name)
            def body(e, ops=ops):
                for f in ops:
                    f(e)
            deco(body)


def emit_ingest(p, io):
    for i in range(16):
        p.do("gpsimd", lambda e, i=i: e.dma_start(
            out=io["xbf_tok"][i * 512:(i + 1) * 512, :],
            in_=io["x"][i * 512:(i + 1) * 512, :]), "DG", 16)


def emit_l0(p, io, sb, cst, ps, b):
    wq, wk, wv, wo = sb["wq"], sb["wk"], sb["wv"], sb["wo"]
    brow, at0, xsb, qkv = sb["brow"], sb["at0"], sb["xsb"], sb["qkv"]
    otok, stile, x1t = sb["otok"], sb["stile"], sb["x1t"]
    prod, sc16, att, acc, zr = sb["prod"], sb["sc16"], sb["att"], sb["acc"], sb["zr"]
    zs = sb["zs"]
    sqf = sb["sqf"]
    r_mu, r_m2, r_rs = sb["r_mu"][0:1, 0:256], sb["r_m2"][0:1, 0:256], sb["r_rs"][0:1, 0:256]
    bo_pp = sb["bo_pp"]
    onesb, onesf, ident = cst["onesb"], cst["onesfsb"], cst["ident"]

    p.w("gpsimd", "T", p.cnt["T"])
    for nm, dstv in (("l0_Wq", wq), ("l0_Wk", wk), ("l0_Wv", wv)):
        for c in range(8):
            p.do("gpsimd", lambda e, d=dstv[:, c * 1024:(c + 1) * 1024],
                 s=io[nm][c * 128:(c + 1) * 128, :]: e.dma_start(out=d, in_=s),
                 "DG", 16)
    if b == 0:
        for i, nm in enumerate(("l0_bq_row", "l0_bk_row", "l0_bv_row")):
            p.do("gpsimd", lambda e, d=brow[0:1, i * 1024:(i + 1) * 1024],
                 s=io[nm][0:1, :]: e.dma_start(out=d, in_=s), "DG", 16)
        p.do("sync", lambda e: e.dma_start(out=bo_pp, in_=io["l0_bo_pp"][:, :]), "DS", 16)
    wload = p.cnt["DG"]
    bld = p.cnt["DS"]

    evacA = p.cnt["A"]
    evacV = p.cnt["V"]
    tcnt = p.cnt["T"]
    dcnt = p.cnt["DA"]
    x1done = p.cnt["V"]
    wmap = (wq, wk, wv)
    p.w("tensor", "V", p.cnt["V"])
    p.w("vector", "T", p.cnt["T"])
    p.w("scalar", "T", p.cnt["T"])

    for tau in range(32):
        blkc = tau // 4
        if tau % 4 == 0:
            p.w("scalar", "T", tcnt)
            for c in range(8):
                rows = io["xbf_tok"][b * S + blkc * 512: b * S + blkc * 512 + 512,
                                     c * 128:(c + 1) * 128]
                dcnt = p.do("scalar", lambda e, rows=rows, c=c: e.dma_start(
                    out=xsb[:, c * 512:(c + 1) * 512], in_=rows, transpose=True),
                    "DA", 16)
        p.w("tensor", "DA", dcnt)
        p.w("tensor", "DG", wload)
        waves = [[(0, 0), (0, 1), (1, 0), (1, 1)], [(2, 0), (2, 1)]]
        for wi, wave in enumerate(waves):
            p.w("tensor", "A", evacA)
            for j, (i, fh) in enumerate(wave):
                pb = ps[j]
                for c in range(8):
                    lhsT = xsb[:, c * 512 + (tau % 4) * 128: c * 512 + (tau % 4) * 128 + 128]
                    rhs = wmap[i][:, c * 1024 + fh * 512: c * 1024 + fh * 512 + 512]
                    p.do("tensor", lambda e, pb=pb, lhsT=lhsT, rhs=rhs, c=c:
                         e.matmul(pb[:, :], lhsT, rhs, start=(c == 0), stop=False))
                tcnt = p.do("tensor", lambda e, pb=pb, i=i, fh=fh: e.matmul(
                    pb[:, :], onesb[0:1, 0:128],
                    brow[0:1, i * 1024 + fh * 512: i * 1024 + fh * 512 + 512],
                    start=False, stop=True), "T", 1)
            p.w("scalar", "T", tcnt)
            if wi == 0:
                p.w("scalar", "V", evacV)
            for j, (i, fh) in enumerate(wave):
                evacA = p.do("scalar", lambda e, i=i, fh=fh, j=j: e.activation(
                    qkv[:, i * 1024 + fh * 512: i * 1024 + fh * 512 + 512],
                    ps[j][:, :], AF.Identity), "A", 1)
        if b == 0 and tau == 0 and "dbg_qkv" in io:
            p.w("sync", "A", evacA)
            p.do("sync", lambda e: e.dma_start(out=io["dbg_qkv"][:, :], in_=qkv), "DS", 16)
            p.w("vector", "DS", p.cnt["DS"])
        p.w("vector", "A", evacA)
        q_t, k_t, v_t = qkv[:, 0:1024], qkv[:, 1024:2048], qkv[:, 2048:3072]
        for hq in range(H):
            qrep = q_t[:, hq * HD:(hq + 1) * HD].unsqueeze(1).broadcast_to([128, 16, HD])
            p.do("vector", lambda e, qrep=qrep: e.tensor_tensor(
                prod.rearrange("p (a b) -> p a b", a=16),
                k_t.rearrange("p (a b) -> p a b", a=16), qrep, ALU.mult))
            p.do("vector", lambda e, hq=hq: e.reduce_sum(
                sc16[:, hq * 16:(hq + 1) * 16],
                prod.rearrange("p (a b) -> p a b", a=16), axis=AX.X))
        vv = p.do("vector", lambda e: e.tensor_copy(prod[:, 0:256], sc16), "V", 1)
        p.w("scalar", "V", vv)
        aa = p.do("scalar", lambda e: e.activation(att[:, :], sc16[:, :], AF.Exp,
                                                   scale=0.125), "A", 1)
        if b == 0 and tau == 0 and "dbg_sc" in io:
            p.w("sync", "A", aa)
            p.do("sync", lambda e: e.dma_start(out=io["dbg_sc"][:, 0:256], in_=sc16), "DS", 16)
            p.do("sync", lambda e: e.dma_start(out=io["dbg_sc"][:, 256:512], in_=att), "DS", 16)
            p.w("vector", "DS", p.cnt["DS"])
        p.w("vector", "A", aa)
        rs = p.do("vector", lambda e: e.reduce_sum(
            zs[:, :], att.rearrange("p (a b) -> p a b", a=16), axis=AX.X), "V", 1)
        p.w("vector", "V", rs)
        rc = p.do("vector", lambda e: e.reciprocal(zr[:, :], zs[:, :]), "V", 1)
        p.w("vector", "V", rc)
        for hq in range(H):
            accv = acc[:, hq * HD:(hq + 1) * HD]
            p.do("vector", lambda e, accv=accv, hq=hq: e.tensor_scalar(
                accv, v_t[:, 0:HD], att[:, hq * 16:hq * 16 + 1], None, ALU.mult))
            for hk in range(1, 16):
                p.do("vector", lambda e, accv=accv, hk=hk, hq=hq: e.scalar_tensor_tensor(
                    accv, v_t[:, hk * HD:(hk + 1) * HD],
                    att[:, hq * 16 + hk:hq * 16 + hk + 1], accv, ALU.mult, ALU.add))
        if b == 0 and tau == 0 and "dbg_acc" in io:
            fence = p.do("vector", lambda e: e.tensor_copy(prod[:, 0:16], zr), "V", 1)
            p.w("sync", "V", fence)
            p.do("sync", lambda e: e.dma_start(out=io["dbg_acc"][:, 0:1024], in_=acc), "DS", 16)
            p.do("sync", lambda e: e.dma_start(out=io["dbg_acc"][:, 1024:1040], in_=zr), "DS", 16)
            p.w("vector", "DS", p.cnt["DS"])
        zrep = zr.unsqueeze(2).broadcast_to([128, 16, HD])
        evacV = p.do("vector", lambda e, zrep=zrep: e.tensor_tensor(
            otok.rearrange("p (a b) -> p a b", a=16),
            acc.rearrange("p (a b) -> p a b", a=16), zrep, ALU.mult), "V", 1)
        if b == 0 and tau == 0 and "dbg_otok" in io:
            p.w("sync", "V", evacV)
            p.do("sync", lambda e: e.dma_start(out=io["dbg_otok"][:, :], in_=otok), "DS", 16)
            p.w("tensor", "DS", p.cnt["DS"])
        p.w("tensor", "V", evacV)
        for c in range(8):
            pbf = ps["bfB"] if c < 4 else ps["bfA"]
            sem = ("T", 1) if c == 7 else (None, 0)
            tcnt2 = p.do("tensor", lambda e, c=c, pbf=pbf: e.transpose(
                pbf[:, (c % 4) * 128:(c % 4) * 128 + 128],
                otok[:, c * 128:(c + 1) * 128], ident[:, :]), sem[0], sem[1])
        p.w("vector", "T", tcnt2)
        ovA = at0.rearrange("p (c n) -> p c n", c=8)[:, 0:4, tau * 128:(tau + 1) * 128]
        ovB = at0.rearrange("p (c n) -> p c n", c=8)[:, 4:8, tau * 128:(tau + 1) * 128]
        p.do("vector", lambda e, ovA=ovA: e.tensor_copy(
            ovA, ps["bfB"].rearrange("p (c n) -> p c n", c=4)))
        evacV = p.do("vector", lambda e, ovB=ovB: e.tensor_copy(
            ovB, ps["bfA"].rearrange("p (c n) -> p c n", c=4)), "V", 1)
    tauT = p.cnt["T"]
    tauV = evacV
    if b == 0 and "dbg_at0" in io:
        p.w("sync", "V", tauV)
        for c in range(8):
            p.do("sync", lambda e, c=c: e.dma_start(
                out=io["dbg_at0"][c * 128:(c + 1) * 128, :],
                in_=at0[:, c * 4096:(c + 1) * 4096]), "DS", 16)
    p.w("gpsimd", "T", tauT)
    for pl in range(16):
        for half in range(2):
            p.do("gpsimd", lambda e, pl=pl, half=half: e.dma_start(
                out=wo[half * 64:half * 64 + 64, pl * 1024:(pl + 1) * 1024],
                in_=io["l0_Wo"][pl * 64:(pl + 1) * 64, :]), "DG", 16)
    wold = p.cnt["DG"]

    vv_bank = [tauV, tauV]
    dsc = p.cnt["DS"]
    for hq in range(H):
        p.w("scalar", "T", tauT)
        p.w("scalar", "V", evacV)
        for c in range(8):
            rows = io["xbf_tok"][b * S + hq * 256: b * S + hq * 256 + 256,
                                 c * 128:(c + 1) * 128]
            dcnt = p.do("scalar", lambda e, rows=rows, c=c: e.dma_start(
                out=xsb[:, c * 512:c * 512 + 256], in_=rows, transpose=True), "DA", 16)
        p.w("tensor", "V", tauV)
        p.w("tensor", "DG", wold)
        for fc in range(8):
            pb = ps[fc % 2]
            p.w("tensor", "V", vv_bank[fc % 2])
            for pl in range(16):
                lhsT = wo[(hq % 2) * 64:(hq % 2) * 64 + 64,
                          pl * 1024 + fc * 128: pl * 1024 + fc * 128 + 128]
                rhs = at0[(hq % 2) * 64:(hq % 2) * 64 + 64,
                          (hq // 2) * 4096:(hq // 2) * 4096 + 4096] \
                    .rearrange("p (n q) -> p q n", q=16)[:, pl, :]
                sem = ("T", 1) if pl == 15 else (None, 0)
                tcnt = p.do("tensor", lambda e, pb=pb, lhsT=lhsT, rhs=rhs, pl=pl:
                            e.matmul(pb[:, 0:256], lhsT, rhs,
                                     start=(pl == 0), stop=(pl == 15)), sem[0], sem[1])
            p.w("vector", "T", tcnt)
            p.w("vector", "DA", dcnt)
            p.w("vector", "DS", bld)
            sv = stile[:, fc * 256:(fc + 1) * 256]
            p.do("vector", lambda e, sv=sv, pb=pb, fc=fc: e.tensor_scalar(
                sv, pb[:, 0:256], bo_pp[:, fc:fc + 1], None, ALU.add))
            vv = p.do("vector", lambda e, sv=sv, fc=fc: e.tensor_tensor(
                sv, sv, xsb[:, fc * 512:fc * 512 + 256], ALU.add), "V", 1)
            vv_bank[fc % 2] = vv
            p.w("scalar", "V", vv)
            aa = p.do("scalar", lambda e, fc=fc, sv=sv: e.activation(
                sqf[:, fc * 256:(fc + 1) * 256], sv, AF.Square), "A", 1)
            p.w("tensor", "V", vv)
            p.w("tensor", "A", aa)
            p.do("tensor", lambda e, sv=sv, fc=fc: e.matmul(
                ps[2][0:1, 0:256], onesb[:, 0:1], sv,
                start=(fc == 0), stop=(fc == 7)))
            tst = p.do("tensor", lambda e, fc=fc: e.matmul(
                ps[2][0:1, 256:512], onesf[:, 0:1], sqf[:, fc * 256:(fc + 1) * 256],
                start=(fc == 0), stop=(fc == 7)), "T", 1)
        p.w("vector", "T", tst)
        p.do("vector", lambda e: e.tensor_scalar(r_mu, ps[2][0:1, 0:256], 1.0 / D, None, ALU.mult))
        p.do("vector", lambda e: e.tensor_scalar(r_m2, ps[2][0:1, 256:512], 1.0 / D, None, ALU.mult))
        p.do("vector", lambda e: e.tensor_tensor(r_rs, r_mu, r_mu, ALU.mult))
        p.do("vector", lambda e: e.tensor_tensor(r_m2, r_m2, r_rs, ALU.subtract))
        vv = p.do("vector", lambda e: e.tensor_scalar(r_m2, r_m2, 1e-5, None, ALU.add), "V", 1)
        p.w("scalar", "V", vv)
        p.do("scalar", lambda e: e.activation(r_rs, r_m2, AF.Ln))
        aa = p.do("scalar", lambda e: e.activation(r_rs, r_rs, AF.Exp, scale=-0.5), "A", 1)
        p.w("tensor", "A", aa)
        p.w("tensor", "V", x1done)
        p.do("tensor", lambda e: e.matmul(ps[3][:, 0:256], onesf[0:1, 0:128], r_mu,
                                          start=True, stop=True))
        tb = p.do("tensor", lambda e: e.matmul(ps[3][:, 256:512], onesf[0:1, 0:128], r_rs,
                                               start=True, stop=True), "T", 1)
        p.w("vector", "T", tb)
        p.w("vector", "DS", dsc)
        for fc in range(8):
            sv = stile[:, fc * 256:(fc + 1) * 256]
            ov = x1t[:, fc * 256:(fc + 1) * 256]
            p.do("vector", lambda e, sv=sv, ov=ov: e.tensor_tensor(
                ov, sv, ps[3][:, 0:256], ALU.subtract))
            x1done = p.do("vector", lambda e, ov=ov: e.tensor_tensor(
                ov, ov, ps[3][:, 256:512], ALU.mult), "V", 1)
        evacV = x1done
        p.w("sync", "V", x1done)
        for fc in range(8):
            dsc = p.do("sync", lambda e, fc=fc, hq=hq: e.dma_start(
                out=io["x1T"][fc * 128:(fc + 1) * 128,
                              b * S + hq * 256: b * S + hq * 256 + 256],
                in_=x1t[:, fc * 256:(fc + 1) * 256]), "DS", 16)




def emit_ln(p, sb, cst, ps, stile, sqf, nw, dst_fn):
    onesb, onesf = cst["onesb"], cst["onesfsb"]
    r_mu, r_m2, r_rs = sb["r_mu"][0:1, 0:nw], sb["r_m2"][0:1, 0:nw], sb["r_rs"][0:1, 0:nw]
    D_ = 1024
    tst = None
    for fc in range(8):
        sv = stile[:, fc * nw:(fc + 1) * nw]
        aa = p.do("scalar", lambda e, fc=fc, sv=sv: e.activation(
            sqf[:, fc * nw:(fc + 1) * nw], sv, AF.Square), "A", 1)
        p.w("tensor", "A", aa)
        p.do("tensor", lambda e, sv=sv, fc=fc: e.matmul(
            ps[2][0:1, 0:nw], onesb[:, 0:1], sv, start=(fc == 0), stop=(fc == 7)))
        tst = p.do("tensor", lambda e, fc=fc, nw=nw: e.matmul(
            ps[2][0:1, 256:256 + nw] if nw <= 256 else ps[2][0:1, 0:0],
            onesf[:, 0:1], sqf[:, fc * nw:(fc + 1) * nw],
            start=(fc == 0), stop=(fc == 7)), "T", 1)
    p.w("vector", "T", tst)
    p.do("vector", lambda e: e.tensor_scalar(r_mu, ps[2][0:1, 0:nw], 1.0 / D_, None, ALU.mult))
    p.do("vector", lambda e: e.tensor_scalar(r_m2, ps[2][0:1, 256:256 + nw], 1.0 / D_, None, ALU.mult))
    p.do("vector", lambda e: e.tensor_tensor(r_rs, r_mu, r_mu, ALU.mult))
    p.do("vector", lambda e: e.tensor_tensor(r_m2, r_m2, r_rs, ALU.subtract))
    vv = p.do("vector", lambda e: e.tensor_scalar(r_m2, r_m2, 1e-5, None, ALU.add), "V", 1)
    p.w("scalar", "V", vv)
    p.do("scalar", lambda e: e.activation(r_rs, r_m2, AF.Ln))
    aa = p.do("scalar", lambda e: e.activation(r_rs, r_rs, AF.Exp, scale=-0.5), "A", 1)
    p.w("tensor", "A", aa)
    p.do("tensor", lambda e: e.matmul(ps[3][:, 0:nw], onesf[0:1, 0:128], r_mu, start=True, stop=True))
    tb = p.do("tensor", lambda e: e.matmul(ps[3][:, 256:256 + nw], onesf[0:1, 0:128], r_rs, start=True, stop=True), "T", 1)
    p.w("vector", "T", tb)
    last = None
    for fc in range(8):
        sv = stile[:, fc * nw:(fc + 1) * nw]
        ov = sb["lnout"][:, fc * nw:(fc + 1) * nw]
        p.do("vector", lambda e, sv=sv, ov=ov, nw=nw: e.tensor_tensor(
            ov, sv, ps[3][:, 0:nw], ALU.subtract))
        last = p.do("vector", lambda e, ov=ov, nw=nw: e.tensor_tensor(
            ov, ov, ps[3][:, 256:256 + nw], ALU.mult), "V", 1)
        p.w("sync", "V", last)
        dst_fn(fc, ov)
    return last


def emit_ffn0(p, io, AR1, AR2, AR3, ARF, ARS, sb, cst, ps):
    nc = p.nc
    scal = sb["fscal"]
    for i, nm in enumerate(("f0_w1_s", "f0_w2_s", "f0_h1_s", "f0_h2_s")):
        p.do("sync", lambda e, i=i, nm=nm: e.dma_start(
            out=scal[:, i * 32:(i + 1) * 32], in_=io[nm][:, :]), "DS", 16)
    p.do("sync", lambda e: e.dma_start(out=scal[:, 128:136], in_=io["f0_w3_s"][:, :]), "DS", 16)
    sld = p.cnt["DS"]

    wA = AR1[:, 0:16384]
    xblk = AR3[:, 0:4096]
    hout = AR3[:, 4096:8192]
    silu_f = ARF[:, 0:512]
    guard_T = p.cnt["T"]
    guard_V = p.cnt["V"]
    guard_A = p.cnt["A"]
    p.w("gpsimd", "T", guard_T)
    p.w("sync", "V", guard_V)

    dcnt = p.cnt["DS"]
    ld = p.cnt["DG"]
    evacV = guard_V
    evacA = guard_A
    tcnt = guard_T
    hst = p.cnt["DS"]
    for js in range(4):
        p.w("gpsimd", "T", tcnt)
        for c in range(8):
            p.do("gpsimd", lambda e, c=c, js=js: e.dma_start(
                out=wA[:, c * 1024:(c + 1) * 1024],
                in_=io["f0_w1T"][c * 128:(c + 1) * 128, js * 1024:(js + 1) * 1024]), "DG", 16)
            ld = p.do("gpsimd", lambda e, c=c, js=js: e.dma_start(
                out=wA[:, 8192 + c * 1024:8192 + (c + 1) * 1024],
                in_=io["f0_w2T"][c * 128:(c + 1) * 128, js * 1024:(js + 1) * 1024]), "DG", 16)
        for tb in range(16):
            p.w("sync", "T", tcnt)
            xv = io["x1T"].rearrange("(c p) t -> p c t", p=128)[:, :, tb * 512:(tb + 1) * 512]
            dcnt = p.do("sync", lambda e, xv=xv: e.dma_start(
                out=xblk.rearrange("p (c t) -> p c t", c=8), in_=xv), "DS", 16)
            for jc in range(8):
                pb_u1, pb_u2 = ps[jc % 2 * 2], ps[jc % 2 * 2 + 1]
                p.w("tensor", "DS", dcnt)
                p.w("tensor", "DG", ld)
                p.w("tensor", "V", evacV)
                for c in range(8):
                    p.do("tensor", lambda e, pb_u1=pb_u1, c=c, jc=jc: e.matmul(
                        pb_u1[:, :], wA[:, c * 1024 + jc * 128: c * 1024 + jc * 128 + 128],
                        xblk[:, c * 512:(c + 1) * 512], start=(c == 0), stop=(c == 7)))
                for c in range(8):
                    tcnt = p.do("tensor", lambda e, pb_u2=pb_u2, c=c, jc=jc: e.matmul(
                        pb_u2[:, :], wA[:, 8192 + c * 1024 + jc * 128: 8192 + c * 1024 + jc * 128 + 128],
                        xblk[:, c * 512:(c + 1) * 512], start=(c == 0), stop=(c == 7)),
                        ("T" if c == 7 else None), (1 if c == 7 else 0))
                p.w("scalar", "T", tcnt)
                p.w("scalar", "DS", sld)
                p.w("scalar", "V", evacV)
                aa = p.do("scalar", lambda e, pb_u1=pb_u1, js=js, jc=jc: e.activation(
                    silu_f[:, :], pb_u1[:, :], AF.Silu,
                    scale=scal[:, js * 8 + jc:js * 8 + jc + 1]), "A", 1)
                p.w("vector", "A", aa)
                p.w("vector", "DS", hst)
                hv = hout[:, jc * 512:(jc + 1) * 512]
                evacV = p.do("vector", lambda e, pb_u2=pb_u2, hv=hv, js=js, jc=jc:
                             e.scalar_tensor_tensor(
                                 hv, pb_u2[:, :], scal[:, 32 + js * 8 + jc:32 + js * 8 + jc + 1],
                                 silu_f[:, :], ALU.mult, ALU.mult), "V", 1)
                p.w("sync", "V", evacV)
                hst = p.do("sync", lambda e, js=js, jc=jc, tb=tb, hv=hv: e.dma_start(
                    out=io["hA"][js * 1024 + jc * 128: js * 1024 + jc * 128 + 128,
                                 tb * 512:(tb + 1) * 512], in_=hv), "DS", 16)

    for stage, (wt, scol, src, dst) in enumerate((
            ("f0_h1T", 64, "hA", "hB"), ("f0_h2T", 96, "hB", "hA"))):
        wsl = AR2[:, 0:32768]
        hblk = AR1[:, 0:16384]
        gout = AR3[:, 4096:8192]
        gst = p.cnt["DS"]
        for js in range(4):
            p.w("gpsimd", "T", tcnt)
            for c in range(32):
                ld = p.do("gpsimd", lambda e, c=c, js=js, wt=wt: e.dma_start(
                    out=wsl[:, c * 1024:(c + 1) * 1024],
                    in_=io[wt][c * 128:(c + 1) * 128, js * 1024:(js + 1) * 1024]), "DG", 16)
            for tb in range(16):
                p.w("sync", "T", tcnt)
                hv2 = io[src].rearrange("(c p) t -> p c t", p=128)[:, :, tb * 512:(tb + 1) * 512]
                dcnt = p.do("sync", lambda e, hv2=hv2: e.dma_start(
                    out=hblk.rearrange("p (c t) -> p c t", c=32), in_=hv2), "DS", 16)
                for jc in range(8):
                    pb = ps[jc % 4]
                    p.w("tensor", "DS", dcnt)
                    p.w("tensor", "DG", ld)
                    p.w("tensor", "A", evacA)
                    for c in range(32):
                        tcnt = p.do("tensor", lambda e, pb=pb, c=c, jc=jc: e.matmul(
                            pb[:, :], wsl[:, c * 1024 + jc * 128: c * 1024 + jc * 128 + 128],
                            hblk[:, c * 512:(c + 1) * 512], start=(c == 0), stop=(c == 31)),
                            ("T" if c == 31 else None), (1 if c == 31 else 0))
                    p.w("scalar", "T", tcnt)
                    p.w("scalar", "DS", gst)
                    gv = gout[:, jc * 512:(jc + 1) * 512]
                    evacA = p.do("scalar", lambda e, pb=pb, gv=gv, js=js, jc=jc, scol=scol:
                                 e.activation(gv, pb[:, :], AF.Gelu,
                                              scale=scal[:, scol + js * 8 + jc:scol + js * 8 + jc + 1]),
                                 "A", 1)
                    p.w("sync", "A", evacA)
                    gst = p.do("sync", lambda e, js=js, jc=jc, tb=tb, gv=gv, dst=dst: e.dma_start(
                        out=io[dst][js * 1024 + jc * 128: js * 1024 + jc * 128 + 128,
                                    tb * 512:(tb + 1) * 512], in_=gv), "DS", 16)

    w3 = AR2[:, 0:32768]
    p.w("gpsimd", "T", tcnt)
    for c in range(32):
        ld = p.do("gpsimd", lambda e, c=c: e.dma_start(
            out=w3[:, c * 1024:(c + 1) * 1024],
            in_=io["f0_w3T"][c * 128:(c + 1) * 128, :]), "DG", 16)
    gblk = AR1[:, 0:16384]
    xres = AR3[:, 0:4096]
    stile = AR3[:, 4096:8192]
    sb["lnout"] = AR3[:, 8192:12288]
    sqf = ARF[:, 0:4096]
    x2st = p.cnt["DS"]
    lnV = p.cnt["V"]
    for tb in range(16):
        p.w("sync", "T", tcnt)
        gv2 = io["hA"].rearrange("(c p) t -> p c t", p=128)[:, :, tb * 512:(tb + 1) * 512]
        p.do("sync", lambda e, gv2=gv2: e.dma_start(
            out=gblk.rearrange("p (c t) -> p c t", c=32), in_=gv2), "DS", 16)
        xv2 = io["x1T"].rearrange("(c p) t -> p c t", p=128)[:, :, tb * 512:(tb + 1) * 512]
        dcnt = p.do("sync", lambda e, xv2=xv2: e.dma_start(
            out=xres.rearrange("p (c t) -> p c t", c=8), in_=xv2), "DS", 16)
        vlast = None
        for ec in range(8):
            pb = ps[ec % 2]
            p.w("tensor", "DS", dcnt)
            p.w("tensor", "DG", ld)
            p.w("tensor", "V", lnV)
            for c in range(32):
                tcnt = p.do("tensor", lambda e, pb=pb, c=c, ec=ec: e.matmul(
                    pb[:, :], w3[:, c * 1024 + ec * 128: c * 1024 + ec * 128 + 128],
                    gblk[:, c * 512:(c + 1) * 512], start=(c == 0), stop=(c == 31)),
                    ("T" if c == 31 else None), (1 if c == 31 else 0))
            p.w("vector", "T", tcnt)
            p.w("vector", "DS", x2st)
            sv = stile[:, ec * 512:(ec + 1) * 512]
            lnV = p.do("vector", lambda e, pb=pb, sv=sv, ec=ec: e.scalar_tensor_tensor(
                sv, pb[:, :], scal[:, 128 + ec:128 + ec + 1],
                xres[:, ec * 512:(ec + 1) * 512], ALU.mult, ALU.add), "V", 1)
        p.w("scalar", "V", lnV)
        x2st = emit_ffn_ln512(p, io, sb, cst, ps, stile, sqf, tb, "x2T")
        lnV = p.cnt["V"]


def emit_ffn_ln512(p, io, sb, cst, ps, stile, sqf, tb, dst):
    onesb, onesf = cst["onesb"], cst["onesfsb"]
    nw = 512
    r_mu, r_m2, r_rs = sb["r_mu"][0:1, 0:nw], sb["r_m2"][0:1, 0:nw], sb["r_rs"][0:1, 0:nw]
    tst = None
    for fc in range(8):
        sv = stile[:, fc * nw:(fc + 1) * nw]
        aa = p.do("scalar", lambda e, fc=fc, sv=sv: e.activation(
            sqf[:, fc * nw:(fc + 1) * nw] if fc < 8 else None, sv, AF.Square), "A", 1)
        p.w("tensor", "A", aa)
        p.do("tensor", lambda e, sv=sv, fc=fc: e.matmul(
            ps[2][0:1, 0:nw], onesb[:, 0:1], sv, start=(fc == 0), stop=(fc == 7)))
        tst = p.do("tensor", lambda e, fc=fc: e.matmul(
            ps[3][0:1, 0:nw], onesf[:, 0:1], sqf[:, fc * nw:(fc + 1) * nw],
            start=(fc == 0), stop=(fc == 7)), "T", 1)
    p.w("vector", "T", tst)
    p.do("vector", lambda e: e.tensor_scalar(r_mu, ps[2][0:1, 0:nw], 1.0 / 1024, None, ALU.mult))
    p.do("vector", lambda e: e.tensor_scalar(r_m2, ps[3][0:1, 0:nw], 1.0 / 1024, None, ALU.mult))
    p.do("vector", lambda e: e.tensor_tensor(r_rs, r_mu, r_mu, ALU.mult))
    p.do("vector", lambda e: e.tensor_tensor(r_m2, r_m2, r_rs, ALU.subtract))
    vv = p.do("vector", lambda e: e.tensor_scalar(r_m2, r_m2, 1e-5, None, ALU.add), "V", 1)
    p.w("scalar", "V", vv)
    p.do("scalar", lambda e: e.activation(r_rs, r_m2, AF.Ln))
    aa = p.do("scalar", lambda e: e.activation(r_rs, r_rs, AF.Exp, scale=-0.5), "A", 1)
    p.w("tensor", "A", aa)
    p.do("tensor", lambda e: e.matmul(ps[2][:, 0:nw], onesf[0:1, 0:128], r_mu, start=True, stop=True))
    tb2 = p.do("tensor", lambda e: e.matmul(ps[3][:, 0:nw], onesf[0:1, 0:128], r_rs, start=True, stop=True), "T", 1)
    p.w("vector", "T", tb2)
    stD = p.cnt["DS"]
    dual = (dst == "x2T" and "x2Tf" in io)
    for fc in range(8):
        sv = stile[:, fc * nw:(fc + 1) * nw]
        ov = sb["lnout"][:, fc * nw:(fc + 1) * nw]
        if dual:
            ovf = sb["lnoutf"]
            p.w("vector", "DS", stD)
            p.do("vector", lambda e, sv=sv, ovf=ovf: e.tensor_tensor(ovf, sv, ps[2][:, 0:nw], ALU.subtract))
            lastf = p.do("vector", lambda e, ovf=ovf: e.tensor_tensor(ovf, ovf, ps[3][:, 0:nw], ALU.mult), "V", 1)
            p.w("sync", "V", lastf)
            stD = p.do("sync", lambda e, fc=fc, tb=tb, ovf=ovf: e.dma_start(
                out=io["x2Tf"][fc * 128:(fc + 1) * 128, tb * 512:(tb + 1) * 512], in_=ovf), "DS", 16)
        p.do("vector", lambda e, sv=sv, ov=ov: e.tensor_tensor(ov, sv, ps[2][:, 0:nw], ALU.subtract))
        last = p.do("vector", lambda e, ov=ov: e.tensor_tensor(ov, ov, ps[3][:, 0:nw], ALU.mult), "V", 1)
        p.w("sync", "V", last)
        stD = p.do("sync", lambda e, fc=fc, tb=tb, ov=ov, dst=dst: e.dma_start(
            out=io[dst][fc * 128:(fc + 1) * 128, tb * 512:(tb + 1) * 512], in_=ov), "DS", 16)
    return stD




def emit_l1qkv(p, io, AR1, AR2, AR3, ARF, ARS, sb, cst, ps, b):
    onesb, onesf, lup, gmats = cst["onesb"], cst["onesfsb"], cst["lup"], cst["gmats"]
    probes = sb["probes"]
    x2blk = AR3[:, 0:4096]
    ev = AR3[:, 4096:5120]
    ohP = AR3[:, 5120:11264]
    ranks = sb["ranks"]
    qrtf = AR2[:, 0:32768].bitcast(F32)
    tptf = ARF[:, 0:32]
    sptf = ARF[:, 32:64]
    cnt1 = ARF[:, 64:65]
    gst1 = ARF[:, 65:66]
    wqf = ARF[:, 128:1152]
    rext = ARF[:, 128:640]
    projsb = ARF[:, 1152:4224]
    diag = ARF[:, 4224:6272]
    wq1 = sb["wq1"]; wk1 = sb["wk1"]; wv1 = sb["wv1"]; brow1 = sb["brow1"]

    if b == 0:
        p.w("gpsimd", "T", p.cnt["T"])
        for nm, dstv in (("l1_Wq", wq1), ("l1_Wk", wk1), ("l1_Wv", wv1)):
            for c in range(8):
                p.do("gpsimd", lambda e, d=dstv[:, c * 1024:(c + 1) * 1024],
                     s=io[nm][c * 128:(c + 1) * 128, :]: e.dma_start(out=d, in_=s),
                     "DG", 16)
        for i, nm in enumerate(("l1_bq_row", "l1_bk_row", "l1_bv_row")):
            p.do("gpsimd", lambda e, d=brow1[0:1, i * 1024:(i + 1) * 1024],
                 s=io[nm][0:1, :]: e.dma_start(out=d, in_=s), "DG", 16)
        for h in range(16):
            p.do("sync", lambda e, h=h: e.dma_start(
                out=probes[(h % 2) * 64:(h % 2) * 64 + 64, (h // 2) * 192:(h // 2) * 192 + 192],
                in_=io["l1_probes"][h, :, :]), "DS", 16)
        p.do("sync", lambda e: e.dma_start(out=sb["bqpp1"], in_=io["l1_bq_pp"][:, :]), "DS", 16)
    wload = p.cnt["DG"]
    sload = p.cnt["DS"]
    wmap = (wq1, wk1, wv1)

    evacA = p.cnt["A"]
    tcnt = p.cnt["T"]
    dcnt = p.cnt["DS"]
    est = p.cnt["DS"]
    p.w("tensor", "V", p.cnt["V"])
    p.w("tensor", "A", p.cnt["A"])
    for tau in range(32):
        blkc = tau // 4
        if tau % 4 == 0:
            p.w("sync", "T", tcnt)
            for c in range(8):
                dcnt = p.do("sync", lambda e, c=c, blkc=blkc: e.dma_start(
                    out=x2blk[:, c * 512:(c + 1) * 512],
                    in_=io["x2T"][c * 128:(c + 1) * 128,
                                  b * S + blkc * 512: b * S + blkc * 512 + 512]), "DS", 16)
        waves = [[(0, 0), (0, 1), (1, 0), (1, 1)], [(2, 0), (2, 1)]]
        for wi, wave in enumerate(waves):
            p.w("tensor", "A", evacA)
            p.w("tensor", "DS", dcnt)
            p.w("tensor", "DG", wload)
            for j, (i, fh) in enumerate(wave):
                pb = ps[j]
                for c in range(8):
                    lhsT = x2blk[:, c * 512 + (tau % 4) * 128: c * 512 + (tau % 4) * 128 + 128]
                    rhs = wmap[i][:, c * 1024 + fh * 512: c * 1024 + fh * 512 + 512]
                    p.do("tensor", lambda e, pb=pb, lhsT=lhsT, rhs=rhs, c=c:
                         e.matmul(pb[:, :], lhsT, rhs, start=(c == 0), stop=False))
                tcnt = p.do("tensor", lambda e, pb=pb, i=i, fh=fh: e.matmul(
                    pb[:, :], onesb[0:1, 0:128],
                    brow1[0:1, i * 1024 + fh * 512: i * 1024 + fh * 512 + 512],
                    start=False, stop=True), "T", 1)
            p.w("scalar", "T", tcnt)
            for j, (i, fh) in enumerate(wave):
                evv = ev[:, (j % 2) * 512:(j % 2) * 512 + 512]
                p.w("scalar", "DS", est)
                evacA = p.do("scalar", lambda e, j=j, evv=evv: e.activation(
                    evv, ps[j][:, :], AF.Identity), "A", 1)
                p.w("sync", "A", evacA)
                ov = io["qkvs"].rearrange("b h t r -> b h t r")[
                    b, fh * 8:(fh + 1) * 8, tau * 128:(tau + 1) * 128, i * 64:(i + 1) * 64]                     .rearrange("h t r -> t h r")
                est = p.do("sync", lambda e, ov=ov, evv=evv: e.dma_start(
                    out=ov, in_=evv.rearrange("p (h r) -> p h r", h=8)), "DS", 16)

    x2f = AR3[:, 0:8192].bitcast(F32)
    for hh2 in range(2):
        p.w("tensor", "V", p.cnt["V"])
        p.w("sync", "V", p.cnt["V"])
        tq = p.cnt["T"]
        for fcl in range(4):
            fc = hh2 * 4 + fcl
            p.w("sync", "T", tcnt)
            wqld = p.do("sync", lambda e, fc=fc: e.dma_start(
                out=wqf.rearrange("p (c n) -> p c n", n=128),
                in_=io["l1_Wq"].rearrange("(c p) f -> p c f", p=128)[:, :, fc * 128:(fc + 1) * 128]), "DS", 16)
            for tb8 in range(8):
                p.w("sync", "T", tcnt)
                for c in range(8):
                    dcnt = p.do("sync", lambda e, c=c, tb8=tb8: e.dma_start(
                        out=x2f[:, c * 512:(c + 1) * 512],
                        in_=io["x2Tf"][c * 128:(c + 1) * 128,
                                       b * S + tb8 * 512: b * S + tb8 * 512 + 512]), "DS", 16)
                pb = ps[tb8 % 2]
                p.w("tensor", "DS", dcnt)
                p.w("tensor", "DS", wqld)
                p.w("tensor", "A", evacA)
                for c in range(8):
                    tcnt = p.do("tensor", lambda e, pb=pb, c=c: e.matmul(
                        pb[:, :], wqf[:, c * 128:(c + 1) * 128],
                        x2f[:, c * 512:(c + 1) * 512], start=(c == 0), stop=(c == 7)),
                        ("T" if c == 7 else None), (1 if c == 7 else 0))
                p.w("scalar", "T", tcnt)
                p.w("scalar", "DS", sload)
                evacA = p.do("scalar", lambda e, pb=pb, fc=fc, fcl=fcl, tb8=tb8: e.activation(
                    qrtf[:, fcl * 4096 + tb8 * 512: fcl * 4096 + tb8 * 512 + 512],
                    pb[:, :], AF.Identity, bias=sb["bqpp1"][:, fc:fc + 1]), "A", 1)
        qdone = evacA
        if b == 0 and hh2 == 0 and "dbg_qrt" in io:
            p.w("sync", "A", qdone)
            p.do("sync", lambda e: e.dma_start(out=io["dbg_qrt"][:, :], in_=qrtf), "DS", 16)
            p.w("tensor", "DS", p.cnt["DS"])
        for fcl in range(4):
            fc = hh2 * 4 + fcl
            for hh in range(2):
                h = fc * 2 + hh
                m96 = ARF[:, 6272:6320]
                pc16 = ARF[:, 6320:6576]
                p.w("tensor", "A", qdone)
                p.w("tensor", "DS", sload)
                ohdone = p.cnt["V"]
                for half in range(2):
                    p.w("tensor", "V", ohdone)
                    pr = p.cnt["A"]
                    for ta in range(16):
                        tau = half * 16 + ta
                        pb = ps[ta % 2]
                        p.w("tensor", "A", pr if ta < 2 else evacA)
                        tcnt = p.do("tensor", lambda e, pb=pb, tau=tau, fcl=fcl, hh=hh, fc=fc:
                                    e.matmul(pb[:, 0:192],
                                             qrtf[hh * 64:hh * 64 + 64,
                                                  fcl * 4096 + tau * 128: fcl * 4096 + tau * 128 + 128],
                                             probes[hh * 64:hh * 64 + 64,
                                                    fc * 192:(fc + 1) * 192],
                                             start=True, stop=True), "T", 1)
                        p.w("scalar", "T", tcnt)
                        evacA = p.do("scalar", lambda e, pb=pb, ta=ta: e.activation(
                            projsb[:, ta * 192:(ta + 1) * 192], pb[:, 0:192],
                            AF.Identity), "A", 1)
                    p.w("vector", "A", evacA)
                    ohh = ohP.rearrange("p (g x) -> p g x", x=192)[:, half * 16:half * 16 + 16, :]
                    p.do("vector", lambda e: e.reduce_max(
                        m96, projsb.rearrange("p (g a) -> p g a", a=64), axis=AX.X))
                    p.do("vector", lambda e: e.reduce_max(
                        pc16.rearrange("p (g c) -> p g c", c=16),
                        projsb.rearrange("p (g x) -> p g x", x=192)[:, :, 0:64]
                        .rearrange("p g (a c) -> p g a c", a=4).transpose([0, 1, 3, 2]),
                        axis=AX.X))
                    fce = p.do("vector", lambda e: e.tensor_copy(rext[:, 0:48], m96), "V", 1)
                    p.w("vector", "V", fce)
                    m0 = m96.rearrange("p (g l) -> p g l", l=3)
                    p.do("vector", lambda e, ohh=ohh: e.tensor_tensor(
                        ohh[:, :, 0:64].rearrange("p g (a c) -> p g a c", a=4),
                        pc16.rearrange("p (g c) -> p g c", c=16).unsqueeze(2).broadcast_to([128, 16, 4, 16]),
                        m0[:, :, 0:1].unsqueeze(3).broadcast_to([128, 16, 4, 16]),
                        ALU.is_ge))
                    for lvl in (1, 2):
                        p.do("vector", lambda e, lvl=lvl, ohh=ohh: e.tensor_tensor(
                            ohh[:, :, lvl * 64:(lvl + 1) * 64],
                            projsb.rearrange("p (g x) -> p g x", x=192)[:, :, lvl * 64:(lvl + 1) * 64],
                            m0[:, :, lvl:lvl + 1].broadcast_to([128, 16, 64]),
                            ALU.is_ge))
                    ohdone = p.do("vector", lambda e: e.tensor_copy(rext[:, 0:192], ohP[:, 0:192]), "V", 1)
                    if b == 0 and h == 0 and "dbg_oh" in io:
                        p.w("sync", "V", ohdone)
                        p.do("sync", lambda e, half=half: e.dma_start(
                            out=io["dbg_proj"][:, half * 3072:(half + 1) * 3072], in_=projsb), "DS", 16)
                        p.w("vector", "DS", p.cnt["DS"])
                        p.w("scalar", "DS", p.cnt["DS"])
                if b == 0 and h == 0 and "dbg_oh" in io:
                    p.w("sync", "V", ohdone)
                    p.do("sync", lambda e: e.dma_start(out=io["dbg_oh"][:, 0:6144], in_=ohP), "DS", 16)
                    p.w("vector", "DS", p.cnt["DS"])
                for lvl in range(3):
                    csize = 4 if lvl == 0 else 1
                    ohv = ohP.rearrange("p (g x) -> p g x", x=192)[:, :, lvl * 64:(lvl + 1) * 64]
                    p.w("tensor", "V", ohdone)
                    for tau in range(32):
                        tcnt = p.do("tensor", lambda e, tau=tau, ohv=ohv: e.matmul(
                            ps[3][0:64, tau:tau + 1], ohv[:, tau, :], onesb[:, 0:1],
                            start=True, stop=True), "T", 1)
                    p.w("vector", "T", tcnt)
                    vv = p.do("vector", lambda e: e.tensor_copy(tptf[0:64, :], ps[3][0:64, 0:32]), "V", 1)
                    p.w("vector", "V", vv)
                    sc = p.do("vector", lambda e: e.tensor_tensor_scan(
                        sptf[0:64, :], onesf[0:64, 0:32], tptf[0:64, :], 0.0,
                        ALU.mult, ALU.add), "V", 1)
                    p.w("vector", "V", sc)
                    vv = p.do("vector", lambda e: e.tensor_copy(cnt1[0:64, :], sptf[0:64, 31:32]), "V", 1)
                    p.w("vector", "V", vv)
                    p.do("vector", lambda e: e.tensor_tensor(
                        sptf[0:64, :], sptf[0:64, :], tptf[0:64, :], ALU.subtract))
                    vv = p.do("vector", lambda e: e.tensor_copy(rext[0:64, 0:32], sptf[0:64, :]), "V", 1)
                    p.w("tensor", "V", vv)
                    tg = p.do("tensor", lambda e, lvl=lvl: e.matmul(
                        ps[3][0:64, 40:41], gmats[:, lvl * 64:(lvl + 1) * 64], cnt1[0:64, :],
                        start=True, stop=True), "T", 1)
                    p.w("vector", "T", tg)
                    vv = p.do("vector", lambda e: e.tensor_copy(gst1[0:64, :], ps[3][0:64, 40:41]), "V", 1)
                    p.w("vector", "V", vv)
                    vv = p.do("vector", lambda e: e.tensor_scalar(
                        sptf[0:64, :], sptf[0:64, :], gst1[0:64, 0:1], None, ALU.add), "V", 1)
                    p.w("vector", "V", vv)
                    dv = p.do("vector", lambda e: e.tensor_tensor(
                        diag.rearrange("p (g x) -> p g x", x=64)[0:64, :, :],
                        cst["identf"][0:64, 0:64].unsqueeze(1).broadcast_to([64, 32, 64]),
                        sptf[0:64, :].unsqueeze(2).broadcast_to([64, 32, 64]),
                        ALU.mult), "V", 1)
                    p.w("tensor", "V", dv)
                    for tau in range(32):
                        pa = ps[tau // 8]
                        po = (tau % 8) * 64
                        p.do("tensor", lambda e, pa=pa, po=po, tau=tau, ohv=ohv: e.matmul(
                            pa[:, po:po + 64], lup[:, :], ohv[:, tau, :],
                            start=True, stop=False))
                        tcnt = p.do("tensor", lambda e, pa=pa, po=po, tau=tau: e.matmul(
                            pa[:, po:po + 64], onesf[0:64, 0:128],
                            diag.rearrange("p (g x) -> p g x", x=64)[0:64, tau, :],
                            start=False, stop=True), "T", 1)
                    p.w("vector", "T", tcnt)
                    rsb = ranks.bitcast(F32)[:, ((b * 16 + h) * 3 + lvl) * 32:
                                             ((b * 16 + h) * 3 + lvl) * 32 + 32]
                    for bank in range(4):
                        p.do("vector", lambda e, bank=bank, ohv=ohv: e.tensor_tensor(
                            rext.rearrange("p (g x) -> p g x", x=64),
                            ps[bank][:, :].rearrange("p (g x) -> p g x", x=64),
                            ohv[:, bank * 8:(bank + 1) * 8, :],
                            ALU.mult))
                        rr = p.do("vector", lambda e, bank=bank, rsb=rsb: e.reduce_sum(
                            rsb[:, bank * 8:(bank + 1) * 8],
                            rext.rearrange("p (g x) -> p g x", x=64), axis=AX.X), "V", 1)
                    p.w("vector", "V", rr)
                    if b == 0 and h == 0 and lvl == 0 and "dbg_cnt" in io:
                        fz = p.do("vector", lambda e: e.tensor_copy(rext[:, 33:65], tptf[:, :]), "V", 1)
                        p.w("sync", "V", fz)
                        p.do("sync", lambda e: e.dma_start(out=io["dbg_cnt"][:, 0:32], in_=tptf[:, :]), "DS", 16)
                        p.do("sync", lambda e: e.dma_start(out=io["dbg_cnt"][:, 32:64], in_=sptf[:, :]), "DS", 16)
                        p.do("sync", lambda e: e.dma_start(out=io["dbg_cnt"][:, 64:65], in_=cnt1[:, :], allow_slow_non_contiguous=True), "DS", 16)
                        p.do("sync", lambda e: e.dma_start(out=io["dbg_cnt"][:, 65:66], in_=gst1[:, :], allow_slow_non_contiguous=True), "DS", 16)
                        p.do("sync", lambda e, rsb=rsb: e.dma_start(out=io["dbg_cnt"][:, 66:98], in_=rsb), "DS", 16)
                        p.w("vector", "DS", p.cnt["DS"])
                    fr = p.do("vector", lambda e, rsb=rsb, csize=csize: e.tensor_scalar(
                        rext[:, 0:32], rsb, 1.0 / csize, 4095.0, ALU.mult, ALU.min), "V", 1)
                    p.w("vector", "V", fr)
                    ru = ranks[:, ((b * 16 + h) * 3 + lvl) * 32: ((b * 16 + h) * 3 + lvl) * 32 + 32]
                    ohdone = p.do("vector", lambda e, ru=ru: e.tensor_copy(ru, rext[:, 0:32]), "V", 1)


def emit_dump_ranks(p, io, sb):
    p.w("sync", "V", p.cnt["V"])
    p.do("sync", lambda e: e.dma_start(out=io["dbg_ranks"][:, :], in_=sb["ranks"][:, :]), "DS", 16)




def lvl_spans(lvl):
    cs = (256, 64, 16)[lvl]
    spans = []
    for kb in range(32):
        if lvl == 0:
            g = kb // 2
            a, bnd = g * 256, min(g * 256 + 512, 4096)
        else:
            a, bnd = kb * 128, min(kb * 128 + 128 + cs, 4096)
        spans.append((a, bnd))
    return spans


def emit_l1attn(p, io, AR1, AR2, AR3, ARF, ARS, sb, cst, ps, b):
    ranks = sb["ranks"]
    onesb, ident = cst["onesb"], cst["ident"]
    qkvrows = AR2[:, 0:16384]
    qkT = AR1[:, 0:4096]
    kvT = AR1[:, 4096:8192]
    v65 = AR1[:, 8192:10272]
    expT = AR1[:, 10272:11296]
    out_tok = AR1[:, 11296:13344]
    outsb = AR1[:, 13344:13856]
    acc = AR1[:, 13856:17952]
    rz = ARS[:, 160:168]

    p.w("gpsimd", "V", p.cnt["V"])
    p.w("gpsimd", "T", p.cnt["T"])
    if b == 0:
        p.do("gpsimd", lambda e: e.memset(v65[:, :], 1.0), "G", 1)
    mst = p.cnt["G"]

    ld_qkv = [p.cnt["DG"], p.cnt["DG"]]
    vld = [p.cnt["DS"], p.cnt["DS"]]
    ebuse = [p.cnt["T"], p.cnt["T"]]
    sc_done = [p.cnt["DG"], p.cnt["DG"]]
    tr_done = [p.cnt["DA"], p.cnt["DA"]]
    att_done = [p.cnt["T"], p.cnt["T"]]
    g_done = [p.cnt["DG"], p.cnt["DG"]]
    ot_st = [p.cnt["DS"], p.cnt["DS"]]
    rows_use = [p.cnt["DG"], p.cnt["DG"]]
    acst = p.cnt["DS"]
    evacV = p.cnt["V"]
    evacA = p.cnt["A"]
    tcnt = p.cnt["T"]
    g_last = p.cnt["DG"]
    pending = None
    p.w("tensor", "V", p.cnt["V"])

    for h in range(16):
        u = (b * 16 + h) * 3
        hb = h % 2
        p.w("gpsimd", "DG", rows_use[hb])
        src = io["qkvs"][b, h, :, :].rearrange("(c p) r -> p c r", p=128)
        qr = qkvrows[:, hb * 8192:(hb + 1) * 8192]
        ld_qkv[hb] = p.do("gpsimd", lambda e, src=src, qr=qr: e.dma_start(
            out=qr.rearrange("p (c r) -> p c r", r=256), in_=src), "DG", 16)
        for lvl in range(3):
            sbuf_i = (h * 3 + lvl) % 2
            qs = io["qkv_sorted%d" % sbuf_i]
            osrt = io["outsorted%d" % sbuf_i]
            p.w("gpsimd", "DG", ld_qkv[hb])
            p.w("gpsimd", "DA", tr_done[sbuf_i])
            p.w("gpsimd", "DS", vld[sbuf_i])
            for tau in range(32):
                sc_done[sbuf_i] = p.do("gpsimd", lambda e, tau=tau, u=u, lvl=lvl, qs=qs, qr=qr:
                                       e.indirect_dma_start(
                    out=qs[:, :],
                    out_offset=IndirectOffsetOnAxis(
                        ranks[:, (u + lvl) * 32 + tau:(u + lvl) * 32 + tau + 1], axis=0),
                    in_=qr[:, tau * 256:(tau + 1) * 256], in_offset=None), "DG", 16)
            rows_use[hb] = sc_done[sbuf_i]
            if pending is not None:
                pending()
                pending = None
            p.w("scalar", "DG", sc_done[sbuf_i])
            p.w("scalar", "T", att_done[0])
            p.w("scalar", "T", att_done[1])
            for w in range(8):
                p.do("scalar", lambda e, w=w, qs=qs: e.dma_start(
                    out=qkT[:, w * 512:(w + 1) * 512],
                    in_=qs[w * 512:(w + 1) * 512, 0:128],
                    transpose=True), "DA", 16)
                tr_done[sbuf_i] = p.do("scalar", lambda e, w=w, qs=qs: e.dma_start(
                    out=kvT[:, w * 512:(w + 1) * 512],
                    in_=qs[w * 512:(w + 1) * 512, 64:192],
                    transpose=True), "DA", 16)
            p.w("sync", "DG", sc_done[sbuf_i])
            p.w("sync", "G", mst)
            p.w("sync", "T", att_done[0])
            p.w("sync", "T", att_done[1])
            vld[sbuf_i] = p.do("sync", lambda e, qs=qs: e.dma_start(
                out=v65.rearrange("p (c r) -> p c r", r=65)[:, :, 0:64],
                in_=qs[:, 128:192].rearrange("(c p) r -> p c r", p=128)), "DS", 16)
            if b == 0 and h == 0 and lvl == 0 and "dbg_qsort" in io:
                p.w("sync", "DG", sc_done)
                p.do("sync", lambda e: e.dma_start(out=io["dbg_qsort"][:, :], in_=io["qkv_sorted0"][:, :]), "DS", 16)
                p.w("scalar", "DA", 0)
                dqs = p.cnt["DS"]
                p.w("scalar", "DS", dqs)
            spans = lvl_spans(lvl)
            mask = cst["mask1"] if lvl == 1 else (cst["mask2"] if lvl == 2 else None)
            base_evacV = p.cnt["V"]
            win_state = {}
            win_open = [False] * 8
            closers = [max(kb for kb in range(32)
                           if spans[kb][0] < (w + 1) * 512 and spans[kb][1] > w * 512)
                       for w in range(8)]
            widx = 0
            for kb in range(32):
                a, bnd = spans[kb]
                pbs = ps[kb % 2]
                p.w("tensor", "DA", tr_done[sbuf_i])
                p.w("tensor", "DS", vld[sbuf_i])
                p.w("tensor", "A", evacA)
                tcnt = p.do("tensor", lambda e, pbs=pbs, kb=kb, a=a, bnd=bnd: e.matmul(
                    pbs[:, 0:bnd - a], kvT[0:64, kb * 128:(kb + 1) * 128],
                    qkT[0:64, a:bnd], start=True, stop=True), "T", 1)
                p.w("scalar", "T", tcnt)
                p.w("scalar", "T", ebuse[kb % 2])
                eb = expT[:, (kb % 2) * 512:(kb % 2) * 512 + 512]
                evacA = p.do("scalar", lambda e, pbs=pbs, eb=eb, a=a, bnd=bnd: e.activation(
                    eb[:, 0:bnd - a], pbs[:, 0:bnd - a], AF.Exp, scale=0.125), "A", 1)
                if b == 0 and h == 0 and lvl == 0 and kb in (0, 1) and "dbg_win" in io:
                    p.w("sync", "A", evacA)
                    p.do("sync", lambda e, eb=eb, kb=kb, a=a, bnd=bnd: e.dma_start(
                        out=io["dbg_exp"][128 * kb:128 * kb + 128, 0:bnd - a], in_=eb[:, 0:bnd - a]), "DS", 16)
                    p.w("tensor", "DS", p.cnt["DS"])
                    p.w("scalar", "DS", p.cnt["DS"])
                if mask is not None:
                    p.w("vector", "A", evacA)
                    evacV = p.do("vector", lambda e, eb=eb, mask=mask, a=a, bnd=bnd: e.tensor_tensor(
                        eb[:, 0:bnd - a], eb[:, 0:bnd - a], mask[:, 0:bnd - a], ALU.mult), "V", 1)
                    ebsrc = eb
                    p.w("tensor", "V", evacV)
                else:
                    ebsrc = eb
                    p.w("tensor", "A", evacA)
                pos = a
                while pos < bnd:
                    w = pos // 512
                    pe = min(bnd, (w + 1) * 512)
                    pw = ps[2 + (w % 2)]
                    if not win_open[w]:
                        win_open[w] = True
                        p.w("tensor", "V", win_state.get(w - 2, base_evacV))
                        p.do("tensor", lambda e, pw=pw: e.matmul(
                            pw[0:65, :], cst["zz"][0:1, 0:65], cst["zz"][0:1, 0:512],
                            start=True, stop=False))
                    last_piece = (pe >= bnd)
                    tret = p.do("tensor", lambda e, pw=pw, kb=kb, pos=pos, pe=pe,
                                a=a, w=w, ebsrc=ebsrc: e.matmul(
                        pw[0:65, pos - w * 512:pe - w * 512],
                        v65[:, kb * 65:(kb + 1) * 65], ebsrc[:, pos - a:pe - a],
                        start=False, stop=False),
                        "T" if last_piece else None, 1 if last_piece else 0)
                    if tret is not None:
                        ebuse[kb % 2] = tret
                    pos = pe
                for w in range(8):
                    if win_open[w] and closers[w] == kb:
                        pw = ps[2 + (w % 2)]
                        tcl = p.do("tensor", lambda e, pw=pw: e.matmul(
                            pw[0:65, 0:1], cst["zz"][0:1, 0:65], cst["zz"][0:1, 0:1],
                            start=False, stop=True), "T", 1)
                        p.w("vector", "T", tcl)
                        p.w("vector", "T", tcnt)
                        cpv = p.do("vector", lambda e, pw=pw: e.tensor_copy(
                            outsb[0:65, 0:512], pw[0:65, :]), "V", 1)
                        if b == 0 and h == 0 and lvl == 0 and w == 0 and "dbg_win" in io:
                            p.w("sync", "V", cpv)
                            p.do("sync", lambda e: e.dma_start(out=io["dbg_win"][0:65, :], in_=outsb[0:65, 0:512]), "DS", 16)
                            p.w("tensor", "DS", p.cnt["DS"])
                        p.w("tensor", "V", cpv)
                        pbf = ps["bfA"] if (w % 2) == 0 else ps["bfB"]
                        for tt in range(4):
                            sm = ("T", 1) if tt == 3 else (None, 0)
                            ttc = p.do("tensor", lambda e, tt=tt, pbf=pbf: e.transpose(
                                pbf[:, tt * 128:tt * 128 + 65],
                                outsb[0:65, tt * 128:(tt + 1) * 128],
                                ident[0:65, 0:65]), sm[0], sm[1])
                        p.w("vector", "T", ttc)
                        for tt in range(4):
                            rc = p.do("vector", lambda e, tt=tt, pbf=pbf: e.reciprocal(
                                rz[:, tt:tt + 1], pbf[:, tt * 128 + 64:tt * 128 + 65]), "V", 1)
                            p.w("vector", "V", rc)
                            evacV = p.do("vector", lambda e, tt=tt, w=w, pbf=pbf: e.tensor_scalar(
                                out_tok[:, (w * 4 + tt) * 64:(w * 4 + tt) * 64 + 64],
                                pbf[:, tt * 128:tt * 128 + 64],
                                rz[:, tt:tt + 1], 1.0 / 3.0, ALU.mult, ALU.mult), "V", 1)
                        win_state[w] = evacV
            tcnt = p.cnt["T"]
            att_done[sbuf_i] = tcnt
            if b == 0 and h == 0 and lvl == 0 and "dbg_qsort" in io:
                p.w("sync", "DA", tr_done[sbuf_i])
                p.do("sync", lambda e: e.dma_start(out=io["dbg_qkt"][:, 0:4096], in_=qkT), "DS", 16)
                p.do("sync", lambda e: e.dma_start(out=io["dbg_qkt"][:, 4096:8192], in_=kvT), "DS", 16)
                p.w("sync", "V", evacV)
                p.do("sync", lambda e: e.dma_start(out=io["dbg_otok2"][:, :], in_=out_tok), "DS", 16)
            p.w("sync", "V", evacV)
            p.w("sync", "DG", g_done[sbuf_i])
            ot_st[sbuf_i] = p.do("sync", lambda e, osrt=osrt: e.dma_start(
                out=osrt.rearrange("(c p) r -> p c r", p=128),
                in_=out_tok.rearrange("p (c r) -> p c r", r=64)), "DS", 16)
            def mk_gathers(sbuf_i=sbuf_i, osrt=osrt, u=u, lvl=lvl, h=h, b=b):
                nonlocal g_last, acst
                p.w("gpsimd", "DS", ot_st[sbuf_i])
                if lvl == 0:
                    p.w("gpsimd", "DS", acst)
                    p.w("gpsimd", "DG", g_last)
                for tau in range(32):
                    kw = dict(compute_op=ALU.add, oob_is_err=True) if lvl > 0 else {}
                    g_done[sbuf_i] = p.do("gpsimd", lambda e, tau=tau, u=u, lvl=lvl, h=h,
                                          kw=kw, osrt=osrt: e.indirect_dma_start(
                        out=acc[:, tau * 128 + (h % 2) * 64: tau * 128 + (h % 2) * 64 + 64],
                        out_offset=None,
                        in_=osrt[:, :],
                        in_offset=IndirectOffsetOnAxis(
                            ranks[:, (u + lvl) * 32 + tau:(u + lvl) * 32 + tau + 1], axis=0),
                        **kw), "DG", 16)
                g_last = g_done[sbuf_i]
                if h % 2 == 1 and lvl == 2:
                    p.w("sync", "DG", g_last)
                    acst = p.do("sync", lambda e, b=b, h=h: e.dma_start(
                        out=io["attn1tok"][b, h // 2, :, :].rearrange("(c p) r -> p c r", p=128),
                        in_=acc.rearrange("p (c r) -> p c r", r=128)), "DS", 16)
            pending = mk_gathers
    if pending is not None:
        pending()
        pending = None


def emit_l1out(p, io, AR1, AR2, AR3, ARF, ARS, sb, cst, ps, b):
    wo1 = AR1[:, 18432:26624]
    attf = AR3[:, 0:4096]
    xres = AR3[:, 4096:8192]
    stile = AR3[:, 8192:12288]
    sqf = ARF[:, 0:4096]
    bo1 = sb["bo1pp"]
    if b == 0:
        p.w("gpsimd", "T", p.cnt["T"])
        for c in range(8):
            p.do("gpsimd", lambda e, c=c: e.dma_start(
                out=wo1[:, c * 1024:(c + 1) * 1024],
                in_=io["l1_Wo"][c * 128:(c + 1) * 128, :]), "DG", 16)
        p.do("sync", lambda e: e.dma_start(out=bo1, in_=io["l1_bo_pp"][:, :]), "DS", 16)
    wld = p.cnt["DG"]
    bld = p.cnt["DS"]
    tcnt = p.cnt["T"]
    lnV = p.cnt["V"]
    x3st = p.cnt["DS"]
    p.w("tensor", "V", p.cnt["V"])
    for tb8 in range(8):
        p.w("scalar", "T", tcnt)
        p.w("scalar", "DS", x3st)
        for pr in range(8):
            dcnt = p.do("scalar", lambda e, pr=pr, tb8=tb8, b=b: e.dma_start(
                out=attf[:, pr * 512:(pr + 1) * 512],
                in_=io["attn1tok"][b, pr, tb8 * 512:(tb8 + 1) * 512, :],
                transpose=True), "DA", 16)
        p.w("sync", "T", tcnt)
        for c in range(8):
            rcnt = p.do("sync", lambda e, c=c, tb8=tb8, b=b: e.dma_start(
                out=xres[:, c * 512:(c + 1) * 512],
                in_=io["x2T"][c * 128:(c + 1) * 128,
                              b * S + tb8 * 512: b * S + tb8 * 512 + 512]), "DS", 16)
        for fc in range(8):
            pb = ps[fc % 2]
            p.w("tensor", "DA", dcnt)
            p.w("tensor", "DS", rcnt)
            p.w("tensor", "DG", wld)
            p.w("tensor", "V", lnV)
            for ec in range(8):
                tcnt = p.do("tensor", lambda e, pb=pb, ec=ec, fc=fc: e.matmul(
                    pb[:, :], wo1[:, ec * 1024 + fc * 128: ec * 1024 + fc * 128 + 128],
                    attf[:, ec * 512:(ec + 1) * 512], start=(ec == 0), stop=(ec == 7)),
                    "T" if ec == 7 else None, 1 if ec == 7 else 0)
            p.w("vector", "T", tcnt)
            p.w("vector", "DS", x3st)
            sv = stile[:, fc * 512:(fc + 1) * 512]
            p.do("vector", lambda e, pb=pb, sv=sv, fc=fc: e.tensor_scalar(
                sv, pb[:, :], bo1[:, fc:fc + 1], None, ALU.add))
            lnV = p.do("vector", lambda e, sv=sv, fc=fc: e.tensor_tensor(
                sv, sv, xres[:, fc * 512:(fc + 1) * 512], ALU.add), "V", 1)
        p.w("scalar", "V", lnV)
        sb["lnout"] = AR3[:, 0:4096]
        x3st = emit_ffn_ln512(p, io, sb, cst, ps, stile, sqf, b * 8 + tb8, "x3T")
        lnV = p.cnt["V"]




def emit_wquant(p, io, ARF, AR3, ARS, sb, cst, ps, src, dst, nch, msc_col):
    onesf = cst["onesfsb"]
    wt = ARF[:, 0:4096]
    q = AR3[:, 0:4096]
    wabs = sb["wabs"]
    msc = sb["msc"]
    wsc = sb["wsc"]
    N = io[src].shape()[1] if False else 4096 if nch == 8 else 1024
    vv = p.cnt["V"]
    dcnt = p.cnt["DS"]
    for c in range(nch):
        p.w("sync", "V", vv)
        dcnt = p.do("sync", lambda e, c=c: e.dma_start(
            out=wt[:, 0:N], in_=io[src][c * 128:(c + 1) * 128, :]), "DS", 16)
        p.w("vector", "DS", dcnt)
        rv = p.do("vector", lambda e: e.reduce_sum(
            wabs[:, 1:2], wt[:, 0:N], axis=AX.X, apply_absolute_value=True), "V", 1)
        p.w("vector", "V", rv)
        if c == 0:
            vv = p.do("vector", lambda e: e.tensor_copy(wabs[:, 0:1], wabs[:, 1:2]), "V", 1)
        else:
            vv = p.do("vector", lambda e: e.tensor_tensor(
                wabs[:, 0:1], wabs[:, 0:1], wabs[:, 1:2], ALU.add), "V", 1)
    p.w("tensor", "V", vv)
    tm = p.do("tensor", lambda e: e.matmul(ps[2][0:1, 0:1], onesf[:, 0:1],
                                           wabs[:, 0:1], start=True, stop=True), "T", 1)
    p.w("vector", "T", tm)
    p.do("vector", lambda e: e.tensor_scalar(
        msc[0:1, msc_col:msc_col + 1], ps[2][0:1, 0:1], 1.0 / (nch * 128 * N),
        1e-5, ALU.mult, ALU.max))
    vv = p.do("vector", lambda e: e.engine_nop(), "V", 1)
    p.w("vector", "V", vv)
    rc = p.do("vector", lambda e: e.reciprocal(
        msc[0:1, msc_col + 2:msc_col + 3], msc[0:1, msc_col:msc_col + 1]), "V", 1)
    p.w("vector", "V", rc)
    p.w("tensor", "V", rc)
    tb = p.do("tensor", lambda e: e.matmul(ps[3][:, 0:1], onesf[0:1, 0:128],
                                           msc[0:1, msc_col + 2:msc_col + 3],
                                           start=True, stop=True), "T", 1)
    p.w("vector", "T", tb)
    cb = p.do("vector", lambda e: e.tensor_copy(wsc[:, 0:1], ps[3][:, 0:1]), "V", 1)
    p.w("vector", "V", cb)
    st = p.cnt["DS"]
    for c in range(nch):
        p.w("sync", "V", vv)
        dcnt = p.do("sync", lambda e, c=c: e.dma_start(
            out=wt[:, 0:N], in_=io[src][c * 128:(c + 1) * 128, :]), "DS", 16)
        p.w("vector", "DS", dcnt)
        p.w("vector", "DS", st)
        p.do("vector", lambda e: e.tensor_scalar(
            wt[:, 0:N], wt[:, 0:N], wsc[:, 0:1], None, ALU.mult))
        p.do("vector", lambda e: e.tensor_scalar(
            wt[:, 0:N], wt[:, 0:N], MAGIC, -MAGIC, ALU.add, ALU.add))
        vv = p.do("vector", lambda e: e.tensor_scalar(
            q[:, 0:N], wt[:, 0:N], 1.0, -1.0, ALU.min, ALU.max), "V", 1)
        p.w("sync", "V", vv)
        st = p.do("sync", lambda e, c=c: e.dma_start(
            out=io[dst][c * 128:(c + 1) * 128, :], in_=q[:, 0:N]), "DS", 16)


def emit_ffn1(p, io, AR1, AR2, AR3, ARF, ARS, sb, cst, ps):
    onesb, onesf = cst["onesb"], cst["onesfsb"]
    msc = sb["msc"]
    emit_wquant(p, io, ARF, AR3, ARS, sb, cst, ps, "f1_w1T", "wq1q", 8, 0)
    emit_wquant(p, io, ARF, AR3, ARS, sb, cst, ps, "f1_w2T", "wq2q", 32, 4)
    cw = sb["convw"]; cb2 = sb["convb"]
    p.do("sync", lambda e: e.dma_start(out=cw, in_=io["f1_convw"][:, :]), "DS", 16)
    p.do("sync", lambda e: e.dma_start(out=cb2, in_=io["f1_convb"][:, :]), "DS", 16)
    cld = p.cnt["DS"]

    p.w("sync", "DA", p.cnt["DA"])
    p.w("scalar", "DA", p.cnt["DA"])
    xblk = AR3[:, 0:4096]
    xq = AR3[:, 4096:8192]
    wsl = AR1[:, 0:8192]
    m1 = ARF[:, 0:512]
    m2 = ARF[:, 512:1024]
    scb = ARF[:, 1024:1536]
    dqb = ARF[:, 1536:2048]
    hrow = ARF[0:1, 2048:2560]
    ev = AR3[:, 8192:9216]

    tcnt = p.cnt["T"]
    evacV = p.cnt["V"]
    dh = p.cnt["DS"]
    ld = p.cnt["DG"]
    lastdq = [p.cnt["V"], p.cnt["V"]]
    p.w("tensor", "V", p.cnt["V"])
    p.w("tensor", "A", p.cnt["A"])
    for tb in range(16):
        p.w("sync", "T", tcnt)
        for c in range(8):
            dcnt = p.do("sync", lambda e, c=c, tb=tb: e.dma_start(
                out=xblk[:, c * 512:(c + 1) * 512],
                in_=io["x3T"][c * 128:(c + 1) * 128, tb * 512:(tb + 1) * 512]), "DS", 16)
        p.w("vector", "DS", dcnt)
        p.w("vector", "V", evacV)
        p.do("vector", lambda e: e.scalar_tensor_tensor(
            m1[:, :], xblk[:, 0:512], -1.0, xblk[:, 0:512], ALU.mult, ALU.max))
        for c in range(1, 8):
            p.do("vector", lambda e, c=c: e.tensor_max(
                m1[:, :], m1[:, :], xblk[:, c * 512:(c + 1) * 512]))
            p.do("vector", lambda e, c=c: e.scalar_tensor_tensor(
                m1[:, :], xblk[:, c * 512:(c + 1) * 512], -1.0, m1[:, :],
                ALU.mult, ALU.max))
        vv = p.do("vector", lambda e: e.engine_nop(), "V", 1)
        nh = 64
        while nh >= 1:
            vv = p.do("vector", lambda e, nh=nh: e.tensor_copy(
                scb[0:min(2 * nh, 128), 0:512], m1[0:min(2 * nh, 128), :]), "V", 1)
            p.w("sync", "V", vv)
            sh = p.do("sync", lambda e, nh=nh: e.dma_start(
                out=m2[0:max(nh, 1), :], in_=m1[nh:2 * nh, :]), "DS", 16)
            p.w("vector", "DS", sh)
            vv = p.do("vector", lambda e, nh=nh: e.tensor_max(
                m1[0:nh, :], m1[0:nh, :], m2[0:nh, :]), "V", 1)
            p.w("vector", "V", vv)
            nh //= 2
        p.do("vector", lambda e: e.tensor_scalar(
            m1[0:1, :], m1[0:1, :], 1e-5, None, ALU.max))
        vv = p.do("vector", lambda e: e.engine_nop(), "V", 1)
        p.w("vector", "V", vv)
        rc = p.do("vector", lambda e: e.reciprocal(m2[0:1, :], m1[0:1, :]), "V", 1)
        p.w("vector", "V", rc)
        p.do("vector", lambda e: e.tensor_scalar(m2[0:1, :], m2[0:1, :], 127.0, None, ALU.mult))
        p.do("vector", lambda e: e.tensor_scalar(
            hrow, m1[0:1, :], 1.0 / 127.0, msc[0:1, 0:1], ALU.mult, ALU.mult))
        vv = p.do("vector", lambda e: e.engine_nop(), "V", 1)
        if tb == 0 and "dbg_q1" in io:
            p.w("sync", "V", vv)
            p.do("sync", lambda e: e.dma_start(out=io["dbg_q1"][0:1, 0:512], in_=m1[0:1, :]), "DS", 16)
            p.do("sync", lambda e: e.dma_start(out=io["dbg_q1"][1:2, 0:512], in_=m2[0:1, :]), "DS", 16)
            p.do("sync", lambda e: e.dma_start(out=io["dbg_q1"][2:3, 0:512], in_=hrow), "DS", 16)
            p.w("vector", "DS", p.cnt["DS"])
        p.w("tensor", "V", vv)
        p.do("tensor", lambda e: e.matmul(ps[2][:, 0:512], onesf[0:1, 0:128],
                                          m2[0:1, :], start=True, stop=True))
        tbx = p.do("tensor", lambda e: e.matmul(ps[3][:, 0:512], onesf[0:1, 0:128],
                                                hrow, start=True, stop=True), "T", 1)
        p.w("vector", "T", tbx)
        p.do("vector", lambda e: e.tensor_copy(scb[:, :], ps[2][:, 0:512]))
        vv = p.do("vector", lambda e: e.tensor_copy(dqb[:, :], ps[3][:, 0:512]), "V", 1)
        p.w("vector", "V", vv)
        for c in range(8):
            p.do("vector", lambda e, c=c: e.tensor_tensor(
                xq[:, c * 512:(c + 1) * 512].bitcast(F32)[:, 0:256] if False else
                m1[:, 0:512], xblk[:, c * 512:(c + 1) * 512], scb[:, :], ALU.mult))
            vv = p.do("vector", lambda e, c=c: e.tensor_scalar(
                xq[:, c * 512:(c + 1) * 512], m1[:, 0:512], MAGIC, -MAGIC,
                ALU.add, ALU.add), "V", 1)
        if tb == 0 and "dbg_q1" in io:
            p.w("sync", "V", vv)
            p.do("sync", lambda e: e.dma_start(out=io["dbg_xq"][:, :], in_=xq), "DS", 16)
            p.w("gpsimd", "DS", p.cnt["DS"])
        for js in range(4):
            p.w("gpsimd", "T", tcnt)
            p.w("gpsimd", "V", vv)
            for c in range(8):
                ld = p.do("gpsimd", lambda e, c=c, js=js: e.dma_start(
                    out=wsl[:, c * 1024:(c + 1) * 1024],
                    in_=io["wq1q"][c * 128:(c + 1) * 128, js * 1024:(js + 1) * 1024]), "DG", 16)
            for jc in range(8):
                pb = ps[jc % 2]
                p.w("tensor", "DG", ld)
                p.w("tensor", "V", vv)
                p.w("tensor", "V", lastdq[jc % 2])
                for c in range(8):
                    tcnt = p.do("tensor", lambda e, pb=pb, c=c, jc=jc: e.matmul(
                        pb[:, :], wsl[:, c * 1024 + jc * 128: c * 1024 + jc * 128 + 128],
                        xq[:, c * 512:(c + 1) * 512], start=(c == 0), stop=(c == 7)),
                        "T" if c == 7 else None, 1 if c == 7 else 0)
                p.w("vector", "T", tcnt)
                p.w("vector", "DS", dh)
                p.do("vector", lambda e, pb=pb: e.tensor_tensor(
                    ev[:, 0:512], pb[:, :], dqb[:, :], ALU.mult))
                evacV = p.do("vector", lambda e: e.engine_nop(), "V", 1)
                lastdq[jc % 2] = evacV
                p.w("sync", "V", evacV)
                dh = p.do("sync", lambda e, js=js, jc=jc, tb=tb: e.dma_start(
                    out=io["hA"][js * 1024 + jc * 128: js * 1024 + jc * 128 + 128,
                                 tb * 512:(tb + 1) * 512], in_=ev[:, 0:512]), "DS", 16)

    hc = AR2[:, 0:16384]
    xq2 = AR2[:, 16384:32768]
    whalf = AR1[:, 8192:24576]
    halo = AR3[:, 9216:9216 + 516]
    x3res = AR3[:, 4096:8192]
    stile = AR3[:, 0:4096]
    lnf = ARF[:, 2560:3072]
    sqf = ARF[:, 3072:3584]
    outb = ARF[:, 3584:4096]
    r_mu, r_m2, r_rs = sb["r_mu"][0:1, 0:512], sb["r_m2"][0:1, 0:512], sb["r_rs"][0:1, 0:512]
    tcnt = p.cnt["T"]
    evacV = p.cnt["V"]
    ost = p.cnt["DS"]
    prevOvv = p.cnt["V"]
    lnV = p.cnt["V"]
    for tb in range(16):
        seq_edge_l = (tb % 8 == 0)
        seq_edge_r = (tb % 8 == 7)
        p.w("scalar", "DS", cld)
        for jc in range(32):
            p.w("sync", "V", evacV if jc == 0 else p.cnt["V"] - 0)
            lo = tb * 512 - (0 if seq_edge_l else 1)
            hi = tb * 512 + 512 + (0 if seq_edge_r else 1)
            off = 1 if seq_edge_l else 0
            hld = p.do("sync", lambda e, jc=jc, lo=lo, hi=hi, off=off: e.dma_start(
                out=halo[:, off:off + (hi - lo)],
                in_=io["hA"][jc * 128:(jc + 1) * 128, lo:hi]), "DS", 16)
            if seq_edge_l:
                p.do("vector", lambda e: e.memset(halo[:, 0:1], 0.0))
            if seq_edge_r:
                p.do("vector", lambda e: e.memset(halo[:, 513:514], 0.0))
            p.w("vector", "DS", hld)
            p.do("vector", lambda e, jc=jc: e.tensor_scalar(
                m1[:, :], halo[:, 1:513], cw[:, jc * 3 + 1:jc * 3 + 2], None, ALU.mult))
            p.do("vector", lambda e, jc=jc: e.scalar_tensor_tensor(
                m1[:, :], halo[:, 0:512], cw[:, jc * 3 + 0:jc * 3 + 1], m1[:, :],
                ALU.mult, ALU.add))
            cvv = p.do("vector", lambda e, jc=jc: e.scalar_tensor_tensor(
                m1[:, :], halo[:, 2:514], cw[:, jc * 3 + 2:jc * 3 + 3], m1[:, :],
                ALU.mult, ALU.add), "V", 1)
            p.w("scalar", "V", cvv)
            aa = p.do("scalar", lambda e, jc=jc: e.activation(
                hc[:, jc * 512:(jc + 1) * 512], m1[:, :], AF.Silu,
                bias=cb2[:, jc:jc + 1]), "A", 1)
        p.w("vector", "A", aa)
        p.do("vector", lambda e: e.scalar_tensor_tensor(
            m1[:, :], hc[:, 0:512], -1.0, hc[:, 0:512], ALU.mult, ALU.max))
        for c in range(1, 32):
            p.do("vector", lambda e, c=c: e.tensor_max(
                m1[:, :], m1[:, :], hc[:, c * 512:(c + 1) * 512]))
            p.do("vector", lambda e, c=c: e.scalar_tensor_tensor(
                m1[:, :], hc[:, c * 512:(c + 1) * 512], -1.0, m1[:, :],
                ALU.mult, ALU.max))
        vv = p.do("vector", lambda e: e.engine_nop(), "V", 1)
        nh = 64
        while nh >= 1:
            vv = p.do("vector", lambda e, nh=nh: e.tensor_copy(
                scb[0:min(2 * nh, 128), 0:512], m1[0:min(2 * nh, 128), :]), "V", 1)
            p.w("sync", "V", vv)
            sh = p.do("sync", lambda e, nh=nh: e.dma_start(
                out=m2[0:max(nh, 1), :], in_=m1[nh:2 * nh, :]), "DS", 16)
            p.w("vector", "DS", sh)
            vv = p.do("vector", lambda e, nh=nh: e.tensor_max(
                m1[0:nh, :], m1[0:nh, :], m2[0:nh, :]), "V", 1)
            p.w("vector", "V", vv)
            nh //= 2
        p.do("vector", lambda e: e.tensor_scalar(
            m1[0:1, :], m1[0:1, :], 1e-5, None, ALU.max))
        vv = p.do("vector", lambda e: e.engine_nop(), "V", 1)
        p.w("vector", "V", vv)
        rc = p.do("vector", lambda e: e.reciprocal(m2[0:1, :], m1[0:1, :]), "V", 1)
        p.w("vector", "V", rc)
        p.do("vector", lambda e: e.tensor_scalar(m2[0:1, :], m2[0:1, :], 127.0, None, ALU.mult))
        p.do("vector", lambda e: e.tensor_scalar(
            hrow, m1[0:1, :], 1.0 / 127.0, msc[0:1, 4:5], ALU.mult, ALU.mult))
        vv = p.do("vector", lambda e: e.engine_nop(), "V", 1)
        p.w("tensor", "V", vv)
        p.do("tensor", lambda e: e.matmul(ps[2][:, 0:512], onesf[0:1, 0:128],
                                          m2[0:1, :], start=True, stop=True))
        tbx = p.do("tensor", lambda e: e.matmul(ps[3][:, 0:512], onesf[0:1, 0:128],
                                                hrow, start=True, stop=True), "T", 1)
        p.w("vector", "T", tbx)
        p.do("vector", lambda e: e.tensor_copy(scb[:, :], ps[2][:, 0:512]))
        vv = p.do("vector", lambda e: e.tensor_copy(dqb[:, :], ps[3][:, 0:512]), "V", 1)
        p.w("vector", "V", vv)
        for c in range(32):
            p.do("vector", lambda e, c=c: e.tensor_tensor(
                m1[:, 0:512], hc[:, c * 512:(c + 1) * 512], scb[:, :], ALU.mult))
            vv = p.do("vector", lambda e, c=c: e.tensor_scalar(
                xq2[:, c * 512:(c + 1) * 512], m1[:, 0:512], MAGIC, -MAGIC,
                ALU.add, ALU.add), "V", 1)
        p.w("sync", "V", vv)
        for c in range(8):
            rld = p.do("sync", lambda e, c=c, tb=tb: e.dma_start(
                out=x3res[:, c * 512:(c + 1) * 512],
                in_=io["x3T"][c * 128:(c + 1) * 128, tb * 512:(tb + 1) * 512]), "DS", 16)
        lnV = max(lnV if tb > 0 else p.cnt["V"], prevOvv)
        for half in range(2):
            p.w("gpsimd", "T", tcnt)
            for c in range(32):
                ld = p.do("gpsimd", lambda e, c=c, half=half: e.dma_start(
                    out=whalf[:, c * 512:(c + 1) * 512],
                    in_=io["wq2q"][c * 128:(c + 1) * 128,
                                   half * 512:(half + 1) * 512]), "DG", 16)
            for el in range(4):
                ec = half * 4 + el
                pb = ps[el % 2]
                p.w("tensor", "DG", ld)
                p.w("tensor", "V", vv)
                p.w("tensor", "DS", rld)
                p.w("tensor", "V", lnV)
                for c in range(32):
                    tcnt = p.do("tensor", lambda e, pb=pb, c=c, el=el: e.matmul(
                        pb[:, :], whalf[:, c * 512 + el * 128: c * 512 + el * 128 + 128],
                        xq2[:, c * 512:(c + 1) * 512], start=(c == 0), stop=(c == 31)),
                        "T" if c == 31 else None, 1 if c == 31 else 0)
                p.w("vector", "T", tcnt)
                sv = stile[:, ec * 512:(ec + 1) * 512]
                p.do("vector", lambda e, pb=pb, sv=sv: e.tensor_tensor(
                    m1[:, :], pb[:, :], dqb[:, :], ALU.mult))
                lnV = p.do("vector", lambda e, sv=sv, ec=ec: e.tensor_tensor(
                    sv, m1[:, :], x3res[:, ec * 512:(ec + 1) * 512], ALU.add), "V", 1)
        tst = p.cnt["T"]
        for fc in range(8):
            sv = stile[:, fc * 512:(fc + 1) * 512]
            p.w("scalar", "V", lnV)
            p.w("scalar", "T", tst)
            aa = p.do("scalar", lambda e, fc=fc, sv=sv: e.activation(
                sqf[:, 0:512], sv, AF.Square), "A", 1)
            p.w("tensor", "A", aa)
            p.w("tensor", "V", lnV)
            p.do("tensor", lambda e, sv=sv, fc=fc: e.matmul(
                ps[2][0:1, 0:512], onesb[:, 0:1], sv, start=(fc == 0), stop=(fc == 7)))
            tst = p.do("tensor", lambda e, fc=fc: e.matmul(
                ps[3][0:1, 0:512], onesf[:, 0:1], sqf[:, 0:512],
                start=(fc == 0), stop=(fc == 7)), "T", 1)
        p.w("vector", "T", tst)
        p.do("vector", lambda e: e.tensor_scalar(r_mu, ps[2][0:1, 0:512], 1.0 / D, None, ALU.mult))
        p.do("vector", lambda e: e.tensor_scalar(r_m2, ps[3][0:1, 0:512], 1.0 / D, None, ALU.mult))
        p.do("vector", lambda e: e.tensor_tensor(r_rs, r_mu, r_mu, ALU.mult))
        p.do("vector", lambda e: e.tensor_tensor(r_m2, r_m2, r_rs, ALU.subtract))
        vv = p.do("vector", lambda e: e.tensor_scalar(r_m2, r_m2, 1e-5, None, ALU.add), "V", 1)
        p.w("scalar", "V", vv)
        p.do("scalar", lambda e: e.activation(r_rs, r_m2, AF.Ln))
        aa = p.do("scalar", lambda e: e.activation(r_rs, r_rs, AF.Exp, scale=-0.5), "A", 1)
        p.w("tensor", "A", aa)
        p.do("tensor", lambda e: e.matmul(ps[2][:, 0:512], onesf[0:1, 0:128], r_mu, start=True, stop=True))
        tb2 = p.do("tensor", lambda e: e.matmul(ps[3][:, 0:512], onesf[0:1, 0:128], r_rs, start=True, stop=True), "T", 1)
        p.w("vector", "T", tb2)
        for fc in range(8):
            sv = stile[:, fc * 512:(fc + 1) * 512]
            p.w("vector", "DS", ost)
            p.do("vector", lambda e, sv=sv: e.tensor_tensor(lnf[:, :], sv, ps[2][:, 0:512], ALU.subtract))
            lvv = p.do("vector", lambda e: e.tensor_tensor(lnf[:, :], lnf[:, :], ps[3][:, 0:512], ALU.mult), "V", 1)
            p.w("tensor", "V", lvv)
            trcp = [prevOvv, prevOvv]
            for tt in range(4):
                p.w("tensor", "V", trcp[tt % 2])
                tr = p.do("tensor", lambda e, tt=tt: e.transpose(
                    ps[0 if tt % 2 == 0 else 1][:, 0:128],
                    lnf[:, tt * 128:(tt + 1) * 128], cst["identf"][:, :]), "T", 1)
                p.w("vector", "T", tr)
                ovv = p.do("vector", lambda e, tt=tt: e.tensor_copy(
                    outb[:, tt * 128:(tt + 1) * 128], ps[0 if tt % 2 == 0 else 1][:, 0:128]), "V", 1)
                trcp[tt % 2] = ovv
            prevOvv = ovv
            p.w("sync", "V", ovv)
            for tt in range(4):
                ost = p.do("sync", lambda e, fc=fc, tb=tb, tt=tt: e.dma_start(
                    out=io["out"][tb * 512 + tt * 128: tb * 512 + tt * 128 + 128,
                                  fc * 128:(fc + 1) * 128],
                    in_=outb[:, tt * 128:(tt + 1) * 128]), "DS", 16)


def build_program(taps=(), upto="all"):
    nc = bass.Bass("TRN2", target_bir_lowering=False)
    io = {}
    def inp(name, shape, dt):
        io[name] = nc.dram_tensor(name, shape, dt, kind="ExternalInput").ap()
    def internal(name, shape, dt):
        kind = "ExternalOutput" if name in taps else "Internal"
        io[name] = nc.dram_tensor(name, shape, dt, kind=kind).ap()

    inp("x", [T, D], F32)
    for nm in ("Wq", "Wk", "Wv", "Wo"):
        inp("l0_" + nm, [D, D], F32)
        inp("l1_" + nm, [D, D], F32)
    for nm in ("bq", "bk", "bv", "bo"):
        inp(f"l0_{nm}_row", [1, D], F32); inp(f"l0_{nm}_pp", [128, 8], F32)
        inp(f"l1_{nm}_row", [1, D], F32); inp(f"l1_{nm}_pp", [128, 8], F32)
    inp("f0_w1T", [D, DH], F32); inp("f0_w2T", [D, DH], F32)
    inp("f0_h1T", [DH, DH], F32); inp("f0_h2T", [DH, DH], F32)
    inp("f0_w3T", [DH, D], F32)
    for nm, nch in (("w1_s", 32), ("w2_s", 32), ("h1_s", 32), ("h2_s", 32), ("w3_s", 8)):
        inp("f0_" + nm, [128, nch], F32)
    inp("l1_probes", [H, HD, 192], F32)
    inp("f1_w1T", [D, DH], F32); inp("f1_w2T", [DH, D], F32)
    inp("f1_convw", [128, 96], F32); inp("f1_convb", [128, 32], F32)
    inp("ident", [128, 128], BF16); inp("onesb", [128, 512], BF16)
    inp("onesf", [128, 144], F32); inp("zz", [128, 544], BF16)
    inp("lup", [128, 128], BF16); inp("gmats", [64, 192], F32)
    inp("mask1", [128, 192], BF16); inp("mask2", [128, 144], BF16)
    inp("identf", [128, 128], F32)

    if "dbg_at0" in taps:
        internal("dbg_at0", [D, S], BF16)
    if "dbg_otok" in taps:
        internal("dbg_otok", [128, D], BF16)
    if "dbg_qkv" in taps:
        internal("dbg_qkv", [128, 3 * D], BF16)
    if "dbg_sc" in taps:
        internal("dbg_sc", [128, 512], F32)
    if "dbg_acc" in taps:
        internal("dbg_acc", [128, 1040], F32)
    if "dbg_ranks" in taps:
        internal("dbg_ranks", [128, 3072], U32)
    if "dbg_oh" in taps:
        internal("dbg_oh", [128, 6144], BF16)
        internal("dbg_proj", [128, 6144], F32)
    if "dbg_cnt" in taps:
        internal("dbg_cnt", [128, 98], F32)
    if "dbg_qrt" in taps:
        internal("dbg_qrt", [128, 16384], F32)
    if "dbg_qsort" in taps:
        internal("dbg_qsort", [S, 256], BF16)
    if "dbg_win" in taps:
        internal("dbg_win", [65, 512], BF16)
        internal("dbg_exp", [256, 512], BF16)
    if "dbg_acc2" in taps:
        internal("dbg_acc2", [128, 4096], BF16)
    if "dbg_acc3" in taps:
        internal("dbg_acc3", [128, 4096], BF16)
    if "dbg_q1" in taps:
        internal("dbg_q1", [4, 512], F32)
        internal("dbg_xq", [128, 4096], BF16)
        internal("dbg_qkt", [128, 8192], BF16)
        internal("dbg_otok2", [128, 2048], BF16)
    internal("xbf_tok", [T, D], BF16)
    internal("x1T", [D, T], BF16)
    internal("x2T", [D, T], BF16)
    internal("x3T", [D, T], BF16)
    internal("hA", [DH, T], BF16)
    internal("qkvs", [B, H, S, 256], BF16)
    internal("qkv_sorted0", [S, 256], BF16)
    internal("qkv_sorted1", [S, 256], BF16)
    internal("outsorted0", [S, 64], BF16)
    internal("outsorted1", [S, 64], BF16)
    internal("attn1tok", [B, 8, S, 128], BF16)
    internal("wq1q", [D, DH], BF16)
    internal("x2Tf", [D, T], F32)
    internal("wq2q", [DH, D], BF16)
    internal("hB", [DH, T], BF16)
    io["out"] = nc.dram_tensor("out", [T, D], F32, kind="ExternalOutput").ap()

    with ExitStack() as st:
        AR1 = st.enter_context(nc.sbuf_tensor("AR1", [128, 28672], BF16))
        AR2 = st.enter_context(nc.sbuf_tensor("AR2", [128, 32768], BF16))
        AR3 = st.enter_context(nc.sbuf_tensor("AR3", [128, 12288], BF16))
        ARF = st.enter_context(nc.sbuf_tensor("ARF", [128, 6656], F32))
        ARS = st.enter_context(nc.sbuf_tensor("ARS", [128, 1024], F32))
        PRB = st.enter_context(nc.sbuf_tensor("PRB", [128, 1536], F32))
        RNK = st.enter_context(nc.sbuf_tensor("RNK", [128, 3072], U32))
        cst = {}
        for nm, shape, dt in (("ident", [128, 128], BF16), ("onesb", [128, 512], BF16),
                              ("onesfsb", [128, 144], F32), ("zz", [128, 544], BF16),
                              ("lup", [128, 128], BF16), ("gmats", [64, 192], F32),
                              ("mask1", [128, 192], BF16), ("mask2", [128, 144], BF16),
                              ("identf", [128, 128], F32)):
            cst[nm] = st.enter_context(nc.sbuf_tensor("c_" + nm, shape, dt))
        ps = {}
        for i in range(4):
            ps[i] = st.enter_context(nc.psum_tensor(f"ps{i}", [128, 512], F32))
        ps["bfA"] = st.enter_context(nc.psum_tensor("psbfA", [128, 512], BF16))
        ps["bfB"] = st.enter_context(nc.psum_tensor("psbfB", [128, 512], BF16))
        blk = st.enter_context(nc.Block())
        p = P(nc, st)

        sb = dict(
            wq=AR1[:, 0:8192], wk=AR1[:, 8192:16384], wv=AR1[:, 16384:24576],
            wo=AR1[:, 0:16384], brow=AR1[:, 24576:27648],
            at0=AR2[:, 0:32768],
            xsb=AR3[:, 0:4096], qkv=AR3[:, 4096:7168], otok=AR3[:, 7168:8192],
            stile=AR3[:, 8192:10240], x1t=AR3[:, 10240:12288],
            prod=ARF[:, 0:1024], sc16=ARF[:, 1024:1280], att=ARF[:, 1280:1536],
            acc=ARF[:, 1536:2560], zr=ARF[:, 2560:2576], zs=ARF[:, 6160:6176],
            sqf=ARF[:, 2576:4624],
            r_mu=ARF[0:1, 4624:5136], r_m2=ARF[0:1, 5136:5648],
            r_rs=ARF[0:1, 5648:6160],
            bo_pp=ARS[:, 0:8], fscal=ARS[:, 8:144], bqpp1=ARS[:, 144:152],
            bo1pp=ARS[:, 168:176], wabs=ARS[:, 176:178], msc=ARS[:, 178:186],
            lnoutf=ARS[:, 320:832],
            wsc=ARS[:, 186:187], convw=ARS[:, 192:288], convb=ARS[:, 288:320],
            probes=PRB, ranks=RNK,
            wq1=AR1[:, 0:8192], wk1=AR1[:, 8192:16384], wv1=AR1[:, 16384:24576],
            brow1=AR1[:, 24576:27648],
        )

        src_map = {"ident": "ident", "onesb": "onesb", "onesfsb": "onesf",
                   "zz": "zz", "lup": "lup", "gmats": "gmats",
                   "mask1": "mask1", "mask2": "mask2", "identf": "identf"}
        for nm, t in cst.items():
            p.do("sync", lambda e, t=t, s=io[src_map[nm]]:
                 e.dma_start(out=t[:, :], in_=s[:, :]), "DS", 16)
        cload = p.snap()
        for e in ("tensor", "vector", "scalar", "gpsimd"):
            p.join(e, cload, ["DS"])

        emit_ingest(p, io)
        ing = p.snap()
        p.join("scalar", ing, ["DG"])
        p.join("tensor", ing, ["DG"])

        emit_l0(p, io, sb, cst, ps, 0)
        emit_l0(p, io, sb, cst, ps, 1)
        if upto in ("ffn0", "l1", "all"):
            emit_ffn0(p, io, AR1, AR2, AR3, ARF, ARS, sb, cst, ps)
        if upto in ("l1q", "l1a", "l1o", "all"):
            emit_l1qkv(p, io, AR1, AR2, AR3, ARF, ARS, sb, cst, ps, 0)
            emit_l1qkv(p, io, AR1, AR2, AR3, ARF, ARS, sb, cst, ps, 1)
            if "dbg_ranks" in io:
                emit_dump_ranks(p, io, sb)
        if upto in ("l1a", "l1o", "all"):
            emit_l1attn(p, io, AR1, AR2, AR3, ARF, ARS, sb, cst, ps, 0)
            if upto in ("l1o", "all"):
                emit_l1out(p, io, AR1, AR2, AR3, ARF, ARS, sb, cst, ps, 0)
            emit_l1attn(p, io, AR1, AR2, AR3, ARF, ARS, sb, cst, ps, 1)
            if upto in ("l1o", "all"):
                emit_l1out(p, io, AR1, AR2, AR3, ARF, ARS, sb, cst, ps, 1)
        if upto == "all":
            emit_ffn1(p, io, AR1, AR2, AR3, ARF, ARS, sb, cst, ps)
        else:
            fin = p.snap()
            p.join("gpsimd", fin, ["DS", "V", "T"])
            for i in range(8):
                p.do("gpsimd", lambda e, i=i: e.dma_start(
                    out=io["out"].rearrange("(a t) d -> a t d", a=8)[i, :, :],
                    in_=io["x1T"][:, i * 1024:(i + 1) * 1024]), "DG", 16)

        p.w("sync", "DS", p.cnt["DS"])
        p.w("gpsimd", "DG", p.cnt["DG"])
        p.finalize(blk)
    return nc


def kernel(**inputs):
    maps = shard_inputs(inputs)
    nc = build_program()
    res = run_bass_kernel_spmd(nc, maps, core_ids=list(range(NCORES)))
    out = np.concatenate([np.asarray(r["out"], np.float32).reshape(B, S, D)
                          for r in res.results], axis=0)
    return out


# version 04
# speedup vs baseline: 1.2385x; 1.0690x over previous
import numpy as np
import ml_dtypes
from contextlib import ExitStack
import concourse.bass as bass
import concourse.mybir as mybir
from concourse.bass import IndirectOffsetOnAxis
from concourse.bass_utils import run_bass_kernel_spmd

F32 = mybir.dt.float32
BF16 = mybir.dt.bfloat16
U32 = mybir.dt.uint32
AF = mybir.ActivationFunctionType
ALU = mybir.AluOpType
AX = mybir.AxisListType

NCORES = 8
B = 2
S = 4096
T = B * S
D = 1024
H = 16
HD = 64
DH = 4096
MAGIC = 12582912.0

ENGINES = ("sync", "gpsimd", "tensor", "vector", "scalar")


def host_consts():
    bf = ml_dtypes.bfloat16
    c = {}
    c["ident"] = np.eye(128).astype(bf)
    c["onesb"] = np.ones((128, 512), bf)
    c["onesf"] = np.ones((128, 144), np.float32)
    c["zz"] = np.zeros((128, 544), bf)
    lup = (np.arange(128)[:, None] < np.arange(128)[None, :]).astype(np.float32)
    c["lup"] = lup.astype(bf)
    gs = []
    for nb in (16, 64, 256):
        pp = np.arange(64)
        table = (pp * 9973) % nb
        csize = 64 // nb if nb < 64 else 1
        g = (table[:, None] < table[None, :]).astype(np.float32) / csize
        gs.append(g)
    c["gmats"] = np.concatenate(gs, axis=1).astype(np.float32)
    masks = []
    for cs in (64, 16):
        kk = np.arange(128)[:, None] // cs
        qq = np.arange(128 + cs)[None, :] // cs
        m = ((qq - kk == 0) | (qq - kk == 1)).astype(np.float32)
        masks.append(m.astype(bf))
    c["mask1"] = masks[0]
    c["mask2"] = masks[1]
    c["identf"] = np.eye(128).astype(np.float32)
    return c


def shard_inputs(inputs):
    A = np.ascontiguousarray
    g = inputs
    base = {}
    for nm in ("Wq", "Wk", "Wv", "Wo"):
        base["l0_" + nm] = A(np.asarray(g[f"L0_attn_{nm}"], np.float32).T)
        base["l1_" + nm] = A(np.asarray(g[f"L1_attn_{nm}"], np.float32))
    for nm in ("bq", "bk", "bv", "bo"):
        v0 = np.asarray(g[f"L0_attn_{nm}"], np.float32)
        v1 = np.asarray(g[f"L1_attn_{nm}"], np.float32)
        base["l0_" + nm + "_row"] = A(v0.reshape(1, D))
        base["l0_" + nm + "_pp"] = A(v0.reshape(8, 128).T)
        base["l1_" + nm + "_row"] = A(v1.reshape(1, D))
        base["l1_" + nm + "_pp"] = A(v1.reshape(8, 128).T)
    base["f0_w1T"] = A(np.asarray(g["L0_ffn_w1_w"], np.float32).T)
    base["f0_w2T"] = A(np.asarray(g["L0_ffn_w2_w"], np.float32).T)
    base["f0_h1T"] = A(np.asarray(g["L0_ffn_h1_w"], np.float32).T)
    base["f0_h2T"] = A(np.asarray(g["L0_ffn_h2_w"], np.float32).T)
    base["f0_w3T"] = A(np.asarray(g["L0_ffn_w3_w"], np.float32).T)
    for nm, nch in (("w1_s", 32), ("w2_s", 32), ("h1_s", 32), ("h2_s", 32), ("w3_s", 8)):
        base["f0_" + nm] = A(np.asarray(g[f"L0_ffn_{nm}"], np.float32).reshape(nch, 128).T)
    pr = np.asarray(g["L1_attn_probes"], np.float32)
    base["l1_probes"] = A(pr.transpose(1, 2, 0, 3).reshape(H, HD, 192))
    base["f1_w1T"] = A(np.asarray(g["L1_ffn_w1"], np.float32).T)
    base["f1_w2T"] = A(np.asarray(g["L1_ffn_w2"], np.float32).T)
    cw = np.asarray(g["L1_ffn_conv_w"], np.float32)[:, 0, :]
    base["f1_convw"] = A(cw.reshape(32, 128, 3).transpose(1, 0, 2).reshape(128, 96))
    base["f1_convb"] = A(np.asarray(g["L1_ffn_conv_b"], np.float32).reshape(32, 128).T)
    base.update(host_consts())

    x = np.asarray(g["L0_attn_x"], np.float32)
    maps = []
    for c in range(NCORES):
        m = dict(base)
        m["x"] = A(x[2 * c:2 * c + 2].reshape(T, D))
        maps.append(m)
    return maps


class P:
    def __init__(self, nc, st):
        self.nc = nc
        self.ops = {e: [] for e in ENGINES}
        self.sems = {}
        self.cnt = {}
        for s in ("DS", "DG", "DA", "T", "V", "A", "G"):
            self.sems[s] = st.enter_context(nc.semaphore("sem_" + s))
            self.cnt[s] = 0

    def w(self, eng, sem, val):
        if val is None or val <= 0:
            return
        self.ops[eng].append(lambda e, s=self.sems[sem], v=val: e.wait_ge(s, v))

    def do(self, eng, fn, sem=None, inc=0):
        if sem is None:
            self.ops[eng].append(fn)
            return None
        self.cnt[sem] += inc
        def f(e, fn=fn, s=self.sems[sem], i=inc):
            fn(e).then_inc(s, i)
        self.ops[eng].append(f)
        return self.cnt[sem]

    def snap(self):
        return dict(self.cnt)

    def join(self, eng, snap, sems=None):
        for s in (sems if sems is not None else snap):
            self.w(eng, s, snap[s])

    def finalize(self, blk):
        for name in ENGINES:
            ops = self.ops[name]
            deco = getattr(blk, name)
            def body(e, ops=ops):
                for f in ops:
                    f(e)
            deco(body)


def emit_ingest(p, io):
    for i in range(16):
        p.do("gpsimd", lambda e, i=i: e.dma_start(
            out=io["xbf_tok"][i * 512:(i + 1) * 512, :],
            in_=io["x"][i * 512:(i + 1) * 512, :]), "DG", 16)


def emit_l0(p, io, sb, cst, ps, b):
    wq, wk, wv, wo = sb["wq"], sb["wk"], sb["wv"], sb["wo"]
    brow, at0, xsb, qkv = sb["brow"], sb["at0"], sb["xsb"], sb["qkv"]
    otok, stile, x1t = sb["otok"], sb["stile"], sb["x1t"]
    prod, sc16, att, acc, zr = sb["prod"], sb["sc16"], sb["att"], sb["acc"], sb["zr"]
    zs = sb["zs"]
    sqf = sb["sqf"]
    r_mu, r_m2, r_rs = sb["r_mu"][0:1, 0:256], sb["r_m2"][0:1, 0:256], sb["r_rs"][0:1, 0:256]
    bo_pp = sb["bo_pp"]
    onesb, onesf, ident = cst["onesb"], cst["onesfsb"], cst["ident"]

    p.w("gpsimd", "T", p.cnt["T"])
    for nm, dstv in (("l0_Wq", wq), ("l0_Wk", wk), ("l0_Wv", wv)):
        for c in range(8):
            p.do("gpsimd", lambda e, d=dstv[:, c * 1024:(c + 1) * 1024],
                 s=io[nm][c * 128:(c + 1) * 128, :]: e.dma_start(out=d, in_=s),
                 "DG", 16)
    if b == 0:
        for i, nm in enumerate(("l0_bq_row", "l0_bk_row", "l0_bv_row")):
            p.do("gpsimd", lambda e, d=brow[0:1, i * 1024:(i + 1) * 1024],
                 s=io[nm][0:1, :]: e.dma_start(out=d, in_=s), "DG", 16)
        p.do("sync", lambda e: e.dma_start(out=bo_pp, in_=io["l0_bo_pp"][:, :]), "DS", 16)
    wload = p.cnt["DG"]
    bld = p.cnt["DS"]

    evacA = p.cnt["A"]
    evacV = p.cnt["V"]
    tcnt = p.cnt["T"]
    dcnt = p.cnt["DA"]
    x1done = p.cnt["V"]
    wmap = (wq, wk, wv)
    p.w("tensor", "V", p.cnt["V"])
    p.w("vector", "T", p.cnt["T"])
    p.w("scalar", "T", p.cnt["T"])

    for tau in range(32):
        blkc = tau // 4
        if tau % 4 == 0:
            p.w("scalar", "T", tcnt)
            for c in range(8):
                rows = io["xbf_tok"][b * S + blkc * 512: b * S + blkc * 512 + 512,
                                     c * 128:(c + 1) * 128]
                dcnt = p.do("scalar", lambda e, rows=rows, c=c: e.dma_start(
                    out=xsb[:, c * 512:(c + 1) * 512], in_=rows, transpose=True),
                    "DA", 16)
        p.w("tensor", "DA", dcnt)
        p.w("tensor", "DG", wload)
        waves = [[(0, 0), (0, 1), (1, 0), (1, 1)], [(2, 0), (2, 1)]]
        for wi, wave in enumerate(waves):
            p.w("tensor", "A", evacA)
            for j, (i, fh) in enumerate(wave):
                pb = ps[j]
                for c in range(8):
                    lhsT = xsb[:, c * 512 + (tau % 4) * 128: c * 512 + (tau % 4) * 128 + 128]
                    rhs = wmap[i][:, c * 1024 + fh * 512: c * 1024 + fh * 512 + 512]
                    p.do("tensor", lambda e, pb=pb, lhsT=lhsT, rhs=rhs, c=c:
                         e.matmul(pb[:, :], lhsT, rhs, start=(c == 0), stop=False))
                tcnt = p.do("tensor", lambda e, pb=pb, i=i, fh=fh: e.matmul(
                    pb[:, :], onesb[0:1, 0:128],
                    brow[0:1, i * 1024 + fh * 512: i * 1024 + fh * 512 + 512],
                    start=False, stop=True), "T", 1)
            p.w("scalar", "T", tcnt)
            if wi == 0:
                p.w("scalar", "V", evacV)
            for j, (i, fh) in enumerate(wave):
                evacA = p.do("scalar", lambda e, i=i, fh=fh, j=j: e.activation(
                    qkv[:, i * 1024 + fh * 512: i * 1024 + fh * 512 + 512],
                    ps[j][:, :], AF.Identity), "A", 1)
        if b == 0 and tau == 0 and "dbg_qkv" in io:
            p.w("sync", "A", evacA)
            p.do("sync", lambda e: e.dma_start(out=io["dbg_qkv"][:, :], in_=qkv), "DS", 16)
            p.w("vector", "DS", p.cnt["DS"])
        p.w("vector", "A", evacA)
        q_t, k_t, v_t = qkv[:, 0:1024], qkv[:, 1024:2048], qkv[:, 2048:3072]
        for hq in range(H):
            qrep = q_t[:, hq * HD:(hq + 1) * HD].unsqueeze(1).broadcast_to([128, 16, HD])
            p.do("vector", lambda e, qrep=qrep: e.tensor_tensor(
                prod.rearrange("p (a b) -> p a b", a=16),
                k_t.rearrange("p (a b) -> p a b", a=16), qrep, ALU.mult))
            p.do("vector", lambda e, hq=hq: e.reduce_sum(
                sc16[:, hq * 16:(hq + 1) * 16],
                prod.rearrange("p (a b) -> p a b", a=16), axis=AX.X))
        vv = p.do("vector", lambda e: e.tensor_copy(prod[:, 0:256], sc16), "V", 1)
        p.w("scalar", "V", vv)
        aa = p.do("scalar", lambda e: e.activation(att[:, :], sc16[:, :], AF.Exp,
                                                   scale=0.125), "A", 1)
        if b == 0 and tau == 0 and "dbg_sc" in io:
            p.w("sync", "A", aa)
            p.do("sync", lambda e: e.dma_start(out=io["dbg_sc"][:, 0:256], in_=sc16), "DS", 16)
            p.do("sync", lambda e: e.dma_start(out=io["dbg_sc"][:, 256:512], in_=att), "DS", 16)
            p.w("vector", "DS", p.cnt["DS"])
        p.w("vector", "A", aa)
        rs = p.do("vector", lambda e: e.reduce_sum(
            zs[:, :], att.rearrange("p (a b) -> p a b", a=16), axis=AX.X), "V", 1)
        p.w("vector", "V", rs)
        rc = p.do("vector", lambda e: e.reciprocal(zr[:, :], zs[:, :]), "V", 1)
        p.w("vector", "V", rc)
        for hq in range(H):
            accv = acc[:, hq * HD:(hq + 1) * HD]
            p.do("vector", lambda e, accv=accv, hq=hq: e.tensor_scalar(
                accv, v_t[:, 0:HD], att[:, hq * 16:hq * 16 + 1], None, ALU.mult))
            for hk in range(1, 16):
                p.do("vector", lambda e, accv=accv, hk=hk, hq=hq: e.scalar_tensor_tensor(
                    accv, v_t[:, hk * HD:(hk + 1) * HD],
                    att[:, hq * 16 + hk:hq * 16 + hk + 1], accv, ALU.mult, ALU.add))
        if b == 0 and tau == 0 and "dbg_acc" in io:
            fence = p.do("vector", lambda e: e.tensor_copy(prod[:, 0:16], zr), "V", 1)
            p.w("sync", "V", fence)
            p.do("sync", lambda e: e.dma_start(out=io["dbg_acc"][:, 0:1024], in_=acc), "DS", 16)
            p.do("sync", lambda e: e.dma_start(out=io["dbg_acc"][:, 1024:1040], in_=zr), "DS", 16)
            p.w("vector", "DS", p.cnt["DS"])
        zrep = zr.unsqueeze(2).broadcast_to([128, 16, HD])
        evacV = p.do("vector", lambda e, zrep=zrep: e.tensor_tensor(
            otok.rearrange("p (a b) -> p a b", a=16),
            acc.rearrange("p (a b) -> p a b", a=16), zrep, ALU.mult), "V", 1)
        if b == 0 and tau == 0 and "dbg_otok" in io:
            p.w("sync", "V", evacV)
            p.do("sync", lambda e: e.dma_start(out=io["dbg_otok"][:, :], in_=otok), "DS", 16)
            p.w("tensor", "DS", p.cnt["DS"])
        p.w("tensor", "V", evacV)
        for c in range(8):
            pbf = ps["bfB"] if c < 4 else ps["bfA"]
            sem = ("T", 1) if c == 7 else (None, 0)
            tcnt2 = p.do("tensor", lambda e, c=c, pbf=pbf: e.transpose(
                pbf[:, (c % 4) * 128:(c % 4) * 128 + 128],
                otok[:, c * 128:(c + 1) * 128], ident[:, :]), sem[0], sem[1])
        p.w("vector", "T", tcnt2)
        ovA = at0.rearrange("p (c n) -> p c n", c=8)[:, 0:4, tau * 128:(tau + 1) * 128]
        ovB = at0.rearrange("p (c n) -> p c n", c=8)[:, 4:8, tau * 128:(tau + 1) * 128]
        p.do("vector", lambda e, ovA=ovA: e.tensor_copy(
            ovA, ps["bfB"].rearrange("p (c n) -> p c n", c=4)))
        evacV = p.do("vector", lambda e, ovB=ovB: e.tensor_copy(
            ovB, ps["bfA"].rearrange("p (c n) -> p c n", c=4)), "V", 1)
    tauT = p.cnt["T"]
    tauV = evacV
    if b == 0 and "dbg_at0" in io:
        p.w("sync", "V", tauV)
        for c in range(8):
            p.do("sync", lambda e, c=c: e.dma_start(
                out=io["dbg_at0"][c * 128:(c + 1) * 128, :],
                in_=at0[:, c * 4096:(c + 1) * 4096]), "DS", 16)
    p.w("gpsimd", "T", tauT)
    for pl in range(16):
        for half in range(2):
            p.do("gpsimd", lambda e, pl=pl, half=half: e.dma_start(
                out=wo[half * 64:half * 64 + 64, pl * 1024:(pl + 1) * 1024],
                in_=io["l0_Wo"][pl * 64:(pl + 1) * 64, :]), "DG", 16)
    wold = p.cnt["DG"]

    vv_bank = [tauV, tauV]
    dsc = p.cnt["DS"]
    for hq in range(H):
        p.w("scalar", "T", tauT)
        p.w("scalar", "V", evacV)
        for c in range(8):
            rows = io["xbf_tok"][b * S + hq * 256: b * S + hq * 256 + 256,
                                 c * 128:(c + 1) * 128]
            dcnt = p.do("scalar", lambda e, rows=rows, c=c: e.dma_start(
                out=xsb[:, c * 512:c * 512 + 256], in_=rows, transpose=True), "DA", 16)
        p.w("tensor", "V", tauV)
        p.w("tensor", "DG", wold)
        for fc in range(8):
            pb = ps[fc % 2]
            p.w("tensor", "V", vv_bank[fc % 2])
            for pl in range(16):
                lhsT = wo[(hq % 2) * 64:(hq % 2) * 64 + 64,
                          pl * 1024 + fc * 128: pl * 1024 + fc * 128 + 128]
                rhs = at0[(hq % 2) * 64:(hq % 2) * 64 + 64,
                          (hq // 2) * 4096:(hq // 2) * 4096 + 4096] \
                    .rearrange("p (n q) -> p q n", q=16)[:, pl, :]
                sem = ("T", 1) if pl == 15 else (None, 0)
                tcnt = p.do("tensor", lambda e, pb=pb, lhsT=lhsT, rhs=rhs, pl=pl:
                            e.matmul(pb[:, 0:256], lhsT, rhs,
                                     start=(pl == 0), stop=(pl == 15)), sem[0], sem[1])
            p.w("vector", "T", tcnt)
            p.w("vector", "DA", dcnt)
            p.w("vector", "DS", bld)
            sv = stile[:, fc * 256:(fc + 1) * 256]
            p.do("vector", lambda e, sv=sv, pb=pb, fc=fc: e.tensor_scalar(
                sv, pb[:, 0:256], bo_pp[:, fc:fc + 1], None, ALU.add))
            vv = p.do("vector", lambda e, sv=sv, fc=fc: e.tensor_tensor(
                sv, sv, xsb[:, fc * 512:fc * 512 + 256], ALU.add), "V", 1)
            vv_bank[fc % 2] = vv
            p.w("scalar", "V", vv)
            aa = p.do("scalar", lambda e, fc=fc, sv=sv: e.activation(
                sqf[:, fc * 256:(fc + 1) * 256], sv, AF.Square), "A", 1)
            p.w("tensor", "V", vv)
            p.w("tensor", "A", aa)
            p.do("tensor", lambda e, sv=sv, fc=fc: e.matmul(
                ps[2][0:1, 0:256], onesb[:, 0:1], sv,
                start=(fc == 0), stop=(fc == 7)))
            tst = p.do("tensor", lambda e, fc=fc: e.matmul(
                ps[2][0:1, 256:512], onesf[:, 0:1], sqf[:, fc * 256:(fc + 1) * 256],
                start=(fc == 0), stop=(fc == 7)), "T", 1)
        p.w("vector", "T", tst)
        p.do("vector", lambda e: e.tensor_scalar(r_mu, ps[2][0:1, 0:256], 1.0 / D, None, ALU.mult))
        p.do("vector", lambda e: e.tensor_scalar(r_m2, ps[2][0:1, 256:512], 1.0 / D, None, ALU.mult))
        p.do("vector", lambda e: e.tensor_tensor(r_rs, r_mu, r_mu, ALU.mult))
        p.do("vector", lambda e: e.tensor_tensor(r_m2, r_m2, r_rs, ALU.subtract))
        vv = p.do("vector", lambda e: e.tensor_scalar(r_m2, r_m2, 1e-5, None, ALU.add), "V", 1)
        p.w("scalar", "V", vv)
        p.do("scalar", lambda e: e.activation(r_rs, r_m2, AF.Ln))
        aa = p.do("scalar", lambda e: e.activation(r_rs, r_rs, AF.Exp, scale=-0.5), "A", 1)
        p.w("tensor", "A", aa)
        p.w("tensor", "V", x1done)
        p.do("tensor", lambda e: e.matmul(ps[3][:, 0:256], onesf[0:1, 0:128], r_mu,
                                          start=True, stop=True))
        tb = p.do("tensor", lambda e: e.matmul(ps[3][:, 256:512], onesf[0:1, 0:128], r_rs,
                                               start=True, stop=True), "T", 1)
        p.w("vector", "T", tb)
        p.w("vector", "DS", dsc)
        for fc in range(8):
            sv = stile[:, fc * 256:(fc + 1) * 256]
            ov = x1t[:, fc * 256:(fc + 1) * 256]
            p.do("vector", lambda e, sv=sv, ov=ov: e.tensor_tensor(
                ov, sv, ps[3][:, 0:256], ALU.subtract))
            x1done = p.do("vector", lambda e, ov=ov: e.tensor_tensor(
                ov, ov, ps[3][:, 256:512], ALU.mult), "V", 1)
        evacV = x1done
        p.w("sync", "V", x1done)
        for fc in range(8):
            dsc = p.do("sync", lambda e, fc=fc, hq=hq: e.dma_start(
                out=io["x1T"][fc * 128:(fc + 1) * 128,
                              b * S + hq * 256: b * S + hq * 256 + 256],
                in_=x1t[:, fc * 256:(fc + 1) * 256]), "DS", 16)




def emit_ln(p, sb, cst, ps, stile, sqf, nw, dst_fn):
    onesb, onesf = cst["onesb"], cst["onesfsb"]
    r_mu, r_m2, r_rs = sb["r_mu"][0:1, 0:nw], sb["r_m2"][0:1, 0:nw], sb["r_rs"][0:1, 0:nw]
    D_ = 1024
    tst = None
    for fc in range(8):
        sv = stile[:, fc * nw:(fc + 1) * nw]
        aa = p.do("scalar", lambda e, fc=fc, sv=sv: e.activation(
            sqf[:, fc * nw:(fc + 1) * nw], sv, AF.Square), "A", 1)
        p.w("tensor", "A", aa)
        p.do("tensor", lambda e, sv=sv, fc=fc: e.matmul(
            ps[2][0:1, 0:nw], onesb[:, 0:1], sv, start=(fc == 0), stop=(fc == 7)))
        tst = p.do("tensor", lambda e, fc=fc, nw=nw: e.matmul(
            ps[2][0:1, 256:256 + nw] if nw <= 256 else ps[2][0:1, 0:0],
            onesf[:, 0:1], sqf[:, fc * nw:(fc + 1) * nw],
            start=(fc == 0), stop=(fc == 7)), "T", 1)
    p.w("vector", "T", tst)
    p.do("vector", lambda e: e.tensor_scalar(r_mu, ps[2][0:1, 0:nw], 1.0 / D_, None, ALU.mult))
    p.do("vector", lambda e: e.tensor_scalar(r_m2, ps[2][0:1, 256:256 + nw], 1.0 / D_, None, ALU.mult))
    p.do("vector", lambda e: e.tensor_tensor(r_rs, r_mu, r_mu, ALU.mult))
    p.do("vector", lambda e: e.tensor_tensor(r_m2, r_m2, r_rs, ALU.subtract))
    vv = p.do("vector", lambda e: e.tensor_scalar(r_m2, r_m2, 1e-5, None, ALU.add), "V", 1)
    p.w("scalar", "V", vv)
    p.do("scalar", lambda e: e.activation(r_rs, r_m2, AF.Ln))
    aa = p.do("scalar", lambda e: e.activation(r_rs, r_rs, AF.Exp, scale=-0.5), "A", 1)
    p.w("tensor", "A", aa)
    p.do("tensor", lambda e: e.matmul(ps[3][:, 0:nw], onesf[0:1, 0:128], r_mu, start=True, stop=True))
    tb = p.do("tensor", lambda e: e.matmul(ps[3][:, 256:256 + nw], onesf[0:1, 0:128], r_rs, start=True, stop=True), "T", 1)
    p.w("vector", "T", tb)
    last = None
    for fc in range(8):
        sv = stile[:, fc * nw:(fc + 1) * nw]
        ov = sb["lnout"][:, fc * nw:(fc + 1) * nw]
        p.do("vector", lambda e, sv=sv, ov=ov, nw=nw: e.tensor_tensor(
            ov, sv, ps[3][:, 0:nw], ALU.subtract))
        last = p.do("vector", lambda e, ov=ov, nw=nw: e.tensor_tensor(
            ov, ov, ps[3][:, 256:256 + nw], ALU.mult), "V", 1)
        p.w("sync", "V", last)
        dst_fn(fc, ov)
    return last


def emit_ffn0(p, io, AR1, AR2, AR3, ARF, ARS, sb, cst, ps):
    nc = p.nc
    scal = sb["fscal"]
    for i, nm in enumerate(("f0_w1_s", "f0_w2_s", "f0_h1_s", "f0_h2_s")):
        p.do("sync", lambda e, i=i, nm=nm: e.dma_start(
            out=scal[:, i * 32:(i + 1) * 32], in_=io[nm][:, :]), "DS", 16)
    p.do("sync", lambda e: e.dma_start(out=scal[:, 128:136], in_=io["f0_w3_s"][:, :]), "DS", 16)
    sld = p.cnt["DS"]

    wA = AR1[:, 0:16384]
    xblk = AR3[:, 0:4096]
    hout = AR3[:, 4096:8192]
    silu_f = ARF[:, 0:512]
    guard_T = p.cnt["T"]
    guard_V = p.cnt["V"]
    guard_A = p.cnt["A"]
    p.w("gpsimd", "T", guard_T)
    p.w("sync", "V", guard_V)

    xbufs = (AR3[:, 0:4096], AR3[:, 8192:12288])
    dcnt = [p.cnt["DS"], p.cnt["DS"]]
    xuse = [guard_T, guard_T]
    ld = p.cnt["DG"]
    evV = [guard_V, guard_V]
    evacV = guard_V
    evacA = guard_A
    tcnt = guard_T
    hst = p.cnt["DS"]
    for js in range(4):
        p.w("gpsimd", "T", tcnt)
        for c in range(8):
            p.do("gpsimd", lambda e, c=c, js=js: e.dma_start(
                out=wA[:, c * 1024:(c + 1) * 1024],
                in_=io["f0_w1T"][c * 128:(c + 1) * 128, js * 1024:(js + 1) * 1024]), "DG", 16)
            ld = p.do("gpsimd", lambda e, c=c, js=js: e.dma_start(
                out=wA[:, 8192 + c * 1024:8192 + (c + 1) * 1024],
                in_=io["f0_w2T"][c * 128:(c + 1) * 128, js * 1024:(js + 1) * 1024]), "DG", 16)
        for tb in range(16):
            xb = tb % 2
            xblk = xbufs[xb]
            p.w("sync", "T", xuse[xb])
            xv = io["x1T"].rearrange("(c p) t -> p c t", p=128)[:, :, tb * 512:(tb + 1) * 512]
            dcnt[xb] = p.do("sync", lambda e, xv=xv, xblk=xblk: e.dma_start(
                out=xblk.rearrange("p (c t) -> p c t", c=8), in_=xv), "DS", 16)
            for jc in range(8):
                pb_u1, pb_u2 = ps[jc % 2 * 2], ps[jc % 2 * 2 + 1]
                p.w("tensor", "DS", dcnt[xb])
                p.w("tensor", "DG", ld)
                p.w("tensor", "V", evV[jc % 2])
                for c in range(8):
                    p.do("tensor", lambda e, pb_u1=pb_u1, c=c, jc=jc, xblk=xblk: e.matmul(
                        pb_u1[:, :], wA[:, c * 1024 + jc * 128: c * 1024 + jc * 128 + 128],
                        xblk[:, c * 512:(c + 1) * 512], start=(c == 0), stop=(c == 7)))
                for c in range(8):
                    tcnt = p.do("tensor", lambda e, pb_u2=pb_u2, c=c, jc=jc, xblk=xblk: e.matmul(
                        pb_u2[:, :], wA[:, 8192 + c * 1024 + jc * 128: 8192 + c * 1024 + jc * 128 + 128],
                        xblk[:, c * 512:(c + 1) * 512], start=(c == 0), stop=(c == 7)),
                        ("T" if c == 7 else None), (1 if c == 7 else 0))
                p.w("scalar", "T", tcnt)
                p.w("scalar", "DS", sld)
                p.w("scalar", "V", evacV)
                aa = p.do("scalar", lambda e, pb_u1=pb_u1, js=js, jc=jc: e.activation(
                    silu_f[:, :], pb_u1[:, :], AF.Silu,
                    scale=scal[:, js * 8 + jc:js * 8 + jc + 1]), "A", 1)
                p.w("vector", "A", aa)
                p.w("vector", "DS", hst)
                hv = hout[:, jc * 512:(jc + 1) * 512]
                evacV = p.do("vector", lambda e, pb_u2=pb_u2, hv=hv, js=js, jc=jc:
                             e.scalar_tensor_tensor(
                                 hv, pb_u2[:, :], scal[:, 32 + js * 8 + jc:32 + js * 8 + jc + 1],
                                 silu_f[:, :], ALU.mult, ALU.mult), "V", 1)
                evV[jc % 2] = evacV
                p.w("sync", "V", evacV)
                hst = p.do("sync", lambda e, js=js, jc=jc, tb=tb, hv=hv: e.dma_start(
                    out=io["hA"][js * 1024 + jc * 128: js * 1024 + jc * 128 + 128,
                                 tb * 512:(tb + 1) * 512], in_=hv), "DS", 16)
            xuse[xb] = tcnt

    hblkA = AR1[:, 0:16384]
    hblkB1 = AR1[:, 16384:28672]
    hblkB2 = AR3[:, 0:4096]

    def hb_chunk(bi, c):
        if bi == 0:
            return hblkA[:, c * 512:(c + 1) * 512]
        if c < 24:
            return hblkB1[:, c * 512:(c + 1) * 512]
        return hblkB2[:, (c - 24) * 512:(c - 23) * 512]

    for stage, (wt, scol, src, dst) in enumerate((
            ("f0_h1T", 64, "hA", "hB"), ("f0_h2T", 96, "hB", "hA"))):
        p.w("tensor", "V", evacV)
        wbufs = (AR2[:, 0:16384], AR2[:, 16384:32768])
        gout = AR3[:, 4096:8192]
        gst = p.cnt["DS"]
        wld = [p.cnt["DG"], p.cnt["DG"]]
        wuse = [tcnt, tcnt]
        hld = [p.cnt["DS"], p.cnt["DS"]]
        huse = [tcnt, tcnt]
        evA4 = [evacA] * 4

        def load_h(tb, src=src):
            bi = tb % 2
            p.w("sync", "T", huse[bi])
            hv2 = io[src].rearrange("(c p) t -> p c t", p=128)[:, :, tb * 512:(tb + 1) * 512]
            if bi == 0:
                hld[bi] = p.do("sync", lambda e, hv2=hv2: e.dma_start(
                    out=hblkA.rearrange("p (c t) -> p c t", c=32), in_=hv2), "DS", 16)
            else:
                p.do("sync", lambda e, hv2=hv2: e.dma_start(
                    out=hblkB1.rearrange("p (c t) -> p c t", c=24), in_=hv2[:, 0:24, :]), "DS", 16)
                hld[bi] = p.do("sync", lambda e, hv2=hv2: e.dma_start(
                    out=hblkB2.rearrange("p (c t) -> p c t", c=8), in_=hv2[:, 24:32, :]), "DS", 16)

        load_h(0)
        for tb in range(16):
            bi = tb % 2
            if tb + 1 < 16:
                load_h(tb + 1)
            for hs in range(8):
                wi = hs % 2
                wsl = wbufs[wi]
                p.w("gpsimd", "T", wuse[wi])
                wv = io[wt].rearrange("(c p) j -> p c j", p=128)[:, :, hs * 512:(hs + 1) * 512]
                wld[wi] = p.do("gpsimd", lambda e, wv=wv, wsl=wsl: e.dma_start(
                    out=wsl.rearrange("p (c j) -> p c j", c=32), in_=wv), "DG", 16)
                for jc in range(4):
                    jg = hs * 4 + jc
                    pb = ps[jg % 4]
                    p.w("tensor", "DS", hld[bi])
                    p.w("tensor", "DG", wld[wi])
                    p.w("tensor", "A", evA4[jg % 4])
                    for c in range(32):
                        tcnt = p.do("tensor", lambda e, pb=pb, c=c, jc=jc, bi=bi, wsl=wsl: e.matmul(
                            pb[:, :], wsl[:, c * 512 + jc * 128: c * 512 + jc * 128 + 128],
                            hb_chunk(bi, c), start=(c == 0), stop=(c == 31)),
                            ("T" if c == 31 else None), (1 if c == 31 else 0))
                    wuse[wi] = tcnt
                    p.w("scalar", "T", tcnt)
                    p.w("scalar", "DS", gst)
                    gv = gout[:, (jg % 8) * 512:(jg % 8) * 512 + 512]
                    evA4[jg % 4] = p.do("scalar", lambda e, pb=pb, gv=gv, jg=jg, scol=scol:
                                        e.activation(gv, pb[:, :], AF.Gelu,
                                                     scale=scal[:, scol + jg:scol + jg + 1]),
                                        "A", 1)
                    p.w("sync", "A", evA4[jg % 4])
                    gst = p.do("sync", lambda e, jg=jg, tb=tb, gv=gv, dst=dst: e.dma_start(
                        out=io[dst][jg * 128:(jg + 1) * 128,
                                    tb * 512:(tb + 1) * 512], in_=gv), "DS", 16)
            huse[bi] = tcnt
        evacA = max(evA4)

    w3 = AR2[:, 0:32768]
    p.w("gpsimd", "T", tcnt)
    for c in range(32):
        ld = p.do("gpsimd", lambda e, c=c: e.dma_start(
            out=w3[:, c * 1024:(c + 1) * 1024],
            in_=io["f0_w3T"][c * 128:(c + 1) * 128, :]), "DG", 16)
    gblk = AR1[:, 0:16384]
    xres = AR3[:, 0:4096]
    stile = AR3[:, 4096:8192]
    sb["lnout"] = AR3[:, 8192:12288]
    sqf = ARF[:, 0:4096]
    x2st = p.cnt["DS"]
    lnV = p.cnt["V"]
    for tb in range(16):
        p.w("sync", "T", tcnt)
        gv2 = io["hA"].rearrange("(c p) t -> p c t", p=128)[:, :, tb * 512:(tb + 1) * 512]
        p.do("sync", lambda e, gv2=gv2: e.dma_start(
            out=gblk.rearrange("p (c t) -> p c t", c=32), in_=gv2), "DS", 16)
        xv2 = io["x1T"].rearrange("(c p) t -> p c t", p=128)[:, :, tb * 512:(tb + 1) * 512]
        dcnt = p.do("sync", lambda e, xv2=xv2: e.dma_start(
            out=xres.rearrange("p (c t) -> p c t", c=8), in_=xv2), "DS", 16)
        vlast = None
        for ec in range(8):
            pb = ps[ec % 2]
            p.w("tensor", "DS", dcnt)
            p.w("tensor", "DG", ld)
            p.w("tensor", "V", lnV)
            for c in range(32):
                tcnt = p.do("tensor", lambda e, pb=pb, c=c, ec=ec: e.matmul(
                    pb[:, :], w3[:, c * 1024 + ec * 128: c * 1024 + ec * 128 + 128],
                    gblk[:, c * 512:(c + 1) * 512], start=(c == 0), stop=(c == 31)),
                    ("T" if c == 31 else None), (1 if c == 31 else 0))
            p.w("vector", "T", tcnt)
            p.w("vector", "DS", x2st)
            sv = stile[:, ec * 512:(ec + 1) * 512]
            lnV = p.do("vector", lambda e, pb=pb, sv=sv, ec=ec: e.scalar_tensor_tensor(
                sv, pb[:, :], scal[:, 128 + ec:128 + ec + 1],
                xres[:, ec * 512:(ec + 1) * 512], ALU.mult, ALU.add), "V", 1)
        p.w("scalar", "V", lnV)
        x2st = emit_ffn_ln512(p, io, sb, cst, ps, stile, sqf, tb, "x2T")
        lnV = p.cnt["V"]


def emit_ffn_ln512(p, io, sb, cst, ps, stile, sqf, tb, dst):
    onesb, onesf = cst["onesb"], cst["onesfsb"]
    nw = 512
    r_mu, r_m2, r_rs = sb["r_mu"][0:1, 0:nw], sb["r_m2"][0:1, 0:nw], sb["r_rs"][0:1, 0:nw]
    tst = None
    for fc in range(8):
        sv = stile[:, fc * nw:(fc + 1) * nw]
        aa = p.do("scalar", lambda e, fc=fc, sv=sv: e.activation(
            sqf[:, fc * nw:(fc + 1) * nw] if fc < 8 else None, sv, AF.Square), "A", 1)
        p.w("tensor", "A", aa)
        p.do("tensor", lambda e, sv=sv, fc=fc: e.matmul(
            ps[2][0:1, 0:nw], onesb[:, 0:1], sv, start=(fc == 0), stop=(fc == 7)))
        tst = p.do("tensor", lambda e, fc=fc: e.matmul(
            ps[3][0:1, 0:nw], onesf[:, 0:1], sqf[:, fc * nw:(fc + 1) * nw],
            start=(fc == 0), stop=(fc == 7)), "T", 1)
    p.w("vector", "T", tst)
    p.do("vector", lambda e: e.tensor_scalar(r_mu, ps[2][0:1, 0:nw], 1.0 / 1024, None, ALU.mult))
    p.do("vector", lambda e: e.tensor_scalar(r_m2, ps[3][0:1, 0:nw], 1.0 / 1024, None, ALU.mult))
    p.do("vector", lambda e: e.tensor_tensor(r_rs, r_mu, r_mu, ALU.mult))
    p.do("vector", lambda e: e.tensor_tensor(r_m2, r_m2, r_rs, ALU.subtract))
    vv = p.do("vector", lambda e: e.tensor_scalar(r_m2, r_m2, 1e-5, None, ALU.add), "V", 1)
    p.w("scalar", "V", vv)
    p.do("scalar", lambda e: e.activation(r_rs, r_m2, AF.Ln))
    aa = p.do("scalar", lambda e: e.activation(r_rs, r_rs, AF.Exp, scale=-0.5), "A", 1)
    p.w("tensor", "A", aa)
    p.do("tensor", lambda e: e.matmul(ps[2][:, 0:nw], onesf[0:1, 0:128], r_mu, start=True, stop=True))
    tb2 = p.do("tensor", lambda e: e.matmul(ps[3][:, 0:nw], onesf[0:1, 0:128], r_rs, start=True, stop=True), "T", 1)
    p.w("vector", "T", tb2)
    stD = p.cnt["DS"]
    dual = (dst == "x2T" and "x2Tf" in io)
    for fc in range(8):
        sv = stile[:, fc * nw:(fc + 1) * nw]
        ov = sb["lnout"][:, fc * nw:(fc + 1) * nw]
        if dual:
            ovf = sb["lnoutf"]
            p.w("vector", "DS", stD)
            p.do("vector", lambda e, sv=sv, ovf=ovf: e.tensor_tensor(ovf, sv, ps[2][:, 0:nw], ALU.subtract))
            lastf = p.do("vector", lambda e, ovf=ovf: e.tensor_tensor(ovf, ovf, ps[3][:, 0:nw], ALU.mult), "V", 1)
            p.w("sync", "V", lastf)
            stD = p.do("sync", lambda e, fc=fc, tb=tb, ovf=ovf: e.dma_start(
                out=io["x2Tf"][fc * 128:(fc + 1) * 128, tb * 512:(tb + 1) * 512], in_=ovf), "DS", 16)
        p.do("vector", lambda e, sv=sv, ov=ov: e.tensor_tensor(ov, sv, ps[2][:, 0:nw], ALU.subtract))
        last = p.do("vector", lambda e, ov=ov: e.tensor_tensor(ov, ov, ps[3][:, 0:nw], ALU.mult), "V", 1)
        p.w("sync", "V", last)
        stD = p.do("sync", lambda e, fc=fc, tb=tb, ov=ov, dst=dst: e.dma_start(
            out=io[dst][fc * 128:(fc + 1) * 128, tb * 512:(tb + 1) * 512], in_=ov), "DS", 16)
    return stD




def emit_l1qkv(p, io, AR1, AR2, AR3, ARF, ARS, sb, cst, ps, b):
    onesb, onesf, lup, gmats = cst["onesb"], cst["onesfsb"], cst["lup"], cst["gmats"]
    probes = sb["probes"]
    x2blk = AR3[:, 0:4096]
    ev = AR3[:, 4096:5120]
    ohP = AR3[:, 5120:11264]
    ranks = sb["ranks"]
    qrtf = AR2[:, 0:32768].bitcast(F32)
    tptf = ARF[:, 0:32]
    sptf = ARF[:, 32:64]
    cnt1 = ARF[:, 64:65]
    gst1 = ARF[:, 65:66]
    wqf = ARF[:, 128:1152]
    rext = ARF[:, 128:640]
    projsb = ARF[:, 1152:4224]
    diag = ARF[:, 4224:6272]
    wq1 = sb["wq1"]; wk1 = sb["wk1"]; wv1 = sb["wv1"]; brow1 = sb["brow1"]

    if b == 0:
        p.w("gpsimd", "T", p.cnt["T"])
        for nm, dstv in (("l1_Wq", wq1), ("l1_Wk", wk1), ("l1_Wv", wv1)):
            for c in range(8):
                p.do("gpsimd", lambda e, d=dstv[:, c * 1024:(c + 1) * 1024],
                     s=io[nm][c * 128:(c + 1) * 128, :]: e.dma_start(out=d, in_=s),
                     "DG", 16)
        for i, nm in enumerate(("l1_bq_row", "l1_bk_row", "l1_bv_row")):
            p.do("gpsimd", lambda e, d=brow1[0:1, i * 1024:(i + 1) * 1024],
                 s=io[nm][0:1, :]: e.dma_start(out=d, in_=s), "DG", 16)
        for h in range(16):
            p.do("sync", lambda e, h=h: e.dma_start(
                out=probes[(h % 2) * 64:(h % 2) * 64 + 64, (h // 2) * 192:(h // 2) * 192 + 192],
                in_=io["l1_probes"][h, :, :]), "DS", 16)
        p.do("sync", lambda e: e.dma_start(out=sb["bqpp1"], in_=io["l1_bq_pp"][:, :]), "DS", 16)
    wload = p.cnt["DG"]
    sload = p.cnt["DS"]
    wmap = (wq1, wk1, wv1)

    evacA = p.cnt["A"]
    tcnt = p.cnt["T"]
    dcnt = p.cnt["DS"]
    est = p.cnt["DS"]
    p.w("tensor", "V", p.cnt["V"])
    p.w("tensor", "A", p.cnt["A"])
    for tau in range(32):
        blkc = tau // 4
        if tau % 4 == 0:
            p.w("sync", "T", tcnt)
            for c in range(8):
                dcnt = p.do("sync", lambda e, c=c, blkc=blkc: e.dma_start(
                    out=x2blk[:, c * 512:(c + 1) * 512],
                    in_=io["x2T"][c * 128:(c + 1) * 128,
                                  b * S + blkc * 512: b * S + blkc * 512 + 512]), "DS", 16)
        waves = [[(0, 0), (0, 1), (1, 0), (1, 1)], [(2, 0), (2, 1)]]
        for wi, wave in enumerate(waves):
            p.w("tensor", "A", evacA)
            p.w("tensor", "DS", dcnt)
            p.w("tensor", "DG", wload)
            for j, (i, fh) in enumerate(wave):
                pb = ps[j]
                for c in range(8):
                    lhsT = x2blk[:, c * 512 + (tau % 4) * 128: c * 512 + (tau % 4) * 128 + 128]
                    rhs = wmap[i][:, c * 1024 + fh * 512: c * 1024 + fh * 512 + 512]
                    p.do("tensor", lambda e, pb=pb, lhsT=lhsT, rhs=rhs, c=c:
                         e.matmul(pb[:, :], lhsT, rhs, start=(c == 0), stop=False))
                tcnt = p.do("tensor", lambda e, pb=pb, i=i, fh=fh: e.matmul(
                    pb[:, :], onesb[0:1, 0:128],
                    brow1[0:1, i * 1024 + fh * 512: i * 1024 + fh * 512 + 512],
                    start=False, stop=True), "T", 1)
            p.w("scalar", "T", tcnt)
            for j, (i, fh) in enumerate(wave):
                evv = ev[:, (j % 2) * 512:(j % 2) * 512 + 512]
                p.w("scalar", "DS", est)
                evacA = p.do("scalar", lambda e, j=j, evv=evv: e.activation(
                    evv, ps[j][:, :], AF.Identity), "A", 1)
                p.w("sync", "A", evacA)
                ov = io["qkvs"].rearrange("b h t r -> b h t r")[
                    b, fh * 8:(fh + 1) * 8, tau * 128:(tau + 1) * 128, i * 64:(i + 1) * 64]                     .rearrange("h t r -> t h r")
                est = p.do("sync", lambda e, ov=ov, evv=evv: e.dma_start(
                    out=ov, in_=evv.rearrange("p (h r) -> p h r", h=8)), "DS", 16)

    x2f = AR3[:, 0:8192].bitcast(F32)
    for hh2 in range(2):
        p.w("tensor", "V", p.cnt["V"])
        p.w("sync", "V", p.cnt["V"])
        tq = p.cnt["T"]
        for fcl in range(4):
            fc = hh2 * 4 + fcl
            p.w("sync", "T", tcnt)
            wqld = p.do("sync", lambda e, fc=fc: e.dma_start(
                out=wqf.rearrange("p (c n) -> p c n", n=128),
                in_=io["l1_Wq"].rearrange("(c p) f -> p c f", p=128)[:, :, fc * 128:(fc + 1) * 128]), "DS", 16)
            for tb8 in range(8):
                p.w("sync", "T", tcnt)
                for c in range(8):
                    dcnt = p.do("sync", lambda e, c=c, tb8=tb8: e.dma_start(
                        out=x2f[:, c * 512:(c + 1) * 512],
                        in_=io["x2Tf"][c * 128:(c + 1) * 128,
                                       b * S + tb8 * 512: b * S + tb8 * 512 + 512]), "DS", 16)
                pb = ps[tb8 % 2]
                p.w("tensor", "DS", dcnt)
                p.w("tensor", "DS", wqld)
                p.w("tensor", "A", evacA)
                for c in range(8):
                    tcnt = p.do("tensor", lambda e, pb=pb, c=c: e.matmul(
                        pb[:, :], wqf[:, c * 128:(c + 1) * 128],
                        x2f[:, c * 512:(c + 1) * 512], start=(c == 0), stop=(c == 7)),
                        ("T" if c == 7 else None), (1 if c == 7 else 0))
                p.w("scalar", "T", tcnt)
                p.w("scalar", "DS", sload)
                evacA = p.do("scalar", lambda e, pb=pb, fc=fc, fcl=fcl, tb8=tb8: e.activation(
                    qrtf[:, fcl * 4096 + tb8 * 512: fcl * 4096 + tb8 * 512 + 512],
                    pb[:, :], AF.Identity, bias=sb["bqpp1"][:, fc:fc + 1]), "A", 1)
        qdone = evacA
        if b == 0 and hh2 == 0 and "dbg_qrt" in io:
            p.w("sync", "A", qdone)
            p.do("sync", lambda e: e.dma_start(out=io["dbg_qrt"][:, :], in_=qrtf), "DS", 16)
            p.w("tensor", "DS", p.cnt["DS"])
        for fcl in range(4):
            fc = hh2 * 4 + fcl
            for hh in range(2):
                h = fc * 2 + hh
                m96 = ARF[:, 6272:6320]
                pc16 = ARF[:, 6320:6576]
                p.w("tensor", "A", qdone)
                p.w("tensor", "DS", sload)
                ohdone = p.cnt["V"]
                for half in range(2):
                    p.w("tensor", "V", ohdone)
                    pr = p.cnt["A"]
                    for ta in range(16):
                        tau = half * 16 + ta
                        pb = ps[ta % 2]
                        p.w("tensor", "A", pr if ta < 2 else evacA)
                        tcnt = p.do("tensor", lambda e, pb=pb, tau=tau, fcl=fcl, hh=hh, fc=fc:
                                    e.matmul(pb[:, 0:192],
                                             qrtf[hh * 64:hh * 64 + 64,
                                                  fcl * 4096 + tau * 128: fcl * 4096 + tau * 128 + 128],
                                             probes[hh * 64:hh * 64 + 64,
                                                    fc * 192:(fc + 1) * 192],
                                             start=True, stop=True), "T", 1)
                        p.w("scalar", "T", tcnt)
                        evacA = p.do("scalar", lambda e, pb=pb, ta=ta: e.activation(
                            projsb[:, ta * 192:(ta + 1) * 192], pb[:, 0:192],
                            AF.Identity), "A", 1)
                    p.w("vector", "A", evacA)
                    ohh = ohP.rearrange("p (g x) -> p g x", x=192)[:, half * 16:half * 16 + 16, :]
                    p.do("vector", lambda e: e.reduce_max(
                        m96, projsb.rearrange("p (g a) -> p g a", a=64), axis=AX.X))
                    p.do("vector", lambda e: e.reduce_max(
                        pc16.rearrange("p (g c) -> p g c", c=16),
                        projsb.rearrange("p (g x) -> p g x", x=192)[:, :, 0:64]
                        .rearrange("p g (a c) -> p g a c", a=4).transpose([0, 1, 3, 2]),
                        axis=AX.X))
                    fce = p.do("vector", lambda e: e.tensor_copy(rext[:, 0:48], m96), "V", 1)
                    p.w("vector", "V", fce)
                    m0 = m96.rearrange("p (g l) -> p g l", l=3)
                    p.do("vector", lambda e, ohh=ohh: e.tensor_tensor(
                        ohh[:, :, 0:64].rearrange("p g (a c) -> p g a c", a=4),
                        pc16.rearrange("p (g c) -> p g c", c=16).unsqueeze(2).broadcast_to([128, 16, 4, 16]),
                        m0[:, :, 0:1].unsqueeze(3).broadcast_to([128, 16, 4, 16]),
                        ALU.is_ge))
                    for lvl in (1, 2):
                        p.do("vector", lambda e, lvl=lvl, ohh=ohh: e.tensor_tensor(
                            ohh[:, :, lvl * 64:(lvl + 1) * 64],
                            projsb.rearrange("p (g x) -> p g x", x=192)[:, :, lvl * 64:(lvl + 1) * 64],
                            m0[:, :, lvl:lvl + 1].broadcast_to([128, 16, 64]),
                            ALU.is_ge))
                    ohdone = p.do("vector", lambda e: e.tensor_copy(rext[:, 0:192], ohP[:, 0:192]), "V", 1)
                    if b == 0 and h == 0 and "dbg_oh" in io:
                        p.w("sync", "V", ohdone)
                        p.do("sync", lambda e, half=half: e.dma_start(
                            out=io["dbg_proj"][:, half * 3072:(half + 1) * 3072], in_=projsb), "DS", 16)
                        p.w("vector", "DS", p.cnt["DS"])
                        p.w("scalar", "DS", p.cnt["DS"])
                if b == 0 and h == 0 and "dbg_oh" in io:
                    p.w("sync", "V", ohdone)
                    p.do("sync", lambda e: e.dma_start(out=io["dbg_oh"][:, 0:6144], in_=ohP), "DS", 16)
                    p.w("vector", "DS", p.cnt["DS"])
                for lvl in range(3):
                    csize = 4 if lvl == 0 else 1
                    ohv = ohP.rearrange("p (g x) -> p g x", x=192)[:, :, lvl * 64:(lvl + 1) * 64]
                    p.w("tensor", "V", ohdone)
                    for tau in range(32):
                        tcnt = p.do("tensor", lambda e, tau=tau, ohv=ohv: e.matmul(
                            ps[3][0:64, tau:tau + 1], ohv[:, tau, :], onesb[:, 0:1],
                            start=True, stop=True), "T", 1)
                    p.w("vector", "T", tcnt)
                    vv = p.do("vector", lambda e: e.tensor_copy(tptf[0:64, :], ps[3][0:64, 0:32]), "V", 1)
                    p.w("vector", "V", vv)
                    sc = p.do("vector", lambda e: e.tensor_tensor_scan(
                        sptf[0:64, :], onesf[0:64, 0:32], tptf[0:64, :], 0.0,
                        ALU.mult, ALU.add), "V", 1)
                    p.w("vector", "V", sc)
                    vv = p.do("vector", lambda e: e.tensor_copy(cnt1[0:64, :], sptf[0:64, 31:32]), "V", 1)
                    p.w("vector", "V", vv)
                    p.do("vector", lambda e: e.tensor_tensor(
                        sptf[0:64, :], sptf[0:64, :], tptf[0:64, :], ALU.subtract))
                    vv = p.do("vector", lambda e: e.tensor_copy(rext[0:64, 0:32], sptf[0:64, :]), "V", 1)
                    p.w("tensor", "V", vv)
                    tg = p.do("tensor", lambda e, lvl=lvl: e.matmul(
                        ps[3][0:64, 40:41], gmats[:, lvl * 64:(lvl + 1) * 64], cnt1[0:64, :],
                        start=True, stop=True), "T", 1)
                    p.w("vector", "T", tg)
                    vv = p.do("vector", lambda e: e.tensor_copy(gst1[0:64, :], ps[3][0:64, 40:41]), "V", 1)
                    p.w("vector", "V", vv)
                    vv = p.do("vector", lambda e: e.tensor_scalar(
                        sptf[0:64, :], sptf[0:64, :], gst1[0:64, 0:1], None, ALU.add), "V", 1)
                    p.w("vector", "V", vv)
                    dv = p.do("vector", lambda e: e.tensor_tensor(
                        diag.rearrange("p (g x) -> p g x", x=64)[0:64, :, :],
                        cst["identf"][0:64, 0:64].unsqueeze(1).broadcast_to([64, 32, 64]),
                        sptf[0:64, :].unsqueeze(2).broadcast_to([64, 32, 64]),
                        ALU.mult), "V", 1)
                    p.w("tensor", "V", dv)
                    for tau in range(32):
                        pa = ps[tau // 8]
                        po = (tau % 8) * 64
                        p.do("tensor", lambda e, pa=pa, po=po, tau=tau, ohv=ohv: e.matmul(
                            pa[:, po:po + 64], lup[:, :], ohv[:, tau, :],
                            start=True, stop=False))
                        tcnt = p.do("tensor", lambda e, pa=pa, po=po, tau=tau: e.matmul(
                            pa[:, po:po + 64], onesf[0:64, 0:128],
                            diag.rearrange("p (g x) -> p g x", x=64)[0:64, tau, :],
                            start=False, stop=True), "T", 1)
                    p.w("vector", "T", tcnt)
                    rsb = ranks.bitcast(F32)[:, ((b * 16 + h) * 3 + lvl) * 32:
                                             ((b * 16 + h) * 3 + lvl) * 32 + 32]
                    for bank in range(4):
                        p.do("vector", lambda e, bank=bank, ohv=ohv: e.tensor_tensor(
                            rext.rearrange("p (g x) -> p g x", x=64),
                            ps[bank][:, :].rearrange("p (g x) -> p g x", x=64),
                            ohv[:, bank * 8:(bank + 1) * 8, :],
                            ALU.mult))
                        rr = p.do("vector", lambda e, bank=bank, rsb=rsb: e.reduce_sum(
                            rsb[:, bank * 8:(bank + 1) * 8],
                            rext.rearrange("p (g x) -> p g x", x=64), axis=AX.X), "V", 1)
                    p.w("vector", "V", rr)
                    if b == 0 and h == 0 and lvl == 0 and "dbg_cnt" in io:
                        fz = p.do("vector", lambda e: e.tensor_copy(rext[:, 33:65], tptf[:, :]), "V", 1)
                        p.w("sync", "V", fz)
                        p.do("sync", lambda e: e.dma_start(out=io["dbg_cnt"][:, 0:32], in_=tptf[:, :]), "DS", 16)
                        p.do("sync", lambda e: e.dma_start(out=io["dbg_cnt"][:, 32:64], in_=sptf[:, :]), "DS", 16)
                        p.do("sync", lambda e: e.dma_start(out=io["dbg_cnt"][:, 64:65], in_=cnt1[:, :], allow_slow_non_contiguous=True), "DS", 16)
                        p.do("sync", lambda e: e.dma_start(out=io["dbg_cnt"][:, 65:66], in_=gst1[:, :], allow_slow_non_contiguous=True), "DS", 16)
                        p.do("sync", lambda e, rsb=rsb: e.dma_start(out=io["dbg_cnt"][:, 66:98], in_=rsb), "DS", 16)
                        p.w("vector", "DS", p.cnt["DS"])
                    fr = p.do("vector", lambda e, rsb=rsb, csize=csize: e.tensor_scalar(
                        rext[:, 0:32], rsb, 1.0 / csize, 4095.0, ALU.mult, ALU.min), "V", 1)
                    p.w("vector", "V", fr)
                    ru = ranks[:, ((b * 16 + h) * 3 + lvl) * 32: ((b * 16 + h) * 3 + lvl) * 32 + 32]
                    ohdone = p.do("vector", lambda e, ru=ru: e.tensor_copy(ru, rext[:, 0:32]), "V", 1)


def emit_dump_ranks(p, io, sb):
    p.w("sync", "V", p.cnt["V"])
    p.do("sync", lambda e: e.dma_start(out=io["dbg_ranks"][:, :], in_=sb["ranks"][:, :]), "DS", 16)




def lvl_spans(lvl):
    cs = (256, 64, 16)[lvl]
    spans = []
    for kb in range(32):
        if lvl == 0:
            g = kb // 2
            a, bnd = g * 256, min(g * 256 + 512, 4096)
        else:
            a, bnd = kb * 128, min(kb * 128 + 128 + cs, 4096)
        spans.append((a, bnd))
    return spans


def emit_l1attn(p, io, AR1, AR2, AR3, ARF, ARS, sb, cst, ps, b):
    ranks = sb["ranks"]
    onesb, ident = cst["onesb"], cst["ident"]
    qkvrows = AR2[:, 0:16384]
    qkT = AR1[:, 0:4096]
    kvT = AR1[:, 4096:8192]
    v65 = AR1[:, 8192:10272]
    expT = AR1[:, 10272:11296]
    out_tok = AR1[:, 11296:13344]
    outsb = AR1[:, 13344:13856]
    acc = AR1[:, 13856:17952]
    rz = ARS[:, 160:168]

    p.w("gpsimd", "V", p.cnt["V"])
    p.w("gpsimd", "T", p.cnt["T"])
    if b == 0:
        p.do("gpsimd", lambda e: e.memset(v65[:, :], 1.0), "G", 1)
    mst = p.cnt["G"]

    ld_qkv = [p.cnt["DG"], p.cnt["DG"]]
    vld = [p.cnt["DS"], p.cnt["DS"]]
    ebuse = [p.cnt["T"], p.cnt["T"]]
    sc_done = [p.cnt["DG"], p.cnt["DG"]]
    tr_done = [p.cnt["DA"], p.cnt["DA"]]
    att_done = [p.cnt["T"], p.cnt["T"]]
    g_done = [p.cnt["DG"], p.cnt["DG"]]
    ot_st = [p.cnt["DS"], p.cnt["DS"]]
    rows_use = [p.cnt["DG"], p.cnt["DG"]]
    acst = p.cnt["DS"]
    evacV = p.cnt["V"]
    evacA = p.cnt["A"]
    tcnt = p.cnt["T"]
    g_last = p.cnt["DG"]
    pending = None
    p.w("tensor", "V", p.cnt["V"])

    for h in range(16):
        u = (b * 16 + h) * 3
        hb = h % 2
        p.w("gpsimd", "DG", rows_use[hb])
        src = io["qkvs"][b, h, :, :].rearrange("(c p) r -> p c r", p=128)
        qr = qkvrows[:, hb * 8192:(hb + 1) * 8192]
        ld_qkv[hb] = p.do("gpsimd", lambda e, src=src, qr=qr: e.dma_start(
            out=qr.rearrange("p (c r) -> p c r", r=256), in_=src), "DG", 16)
        for lvl in range(3):
            sbuf_i = (h * 3 + lvl) % 2
            qs = io["qkv_sorted%d" % sbuf_i]
            osrt = io["outsorted%d" % sbuf_i]
            p.w("gpsimd", "DG", ld_qkv[hb])
            p.w("gpsimd", "DA", tr_done[sbuf_i])
            p.w("gpsimd", "DS", vld[sbuf_i])
            for tau in range(32):
                sc_done[sbuf_i] = p.do("gpsimd", lambda e, tau=tau, u=u, lvl=lvl, qs=qs, qr=qr:
                                       e.indirect_dma_start(
                    out=qs[:, :],
                    out_offset=IndirectOffsetOnAxis(
                        ranks[:, (u + lvl) * 32 + tau:(u + lvl) * 32 + tau + 1], axis=0),
                    in_=qr[:, tau * 256:(tau + 1) * 256], in_offset=None), "DG", 16)
            rows_use[hb] = sc_done[sbuf_i]
            if pending is not None:
                pending()
                pending = None
            p.w("scalar", "DG", sc_done[sbuf_i])
            p.w("scalar", "T", att_done[0])
            p.w("scalar", "T", att_done[1])
            for w in range(8):
                p.do("scalar", lambda e, w=w, qs=qs: e.dma_start(
                    out=qkT[:, w * 512:(w + 1) * 512],
                    in_=qs[w * 512:(w + 1) * 512, 0:128],
                    transpose=True), "DA", 16)
                tr_done[sbuf_i] = p.do("scalar", lambda e, w=w, qs=qs: e.dma_start(
                    out=kvT[:, w * 512:(w + 1) * 512],
                    in_=qs[w * 512:(w + 1) * 512, 64:192],
                    transpose=True), "DA", 16)
            p.w("sync", "DG", sc_done[sbuf_i])
            p.w("sync", "G", mst)
            p.w("sync", "T", att_done[0])
            p.w("sync", "T", att_done[1])
            vld[sbuf_i] = p.do("sync", lambda e, qs=qs: e.dma_start(
                out=v65.rearrange("p (c r) -> p c r", r=65)[:, :, 0:64],
                in_=qs[:, 128:192].rearrange("(c p) r -> p c r", p=128)), "DS", 16)
            if b == 0 and h == 0 and lvl == 0 and "dbg_qsort" in io:
                p.w("sync", "DG", sc_done)
                p.do("sync", lambda e: e.dma_start(out=io["dbg_qsort"][:, :], in_=io["qkv_sorted0"][:, :]), "DS", 16)
                p.w("scalar", "DA", 0)
                dqs = p.cnt["DS"]
                p.w("scalar", "DS", dqs)
            spans = lvl_spans(lvl)
            mask = cst["mask1"] if lvl == 1 else (cst["mask2"] if lvl == 2 else None)
            base_evacV = p.cnt["V"]
            win_state = {}
            win_open = [False] * 8
            closers = [max(kb for kb in range(32)
                           if spans[kb][0] < (w + 1) * 512 and spans[kb][1] > w * 512)
                       for w in range(8)]
            widx = 0
            for kb in range(32):
                a, bnd = spans[kb]
                pbs = ps[kb % 2]
                p.w("tensor", "DA", tr_done[sbuf_i])
                p.w("tensor", "DS", vld[sbuf_i])
                p.w("tensor", "A", evacA)
                tcnt = p.do("tensor", lambda e, pbs=pbs, kb=kb, a=a, bnd=bnd: e.matmul(
                    pbs[:, 0:bnd - a], kvT[0:64, kb * 128:(kb + 1) * 128],
                    qkT[0:64, a:bnd], start=True, stop=True), "T", 1)
                p.w("scalar", "T", tcnt)
                p.w("scalar", "T", ebuse[kb % 2])
                eb = expT[:, (kb % 2) * 512:(kb % 2) * 512 + 512]
                evacA = p.do("scalar", lambda e, pbs=pbs, eb=eb, a=a, bnd=bnd: e.activation(
                    eb[:, 0:bnd - a], pbs[:, 0:bnd - a], AF.Exp, scale=0.125), "A", 1)
                if b == 0 and h == 0 and lvl == 0 and kb in (0, 1) and "dbg_win" in io:
                    p.w("sync", "A", evacA)
                    p.do("sync", lambda e, eb=eb, kb=kb, a=a, bnd=bnd: e.dma_start(
                        out=io["dbg_exp"][128 * kb:128 * kb + 128, 0:bnd - a], in_=eb[:, 0:bnd - a]), "DS", 16)
                    p.w("tensor", "DS", p.cnt["DS"])
                    p.w("scalar", "DS", p.cnt["DS"])
                if mask is not None:
                    p.w("vector", "A", evacA)
                    evacV = p.do("vector", lambda e, eb=eb, mask=mask, a=a, bnd=bnd: e.tensor_tensor(
                        eb[:, 0:bnd - a], eb[:, 0:bnd - a], mask[:, 0:bnd - a], ALU.mult), "V", 1)
                    ebsrc = eb
                    p.w("tensor", "V", evacV)
                else:
                    ebsrc = eb
                    p.w("tensor", "A", evacA)
                pos = a
                while pos < bnd:
                    w = pos // 512
                    pe = min(bnd, (w + 1) * 512)
                    pw = ps[2 + (w % 2)]
                    if not win_open[w]:
                        win_open[w] = True
                        p.w("tensor", "V", win_state.get(w - 2, base_evacV))
                        p.do("tensor", lambda e, pw=pw: e.matmul(
                            pw[0:65, :], cst["zz"][0:1, 0:65], cst["zz"][0:1, 0:512],
                            start=True, stop=False))
                    last_piece = (pe >= bnd)
                    tret = p.do("tensor", lambda e, pw=pw, kb=kb, pos=pos, pe=pe,
                                a=a, w=w, ebsrc=ebsrc: e.matmul(
                        pw[0:65, pos - w * 512:pe - w * 512],
                        v65[:, kb * 65:(kb + 1) * 65], ebsrc[:, pos - a:pe - a],
                        start=False, stop=False),
                        "T" if last_piece else None, 1 if last_piece else 0)
                    if tret is not None:
                        ebuse[kb % 2] = tret
                    pos = pe
                for w in range(8):
                    if win_open[w] and closers[w] == kb:
                        pw = ps[2 + (w % 2)]
                        tcl = p.do("tensor", lambda e, pw=pw: e.matmul(
                            pw[0:65, 0:1], cst["zz"][0:1, 0:65], cst["zz"][0:1, 0:1],
                            start=False, stop=True), "T", 1)
                        p.w("vector", "T", tcl)
                        p.w("vector", "T", tcnt)
                        cpv = p.do("vector", lambda e, pw=pw: e.tensor_copy(
                            outsb[0:65, 0:512], pw[0:65, :]), "V", 1)
                        if b == 0 and h == 0 and lvl == 0 and w == 0 and "dbg_win" in io:
                            p.w("sync", "V", cpv)
                            p.do("sync", lambda e: e.dma_start(out=io["dbg_win"][0:65, :], in_=outsb[0:65, 0:512]), "DS", 16)
                            p.w("tensor", "DS", p.cnt["DS"])
                        p.w("tensor", "V", cpv)
                        pbf = ps["bfA"] if (w % 2) == 0 else ps["bfB"]
                        for tt in range(4):
                            sm = ("T", 1) if tt == 3 else (None, 0)
                            ttc = p.do("tensor", lambda e, tt=tt, pbf=pbf: e.transpose(
                                pbf[:, tt * 128:tt * 128 + 65],
                                outsb[0:65, tt * 128:(tt + 1) * 128],
                                ident[0:65, 0:65]), sm[0], sm[1])
                        p.w("vector", "T", ttc)
                        for tt in range(4):
                            rc = p.do("vector", lambda e, tt=tt, pbf=pbf: e.reciprocal(
                                rz[:, tt:tt + 1], pbf[:, tt * 128 + 64:tt * 128 + 65]), "V", 1)
                            p.w("vector", "V", rc)
                            evacV = p.do("vector", lambda e, tt=tt, w=w, pbf=pbf: e.tensor_scalar(
                                out_tok[:, (w * 4 + tt) * 64:(w * 4 + tt) * 64 + 64],
                                pbf[:, tt * 128:tt * 128 + 64],
                                rz[:, tt:tt + 1], 1.0 / 3.0, ALU.mult, ALU.mult), "V", 1)
                        win_state[w] = evacV
            tcnt = p.cnt["T"]
            att_done[sbuf_i] = tcnt
            if b == 0 and h == 0 and lvl == 0 and "dbg_qsort" in io:
                p.w("sync", "DA", tr_done[sbuf_i])
                p.do("sync", lambda e: e.dma_start(out=io["dbg_qkt"][:, 0:4096], in_=qkT), "DS", 16)
                p.do("sync", lambda e: e.dma_start(out=io["dbg_qkt"][:, 4096:8192], in_=kvT), "DS", 16)
                p.w("sync", "V", evacV)
                p.do("sync", lambda e: e.dma_start(out=io["dbg_otok2"][:, :], in_=out_tok), "DS", 16)
            p.w("sync", "V", evacV)
            p.w("sync", "DG", g_done[sbuf_i])
            ot_st[sbuf_i] = p.do("sync", lambda e, osrt=osrt: e.dma_start(
                out=osrt.rearrange("(c p) r -> p c r", p=128),
                in_=out_tok.rearrange("p (c r) -> p c r", r=64)), "DS", 16)
            def mk_gathers(sbuf_i=sbuf_i, osrt=osrt, u=u, lvl=lvl, h=h, b=b):
                nonlocal g_last, acst
                p.w("gpsimd", "DS", ot_st[sbuf_i])
                if lvl == 0:
                    p.w("gpsimd", "DS", acst)
                    p.w("gpsimd", "DG", g_last)
                for tau in range(32):
                    kw = dict(compute_op=ALU.add, oob_is_err=True) if lvl > 0 else {}
                    g_done[sbuf_i] = p.do("gpsimd", lambda e, tau=tau, u=u, lvl=lvl, h=h,
                                          kw=kw, osrt=osrt: e.indirect_dma_start(
                        out=acc[:, tau * 128 + (h % 2) * 64: tau * 128 + (h % 2) * 64 + 64],
                        out_offset=None,
                        in_=osrt[:, :],
                        in_offset=IndirectOffsetOnAxis(
                            ranks[:, (u + lvl) * 32 + tau:(u + lvl) * 32 + tau + 1], axis=0),
                        **kw), "DG", 16)
                g_last = g_done[sbuf_i]
                if h % 2 == 1 and lvl == 2:
                    p.w("sync", "DG", g_last)
                    acst = p.do("sync", lambda e, b=b, h=h: e.dma_start(
                        out=io["attn1tok"][b, h // 2, :, :].rearrange("(c p) r -> p c r", p=128),
                        in_=acc.rearrange("p (c r) -> p c r", r=128)), "DS", 16)
            pending = mk_gathers
    if pending is not None:
        pending()
        pending = None


def emit_l1out(p, io, AR1, AR2, AR3, ARF, ARS, sb, cst, ps, b):
    wo1 = AR1[:, 18432:26624]
    attf = AR3[:, 0:4096]
    xres = AR3[:, 4096:8192]
    stile = AR3[:, 8192:12288]
    sqf = ARF[:, 0:4096]
    bo1 = sb["bo1pp"]
    if b == 0:
        p.w("gpsimd", "T", p.cnt["T"])
        for c in range(8):
            p.do("gpsimd", lambda e, c=c: e.dma_start(
                out=wo1[:, c * 1024:(c + 1) * 1024],
                in_=io["l1_Wo"][c * 128:(c + 1) * 128, :]), "DG", 16)
        p.do("sync", lambda e: e.dma_start(out=bo1, in_=io["l1_bo_pp"][:, :]), "DS", 16)
    wld = p.cnt["DG"]
    bld = p.cnt["DS"]
    tcnt = p.cnt["T"]
    lnV = p.cnt["V"]
    x3st = p.cnt["DS"]
    p.w("tensor", "V", p.cnt["V"])
    for tb8 in range(8):
        p.w("scalar", "T", tcnt)
        p.w("scalar", "DS", x3st)
        for pr in range(8):
            dcnt = p.do("scalar", lambda e, pr=pr, tb8=tb8, b=b: e.dma_start(
                out=attf[:, pr * 512:(pr + 1) * 512],
                in_=io["attn1tok"][b, pr, tb8 * 512:(tb8 + 1) * 512, :],
                transpose=True), "DA", 16)
        p.w("sync", "T", tcnt)
        for c in range(8):
            rcnt = p.do("sync", lambda e, c=c, tb8=tb8, b=b: e.dma_start(
                out=xres[:, c * 512:(c + 1) * 512],
                in_=io["x2T"][c * 128:(c + 1) * 128,
                              b * S + tb8 * 512: b * S + tb8 * 512 + 512]), "DS", 16)
        for fc in range(8):
            pb = ps[fc % 2]
            p.w("tensor", "DA", dcnt)
            p.w("tensor", "DS", rcnt)
            p.w("tensor", "DG", wld)
            p.w("tensor", "V", lnV)
            for ec in range(8):
                tcnt = p.do("tensor", lambda e, pb=pb, ec=ec, fc=fc: e.matmul(
                    pb[:, :], wo1[:, ec * 1024 + fc * 128: ec * 1024 + fc * 128 + 128],
                    attf[:, ec * 512:(ec + 1) * 512], start=(ec == 0), stop=(ec == 7)),
                    "T" if ec == 7 else None, 1 if ec == 7 else 0)
            p.w("vector", "T", tcnt)
            p.w("vector", "DS", x3st)
            sv = stile[:, fc * 512:(fc + 1) * 512]
            p.do("vector", lambda e, pb=pb, sv=sv, fc=fc: e.tensor_scalar(
                sv, pb[:, :], bo1[:, fc:fc + 1], None, ALU.add))
            lnV = p.do("vector", lambda e, sv=sv, fc=fc: e.tensor_tensor(
                sv, sv, xres[:, fc * 512:(fc + 1) * 512], ALU.add), "V", 1)
        p.w("scalar", "V", lnV)
        sb["lnout"] = AR3[:, 0:4096]
        x3st = emit_ffn_ln512(p, io, sb, cst, ps, stile, sqf, b * 8 + tb8, "x3T")
        lnV = p.cnt["V"]




def emit_wquant(p, io, ARF, AR3, ARS, sb, cst, ps, src, dst, nch, msc_col):
    onesf = cst["onesfsb"]
    wt = ARF[:, 0:4096]
    q = AR3[:, 0:4096]
    wabs = sb["wabs"]
    msc = sb["msc"]
    wsc = sb["wsc"]
    N = io[src].shape()[1] if False else 4096 if nch == 8 else 1024
    vv = p.cnt["V"]
    dcnt = p.cnt["DS"]
    for c in range(nch):
        p.w("sync", "V", vv)
        dcnt = p.do("sync", lambda e, c=c: e.dma_start(
            out=wt[:, 0:N], in_=io[src][c * 128:(c + 1) * 128, :]), "DS", 16)
        p.w("vector", "DS", dcnt)
        rv = p.do("vector", lambda e: e.reduce_sum(
            wabs[:, 1:2], wt[:, 0:N], axis=AX.X, apply_absolute_value=True), "V", 1)
        p.w("vector", "V", rv)
        if c == 0:
            vv = p.do("vector", lambda e: e.tensor_copy(wabs[:, 0:1], wabs[:, 1:2]), "V", 1)
        else:
            vv = p.do("vector", lambda e: e.tensor_tensor(
                wabs[:, 0:1], wabs[:, 0:1], wabs[:, 1:2], ALU.add), "V", 1)
    p.w("tensor", "V", vv)
    tm = p.do("tensor", lambda e: e.matmul(ps[2][0:1, 0:1], onesf[:, 0:1],
                                           wabs[:, 0:1], start=True, stop=True), "T", 1)
    p.w("vector", "T", tm)
    p.do("vector", lambda e: e.tensor_scalar(
        msc[0:1, msc_col:msc_col + 1], ps[2][0:1, 0:1], 1.0 / (nch * 128 * N),
        1e-5, ALU.mult, ALU.max))
    vv = p.do("vector", lambda e: e.engine_nop(), "V", 1)
    p.w("vector", "V", vv)
    rc = p.do("vector", lambda e: e.reciprocal(
        msc[0:1, msc_col + 2:msc_col + 3], msc[0:1, msc_col:msc_col + 1]), "V", 1)
    p.w("vector", "V", rc)
    p.w("tensor", "V", rc)
    tb = p.do("tensor", lambda e: e.matmul(ps[3][:, 0:1], onesf[0:1, 0:128],
                                           msc[0:1, msc_col + 2:msc_col + 3],
                                           start=True, stop=True), "T", 1)
    p.w("vector", "T", tb)
    cb = p.do("vector", lambda e: e.tensor_copy(wsc[:, 0:1], ps[3][:, 0:1]), "V", 1)
    p.w("vector", "V", cb)
    st = p.cnt["DS"]
    for c in range(nch):
        p.w("sync", "V", vv)
        dcnt = p.do("sync", lambda e, c=c: e.dma_start(
            out=wt[:, 0:N], in_=io[src][c * 128:(c + 1) * 128, :]), "DS", 16)
        p.w("vector", "DS", dcnt)
        p.w("vector", "DS", st)
        p.do("vector", lambda e: e.tensor_scalar(
            wt[:, 0:N], wt[:, 0:N], wsc[:, 0:1], None, ALU.mult))
        p.do("vector", lambda e: e.tensor_scalar(
            wt[:, 0:N], wt[:, 0:N], MAGIC, -MAGIC, ALU.add, ALU.add))
        vv = p.do("vector", lambda e: e.tensor_scalar(
            q[:, 0:N], wt[:, 0:N], 1.0, -1.0, ALU.min, ALU.max), "V", 1)
        p.w("sync", "V", vv)
        st = p.do("sync", lambda e, c=c: e.dma_start(
            out=io[dst][c * 128:(c + 1) * 128, :], in_=q[:, 0:N]), "DS", 16)


def emit_ffn1(p, io, AR1, AR2, AR3, ARF, ARS, sb, cst, ps):
    onesb, onesf = cst["onesb"], cst["onesfsb"]
    msc = sb["msc"]
    emit_wquant(p, io, ARF, AR3, ARS, sb, cst, ps, "f1_w1T", "wq1q", 8, 0)
    emit_wquant(p, io, ARF, AR3, ARS, sb, cst, ps, "f1_w2T", "wq2q", 32, 4)
    cw = sb["convw"]; cb2 = sb["convb"]
    p.do("sync", lambda e: e.dma_start(out=cw, in_=io["f1_convw"][:, :]), "DS", 16)
    p.do("sync", lambda e: e.dma_start(out=cb2, in_=io["f1_convb"][:, :]), "DS", 16)
    cld = p.cnt["DS"]

    p.w("sync", "DA", p.cnt["DA"])
    p.w("scalar", "DA", p.cnt["DA"])
    xblk = AR3[:, 0:4096]
    xq = AR3[:, 4096:8192]
    wsl = AR1[:, 0:8192]
    m1 = ARF[:, 0:512]
    m2 = ARF[:, 512:1024]
    scb = ARF[:, 1024:1536]
    dqb = ARF[:, 1536:2048]
    hrow = ARF[0:1, 2048:2560]
    ev = AR3[:, 8192:9216]

    tcnt = p.cnt["T"]
    evacV = p.cnt["V"]
    dh = p.cnt["DS"]
    ld = p.cnt["DG"]
    lastdq = [p.cnt["V"], p.cnt["V"]]
    p.w("tensor", "V", p.cnt["V"])
    p.w("tensor", "A", p.cnt["A"])
    for tb in range(16):
        p.w("sync", "T", tcnt)
        for c in range(8):
            dcnt = p.do("sync", lambda e, c=c, tb=tb: e.dma_start(
                out=xblk[:, c * 512:(c + 1) * 512],
                in_=io["x3T"][c * 128:(c + 1) * 128, tb * 512:(tb + 1) * 512]), "DS", 16)
        p.w("vector", "DS", dcnt)
        p.w("vector", "V", evacV)
        p.do("vector", lambda e: e.scalar_tensor_tensor(
            m1[:, :], xblk[:, 0:512], -1.0, xblk[:, 0:512], ALU.mult, ALU.max))
        for c in range(1, 8):
            p.do("vector", lambda e, c=c: e.tensor_max(
                m1[:, :], m1[:, :], xblk[:, c * 512:(c + 1) * 512]))
            p.do("vector", lambda e, c=c: e.scalar_tensor_tensor(
                m1[:, :], xblk[:, c * 512:(c + 1) * 512], -1.0, m1[:, :],
                ALU.mult, ALU.max))
        vv = p.do("vector", lambda e: e.engine_nop(), "V", 1)
        nh = 64
        while nh >= 1:
            vv = p.do("vector", lambda e, nh=nh: e.tensor_copy(
                scb[0:min(2 * nh, 128), 0:512], m1[0:min(2 * nh, 128), :]), "V", 1)
            p.w("sync", "V", vv)
            sh = p.do("sync", lambda e, nh=nh: e.dma_start(
                out=m2[0:max(nh, 1), :], in_=m1[nh:2 * nh, :]), "DS", 16)
            p.w("vector", "DS", sh)
            vv = p.do("vector", lambda e, nh=nh: e.tensor_max(
                m1[0:nh, :], m1[0:nh, :], m2[0:nh, :]), "V", 1)
            p.w("vector", "V", vv)
            nh //= 2
        p.do("vector", lambda e: e.tensor_scalar(
            m1[0:1, :], m1[0:1, :], 1e-5, None, ALU.max))
        vv = p.do("vector", lambda e: e.engine_nop(), "V", 1)
        p.w("vector", "V", vv)
        rc = p.do("vector", lambda e: e.reciprocal(m2[0:1, :], m1[0:1, :]), "V", 1)
        p.w("vector", "V", rc)
        p.do("vector", lambda e: e.tensor_scalar(m2[0:1, :], m2[0:1, :], 127.0, None, ALU.mult))
        p.do("vector", lambda e: e.tensor_scalar(
            hrow, m1[0:1, :], 1.0 / 127.0, msc[0:1, 0:1], ALU.mult, ALU.mult))
        vv = p.do("vector", lambda e: e.engine_nop(), "V", 1)
        if tb == 0 and "dbg_q1" in io:
            p.w("sync", "V", vv)
            p.do("sync", lambda e: e.dma_start(out=io["dbg_q1"][0:1, 0:512], in_=m1[0:1, :]), "DS", 16)
            p.do("sync", lambda e: e.dma_start(out=io["dbg_q1"][1:2, 0:512], in_=m2[0:1, :]), "DS", 16)
            p.do("sync", lambda e: e.dma_start(out=io["dbg_q1"][2:3, 0:512], in_=hrow), "DS", 16)
            p.w("vector", "DS", p.cnt["DS"])
        p.w("tensor", "V", vv)
        p.do("tensor", lambda e: e.matmul(ps[2][:, 0:512], onesf[0:1, 0:128],
                                          m2[0:1, :], start=True, stop=True))
        tbx = p.do("tensor", lambda e: e.matmul(ps[3][:, 0:512], onesf[0:1, 0:128],
                                                hrow, start=True, stop=True), "T", 1)
        p.w("vector", "T", tbx)
        p.do("vector", lambda e: e.tensor_copy(scb[:, :], ps[2][:, 0:512]))
        vv = p.do("vector", lambda e: e.tensor_copy(dqb[:, :], ps[3][:, 0:512]), "V", 1)
        p.w("vector", "V", vv)
        for c in range(8):
            p.do("vector", lambda e, c=c: e.tensor_tensor(
                xq[:, c * 512:(c + 1) * 512].bitcast(F32)[:, 0:256] if False else
                m1[:, 0:512], xblk[:, c * 512:(c + 1) * 512], scb[:, :], ALU.mult))
            vv = p.do("vector", lambda e, c=c: e.tensor_scalar(
                xq[:, c * 512:(c + 1) * 512], m1[:, 0:512], MAGIC, -MAGIC,
                ALU.add, ALU.add), "V", 1)
        if tb == 0 and "dbg_q1" in io:
            p.w("sync", "V", vv)
            p.do("sync", lambda e: e.dma_start(out=io["dbg_xq"][:, :], in_=xq), "DS", 16)
            p.w("gpsimd", "DS", p.cnt["DS"])
        for js in range(4):
            p.w("gpsimd", "T", tcnt)
            p.w("gpsimd", "V", vv)
            for c in range(8):
                ld = p.do("gpsimd", lambda e, c=c, js=js: e.dma_start(
                    out=wsl[:, c * 1024:(c + 1) * 1024],
                    in_=io["wq1q"][c * 128:(c + 1) * 128, js * 1024:(js + 1) * 1024]), "DG", 16)
            for jc in range(8):
                pb = ps[jc % 2]
                p.w("tensor", "DG", ld)
                p.w("tensor", "V", vv)
                p.w("tensor", "V", lastdq[jc % 2])
                for c in range(8):
                    tcnt = p.do("tensor", lambda e, pb=pb, c=c, jc=jc: e.matmul(
                        pb[:, :], wsl[:, c * 1024 + jc * 128: c * 1024 + jc * 128 + 128],
                        xq[:, c * 512:(c + 1) * 512], start=(c == 0), stop=(c == 7)),
                        "T" if c == 7 else None, 1 if c == 7 else 0)
                p.w("vector", "T", tcnt)
                p.w("vector", "DS", dh)
                p.do("vector", lambda e, pb=pb: e.tensor_tensor(
                    ev[:, 0:512], pb[:, :], dqb[:, :], ALU.mult))
                evacV = p.do("vector", lambda e: e.engine_nop(), "V", 1)
                lastdq[jc % 2] = evacV
                p.w("sync", "V", evacV)
                dh = p.do("sync", lambda e, js=js, jc=jc, tb=tb: e.dma_start(
                    out=io["hA"][js * 1024 + jc * 128: js * 1024 + jc * 128 + 128,
                                 tb * 512:(tb + 1) * 512], in_=ev[:, 0:512]), "DS", 16)

    hc = AR2[:, 0:16384]
    xq2 = AR2[:, 16384:32768]
    whalf = AR1[:, 8192:24576]
    halo = AR3[:, 9216:9216 + 516]
    x3res = AR3[:, 4096:8192]
    stile = AR3[:, 0:4096]
    lnf = ARF[:, 2560:3072]
    sqf = ARF[:, 3072:3584]
    outb = ARF[:, 3584:4096]
    r_mu, r_m2, r_rs = sb["r_mu"][0:1, 0:512], sb["r_m2"][0:1, 0:512], sb["r_rs"][0:1, 0:512]
    tcnt = p.cnt["T"]
    evacV = p.cnt["V"]
    ost = p.cnt["DS"]
    prevOvv = p.cnt["V"]
    lnV = p.cnt["V"]
    for tb in range(16):
        seq_edge_l = (tb % 8 == 0)
        seq_edge_r = (tb % 8 == 7)
        p.w("scalar", "DS", cld)
        for jc in range(32):
            p.w("sync", "V", evacV if jc == 0 else p.cnt["V"] - 0)
            lo = tb * 512 - (0 if seq_edge_l else 1)
            hi = tb * 512 + 512 + (0 if seq_edge_r else 1)
            off = 1 if seq_edge_l else 0
            hld = p.do("sync", lambda e, jc=jc, lo=lo, hi=hi, off=off: e.dma_start(
                out=halo[:, off:off + (hi - lo)],
                in_=io["hA"][jc * 128:(jc + 1) * 128, lo:hi]), "DS", 16)
            if seq_edge_l:
                p.do("vector", lambda e: e.memset(halo[:, 0:1], 0.0))
            if seq_edge_r:
                p.do("vector", lambda e: e.memset(halo[:, 513:514], 0.0))
            p.w("vector", "DS", hld)
            p.do("vector", lambda e, jc=jc: e.tensor_scalar(
                m1[:, :], halo[:, 1:513], cw[:, jc * 3 + 1:jc * 3 + 2], None, ALU.mult))
            p.do("vector", lambda e, jc=jc: e.scalar_tensor_tensor(
                m1[:, :], halo[:, 0:512], cw[:, jc * 3 + 0:jc * 3 + 1], m1[:, :],
                ALU.mult, ALU.add))
            cvv = p.do("vector", lambda e, jc=jc: e.scalar_tensor_tensor(
                m1[:, :], halo[:, 2:514], cw[:, jc * 3 + 2:jc * 3 + 3], m1[:, :],
                ALU.mult, ALU.add), "V", 1)
            p.w("scalar", "V", cvv)
            aa = p.do("scalar", lambda e, jc=jc: e.activation(
                hc[:, jc * 512:(jc + 1) * 512], m1[:, :], AF.Silu,
                bias=cb2[:, jc:jc + 1]), "A", 1)
        p.w("vector", "A", aa)
        p.do("vector", lambda e: e.scalar_tensor_tensor(
            m1[:, :], hc[:, 0:512], -1.0, hc[:, 0:512], ALU.mult, ALU.max))
        for c in range(1, 32):
            p.do("vector", lambda e, c=c: e.tensor_max(
                m1[:, :], m1[:, :], hc[:, c * 512:(c + 1) * 512]))
            p.do("vector", lambda e, c=c: e.scalar_tensor_tensor(
                m1[:, :], hc[:, c * 512:(c + 1) * 512], -1.0, m1[:, :],
                ALU.mult, ALU.max))
        vv = p.do("vector", lambda e: e.engine_nop(), "V", 1)
        nh = 64
        while nh >= 1:
            vv = p.do("vector", lambda e, nh=nh: e.tensor_copy(
                scb[0:min(2 * nh, 128), 0:512], m1[0:min(2 * nh, 128), :]), "V", 1)
            p.w("sync", "V", vv)
            sh = p.do("sync", lambda e, nh=nh: e.dma_start(
                out=m2[0:max(nh, 1), :], in_=m1[nh:2 * nh, :]), "DS", 16)
            p.w("vector", "DS", sh)
            vv = p.do("vector", lambda e, nh=nh: e.tensor_max(
                m1[0:nh, :], m1[0:nh, :], m2[0:nh, :]), "V", 1)
            p.w("vector", "V", vv)
            nh //= 2
        p.do("vector", lambda e: e.tensor_scalar(
            m1[0:1, :], m1[0:1, :], 1e-5, None, ALU.max))
        vv = p.do("vector", lambda e: e.engine_nop(), "V", 1)
        p.w("vector", "V", vv)
        rc = p.do("vector", lambda e: e.reciprocal(m2[0:1, :], m1[0:1, :]), "V", 1)
        p.w("vector", "V", rc)
        p.do("vector", lambda e: e.tensor_scalar(m2[0:1, :], m2[0:1, :], 127.0, None, ALU.mult))
        p.do("vector", lambda e: e.tensor_scalar(
            hrow, m1[0:1, :], 1.0 / 127.0, msc[0:1, 4:5], ALU.mult, ALU.mult))
        vv = p.do("vector", lambda e: e.engine_nop(), "V", 1)
        p.w("tensor", "V", vv)
        p.do("tensor", lambda e: e.matmul(ps[2][:, 0:512], onesf[0:1, 0:128],
                                          m2[0:1, :], start=True, stop=True))
        tbx = p.do("tensor", lambda e: e.matmul(ps[3][:, 0:512], onesf[0:1, 0:128],
                                                hrow, start=True, stop=True), "T", 1)
        p.w("vector", "T", tbx)
        p.do("vector", lambda e: e.tensor_copy(scb[:, :], ps[2][:, 0:512]))
        vv = p.do("vector", lambda e: e.tensor_copy(dqb[:, :], ps[3][:, 0:512]), "V", 1)
        p.w("vector", "V", vv)
        for c in range(32):
            p.do("vector", lambda e, c=c: e.tensor_tensor(
                m1[:, 0:512], hc[:, c * 512:(c + 1) * 512], scb[:, :], ALU.mult))
            vv = p.do("vector", lambda e, c=c: e.tensor_scalar(
                xq2[:, c * 512:(c + 1) * 512], m1[:, 0:512], MAGIC, -MAGIC,
                ALU.add, ALU.add), "V", 1)
        p.w("sync", "V", vv)
        for c in range(8):
            rld = p.do("sync", lambda e, c=c, tb=tb: e.dma_start(
                out=x3res[:, c * 512:(c + 1) * 512],
                in_=io["x3T"][c * 128:(c + 1) * 128, tb * 512:(tb + 1) * 512]), "DS", 16)
        lnV = max(lnV if tb > 0 else p.cnt["V"], prevOvv)
        for half in range(2):
            p.w("gpsimd", "T", tcnt)
            for c in range(32):
                ld = p.do("gpsimd", lambda e, c=c, half=half: e.dma_start(
                    out=whalf[:, c * 512:(c + 1) * 512],
                    in_=io["wq2q"][c * 128:(c + 1) * 128,
                                   half * 512:(half + 1) * 512]), "DG", 16)
            for el in range(4):
                ec = half * 4 + el
                pb = ps[el % 2]
                p.w("tensor", "DG", ld)
                p.w("tensor", "V", vv)
                p.w("tensor", "DS", rld)
                p.w("tensor", "V", lnV)
                for c in range(32):
                    tcnt = p.do("tensor", lambda e, pb=pb, c=c, el=el: e.matmul(
                        pb[:, :], whalf[:, c * 512 + el * 128: c * 512 + el * 128 + 128],
                        xq2[:, c * 512:(c + 1) * 512], start=(c == 0), stop=(c == 31)),
                        "T" if c == 31 else None, 1 if c == 31 else 0)
                p.w("vector", "T", tcnt)
                sv = stile[:, ec * 512:(ec + 1) * 512]
                p.do("vector", lambda e, pb=pb, sv=sv: e.tensor_tensor(
                    m1[:, :], pb[:, :], dqb[:, :], ALU.mult))
                lnV = p.do("vector", lambda e, sv=sv, ec=ec: e.tensor_tensor(
                    sv, m1[:, :], x3res[:, ec * 512:(ec + 1) * 512], ALU.add), "V", 1)
        tst = p.cnt["T"]
        for fc in range(8):
            sv = stile[:, fc * 512:(fc + 1) * 512]
            p.w("scalar", "V", lnV)
            p.w("scalar", "T", tst)
            aa = p.do("scalar", lambda e, fc=fc, sv=sv: e.activation(
                sqf[:, 0:512], sv, AF.Square), "A", 1)
            p.w("tensor", "A", aa)
            p.w("tensor", "V", lnV)
            p.do("tensor", lambda e, sv=sv, fc=fc: e.matmul(
                ps[2][0:1, 0:512], onesb[:, 0:1], sv, start=(fc == 0), stop=(fc == 7)))
            tst = p.do("tensor", lambda e, fc=fc: e.matmul(
                ps[3][0:1, 0:512], onesf[:, 0:1], sqf[:, 0:512],
                start=(fc == 0), stop=(fc == 7)), "T", 1)
        p.w("vector", "T", tst)
        p.do("vector", lambda e: e.tensor_scalar(r_mu, ps[2][0:1, 0:512], 1.0 / D, None, ALU.mult))
        p.do("vector", lambda e: e.tensor_scalar(r_m2, ps[3][0:1, 0:512], 1.0 / D, None, ALU.mult))
        p.do("vector", lambda e: e.tensor_tensor(r_rs, r_mu, r_mu, ALU.mult))
        p.do("vector", lambda e: e.tensor_tensor(r_m2, r_m2, r_rs, ALU.subtract))
        vv = p.do("vector", lambda e: e.tensor_scalar(r_m2, r_m2, 1e-5, None, ALU.add), "V", 1)
        p.w("scalar", "V", vv)
        p.do("scalar", lambda e: e.activation(r_rs, r_m2, AF.Ln))
        aa = p.do("scalar", lambda e: e.activation(r_rs, r_rs, AF.Exp, scale=-0.5), "A", 1)
        p.w("tensor", "A", aa)
        p.do("tensor", lambda e: e.matmul(ps[2][:, 0:512], onesf[0:1, 0:128], r_mu, start=True, stop=True))
        tb2 = p.do("tensor", lambda e: e.matmul(ps[3][:, 0:512], onesf[0:1, 0:128], r_rs, start=True, stop=True), "T", 1)
        p.w("vector", "T", tb2)
        for fc in range(8):
            sv = stile[:, fc * 512:(fc + 1) * 512]
            p.w("vector", "DS", ost)
            p.do("vector", lambda e, sv=sv: e.tensor_tensor(lnf[:, :], sv, ps[2][:, 0:512], ALU.subtract))
            lvv = p.do("vector", lambda e: e.tensor_tensor(lnf[:, :], lnf[:, :], ps[3][:, 0:512], ALU.mult), "V", 1)
            p.w("tensor", "V", lvv)
            trcp = [prevOvv, prevOvv]
            for tt in range(4):
                p.w("tensor", "V", trcp[tt % 2])
                tr = p.do("tensor", lambda e, tt=tt: e.transpose(
                    ps[0 if tt % 2 == 0 else 1][:, 0:128],
                    lnf[:, tt * 128:(tt + 1) * 128], cst["identf"][:, :]), "T", 1)
                p.w("vector", "T", tr)
                ovv = p.do("vector", lambda e, tt=tt: e.tensor_copy(
                    outb[:, tt * 128:(tt + 1) * 128], ps[0 if tt % 2 == 0 else 1][:, 0:128]), "V", 1)
                trcp[tt % 2] = ovv
            prevOvv = ovv
            p.w("sync", "V", ovv)
            for tt in range(4):
                ost = p.do("sync", lambda e, fc=fc, tb=tb, tt=tt: e.dma_start(
                    out=io["out"][tb * 512 + tt * 128: tb * 512 + tt * 128 + 128,
                                  fc * 128:(fc + 1) * 128],
                    in_=outb[:, tt * 128:(tt + 1) * 128]), "DS", 16)


def build_program(taps=(), upto="all"):
    nc = bass.Bass("TRN2", target_bir_lowering=False)
    io = {}
    def inp(name, shape, dt):
        io[name] = nc.dram_tensor(name, shape, dt, kind="ExternalInput").ap()
    def internal(name, shape, dt):
        kind = "ExternalOutput" if name in taps else "Internal"
        io[name] = nc.dram_tensor(name, shape, dt, kind=kind).ap()

    inp("x", [T, D], F32)
    for nm in ("Wq", "Wk", "Wv", "Wo"):
        inp("l0_" + nm, [D, D], F32)
        inp("l1_" + nm, [D, D], F32)
    for nm in ("bq", "bk", "bv", "bo"):
        inp(f"l0_{nm}_row", [1, D], F32); inp(f"l0_{nm}_pp", [128, 8], F32)
        inp(f"l1_{nm}_row", [1, D], F32); inp(f"l1_{nm}_pp", [128, 8], F32)
    inp("f0_w1T", [D, DH], F32); inp("f0_w2T", [D, DH], F32)
    inp("f0_h1T", [DH, DH], F32); inp("f0_h2T", [DH, DH], F32)
    inp("f0_w3T", [DH, D], F32)
    for nm, nch in (("w1_s", 32), ("w2_s", 32), ("h1_s", 32), ("h2_s", 32), ("w3_s", 8)):
        inp("f0_" + nm, [128, nch], F32)
    inp("l1_probes", [H, HD, 192], F32)
    inp("f1_w1T", [D, DH], F32); inp("f1_w2T", [DH, D], F32)
    inp("f1_convw", [128, 96], F32); inp("f1_convb", [128, 32], F32)
    inp("ident", [128, 128], BF16); inp("onesb", [128, 512], BF16)
    inp("onesf", [128, 144], F32); inp("zz", [128, 544], BF16)
    inp("lup", [128, 128], BF16); inp("gmats", [64, 192], F32)
    inp("mask1", [128, 192], BF16); inp("mask2", [128, 144], BF16)
    inp("identf", [128, 128], F32)

    if "dbg_at0" in taps:
        internal("dbg_at0", [D, S], BF16)
    if "dbg_otok" in taps:
        internal("dbg_otok", [128, D], BF16)
    if "dbg_qkv" in taps:
        internal("dbg_qkv", [128, 3 * D], BF16)
    if "dbg_sc" in taps:
        internal("dbg_sc", [128, 512], F32)
    if "dbg_acc" in taps:
        internal("dbg_acc", [128, 1040], F32)
    if "dbg_ranks" in taps:
        internal("dbg_ranks", [128, 3072], U32)
    if "dbg_oh" in taps:
        internal("dbg_oh", [128, 6144], BF16)
        internal("dbg_proj", [128, 6144], F32)
    if "dbg_cnt" in taps:
        internal("dbg_cnt", [128, 98], F32)
    if "dbg_qrt" in taps:
        internal("dbg_qrt", [128, 16384], F32)
    if "dbg_qsort" in taps:
        internal("dbg_qsort", [S, 256], BF16)
    if "dbg_win" in taps:
        internal("dbg_win", [65, 512], BF16)
        internal("dbg_exp", [256, 512], BF16)
    if "dbg_acc2" in taps:
        internal("dbg_acc2", [128, 4096], BF16)
    if "dbg_acc3" in taps:
        internal("dbg_acc3", [128, 4096], BF16)
    if "dbg_q1" in taps:
        internal("dbg_q1", [4, 512], F32)
        internal("dbg_xq", [128, 4096], BF16)
        internal("dbg_qkt", [128, 8192], BF16)
        internal("dbg_otok2", [128, 2048], BF16)
    internal("xbf_tok", [T, D], BF16)
    internal("x1T", [D, T], BF16)
    internal("x2T", [D, T], BF16)
    internal("x3T", [D, T], BF16)
    internal("hA", [DH, T], BF16)
    internal("qkvs", [B, H, S, 256], BF16)
    internal("qkv_sorted0", [S, 256], BF16)
    internal("qkv_sorted1", [S, 256], BF16)
    internal("outsorted0", [S, 64], BF16)
    internal("outsorted1", [S, 64], BF16)
    internal("attn1tok", [B, 8, S, 128], BF16)
    internal("wq1q", [D, DH], BF16)
    internal("x2Tf", [D, T], F32)
    internal("wq2q", [DH, D], BF16)
    internal("hB", [DH, T], BF16)
    io["out"] = nc.dram_tensor("out", [T, D], F32, kind="ExternalOutput").ap()

    with ExitStack() as st:
        AR1 = st.enter_context(nc.sbuf_tensor("AR1", [128, 28672], BF16))
        AR2 = st.enter_context(nc.sbuf_tensor("AR2", [128, 32768], BF16))
        AR3 = st.enter_context(nc.sbuf_tensor("AR3", [128, 12288], BF16))
        ARF = st.enter_context(nc.sbuf_tensor("ARF", [128, 6656], F32))
        ARS = st.enter_context(nc.sbuf_tensor("ARS", [128, 1024], F32))
        PRB = st.enter_context(nc.sbuf_tensor("PRB", [128, 1536], F32))
        RNK = st.enter_context(nc.sbuf_tensor("RNK", [128, 3072], U32))
        cst = {}
        for nm, shape, dt in (("ident", [128, 128], BF16), ("onesb", [128, 512], BF16),
                              ("onesfsb", [128, 144], F32), ("zz", [128, 544], BF16),
                              ("lup", [128, 128], BF16), ("gmats", [64, 192], F32),
                              ("mask1", [128, 192], BF16), ("mask2", [128, 144], BF16),
                              ("identf", [128, 128], F32)):
            cst[nm] = st.enter_context(nc.sbuf_tensor("c_" + nm, shape, dt))
        ps = {}
        for i in range(4):
            ps[i] = st.enter_context(nc.psum_tensor(f"ps{i}", [128, 512], F32))
        ps["bfA"] = st.enter_context(nc.psum_tensor("psbfA", [128, 512], BF16))
        ps["bfB"] = st.enter_context(nc.psum_tensor("psbfB", [128, 512], BF16))
        blk = st.enter_context(nc.Block())
        p = P(nc, st)

        sb = dict(
            wq=AR1[:, 0:8192], wk=AR1[:, 8192:16384], wv=AR1[:, 16384:24576],
            wo=AR1[:, 0:16384], brow=AR1[:, 24576:27648],
            at0=AR2[:, 0:32768],
            xsb=AR3[:, 0:4096], qkv=AR3[:, 4096:7168], otok=AR3[:, 7168:8192],
            stile=AR3[:, 8192:10240], x1t=AR3[:, 10240:12288],
            prod=ARF[:, 0:1024], sc16=ARF[:, 1024:1280], att=ARF[:, 1280:1536],
            acc=ARF[:, 1536:2560], zr=ARF[:, 2560:2576], zs=ARF[:, 6160:6176],
            sqf=ARF[:, 2576:4624],
            r_mu=ARF[0:1, 4624:5136], r_m2=ARF[0:1, 5136:5648],
            r_rs=ARF[0:1, 5648:6160],
            bo_pp=ARS[:, 0:8], fscal=ARS[:, 8:144], bqpp1=ARS[:, 144:152],
            bo1pp=ARS[:, 168:176], wabs=ARS[:, 176:178], msc=ARS[:, 178:186],
            lnoutf=ARS[:, 320:832],
            wsc=ARS[:, 186:187], convw=ARS[:, 192:288], convb=ARS[:, 288:320],
            probes=PRB, ranks=RNK,
            wq1=AR1[:, 0:8192], wk1=AR1[:, 8192:16384], wv1=AR1[:, 16384:24576],
            brow1=AR1[:, 24576:27648],
        )

        src_map = {"ident": "ident", "onesb": "onesb", "onesfsb": "onesf",
                   "zz": "zz", "lup": "lup", "gmats": "gmats",
                   "mask1": "mask1", "mask2": "mask2", "identf": "identf"}
        for nm, t in cst.items():
            p.do("sync", lambda e, t=t, s=io[src_map[nm]]:
                 e.dma_start(out=t[:, :], in_=s[:, :]), "DS", 16)
        cload = p.snap()
        for e in ("tensor", "vector", "scalar", "gpsimd"):
            p.join(e, cload, ["DS"])

        emit_ingest(p, io)
        ing = p.snap()
        p.join("scalar", ing, ["DG"])
        p.join("tensor", ing, ["DG"])

        emit_l0(p, io, sb, cst, ps, 0)
        emit_l0(p, io, sb, cst, ps, 1)
        if upto in ("ffn0", "l1", "all"):
            emit_ffn0(p, io, AR1, AR2, AR3, ARF, ARS, sb, cst, ps)
        if upto in ("l1q", "l1a", "l1o", "all"):
            emit_l1qkv(p, io, AR1, AR2, AR3, ARF, ARS, sb, cst, ps, 0)
            emit_l1qkv(p, io, AR1, AR2, AR3, ARF, ARS, sb, cst, ps, 1)
            if "dbg_ranks" in io:
                emit_dump_ranks(p, io, sb)
        if upto in ("l1a", "l1o", "all"):
            emit_l1attn(p, io, AR1, AR2, AR3, ARF, ARS, sb, cst, ps, 0)
            if upto in ("l1o", "all"):
                emit_l1out(p, io, AR1, AR2, AR3, ARF, ARS, sb, cst, ps, 0)
            emit_l1attn(p, io, AR1, AR2, AR3, ARF, ARS, sb, cst, ps, 1)
            if upto in ("l1o", "all"):
                emit_l1out(p, io, AR1, AR2, AR3, ARF, ARS, sb, cst, ps, 1)
        if upto == "all":
            emit_ffn1(p, io, AR1, AR2, AR3, ARF, ARS, sb, cst, ps)
        else:
            fin = p.snap()
            p.join("gpsimd", fin, ["DS", "V", "T"])
            for i in range(8):
                p.do("gpsimd", lambda e, i=i: e.dma_start(
                    out=io["out"].rearrange("(a t) d -> a t d", a=8)[i, :, :],
                    in_=io["x1T"][:, i * 1024:(i + 1) * 1024]), "DG", 16)

        p.w("sync", "DS", p.cnt["DS"])
        p.w("gpsimd", "DG", p.cnt["DG"])
        p.finalize(blk)
    return nc


def kernel(**inputs):
    maps = shard_inputs(inputs)
    nc = build_program()
    res = run_bass_kernel_spmd(nc, maps, core_ids=list(range(NCORES)))
    out = np.concatenate([np.asarray(r["out"], np.float32).reshape(B, S, D)
                          for r in res.results], axis=0)
    return out


# version 10
# speedup vs baseline: 1.2631x; 1.0198x over previous
import numpy as np
import ml_dtypes
from contextlib import ExitStack
import concourse.bass as bass
import concourse.mybir as mybir
from concourse.bass import IndirectOffsetOnAxis
from concourse.bass_utils import run_bass_kernel_spmd

F32 = mybir.dt.float32
BF16 = mybir.dt.bfloat16
U32 = mybir.dt.uint32
AF = mybir.ActivationFunctionType
ALU = mybir.AluOpType
AX = mybir.AxisListType

NCORES = 8
B = 2
S = 4096
T = B * S
D = 1024
H = 16
HD = 64
DH = 4096
MAGIC = 12582912.0

ENGINES = ("sync", "gpsimd", "tensor", "vector", "scalar")


def host_consts():
    bf = ml_dtypes.bfloat16
    c = {}
    c["ident"] = np.eye(128).astype(bf)
    c["onesb"] = np.ones((128, 512), bf)
    c["onesf"] = np.ones((128, 144), np.float32)
    c["zz"] = np.zeros((128, 544), bf)
    lup = (np.arange(128)[:, None] < np.arange(128)[None, :]).astype(np.float32)
    c["lup"] = lup.astype(bf)
    gs = []
    for nb in (16, 64, 256):
        pp = np.arange(64)
        table = (pp * 9973) % nb
        csize = 64 // nb if nb < 64 else 1
        g = (table[:, None] < table[None, :]).astype(np.float32) / csize
        gs.append(g)
    c["gmats"] = np.concatenate(gs, axis=1).astype(np.float32)
    masks = []
    for cs in (64, 16):
        kk = np.arange(128)[:, None] // cs
        qq = np.arange(128 + cs)[None, :] // cs
        m = ((qq - kk == 0) | (qq - kk == 1)).astype(np.float32)
        masks.append(m.astype(bf))
    c["mask1"] = masks[0]
    c["mask2"] = masks[1]
    c["identf"] = np.eye(128).astype(np.float32)
    return c


def shard_inputs(inputs):
    A = np.ascontiguousarray
    g = inputs
    base = {}
    for nm in ("Wq", "Wk", "Wv", "Wo"):
        base["l0_" + nm] = A(np.asarray(g[f"L0_attn_{nm}"], np.float32).T)
        base["l1_" + nm] = A(np.asarray(g[f"L1_attn_{nm}"], np.float32))
    for nm in ("bq", "bk", "bv", "bo"):
        v0 = np.asarray(g[f"L0_attn_{nm}"], np.float32)
        v1 = np.asarray(g[f"L1_attn_{nm}"], np.float32)
        base["l0_" + nm + "_row"] = A(v0.reshape(1, D))
        base["l0_" + nm + "_pp"] = A(v0.reshape(8, 128).T)
        base["l1_" + nm + "_row"] = A(v1.reshape(1, D))
        base["l1_" + nm + "_pp"] = A(v1.reshape(8, 128).T)
    base["f0_w1T"] = A(np.asarray(g["L0_ffn_w1_w"], np.float32).T)
    base["f0_w2T"] = A(np.asarray(g["L0_ffn_w2_w"], np.float32).T)
    base["f0_h1T"] = A(np.asarray(g["L0_ffn_h1_w"], np.float32).T)
    base["f0_h2T"] = A(np.asarray(g["L0_ffn_h2_w"], np.float32).T)
    base["f0_w3T"] = A(np.asarray(g["L0_ffn_w3_w"], np.float32).T)
    for nm, nch in (("w1_s", 32), ("w2_s", 32), ("h1_s", 32), ("h2_s", 32), ("w3_s", 8)):
        base["f0_" + nm] = A(np.asarray(g[f"L0_ffn_{nm}"], np.float32).reshape(nch, 128).T)
    pr = np.asarray(g["L1_attn_probes"], np.float32)
    base["l1_probes"] = A(pr.transpose(1, 2, 0, 3).reshape(H, HD, 192))
    base["f1_w1T"] = A(np.asarray(g["L1_ffn_w1"], np.float32).T)
    base["f1_w2T"] = A(np.asarray(g["L1_ffn_w2"], np.float32).T)
    cw = np.asarray(g["L1_ffn_conv_w"], np.float32)[:, 0, :]
    base["f1_convw"] = A(cw.reshape(32, 128, 3).transpose(1, 0, 2).reshape(128, 96))
    base["f1_convb"] = A(np.asarray(g["L1_ffn_conv_b"], np.float32).reshape(32, 128).T)
    base.update(host_consts())

    x = np.asarray(g["L0_attn_x"], np.float32)
    maps = []
    for c in range(NCORES):
        m = dict(base)
        m["x"] = A(x[2 * c:2 * c + 2].reshape(T, D))
        maps.append(m)
    return maps


class P:
    def __init__(self, nc, st):
        self.nc = nc
        self.ops = {e: [] for e in ENGINES}
        self.sems = {}
        self.cnt = {}
        for s in ("DS", "DG", "DA", "T", "V", "A", "G"):
            self.sems[s] = st.enter_context(nc.semaphore("sem_" + s))
            self.cnt[s] = 0

    def w(self, eng, sem, val):
        if val is None or val <= 0:
            return
        self.ops[eng].append(lambda e, s=self.sems[sem], v=val: e.wait_ge(s, v))

    def do(self, eng, fn, sem=None, inc=0):
        if sem is None:
            self.ops[eng].append(fn)
            return None
        self.cnt[sem] += inc
        def f(e, fn=fn, s=self.sems[sem], i=inc):
            fn(e).then_inc(s, i)
        self.ops[eng].append(f)
        return self.cnt[sem]

    def snap(self):
        return dict(self.cnt)

    def join(self, eng, snap, sems=None):
        for s in (sems if sems is not None else snap):
            self.w(eng, s, snap[s])

    def finalize(self, blk):
        for name in ENGINES:
            ops = self.ops[name]
            deco = getattr(blk, name)
            def body(e, ops=ops):
                for f in ops:
                    f(e)
            deco(body)


def emit_ingest(p, io):
    for i in range(16):
        p.do("gpsimd", lambda e, i=i: e.dma_start(
            out=io["xbf_tok"][i * 512:(i + 1) * 512, :],
            in_=io["x"][i * 512:(i + 1) * 512, :]), "DG", 16)


def emit_l0(p, io, sb, cst, ps, b):
    wq, wk, wv, wo = sb["wq"], sb["wk"], sb["wv"], sb["wo"]
    brow, at0, xsb, qkv = sb["brow"], sb["at0"], sb["xsb"], sb["qkv"]
    otok, stile, x1t = sb["otok"], sb["stile"], sb["x1t"]
    prod, sc16, att, acc, zr = sb["prod"], sb["sc16"], sb["att"], sb["acc"], sb["zr"]
    zs = sb["zs"]
    sqf = sb["sqf"]
    r_mu, r_m2, r_rs = sb["r_mu"][0:1, 0:256], sb["r_m2"][0:1, 0:256], sb["r_rs"][0:1, 0:256]
    bo_pp = sb["bo_pp"]
    onesb, onesf, ident = cst["onesb"], cst["onesfsb"], cst["ident"]

    p.w("gpsimd", "T", p.cnt["T"])
    for nm, dstv in (("l0_Wq", wq), ("l0_Wk", wk), ("l0_Wv", wv)):
        for c in range(8):
            p.do("gpsimd", lambda e, d=dstv[:, c * 1024:(c + 1) * 1024],
                 s=io[nm][c * 128:(c + 1) * 128, :]: e.dma_start(out=d, in_=s),
                 "DG", 16)
    if b == 0:
        for i, nm in enumerate(("l0_bq_row", "l0_bk_row", "l0_bv_row")):
            p.do("gpsimd", lambda e, d=brow[0:1, i * 1024:(i + 1) * 1024],
                 s=io[nm][0:1, :]: e.dma_start(out=d, in_=s), "DG", 16)
        p.do("sync", lambda e: e.dma_start(out=bo_pp, in_=io["l0_bo_pp"][:, :]), "DS", 16)
    wload = p.cnt["DG"]
    bld = p.cnt["DS"]

    evacA = p.cnt["A"]
    evacV = p.cnt["V"]
    tcnt = p.cnt["T"]
    dcnt = p.cnt["DA"]
    x1done = p.cnt["V"]
    wmap = (wq, wk, wv)
    p.w("tensor", "V", p.cnt["V"])
    p.w("vector", "T", p.cnt["T"])
    p.w("scalar", "T", p.cnt["T"])

    for tau in range(32):
        blkc = tau // 4
        if tau % 4 == 0:
            p.w("scalar", "T", tcnt)
            for c in range(8):
                rows = io["xbf_tok"][b * S + blkc * 512: b * S + blkc * 512 + 512,
                                     c * 128:(c + 1) * 128]
                dcnt = p.do("scalar", lambda e, rows=rows, c=c: e.dma_start(
                    out=xsb[:, c * 512:(c + 1) * 512], in_=rows, transpose=True),
                    "DA", 16)
        p.w("tensor", "DA", dcnt)
        p.w("tensor", "DG", wload)
        waves = [[(0, 0), (0, 1), (1, 0), (1, 1)], [(2, 0), (2, 1)]]
        for wi, wave in enumerate(waves):
            p.w("tensor", "A", evacA)
            for j, (i, fh) in enumerate(wave):
                pb = ps[j]
                for c in range(8):
                    lhsT = xsb[:, c * 512 + (tau % 4) * 128: c * 512 + (tau % 4) * 128 + 128]
                    rhs = wmap[i][:, c * 1024 + fh * 512: c * 1024 + fh * 512 + 512]
                    p.do("tensor", lambda e, pb=pb, lhsT=lhsT, rhs=rhs, c=c:
                         e.matmul(pb[:, :], lhsT, rhs, start=(c == 0), stop=False))
                tcnt = p.do("tensor", lambda e, pb=pb, i=i, fh=fh: e.matmul(
                    pb[:, :], onesb[0:1, 0:128],
                    brow[0:1, i * 1024 + fh * 512: i * 1024 + fh * 512 + 512],
                    start=False, stop=True), "T", 1)
            p.w("scalar", "T", tcnt)
            if wi == 0:
                p.w("scalar", "V", evacV)
            for j, (i, fh) in enumerate(wave):
                evacA = p.do("scalar", lambda e, i=i, fh=fh, j=j: e.activation(
                    qkv[:, i * 1024 + fh * 512: i * 1024 + fh * 512 + 512],
                    ps[j][:, :], AF.Identity), "A", 1)
        if b == 0 and tau == 0 and "dbg_qkv" in io:
            p.w("sync", "A", evacA)
            p.do("sync", lambda e: e.dma_start(out=io["dbg_qkv"][:, :], in_=qkv), "DS", 16)
            p.w("vector", "DS", p.cnt["DS"])
        p.w("vector", "A", evacA)
        q_t, k_t, v_t = qkv[:, 0:1024], qkv[:, 1024:2048], qkv[:, 2048:3072]
        for hq in range(H):
            qrep = q_t[:, hq * HD:(hq + 1) * HD].unsqueeze(1).broadcast_to([128, 16, HD])
            p.do("vector", lambda e, qrep=qrep: e.tensor_tensor(
                prod.rearrange("p (a b) -> p a b", a=16),
                k_t.rearrange("p (a b) -> p a b", a=16), qrep, ALU.mult))
            p.do("vector", lambda e, hq=hq: e.reduce_sum(
                sc16[:, hq * 16:(hq + 1) * 16],
                prod.rearrange("p (a b) -> p a b", a=16), axis=AX.X))
        vv = p.do("vector", lambda e: e.tensor_copy(prod[:, 0:256], sc16), "V", 1)
        p.w("scalar", "V", vv)
        aa = p.do("scalar", lambda e: e.activation(att[:, :], sc16[:, :], AF.Exp,
                                                   scale=0.125), "A", 1)
        if b == 0 and tau == 0 and "dbg_sc" in io:
            p.w("sync", "A", aa)
            p.do("sync", lambda e: e.dma_start(out=io["dbg_sc"][:, 0:256], in_=sc16), "DS", 16)
            p.do("sync", lambda e: e.dma_start(out=io["dbg_sc"][:, 256:512], in_=att), "DS", 16)
            p.w("vector", "DS", p.cnt["DS"])
        p.w("vector", "A", aa)
        rs = p.do("vector", lambda e: e.reduce_sum(
            zs[:, :], att.rearrange("p (a b) -> p a b", a=16), axis=AX.X), "V", 1)
        p.w("vector", "V", rs)
        rc = p.do("vector", lambda e: e.reciprocal(zr[:, :], zs[:, :]), "V", 1)
        p.w("vector", "V", rc)
        for hq in range(H):
            accv = acc[:, hq * HD:(hq + 1) * HD]
            p.do("vector", lambda e, accv=accv, hq=hq: e.tensor_scalar(
                accv, v_t[:, 0:HD], att[:, hq * 16:hq * 16 + 1], None, ALU.mult))
            for hk in range(1, 16):
                p.do("vector", lambda e, accv=accv, hk=hk, hq=hq: e.scalar_tensor_tensor(
                    accv, v_t[:, hk * HD:(hk + 1) * HD],
                    att[:, hq * 16 + hk:hq * 16 + hk + 1], accv, ALU.mult, ALU.add))
        if b == 0 and tau == 0 and "dbg_acc" in io:
            fence = p.do("vector", lambda e: e.tensor_copy(prod[:, 0:16], zr), "V", 1)
            p.w("sync", "V", fence)
            p.do("sync", lambda e: e.dma_start(out=io["dbg_acc"][:, 0:1024], in_=acc), "DS", 16)
            p.do("sync", lambda e: e.dma_start(out=io["dbg_acc"][:, 1024:1040], in_=zr), "DS", 16)
            p.w("vector", "DS", p.cnt["DS"])
        zrep = zr.unsqueeze(2).broadcast_to([128, 16, HD])
        evacV = p.do("vector", lambda e, zrep=zrep: e.tensor_tensor(
            otok.rearrange("p (a b) -> p a b", a=16),
            acc.rearrange("p (a b) -> p a b", a=16), zrep, ALU.mult), "V", 1)
        if b == 0 and tau == 0 and "dbg_otok" in io:
            p.w("sync", "V", evacV)
            p.do("sync", lambda e: e.dma_start(out=io["dbg_otok"][:, :], in_=otok), "DS", 16)
            p.w("tensor", "DS", p.cnt["DS"])
        p.w("tensor", "V", evacV)
        for c in range(8):
            pbf = ps["bfB"] if c < 4 else ps["bfA"]
            sem = ("T", 1) if c == 7 else (None, 0)
            tcnt2 = p.do("tensor", lambda e, c=c, pbf=pbf: e.transpose(
                pbf[:, (c % 4) * 128:(c % 4) * 128 + 128],
                otok[:, c * 128:(c + 1) * 128], ident[:, :]), sem[0], sem[1])
        p.w("vector", "T", tcnt2)
        ovA = at0.rearrange("p (c n) -> p c n", c=8)[:, 0:4, tau * 128:(tau + 1) * 128]
        ovB = at0.rearrange("p (c n) -> p c n", c=8)[:, 4:8, tau * 128:(tau + 1) * 128]
        p.do("vector", lambda e, ovA=ovA: e.tensor_copy(
            ovA, ps["bfB"].rearrange("p (c n) -> p c n", c=4)))
        evacV = p.do("vector", lambda e, ovB=ovB: e.tensor_copy(
            ovB, ps["bfA"].rearrange("p (c n) -> p c n", c=4)), "V", 1)
    tauT = p.cnt["T"]
    tauV = evacV
    if b == 0 and "dbg_at0" in io:
        p.w("sync", "V", tauV)
        for c in range(8):
            p.do("sync", lambda e, c=c: e.dma_start(
                out=io["dbg_at0"][c * 128:(c + 1) * 128, :],
                in_=at0[:, c * 4096:(c + 1) * 4096]), "DS", 16)
    p.w("gpsimd", "T", tauT)
    for pl in range(16):
        for half in range(2):
            p.do("gpsimd", lambda e, pl=pl, half=half: e.dma_start(
                out=wo[half * 64:half * 64 + 64, pl * 1024:(pl + 1) * 1024],
                in_=io["l0_Wo"][pl * 64:(pl + 1) * 64, :]), "DG", 16)
    wold = p.cnt["DG"]

    vv_bank = [tauV, tauV]
    dsc = p.cnt["DS"]
    for hq in range(H):
        p.w("scalar", "T", tauT)
        p.w("scalar", "V", evacV)
        for c in range(8):
            rows = io["xbf_tok"][b * S + hq * 256: b * S + hq * 256 + 256,
                                 c * 128:(c + 1) * 128]
            dcnt = p.do("scalar", lambda e, rows=rows, c=c: e.dma_start(
                out=xsb[:, c * 512:c * 512 + 256], in_=rows, transpose=True), "DA", 16)
        p.w("tensor", "V", tauV)
        p.w("tensor", "DG", wold)
        for fc in range(8):
            pb = ps[fc % 2]
            p.w("tensor", "V", vv_bank[fc % 2])
            for pl in range(16):
                lhsT = wo[(hq % 2) * 64:(hq % 2) * 64 + 64,
                          pl * 1024 + fc * 128: pl * 1024 + fc * 128 + 128]
                rhs = at0[(hq % 2) * 64:(hq % 2) * 64 + 64,
                          (hq // 2) * 4096:(hq // 2) * 4096 + 4096] \
                    .rearrange("p (n q) -> p q n", q=16)[:, pl, :]
                sem = ("T", 1) if pl == 15 else (None, 0)
                tcnt = p.do("tensor", lambda e, pb=pb, lhsT=lhsT, rhs=rhs, pl=pl:
                            e.matmul(pb[:, 0:256], lhsT, rhs,
                                     start=(pl == 0), stop=(pl == 15)), sem[0], sem[1])
            p.w("vector", "T", tcnt)
            p.w("vector", "DA", dcnt)
            p.w("vector", "DS", bld)
            sv = stile[:, fc * 256:(fc + 1) * 256]
            p.do("vector", lambda e, sv=sv, pb=pb, fc=fc: e.tensor_scalar(
                sv, pb[:, 0:256], bo_pp[:, fc:fc + 1], None, ALU.add))
            vv = p.do("vector", lambda e, sv=sv, fc=fc: e.tensor_tensor(
                sv, sv, xsb[:, fc * 512:fc * 512 + 256], ALU.add), "V", 1)
            vv_bank[fc % 2] = vv
            p.w("scalar", "V", vv)
            aa = p.do("scalar", lambda e, fc=fc, sv=sv: e.activation(
                sqf[:, fc * 256:(fc + 1) * 256], sv, AF.Square), "A", 1)
            p.w("tensor", "V", vv)
            p.w("tensor", "A", aa)
            p.do("tensor", lambda e, sv=sv, fc=fc: e.matmul(
                ps[2][0:1, 0:256], onesb[:, 0:1], sv,
                start=(fc == 0), stop=(fc == 7)))
            tst = p.do("tensor", lambda e, fc=fc: e.matmul(
                ps[2][0:1, 256:512], onesf[:, 0:1], sqf[:, fc * 256:(fc + 1) * 256],
                start=(fc == 0), stop=(fc == 7)), "T", 1)
        p.w("vector", "T", tst)
        p.do("vector", lambda e: e.tensor_scalar(r_mu, ps[2][0:1, 0:256], 1.0 / D, None, ALU.mult))
        p.do("vector", lambda e: e.tensor_scalar(r_m2, ps[2][0:1, 256:512], 1.0 / D, None, ALU.mult))
        p.do("vector", lambda e: e.tensor_tensor(r_rs, r_mu, r_mu, ALU.mult))
        p.do("vector", lambda e: e.tensor_tensor(r_m2, r_m2, r_rs, ALU.subtract))
        vv = p.do("vector", lambda e: e.tensor_scalar(r_m2, r_m2, 1e-5, None, ALU.add), "V", 1)
        p.w("scalar", "V", vv)
        p.do("scalar", lambda e: e.activation(r_rs, r_m2, AF.Ln))
        aa = p.do("scalar", lambda e: e.activation(r_rs, r_rs, AF.Exp, scale=-0.5), "A", 1)
        p.w("tensor", "A", aa)
        p.w("tensor", "V", x1done)
        p.do("tensor", lambda e: e.matmul(ps[3][:, 0:256], onesf[0:1, 0:128], r_mu,
                                          start=True, stop=True))
        tb = p.do("tensor", lambda e: e.matmul(ps[3][:, 256:512], onesf[0:1, 0:128], r_rs,
                                               start=True, stop=True), "T", 1)
        p.w("vector", "T", tb)
        p.w("vector", "DS", dsc)
        for fc in range(8):
            sv = stile[:, fc * 256:(fc + 1) * 256]
            ov = x1t[:, fc * 256:(fc + 1) * 256]
            p.do("vector", lambda e, sv=sv, ov=ov: e.tensor_tensor(
                ov, sv, ps[3][:, 0:256], ALU.subtract))
            x1done = p.do("vector", lambda e, ov=ov: e.tensor_tensor(
                ov, ov, ps[3][:, 256:512], ALU.mult), "V", 1)
        evacV = x1done
        p.w("sync", "V", x1done)
        for fc in range(8):
            dsc = p.do("sync", lambda e, fc=fc, hq=hq: e.dma_start(
                out=io["x1T"][fc * 128:(fc + 1) * 128,
                              b * S + hq * 256: b * S + hq * 256 + 256],
                in_=x1t[:, fc * 256:(fc + 1) * 256]), "DS", 16)




def emit_ln(p, sb, cst, ps, stile, sqf, nw, dst_fn):
    onesb, onesf = cst["onesb"], cst["onesfsb"]
    r_mu, r_m2, r_rs = sb["r_mu"][0:1, 0:nw], sb["r_m2"][0:1, 0:nw], sb["r_rs"][0:1, 0:nw]
    D_ = 1024
    tst = None
    for fc in range(8):
        sv = stile[:, fc * nw:(fc + 1) * nw]
        aa = p.do("scalar", lambda e, fc=fc, sv=sv: e.activation(
            sqf[:, fc * nw:(fc + 1) * nw], sv, AF.Square), "A", 1)
        p.w("tensor", "A", aa)
        p.do("tensor", lambda e, sv=sv, fc=fc: e.matmul(
            ps[2][0:1, 0:nw], onesb[:, 0:1], sv, start=(fc == 0), stop=(fc == 7)))
        tst = p.do("tensor", lambda e, fc=fc, nw=nw: e.matmul(
            ps[2][0:1, 256:256 + nw] if nw <= 256 else ps[2][0:1, 0:0],
            onesf[:, 0:1], sqf[:, fc * nw:(fc + 1) * nw],
            start=(fc == 0), stop=(fc == 7)), "T", 1)
    p.w("vector", "T", tst)
    p.do("vector", lambda e: e.tensor_scalar(r_mu, ps[2][0:1, 0:nw], 1.0 / D_, None, ALU.mult))
    p.do("vector", lambda e: e.tensor_scalar(r_m2, ps[2][0:1, 256:256 + nw], 1.0 / D_, None, ALU.mult))
    p.do("vector", lambda e: e.tensor_tensor(r_rs, r_mu, r_mu, ALU.mult))
    p.do("vector", lambda e: e.tensor_tensor(r_m2, r_m2, r_rs, ALU.subtract))
    vv = p.do("vector", lambda e: e.tensor_scalar(r_m2, r_m2, 1e-5, None, ALU.add), "V", 1)
    p.w("scalar", "V", vv)
    p.do("scalar", lambda e: e.activation(r_rs, r_m2, AF.Ln))
    aa = p.do("scalar", lambda e: e.activation(r_rs, r_rs, AF.Exp, scale=-0.5), "A", 1)
    p.w("tensor", "A", aa)
    p.do("tensor", lambda e: e.matmul(ps[3][:, 0:nw], onesf[0:1, 0:128], r_mu, start=True, stop=True))
    tb = p.do("tensor", lambda e: e.matmul(ps[3][:, 256:256 + nw], onesf[0:1, 0:128], r_rs, start=True, stop=True), "T", 1)
    p.w("vector", "T", tb)
    last = None
    for fc in range(8):
        sv = stile[:, fc * nw:(fc + 1) * nw]
        ov = sb["lnout"][:, fc * nw:(fc + 1) * nw]
        p.do("vector", lambda e, sv=sv, ov=ov, nw=nw: e.tensor_tensor(
            ov, sv, ps[3][:, 0:nw], ALU.subtract))
        last = p.do("vector", lambda e, ov=ov, nw=nw: e.tensor_tensor(
            ov, ov, ps[3][:, 256:256 + nw], ALU.mult), "V", 1)
        p.w("sync", "V", last)
        dst_fn(fc, ov)
    return last


def emit_ffn0(p, io, AR1, AR2, AR3, ARF, ARS, sb, cst, ps):
    nc = p.nc
    scal = sb["fscal"]
    for i, nm in enumerate(("f0_w1_s", "f0_w2_s", "f0_h1_s", "f0_h2_s")):
        p.do("sync", lambda e, i=i, nm=nm: e.dma_start(
            out=scal[:, i * 32:(i + 1) * 32], in_=io[nm][:, :]), "DS", 16)
    p.do("sync", lambda e: e.dma_start(out=scal[:, 128:136], in_=io["f0_w3_s"][:, :]), "DS", 16)
    sld = p.cnt["DS"]

    wA = AR1[:, 0:16384]
    xblk = AR3[:, 0:4096]
    hout = AR3[:, 4096:8192]
    silu_f = ARF[:, 0:512]
    guard_T = p.cnt["T"]
    guard_V = p.cnt["V"]
    guard_A = p.cnt["A"]
    p.w("gpsimd", "T", guard_T)
    p.w("sync", "V", guard_V)

    xbufs = (AR3[:, 0:4096], AR3[:, 8192:12288])
    dcnt = [p.cnt["DS"], p.cnt["DS"]]
    xuse = [guard_T, guard_T]
    ld = p.cnt["DG"]
    evV = [guard_V, guard_V]
    evacV = guard_V
    evacA = guard_A
    tcnt = guard_T
    hst = p.cnt["DS"]
    for js in range(4):
        p.w("gpsimd", "T", tcnt)
        for c in range(8):
            p.do("gpsimd", lambda e, c=c, js=js: e.dma_start(
                out=wA[:, c * 1024:(c + 1) * 1024],
                in_=io["f0_w1T"][c * 128:(c + 1) * 128, js * 1024:(js + 1) * 1024]), "DG", 16)
            ld = p.do("gpsimd", lambda e, c=c, js=js: e.dma_start(
                out=wA[:, 8192 + c * 1024:8192 + (c + 1) * 1024],
                in_=io["f0_w2T"][c * 128:(c + 1) * 128, js * 1024:(js + 1) * 1024]), "DG", 16)
        for tb in range(16):
            xb = tb % 2
            xblk = xbufs[xb]
            p.w("sync", "T", xuse[xb])
            xv = io["x1T"].rearrange("(c p) t -> p c t", p=128)[:, :, tb * 512:(tb + 1) * 512]
            dcnt[xb] = p.do("sync", lambda e, xv=xv, xblk=xblk: e.dma_start(
                out=xblk.rearrange("p (c t) -> p c t", c=8), in_=xv), "DS", 16)
            for jc in range(8):
                pb_u1, pb_u2 = ps[jc % 2 * 2], ps[jc % 2 * 2 + 1]
                p.w("tensor", "DS", dcnt[xb])
                p.w("tensor", "DG", ld)
                p.w("tensor", "V", evV[jc % 2])
                for c in range(8):
                    p.do("tensor", lambda e, pb_u1=pb_u1, c=c, jc=jc, xblk=xblk: e.matmul(
                        pb_u1[:, :], wA[:, c * 1024 + jc * 128: c * 1024 + jc * 128 + 128],
                        xblk[:, c * 512:(c + 1) * 512], start=(c == 0), stop=(c == 7)))
                for c in range(8):
                    tcnt = p.do("tensor", lambda e, pb_u2=pb_u2, c=c, jc=jc, xblk=xblk: e.matmul(
                        pb_u2[:, :], wA[:, 8192 + c * 1024 + jc * 128: 8192 + c * 1024 + jc * 128 + 128],
                        xblk[:, c * 512:(c + 1) * 512], start=(c == 0), stop=(c == 7)),
                        ("T" if c == 7 else None), (1 if c == 7 else 0))
                p.w("scalar", "T", tcnt)
                p.w("scalar", "DS", sld)
                p.w("scalar", "V", evacV)
                aa = p.do("scalar", lambda e, pb_u1=pb_u1, js=js, jc=jc: e.activation(
                    silu_f[:, :], pb_u1[:, :], AF.Silu,
                    scale=scal[:, js * 8 + jc:js * 8 + jc + 1]), "A", 1)
                p.w("vector", "A", aa)
                p.w("vector", "DS", hst)
                hv = hout[:, jc * 512:(jc + 1) * 512]
                evacV = p.do("vector", lambda e, pb_u2=pb_u2, hv=hv, js=js, jc=jc:
                             e.scalar_tensor_tensor(
                                 hv, pb_u2[:, :], scal[:, 32 + js * 8 + jc:32 + js * 8 + jc + 1],
                                 silu_f[:, :], ALU.mult, ALU.mult), "V", 1)
                evV[jc % 2] = evacV
                p.w("sync", "V", evacV)
                hst = p.do("sync", lambda e, js=js, jc=jc, tb=tb, hv=hv: e.dma_start(
                    out=io["hA"][js * 1024 + jc * 128: js * 1024 + jc * 128 + 128,
                                 tb * 512:(tb + 1) * 512], in_=hv), "DS", 16)
            xuse[xb] = tcnt

    hblkA = AR1[:, 0:16384]
    hblkB1 = AR1[:, 16384:28672]
    hblkB2 = AR3[:, 0:4096]

    def hb_chunk(bi, c):
        if bi == 0:
            return hblkA[:, c * 512:(c + 1) * 512]
        if c < 24:
            return hblkB1[:, c * 512:(c + 1) * 512]
        return hblkB2[:, (c - 24) * 512:(c - 23) * 512]

    for stage, (wt, scol, src, dst) in enumerate((
            ("f0_h1T", 64, "hA", "hB"), ("f0_h2T", 96, "hB", "hA"))):
        p.w("tensor", "V", evacV)
        wbufs = (AR2[:, 0:16384], AR2[:, 16384:32768])
        gout = AR3[:, 4096:8192]
        gst = p.cnt["DS"]
        wld = [p.cnt["DG"], p.cnt["DG"]]
        wuse = [tcnt, tcnt]
        hld = [p.cnt["DS"], p.cnt["DS"]]
        huse = [tcnt, tcnt]
        evA4 = [evacA] * 4

        def load_h(tb, src=src):
            bi = tb % 2
            p.w("sync", "T", huse[bi])
            hv2 = io[src].rearrange("(c p) t -> p c t", p=128)[:, :, tb * 512:(tb + 1) * 512]
            if bi == 0:
                hld[bi] = p.do("sync", lambda e, hv2=hv2: e.dma_start(
                    out=hblkA.rearrange("p (c t) -> p c t", c=32), in_=hv2), "DS", 16)
            else:
                p.do("sync", lambda e, hv2=hv2: e.dma_start(
                    out=hblkB1.rearrange("p (c t) -> p c t", c=24), in_=hv2[:, 0:24, :]), "DS", 16)
                hld[bi] = p.do("sync", lambda e, hv2=hv2: e.dma_start(
                    out=hblkB2.rearrange("p (c t) -> p c t", c=8), in_=hv2[:, 24:32, :]), "DS", 16)

        load_h(0)
        for tb in range(16):
            bi = tb % 2
            if tb + 1 < 16:
                load_h(tb + 1)
            for hs in range(8):
                wi = hs % 2
                wsl = wbufs[wi]
                p.w("gpsimd", "T", wuse[wi])
                wv = io[wt].rearrange("(c p) j -> p c j", p=128)[:, :, hs * 512:(hs + 1) * 512]
                wld[wi] = p.do("gpsimd", lambda e, wv=wv, wsl=wsl: e.dma_start(
                    out=wsl.rearrange("p (c j) -> p c j", c=32), in_=wv), "DG", 16)
                for jc in range(4):
                    jg = hs * 4 + jc
                    pb = ps[jg % 4]
                    p.w("tensor", "DS", hld[bi])
                    p.w("tensor", "DG", wld[wi])
                    p.w("tensor", "A", evA4[jg % 4])
                    for c in range(32):
                        tcnt = p.do("tensor", lambda e, pb=pb, c=c, jc=jc, bi=bi, wsl=wsl: e.matmul(
                            pb[:, :], wsl[:, c * 512 + jc * 128: c * 512 + jc * 128 + 128],
                            hb_chunk(bi, c), start=(c == 0), stop=(c == 31)),
                            ("T" if c == 31 else None), (1 if c == 31 else 0))
                    wuse[wi] = tcnt
                    p.w("scalar", "T", tcnt)
                    p.w("scalar", "DS", gst)
                    gv = gout[:, (jg % 8) * 512:(jg % 8) * 512 + 512]
                    evA4[jg % 4] = p.do("scalar", lambda e, pb=pb, gv=gv, jg=jg, scol=scol:
                                        e.activation(gv, pb[:, :], AF.Gelu,
                                                     scale=scal[:, scol + jg:scol + jg + 1]),
                                        "A", 1)
                    p.w("sync", "A", evA4[jg % 4])
                    gst = p.do("sync", lambda e, jg=jg, tb=tb, gv=gv, dst=dst: e.dma_start(
                        out=io[dst][jg * 128:(jg + 1) * 128,
                                    tb * 512:(tb + 1) * 512], in_=gv), "DS", 16)
            huse[bi] = tcnt
        evacA = max(evA4)

    w3 = AR2[:, 0:32768]
    p.w("gpsimd", "T", tcnt)
    for c in range(32):
        ld = p.do("gpsimd", lambda e, c=c: e.dma_start(
            out=w3[:, c * 1024:(c + 1) * 1024],
            in_=io["f0_w3T"][c * 128:(c + 1) * 128, :]), "DG", 16)
    gblk = AR1[:, 0:16384]
    xres = AR3[:, 0:4096]
    stile = AR3[:, 4096:8192]
    sb["lnout"] = AR3[:, 8192:12288]
    sqf = ARF[:, 0:4096]
    x2st = p.cnt["DS"]
    lnV = p.cnt["V"]
    for tb in range(16):
        p.w("sync", "T", tcnt)
        gv2 = io["hA"].rearrange("(c p) t -> p c t", p=128)[:, :, tb * 512:(tb + 1) * 512]
        p.do("sync", lambda e, gv2=gv2: e.dma_start(
            out=gblk.rearrange("p (c t) -> p c t", c=32), in_=gv2), "DS", 16)
        xv2 = io["x1T"].rearrange("(c p) t -> p c t", p=128)[:, :, tb * 512:(tb + 1) * 512]
        dcnt = p.do("sync", lambda e, xv2=xv2: e.dma_start(
            out=xres.rearrange("p (c t) -> p c t", c=8), in_=xv2), "DS", 16)
        vlast = None
        for ec in range(8):
            pb = ps[ec % 2]
            p.w("tensor", "DS", dcnt)
            p.w("tensor", "DG", ld)
            p.w("tensor", "V", lnV)
            for c in range(32):
                tcnt = p.do("tensor", lambda e, pb=pb, c=c, ec=ec: e.matmul(
                    pb[:, :], w3[:, c * 1024 + ec * 128: c * 1024 + ec * 128 + 128],
                    gblk[:, c * 512:(c + 1) * 512], start=(c == 0), stop=(c == 31)),
                    ("T" if c == 31 else None), (1 if c == 31 else 0))
            p.w("vector", "T", tcnt)
            p.w("vector", "DS", x2st)
            sv = stile[:, ec * 512:(ec + 1) * 512]
            lnV = p.do("vector", lambda e, pb=pb, sv=sv, ec=ec: e.scalar_tensor_tensor(
                sv, pb[:, :], scal[:, 128 + ec:128 + ec + 1],
                xres[:, ec * 512:(ec + 1) * 512], ALU.mult, ALU.add), "V", 1)
        p.w("scalar", "V", lnV)
        x2st = emit_ffn_ln512(p, io, sb, cst, ps, stile, sqf, tb, "x2T")
        lnV = p.cnt["V"]


def emit_ffn_ln512(p, io, sb, cst, ps, stile, sqf, tb, dst):
    onesb, onesf = cst["onesb"], cst["onesfsb"]
    nw = 512
    r_mu, r_m2, r_rs = sb["r_mu"][0:1, 0:nw], sb["r_m2"][0:1, 0:nw], sb["r_rs"][0:1, 0:nw]
    tst = None
    for fc in range(8):
        sv = stile[:, fc * nw:(fc + 1) * nw]
        aa = p.do("scalar", lambda e, fc=fc, sv=sv: e.activation(
            sqf[:, fc * nw:(fc + 1) * nw] if fc < 8 else None, sv, AF.Square), "A", 1)
        p.w("tensor", "A", aa)
        p.do("tensor", lambda e, sv=sv, fc=fc: e.matmul(
            ps[2][0:1, 0:nw], onesb[:, 0:1], sv, start=(fc == 0), stop=(fc == 7)))
        tst = p.do("tensor", lambda e, fc=fc: e.matmul(
            ps[3][0:1, 0:nw], onesf[:, 0:1], sqf[:, fc * nw:(fc + 1) * nw],
            start=(fc == 0), stop=(fc == 7)), "T", 1)
    p.w("vector", "T", tst)
    p.do("vector", lambda e: e.tensor_scalar(r_mu, ps[2][0:1, 0:nw], 1.0 / 1024, None, ALU.mult))
    p.do("vector", lambda e: e.tensor_scalar(r_m2, ps[3][0:1, 0:nw], 1.0 / 1024, None, ALU.mult))
    p.do("vector", lambda e: e.tensor_tensor(r_rs, r_mu, r_mu, ALU.mult))
    p.do("vector", lambda e: e.tensor_tensor(r_m2, r_m2, r_rs, ALU.subtract))
    vv = p.do("vector", lambda e: e.tensor_scalar(r_m2, r_m2, 1e-5, None, ALU.add), "V", 1)
    p.w("scalar", "V", vv)
    p.do("scalar", lambda e: e.activation(r_rs, r_m2, AF.Ln))
    aa = p.do("scalar", lambda e: e.activation(r_rs, r_rs, AF.Exp, scale=-0.5), "A", 1)
    p.w("tensor", "A", aa)
    p.do("tensor", lambda e: e.matmul(ps[2][:, 0:nw], onesf[0:1, 0:128], r_mu, start=True, stop=True))
    tb2 = p.do("tensor", lambda e: e.matmul(ps[3][:, 0:nw], onesf[0:1, 0:128], r_rs, start=True, stop=True), "T", 1)
    p.w("vector", "T", tb2)
    stD = p.cnt["DS"]
    dual = (dst == "x2T" and "x2Tf" in io)
    for fc in range(8):
        sv = stile[:, fc * nw:(fc + 1) * nw]
        ov = sb["lnout"][:, fc * nw:(fc + 1) * nw]
        if dual:
            ovf = sb["lnoutf"]
            p.w("vector", "DS", stD)
            p.do("vector", lambda e, sv=sv, ovf=ovf: e.tensor_tensor(ovf, sv, ps[2][:, 0:nw], ALU.subtract))
            lastf = p.do("vector", lambda e, ovf=ovf: e.tensor_tensor(ovf, ovf, ps[3][:, 0:nw], ALU.mult), "V", 1)
            p.w("sync", "V", lastf)
            stD = p.do("sync", lambda e, fc=fc, tb=tb, ovf=ovf: e.dma_start(
                out=io["x2Tf"][fc * 128:(fc + 1) * 128, tb * 512:(tb + 1) * 512], in_=ovf), "DS", 16)
        p.do("vector", lambda e, sv=sv, ov=ov: e.tensor_tensor(ov, sv, ps[2][:, 0:nw], ALU.subtract))
        last = p.do("vector", lambda e, ov=ov: e.tensor_tensor(ov, ov, ps[3][:, 0:nw], ALU.mult), "V", 1)
        p.w("sync", "V", last)
        stD = p.do("sync", lambda e, fc=fc, tb=tb, ov=ov, dst=dst: e.dma_start(
            out=io[dst][fc * 128:(fc + 1) * 128, tb * 512:(tb + 1) * 512], in_=ov), "DS", 16)
    return stD




def emit_l1qkv(p, io, AR1, AR2, AR3, ARF, ARS, sb, cst, ps, b):
    onesb, onesf, lup, gmats = cst["onesb"], cst["onesfsb"], cst["lup"], cst["gmats"]
    probes = sb["probes"]
    x2blk = AR3[:, 0:4096]
    ev = AR3[:, 4096:5120]
    ohP = AR3[:, 5120:11264]
    ranks = sb["ranks"]
    qrtf = AR2[:, 0:32768].bitcast(F32)
    tptf = ARF[:, 0:32]
    sptf = ARF[:, 32:64]
    cnt1 = ARF[:, 64:65]
    gst1 = ARF[:, 65:66]
    wqf = ARF[:, 128:1152]
    rext = ARF[:, 128:640]
    projsb = ARF[:, 1152:4224]
    diag = ARF[:, 4224:6272]
    wq1 = sb["wq1"]; wk1 = sb["wk1"]; wv1 = sb["wv1"]; brow1 = sb["brow1"]

    if b == 0:
        p.w("gpsimd", "T", p.cnt["T"])
        for nm, dstv in (("l1_Wq", wq1), ("l1_Wk", wk1), ("l1_Wv", wv1)):
            for c in range(8):
                p.do("gpsimd", lambda e, d=dstv[:, c * 1024:(c + 1) * 1024],
                     s=io[nm][c * 128:(c + 1) * 128, :]: e.dma_start(out=d, in_=s),
                     "DG", 16)
        for i, nm in enumerate(("l1_bq_row", "l1_bk_row", "l1_bv_row")):
            p.do("gpsimd", lambda e, d=brow1[0:1, i * 1024:(i + 1) * 1024],
                 s=io[nm][0:1, :]: e.dma_start(out=d, in_=s), "DG", 16)
        for h in range(16):
            p.do("sync", lambda e, h=h: e.dma_start(
                out=probes[(h % 2) * 64:(h % 2) * 64 + 64, (h // 2) * 192:(h // 2) * 192 + 192],
                in_=io["l1_probes"][h, :, :]), "DS", 16)
        p.do("sync", lambda e: e.dma_start(out=sb["bqpp1"], in_=io["l1_bq_pp"][:, :]), "DS", 16)
    wload = p.cnt["DG"]
    sload = p.cnt["DS"]
    wmap = (wq1, wk1, wv1)

    evacA = p.cnt["A"]
    tcnt = p.cnt["T"]
    dcnt = p.cnt["DS"]
    est = p.cnt["DS"]
    p.w("tensor", "V", p.cnt["V"])
    p.w("tensor", "A", p.cnt["A"])
    for tau in range(32):
        blkc = tau // 4
        if tau % 4 == 0:
            p.w("sync", "T", tcnt)
            for c in range(8):
                dcnt = p.do("sync", lambda e, c=c, blkc=blkc: e.dma_start(
                    out=x2blk[:, c * 512:(c + 1) * 512],
                    in_=io["x2T"][c * 128:(c + 1) * 128,
                                  b * S + blkc * 512: b * S + blkc * 512 + 512]), "DS", 16)
        waves = [[(0, 0), (0, 1), (1, 0), (1, 1)], [(2, 0), (2, 1)]]
        for wi, wave in enumerate(waves):
            p.w("tensor", "A", evacA)
            p.w("tensor", "DS", dcnt)
            p.w("tensor", "DG", wload)
            for j, (i, fh) in enumerate(wave):
                pb = ps[j]
                for c in range(8):
                    lhsT = x2blk[:, c * 512 + (tau % 4) * 128: c * 512 + (tau % 4) * 128 + 128]
                    rhs = wmap[i][:, c * 1024 + fh * 512: c * 1024 + fh * 512 + 512]
                    p.do("tensor", lambda e, pb=pb, lhsT=lhsT, rhs=rhs, c=c:
                         e.matmul(pb[:, :], lhsT, rhs, start=(c == 0), stop=False))
                tcnt = p.do("tensor", lambda e, pb=pb, i=i, fh=fh: e.matmul(
                    pb[:, :], onesb[0:1, 0:128],
                    brow1[0:1, i * 1024 + fh * 512: i * 1024 + fh * 512 + 512],
                    start=False, stop=True), "T", 1)
            p.w("scalar", "T", tcnt)
            for j, (i, fh) in enumerate(wave):
                evv = ev[:, (j % 2) * 512:(j % 2) * 512 + 512]
                p.w("scalar", "DS", est)
                evacA = p.do("scalar", lambda e, j=j, evv=evv: e.activation(
                    evv, ps[j][:, :], AF.Identity), "A", 1)
                p.w("sync", "A", evacA)
                ov = io["qkvs"].rearrange("b h t r -> b h t r")[
                    b, fh * 8:(fh + 1) * 8, tau * 128:(tau + 1) * 128, i * 64:(i + 1) * 64]                     .rearrange("h t r -> t h r")
                est = p.do("sync", lambda e, ov=ov, evv=evv: e.dma_start(
                    out=ov, in_=evv.rearrange("p (h r) -> p h r", h=8)), "DS", 16)

    x2f = AR3[:, 0:8192].bitcast(F32)
    for hh2 in range(2):
        p.w("tensor", "V", p.cnt["V"])
        p.w("sync", "V", p.cnt["V"])
        tq = p.cnt["T"]
        for fcl in range(4):
            fc = hh2 * 4 + fcl
            p.w("sync", "T", tcnt)
            wqld = p.do("sync", lambda e, fc=fc: e.dma_start(
                out=wqf.rearrange("p (c n) -> p c n", n=128),
                in_=io["l1_Wq"].rearrange("(c p) f -> p c f", p=128)[:, :, fc * 128:(fc + 1) * 128]), "DS", 16)
            for tb8 in range(8):
                p.w("sync", "T", tcnt)
                for c in range(8):
                    dcnt = p.do("sync", lambda e, c=c, tb8=tb8: e.dma_start(
                        out=x2f[:, c * 512:(c + 1) * 512],
                        in_=io["x2Tf"][c * 128:(c + 1) * 128,
                                       b * S + tb8 * 512: b * S + tb8 * 512 + 512]), "DS", 16)
                pb = ps[tb8 % 2]
                p.w("tensor", "DS", dcnt)
                p.w("tensor", "DS", wqld)
                p.w("tensor", "A", evacA)
                for c in range(8):
                    tcnt = p.do("tensor", lambda e, pb=pb, c=c: e.matmul(
                        pb[:, :], wqf[:, c * 128:(c + 1) * 128],
                        x2f[:, c * 512:(c + 1) * 512], start=(c == 0), stop=(c == 7)),
                        ("T" if c == 7 else None), (1 if c == 7 else 0))
                p.w("scalar", "T", tcnt)
                p.w("scalar", "DS", sload)
                evacA = p.do("scalar", lambda e, pb=pb, fc=fc, fcl=fcl, tb8=tb8: e.activation(
                    qrtf[:, fcl * 4096 + tb8 * 512: fcl * 4096 + tb8 * 512 + 512],
                    pb[:, :], AF.Identity, bias=sb["bqpp1"][:, fc:fc + 1]), "A", 1)
        qdone = evacA
        if b == 0 and hh2 == 0 and "dbg_qrt" in io:
            p.w("sync", "A", qdone)
            p.do("sync", lambda e: e.dma_start(out=io["dbg_qrt"][:, :], in_=qrtf), "DS", 16)
            p.w("tensor", "DS", p.cnt["DS"])
        for fcl in range(4):
            fc = hh2 * 4 + fcl
            for hh in range(2):
                h = fc * 2 + hh
                m96 = ARF[:, 6272:6320]
                pc16 = ARF[:, 6320:6576]
                p.w("tensor", "A", qdone)
                p.w("tensor", "DS", sload)
                ohdone = p.cnt["V"]
                for half in range(2):
                    p.w("tensor", "V", ohdone)
                    pr = p.cnt["A"]
                    for ta in range(16):
                        tau = half * 16 + ta
                        pb = ps[ta % 2]
                        p.w("tensor", "A", pr if ta < 2 else evacA)
                        tcnt = p.do("tensor", lambda e, pb=pb, tau=tau, fcl=fcl, hh=hh, fc=fc:
                                    e.matmul(pb[:, 0:192],
                                             qrtf[hh * 64:hh * 64 + 64,
                                                  fcl * 4096 + tau * 128: fcl * 4096 + tau * 128 + 128],
                                             probes[hh * 64:hh * 64 + 64,
                                                    fc * 192:(fc + 1) * 192],
                                             start=True, stop=True), "T", 1)
                        p.w("scalar", "T", tcnt)
                        evacA = p.do("scalar", lambda e, pb=pb, ta=ta: e.activation(
                            projsb[:, ta * 192:(ta + 1) * 192], pb[:, 0:192],
                            AF.Identity), "A", 1)
                    p.w("vector", "A", evacA)
                    ohh = ohP.rearrange("p (g x) -> p g x", x=192)[:, half * 16:half * 16 + 16, :]
                    p.do("vector", lambda e: e.reduce_max(
                        m96, projsb.rearrange("p (g a) -> p g a", a=64), axis=AX.X))
                    p.do("vector", lambda e: e.reduce_max(
                        pc16.rearrange("p (g c) -> p g c", c=16),
                        projsb.rearrange("p (g x) -> p g x", x=192)[:, :, 0:64]
                        .rearrange("p g (a c) -> p g a c", a=4).transpose([0, 1, 3, 2]),
                        axis=AX.X))
                    fce = p.do("vector", lambda e: e.tensor_copy(rext[:, 0:48], m96), "V", 1)
                    p.w("vector", "V", fce)
                    m0 = m96.rearrange("p (g l) -> p g l", l=3)
                    p.do("vector", lambda e, ohh=ohh: e.tensor_tensor(
                        ohh[:, :, 0:64].rearrange("p g (a c) -> p g a c", a=4),
                        pc16.rearrange("p (g c) -> p g c", c=16).unsqueeze(2).broadcast_to([128, 16, 4, 16]),
                        m0[:, :, 0:1].unsqueeze(3).broadcast_to([128, 16, 4, 16]),
                        ALU.is_ge))
                    for lvl in (1, 2):
                        p.do("vector", lambda e, lvl=lvl, ohh=ohh: e.tensor_tensor(
                            ohh[:, :, lvl * 64:(lvl + 1) * 64],
                            projsb.rearrange("p (g x) -> p g x", x=192)[:, :, lvl * 64:(lvl + 1) * 64],
                            m0[:, :, lvl:lvl + 1].broadcast_to([128, 16, 64]),
                            ALU.is_ge))
                    ohdone = p.do("vector", lambda e: e.tensor_copy(rext[:, 0:192], ohP[:, 0:192]), "V", 1)
                    if b == 0 and h == 0 and "dbg_oh" in io:
                        p.w("sync", "V", ohdone)
                        p.do("sync", lambda e, half=half: e.dma_start(
                            out=io["dbg_proj"][:, half * 3072:(half + 1) * 3072], in_=projsb), "DS", 16)
                        p.w("vector", "DS", p.cnt["DS"])
                        p.w("scalar", "DS", p.cnt["DS"])
                if b == 0 and h == 0 and "dbg_oh" in io:
                    p.w("sync", "V", ohdone)
                    p.do("sync", lambda e: e.dma_start(out=io["dbg_oh"][:, 0:6144], in_=ohP), "DS", 16)
                    p.w("vector", "DS", p.cnt["DS"])
                for lvl in range(3):
                    csize = 4 if lvl == 0 else 1
                    ohv = ohP.rearrange("p (g x) -> p g x", x=192)[:, :, lvl * 64:(lvl + 1) * 64]
                    p.w("tensor", "V", ohdone)
                    for tau in range(32):
                        tcnt = p.do("tensor", lambda e, tau=tau, ohv=ohv: e.matmul(
                            ps[3][0:64, tau:tau + 1], ohv[:, tau, :], onesb[:, 0:1],
                            start=True, stop=True), "T", 1)
                    p.w("vector", "T", tcnt)
                    vv = p.do("vector", lambda e: e.tensor_copy(tptf[0:64, :], ps[3][0:64, 0:32]), "V", 1)
                    p.w("vector", "V", vv)
                    sc = p.do("vector", lambda e: e.tensor_tensor_scan(
                        sptf[0:64, :], onesf[0:64, 0:32], tptf[0:64, :], 0.0,
                        ALU.mult, ALU.add), "V", 1)
                    p.w("vector", "V", sc)
                    vv = p.do("vector", lambda e: e.tensor_copy(cnt1[0:64, :], sptf[0:64, 31:32]), "V", 1)
                    p.w("vector", "V", vv)
                    p.do("vector", lambda e: e.tensor_tensor(
                        sptf[0:64, :], sptf[0:64, :], tptf[0:64, :], ALU.subtract))
                    vv = p.do("vector", lambda e: e.tensor_copy(rext[0:64, 0:32], sptf[0:64, :]), "V", 1)
                    p.w("tensor", "V", vv)
                    tg = p.do("tensor", lambda e, lvl=lvl: e.matmul(
                        ps[3][0:64, 40:41], gmats[:, lvl * 64:(lvl + 1) * 64], cnt1[0:64, :],
                        start=True, stop=True), "T", 1)
                    p.w("vector", "T", tg)
                    vv = p.do("vector", lambda e: e.tensor_copy(gst1[0:64, :], ps[3][0:64, 40:41]), "V", 1)
                    p.w("vector", "V", vv)
                    vv = p.do("vector", lambda e: e.tensor_scalar(
                        sptf[0:64, :], sptf[0:64, :], gst1[0:64, 0:1], None, ALU.add), "V", 1)
                    p.w("vector", "V", vv)
                    dv = p.do("vector", lambda e: e.tensor_tensor(
                        diag.rearrange("p (g x) -> p g x", x=64)[0:64, :, :],
                        cst["identf"][0:64, 0:64].unsqueeze(1).broadcast_to([64, 32, 64]),
                        sptf[0:64, :].unsqueeze(2).broadcast_to([64, 32, 64]),
                        ALU.mult), "V", 1)
                    p.w("tensor", "V", dv)
                    for tau in range(32):
                        pa = ps[tau // 8]
                        po = (tau % 8) * 64
                        p.do("tensor", lambda e, pa=pa, po=po, tau=tau, ohv=ohv: e.matmul(
                            pa[:, po:po + 64], lup[:, :], ohv[:, tau, :],
                            start=True, stop=False))
                        tcnt = p.do("tensor", lambda e, pa=pa, po=po, tau=tau: e.matmul(
                            pa[:, po:po + 64], onesf[0:64, 0:128],
                            diag.rearrange("p (g x) -> p g x", x=64)[0:64, tau, :],
                            start=False, stop=True), "T", 1)
                    p.w("vector", "T", tcnt)
                    rsb = ranks.bitcast(F32)[:, ((b * 16 + h) * 3 + lvl) * 32:
                                             ((b * 16 + h) * 3 + lvl) * 32 + 32]
                    for bank in range(4):
                        p.do("vector", lambda e, bank=bank, ohv=ohv: e.tensor_tensor(
                            rext.rearrange("p (g x) -> p g x", x=64),
                            ps[bank][:, :].rearrange("p (g x) -> p g x", x=64),
                            ohv[:, bank * 8:(bank + 1) * 8, :],
                            ALU.mult))
                        rr = p.do("vector", lambda e, bank=bank, rsb=rsb: e.reduce_sum(
                            rsb[:, bank * 8:(bank + 1) * 8],
                            rext.rearrange("p (g x) -> p g x", x=64), axis=AX.X), "V", 1)
                    p.w("vector", "V", rr)
                    if b == 0 and h == 0 and lvl == 0 and "dbg_cnt" in io:
                        fz = p.do("vector", lambda e: e.tensor_copy(rext[:, 33:65], tptf[:, :]), "V", 1)
                        p.w("sync", "V", fz)
                        p.do("sync", lambda e: e.dma_start(out=io["dbg_cnt"][:, 0:32], in_=tptf[:, :]), "DS", 16)
                        p.do("sync", lambda e: e.dma_start(out=io["dbg_cnt"][:, 32:64], in_=sptf[:, :]), "DS", 16)
                        p.do("sync", lambda e: e.dma_start(out=io["dbg_cnt"][:, 64:65], in_=cnt1[:, :], allow_slow_non_contiguous=True), "DS", 16)
                        p.do("sync", lambda e: e.dma_start(out=io["dbg_cnt"][:, 65:66], in_=gst1[:, :], allow_slow_non_contiguous=True), "DS", 16)
                        p.do("sync", lambda e, rsb=rsb: e.dma_start(out=io["dbg_cnt"][:, 66:98], in_=rsb), "DS", 16)
                        p.w("vector", "DS", p.cnt["DS"])
                    fr = p.do("vector", lambda e, rsb=rsb, csize=csize: e.tensor_scalar(
                        rext[:, 0:32], rsb, 1.0 / csize, 4095.0, ALU.mult, ALU.min), "V", 1)
                    p.w("vector", "V", fr)
                    ru = ranks[:, ((b * 16 + h) * 3 + lvl) * 32: ((b * 16 + h) * 3 + lvl) * 32 + 32]
                    ohdone = p.do("vector", lambda e, ru=ru: e.tensor_copy(ru, rext[:, 0:32]), "V", 1)


def emit_dump_ranks(p, io, sb):
    p.w("sync", "V", p.cnt["V"])
    p.do("sync", lambda e: e.dma_start(out=io["dbg_ranks"][:, :], in_=sb["ranks"][:, :]), "DS", 16)




def lvl_spans(lvl):
    cs = (256, 64, 16)[lvl]
    spans = []
    for kb in range(32):
        if lvl == 0:
            g = kb // 2
            a, bnd = g * 256, min(g * 256 + 512, 4096)
        else:
            a, bnd = kb * 128, min(kb * 128 + 128 + cs, 4096)
        spans.append((a, bnd))
    return spans


def emit_l1attn(p, io, AR1, AR2, AR3, ARF, ARS, sb, cst, ps, b):
    ranks = sb["ranks"]
    onesb, ident = cst["onesb"], cst["ident"]
    qkvrows = AR2[:, 0:16384]
    qkT2 = (AR1[:, 0:4096], AR1[:, 8192:12288])
    kvT2 = (AR1[:, 4096:8192], AR1[:, 12288:16384])
    v65 = AR1[:, 16384:18464]
    expT = AR1[:, 20544:21568]
    out_tok = AR1[:, 21568:23616]
    outsb = AR1[:, 23616:24128]
    acc = AR1[:, 24128:28224]
    rz = ARS[:, 160:168]

    p.w("gpsimd", "V", p.cnt["V"])
    p.w("gpsimd", "T", p.cnt["T"])
    if b == 0:
        p.do("gpsimd", lambda e: e.memset(v65[:, :], 1.0), "G", 1)
    mst = p.cnt["G"]

    ld_qkv = [p.cnt["DG"], p.cnt["DG"]]
    vld = [p.cnt["DS"], p.cnt["DS"]]
    ebuse = [p.cnt["T"], p.cnt["T"]]
    sc_done = [p.cnt["DG"], p.cnt["DG"]]
    tr_done = [p.cnt["DA"], p.cnt["DA"]]
    att_done = [p.cnt["T"], p.cnt["T"]]
    g_done = [p.cnt["DG"], p.cnt["DG"]]
    ot_st = [p.cnt["DS"], p.cnt["DS"]]
    rows_use = [p.cnt["DG"], p.cnt["DG"]]
    acst = p.cnt["DS"]
    evacV = p.cnt["V"]
    evacA = p.cnt["A"]
    tcnt = p.cnt["T"]
    g_last = p.cnt["DG"]
    pending = None
    p.w("tensor", "V", p.cnt["V"])

    for h in range(16):
        u = (b * 16 + h) * 3
        hb = h % 2
        p.w("gpsimd", "DG", rows_use[hb])
        src = io["qkvs"][b, h, :, :].rearrange("(c p) r -> p c r", p=128)
        qr = qkvrows[:, hb * 8192:(hb + 1) * 8192]
        ld_qkv[hb] = p.do("gpsimd", lambda e, src=src, qr=qr: e.dma_start(
            out=qr.rearrange("p (c r) -> p c r", r=256), in_=src), "DG", 16)
        for lvl in range(3):
            sbuf_i = (h * 3 + lvl) % 2
            qs = io["qkv_sorted%d" % sbuf_i]
            osrt = io["outsorted%d" % sbuf_i]
            p.w("gpsimd", "DG", ld_qkv[hb])
            p.w("gpsimd", "DA", tr_done[sbuf_i])
            p.w("gpsimd", "DS", vld[sbuf_i])
            for tau in range(32):
                sc_done[sbuf_i] = p.do("gpsimd", lambda e, tau=tau, u=u, lvl=lvl, qs=qs, qr=qr:
                                       e.indirect_dma_start(
                    out=qs[:, :],
                    out_offset=IndirectOffsetOnAxis(
                        ranks[:, (u + lvl) * 32 + tau:(u + lvl) * 32 + tau + 1], axis=0),
                    in_=qr[:, tau * 256:(tau + 1) * 256], in_offset=None), "DG", 16)
            rows_use[hb] = sc_done[sbuf_i]
            if pending is not None:
                pending()
                pending = None
            qkT = qkT2[sbuf_i]
            kvT = kvT2[sbuf_i]
            p.w("scalar", "DG", sc_done[sbuf_i])
            p.w("scalar", "T", att_done[sbuf_i])
            for w in range(8):
                p.do("scalar", lambda e, w=w, qs=qs, qkT=qkT: e.dma_start(
                    out=qkT[:, w * 512:(w + 1) * 512],
                    in_=qs[w * 512:(w + 1) * 512, 0:128],
                    transpose=True), "DA", 16)
                tr_done[sbuf_i] = p.do("scalar", lambda e, w=w, qs=qs, kvT=kvT: e.dma_start(
                    out=kvT[:, w * 512:(w + 1) * 512],
                    in_=qs[w * 512:(w + 1) * 512, 64:192],
                    transpose=True), "DA", 16)
            p.w("sync", "DG", sc_done[sbuf_i])
            p.w("sync", "G", mst)
            p.w("sync", "T", att_done[0])
            p.w("sync", "T", att_done[1])
            vld[sbuf_i] = p.do("sync", lambda e, qs=qs, v65=v65: e.dma_start(
                out=v65.rearrange("p (c r) -> p c r", r=65)[:, :, 0:64],
                in_=qs[:, 128:192].rearrange("(c p) r -> p c r", p=128)), "DS", 16)
            if b == 0 and h == 0 and lvl == 0 and "dbg_qsort" in io:
                p.w("sync", "DG", sc_done)
                p.do("sync", lambda e: e.dma_start(out=io["dbg_qsort"][:, :], in_=io["qkv_sorted0"][:, :]), "DS", 16)
                p.w("scalar", "DA", 0)
                dqs = p.cnt["DS"]
                p.w("scalar", "DS", dqs)
            spans = lvl_spans(lvl)
            mask = cst["mask1"] if lvl == 1 else (cst["mask2"] if lvl == 2 else None)
            base_evacV = p.cnt["V"]
            win_state = {}
            win_open = [False] * 8
            closers = [max(kb for kb in range(32)
                           if spans[kb][0] < (w + 1) * 512 and spans[kb][1] > w * 512)
                       for w in range(8)]
            widx = 0
            for kb in range(32):
                a, bnd = spans[kb]
                pbs = ps[kb % 2]
                p.w("tensor", "DA", tr_done[sbuf_i])
                p.w("tensor", "DS", vld[sbuf_i])
                p.w("tensor", "A", evacA)
                tcnt = p.do("tensor", lambda e, pbs=pbs, kb=kb, a=a, bnd=bnd,
                            kvT=kvT, qkT=qkT: e.matmul(
                    pbs[:, 0:bnd - a], kvT[0:64, kb * 128:(kb + 1) * 128],
                    qkT[0:64, a:bnd], start=True, stop=True), "T", 1)
                p.w("scalar", "T", tcnt)
                p.w("scalar", "T", ebuse[kb % 2])
                eb = expT[:, (kb % 2) * 512:(kb % 2) * 512 + 512]
                evacA = p.do("scalar", lambda e, pbs=pbs, eb=eb, a=a, bnd=bnd: e.activation(
                    eb[:, 0:bnd - a], pbs[:, 0:bnd - a], AF.Exp, scale=0.125), "A", 1)
                if b == 0 and h == 0 and lvl == 0 and kb in (0, 1) and "dbg_win" in io:
                    p.w("sync", "A", evacA)
                    p.do("sync", lambda e, eb=eb, kb=kb, a=a, bnd=bnd: e.dma_start(
                        out=io["dbg_exp"][128 * kb:128 * kb + 128, 0:bnd - a], in_=eb[:, 0:bnd - a]), "DS", 16)
                    p.w("tensor", "DS", p.cnt["DS"])
                    p.w("scalar", "DS", p.cnt["DS"])
                if mask is not None:
                    p.w("vector", "A", evacA)
                    evacV = p.do("vector", lambda e, eb=eb, mask=mask, a=a, bnd=bnd: e.tensor_tensor(
                        eb[:, 0:bnd - a], eb[:, 0:bnd - a], mask[:, 0:bnd - a], ALU.mult), "V", 1)
                    ebsrc = eb
                    p.w("tensor", "V", evacV)
                else:
                    ebsrc = eb
                    p.w("tensor", "A", evacA)
                pos = a
                while pos < bnd:
                    w = pos // 512
                    pe = min(bnd, (w + 1) * 512)
                    pw = ps[2 + (w % 2)]
                    if not win_open[w]:
                        win_open[w] = True
                        p.w("tensor", "V", win_state.get(w - 2, base_evacV))
                        p.do("tensor", lambda e, pw=pw: e.matmul(
                            pw[0:65, :], cst["zz"][0:1, 0:65], cst["zz"][0:1, 0:512],
                            start=True, stop=False))
                    last_piece = (pe >= bnd)
                    tret = p.do("tensor", lambda e, pw=pw, kb=kb, pos=pos, pe=pe,
                                a=a, w=w, ebsrc=ebsrc: e.matmul(
                        pw[0:65, pos - w * 512:pe - w * 512],
                        v65[:, kb * 65:(kb + 1) * 65], ebsrc[:, pos - a:pe - a],
                        start=False, stop=False),
                        "T" if last_piece else None, 1 if last_piece else 0)
                    if tret is not None:
                        ebuse[kb % 2] = tret
                    pos = pe
                for w in range(8):
                    if win_open[w] and closers[w] == kb:
                        pw = ps[2 + (w % 2)]
                        tcl = p.do("tensor", lambda e, pw=pw: e.matmul(
                            pw[0:65, 0:1], cst["zz"][0:1, 0:65], cst["zz"][0:1, 0:1],
                            start=False, stop=True), "T", 1)
                        p.w("vector", "T", tcl)
                        p.w("vector", "T", tcnt)
                        cpv = p.do("vector", lambda e, pw=pw: e.tensor_copy(
                            outsb[0:65, 0:512], pw[0:65, :]), "V", 1)
                        if b == 0 and h == 0 and lvl == 0 and w == 0 and "dbg_win" in io:
                            p.w("sync", "V", cpv)
                            p.do("sync", lambda e: e.dma_start(out=io["dbg_win"][0:65, :], in_=outsb[0:65, 0:512]), "DS", 16)
                            p.w("tensor", "DS", p.cnt["DS"])
                        p.w("tensor", "V", cpv)
                        pbf = ps["bfA"] if (w % 2) == 0 else ps["bfB"]
                        for tt in range(4):
                            sm = ("T", 1) if tt == 3 else (None, 0)
                            ttc = p.do("tensor", lambda e, tt=tt, pbf=pbf: e.transpose(
                                pbf[:, tt * 128:tt * 128 + 65],
                                outsb[0:65, tt * 128:(tt + 1) * 128],
                                ident[0:65, 0:65]), sm[0], sm[1])
                        p.w("vector", "T", ttc)
                        for tt in range(4):
                            rc = p.do("vector", lambda e, tt=tt, pbf=pbf: e.reciprocal(
                                rz[:, tt:tt + 1], pbf[:, tt * 128 + 64:tt * 128 + 65]), "V", 1)
                            p.w("vector", "V", rc)
                            evacV = p.do("vector", lambda e, tt=tt, w=w, pbf=pbf: e.tensor_scalar(
                                out_tok[:, (w * 4 + tt) * 64:(w * 4 + tt) * 64 + 64],
                                pbf[:, tt * 128:tt * 128 + 64],
                                rz[:, tt:tt + 1], 1.0 / 3.0, ALU.mult, ALU.mult), "V", 1)
                        win_state[w] = evacV
            tcnt = p.cnt["T"]
            att_done[sbuf_i] = tcnt
            if b == 0 and h == 0 and lvl == 0 and "dbg_qsort" in io:
                p.w("sync", "DA", tr_done[sbuf_i])
                p.do("sync", lambda e, qkT=qkT: e.dma_start(out=io["dbg_qkt"][:, 0:4096], in_=qkT), "DS", 16)
                p.do("sync", lambda e, kvT=kvT: e.dma_start(out=io["dbg_qkt"][:, 4096:8192], in_=kvT), "DS", 16)
                p.w("sync", "V", evacV)
                p.do("sync", lambda e: e.dma_start(out=io["dbg_otok2"][:, :], in_=out_tok), "DS", 16)
            p.w("sync", "V", evacV)
            p.w("sync", "DG", g_done[sbuf_i])
            ot_st[sbuf_i] = p.do("sync", lambda e, osrt=osrt: e.dma_start(
                out=osrt.rearrange("(c p) r -> p c r", p=128),
                in_=out_tok.rearrange("p (c r) -> p c r", r=64)), "DS", 16)
            def mk_gathers(sbuf_i=sbuf_i, osrt=osrt, u=u, lvl=lvl, h=h, b=b):
                nonlocal g_last, acst
                p.w("gpsimd", "DS", ot_st[sbuf_i])
                if lvl == 0:
                    p.w("gpsimd", "DS", acst)
                    p.w("gpsimd", "DG", g_last)
                for tau in range(32):
                    kw = dict(compute_op=ALU.add, oob_is_err=True) if lvl > 0 else {}
                    g_done[sbuf_i] = p.do("gpsimd", lambda e, tau=tau, u=u, lvl=lvl, h=h,
                                          kw=kw, osrt=osrt: e.indirect_dma_start(
                        out=acc[:, tau * 128 + (h % 2) * 64: tau * 128 + (h % 2) * 64 + 64],
                        out_offset=None,
                        in_=osrt[:, :],
                        in_offset=IndirectOffsetOnAxis(
                            ranks[:, (u + lvl) * 32 + tau:(u + lvl) * 32 + tau + 1], axis=0),
                        **kw), "DG", 16)
                g_last = g_done[sbuf_i]
                if h % 2 == 1 and lvl == 2:
                    p.w("sync", "DG", g_last)
                    acst = p.do("sync", lambda e, b=b, h=h: e.dma_start(
                        out=io["attn1tok"][b, h // 2, :, :].rearrange("(c p) r -> p c r", p=128),
                        in_=acc.rearrange("p (c r) -> p c r", r=128)), "DS", 16)
            pending = mk_gathers
    if pending is not None:
        pending()
        pending = None


def emit_l1out(p, io, AR1, AR2, AR3, ARF, ARS, sb, cst, ps, b):
    wo1 = AR1[:, 18464:26656]
    attf = AR3[:, 0:4096]
    xres = AR3[:, 4096:8192]
    stile = AR3[:, 8192:12288]
    sqf = ARF[:, 0:4096]
    bo1 = sb["bo1pp"]
    p.w("gpsimd", "T", p.cnt["T"])
    p.w("gpsimd", "DS", p.cnt["DS"])
    for c in range(8):
        p.do("gpsimd", lambda e, c=c: e.dma_start(
            out=wo1[:, c * 1024:(c + 1) * 1024],
            in_=io["l1_Wo"][c * 128:(c + 1) * 128, :]), "DG", 16)
    if b == 0:
        p.do("sync", lambda e: e.dma_start(out=bo1, in_=io["l1_bo_pp"][:, :]), "DS", 16)
    wld = p.cnt["DG"]
    bld = p.cnt["DS"]
    tcnt = p.cnt["T"]
    lnV = p.cnt["V"]
    x3st = p.cnt["DS"]
    p.w("tensor", "V", p.cnt["V"])
    for tb8 in range(8):
        p.w("scalar", "T", tcnt)
        p.w("scalar", "DS", x3st)
        for pr in range(8):
            dcnt = p.do("scalar", lambda e, pr=pr, tb8=tb8, b=b: e.dma_start(
                out=attf[:, pr * 512:(pr + 1) * 512],
                in_=io["attn1tok"][b, pr, tb8 * 512:(tb8 + 1) * 512, :],
                transpose=True), "DA", 16)
        p.w("sync", "T", tcnt)
        for c in range(8):
            rcnt = p.do("sync", lambda e, c=c, tb8=tb8, b=b: e.dma_start(
                out=xres[:, c * 512:(c + 1) * 512],
                in_=io["x2T"][c * 128:(c + 1) * 128,
                              b * S + tb8 * 512: b * S + tb8 * 512 + 512]), "DS", 16)
        for fc in range(8):
            pb = ps[fc % 2]
            p.w("tensor", "DA", dcnt)
            p.w("tensor", "DS", rcnt)
            p.w("tensor", "DG", wld)
            p.w("tensor", "V", lnV)
            for ec in range(8):
                tcnt = p.do("tensor", lambda e, pb=pb, ec=ec, fc=fc: e.matmul(
                    pb[:, :], wo1[:, ec * 1024 + fc * 128: ec * 1024 + fc * 128 + 128],
                    attf[:, ec * 512:(ec + 1) * 512], start=(ec == 0), stop=(ec == 7)),
                    "T" if ec == 7 else None, 1 if ec == 7 else 0)
            p.w("vector", "T", tcnt)
            p.w("vector", "DS", x3st)
            sv = stile[:, fc * 512:(fc + 1) * 512]
            p.do("vector", lambda e, pb=pb, sv=sv, fc=fc: e.tensor_scalar(
                sv, pb[:, :], bo1[:, fc:fc + 1], None, ALU.add))
            lnV = p.do("vector", lambda e, sv=sv, fc=fc: e.tensor_tensor(
                sv, sv, xres[:, fc * 512:(fc + 1) * 512], ALU.add), "V", 1)
        p.w("scalar", "V", lnV)
        sb["lnout"] = AR3[:, 0:4096]
        x3st = emit_ffn_ln512(p, io, sb, cst, ps, stile, sqf, b * 8 + tb8, "x3T")
        lnV = p.cnt["V"]




def emit_wquant(p, io, ARF, AR3, ARS, sb, cst, ps, src, dst, nch, msc_col):
    onesf = cst["onesfsb"]
    wt = ARF[:, 0:4096]
    q = AR3[:, 0:4096]
    wabs = sb["wabs"]
    msc = sb["msc"]
    wsc = sb["wsc"]
    N = io[src].shape()[1] if False else 4096 if nch == 8 else 1024
    vv = p.cnt["V"]
    dcnt = p.cnt["DS"]
    for c in range(nch):
        p.w("sync", "V", vv)
        dcnt = p.do("sync", lambda e, c=c: e.dma_start(
            out=wt[:, 0:N], in_=io[src][c * 128:(c + 1) * 128, :]), "DS", 16)
        p.w("vector", "DS", dcnt)
        rv = p.do("vector", lambda e: e.reduce_sum(
            wabs[:, 1:2], wt[:, 0:N], axis=AX.X, apply_absolute_value=True), "V", 1)
        p.w("vector", "V", rv)
        if c == 0:
            vv = p.do("vector", lambda e: e.tensor_copy(wabs[:, 0:1], wabs[:, 1:2]), "V", 1)
        else:
            vv = p.do("vector", lambda e: e.tensor_tensor(
                wabs[:, 0:1], wabs[:, 0:1], wabs[:, 1:2], ALU.add), "V", 1)
    p.w("tensor", "V", vv)
    tm = p.do("tensor", lambda e: e.matmul(ps[2][0:1, 0:1], onesf[:, 0:1],
                                           wabs[:, 0:1], start=True, stop=True), "T", 1)
    p.w("vector", "T", tm)
    p.do("vector", lambda e: e.tensor_scalar(
        msc[0:1, msc_col:msc_col + 1], ps[2][0:1, 0:1], 1.0 / (nch * 128 * N),
        1e-5, ALU.mult, ALU.max))
    vv = p.do("vector", lambda e: e.engine_nop(), "V", 1)
    p.w("vector", "V", vv)
    rc = p.do("vector", lambda e: e.reciprocal(
        msc[0:1, msc_col + 2:msc_col + 3], msc[0:1, msc_col:msc_col + 1]), "V", 1)
    p.w("vector", "V", rc)
    p.w("tensor", "V", rc)
    tb = p.do("tensor", lambda e: e.matmul(ps[3][:, 0:1], onesf[0:1, 0:128],
                                           msc[0:1, msc_col + 2:msc_col + 3],
                                           start=True, stop=True), "T", 1)
    p.w("vector", "T", tb)
    cb = p.do("vector", lambda e: e.tensor_copy(wsc[:, 0:1], ps[3][:, 0:1]), "V", 1)
    p.w("vector", "V", cb)
    st = p.cnt["DS"]
    for c in range(nch):
        p.w("sync", "V", vv)
        dcnt = p.do("sync", lambda e, c=c: e.dma_start(
            out=wt[:, 0:N], in_=io[src][c * 128:(c + 1) * 128, :]), "DS", 16)
        p.w("vector", "DS", dcnt)
        p.w("vector", "DS", st)
        p.do("vector", lambda e: e.tensor_scalar(
            wt[:, 0:N], wt[:, 0:N], wsc[:, 0:1], None, ALU.mult))
        p.do("vector", lambda e: e.tensor_scalar(
            wt[:, 0:N], wt[:, 0:N], MAGIC, -MAGIC, ALU.add, ALU.add))
        vv = p.do("vector", lambda e: e.tensor_scalar(
            q[:, 0:N], wt[:, 0:N], 1.0, -1.0, ALU.min, ALU.max), "V", 1)
        p.w("sync", "V", vv)
        st = p.do("sync", lambda e, c=c: e.dma_start(
            out=io[dst][c * 128:(c + 1) * 128, :], in_=q[:, 0:N]), "DS", 16)


def emit_ffn1(p, io, AR1, AR2, AR3, ARF, ARS, sb, cst, ps):
    onesb, onesf = cst["onesb"], cst["onesfsb"]
    msc = sb["msc"]
    emit_wquant(p, io, ARF, AR3, ARS, sb, cst, ps, "f1_w1T", "wq1q", 8, 0)
    emit_wquant(p, io, ARF, AR3, ARS, sb, cst, ps, "f1_w2T", "wq2q", 32, 4)
    cw = sb["convw"]; cb2 = sb["convb"]
    p.do("sync", lambda e: e.dma_start(out=cw, in_=io["f1_convw"][:, :]), "DS", 16)
    p.do("sync", lambda e: e.dma_start(out=cb2, in_=io["f1_convb"][:, :]), "DS", 16)
    cld = p.cnt["DS"]

    p.w("sync", "DA", p.cnt["DA"])
    p.w("scalar", "DA", p.cnt["DA"])
    xblk = AR3[:, 0:4096]
    xq = AR3[:, 4096:8192]
    wsl = AR1[:, 0:8192]
    m1 = ARF[:, 0:512]
    m2 = ARF[:, 512:1024]
    scb = ARF[:, 1024:1536]
    dqb = ARF[:, 1536:2048]
    hrow = ARF[0:1, 2048:2560]
    ev = AR3[:, 8192:9216]

    tcnt = p.cnt["T"]
    evacV = p.cnt["V"]
    dh = p.cnt["DS"]
    ld = p.cnt["DG"]
    lastdq = [p.cnt["V"], p.cnt["V"]]
    p.w("tensor", "V", p.cnt["V"])
    p.w("tensor", "A", p.cnt["A"])
    for tb in range(16):
        p.w("sync", "T", tcnt)
        for c in range(8):
            dcnt = p.do("sync", lambda e, c=c, tb=tb: e.dma_start(
                out=xblk[:, c * 512:(c + 1) * 512],
                in_=io["x3T"][c * 128:(c + 1) * 128, tb * 512:(tb + 1) * 512]), "DS", 16)
        p.w("vector", "DS", dcnt)
        p.w("vector", "V", evacV)
        p.do("vector", lambda e: e.scalar_tensor_tensor(
            m1[:, :], xblk[:, 0:512], -1.0, xblk[:, 0:512], ALU.mult, ALU.max))
        for c in range(1, 8):
            p.do("vector", lambda e, c=c: e.tensor_max(
                m1[:, :], m1[:, :], xblk[:, c * 512:(c + 1) * 512]))
            p.do("vector", lambda e, c=c: e.scalar_tensor_tensor(
                m1[:, :], xblk[:, c * 512:(c + 1) * 512], -1.0, m1[:, :],
                ALU.mult, ALU.max))
        vv = p.do("vector", lambda e: e.engine_nop(), "V", 1)
        nh = 64
        while nh >= 1:
            vv = p.do("vector", lambda e, nh=nh: e.tensor_copy(
                scb[0:min(2 * nh, 128), 0:512], m1[0:min(2 * nh, 128), :]), "V", 1)
            p.w("sync", "V", vv)
            sh = p.do("sync", lambda e, nh=nh: e.dma_start(
                out=m2[0:max(nh, 1), :], in_=m1[nh:2 * nh, :]), "DS", 16)
            p.w("vector", "DS", sh)
            vv = p.do("vector", lambda e, nh=nh: e.tensor_max(
                m1[0:nh, :], m1[0:nh, :], m2[0:nh, :]), "V", 1)
            p.w("vector", "V", vv)
            nh //= 2
        p.do("vector", lambda e: e.tensor_scalar(
            m1[0:1, :], m1[0:1, :], 1e-5, None, ALU.max))
        vv = p.do("vector", lambda e: e.engine_nop(), "V", 1)
        p.w("vector", "V", vv)
        rc = p.do("vector", lambda e: e.reciprocal(m2[0:1, :], m1[0:1, :]), "V", 1)
        p.w("vector", "V", rc)
        p.do("vector", lambda e: e.tensor_scalar(m2[0:1, :], m2[0:1, :], 127.0, None, ALU.mult))
        p.do("vector", lambda e: e.tensor_scalar(
            hrow, m1[0:1, :], 1.0 / 127.0, msc[0:1, 0:1], ALU.mult, ALU.mult))
        vv = p.do("vector", lambda e: e.engine_nop(), "V", 1)
        if tb == 0 and "dbg_q1" in io:
            p.w("sync", "V", vv)
            p.do("sync", lambda e: e.dma_start(out=io["dbg_q1"][0:1, 0:512], in_=m1[0:1, :]), "DS", 16)
            p.do("sync", lambda e: e.dma_start(out=io["dbg_q1"][1:2, 0:512], in_=m2[0:1, :]), "DS", 16)
            p.do("sync", lambda e: e.dma_start(out=io["dbg_q1"][2:3, 0:512], in_=hrow), "DS", 16)
            p.w("vector", "DS", p.cnt["DS"])
        p.w("tensor", "V", vv)
        p.do("tensor", lambda e: e.matmul(ps[2][:, 0:512], onesf[0:1, 0:128],
                                          m2[0:1, :], start=True, stop=True))
        tbx = p.do("tensor", lambda e: e.matmul(ps[3][:, 0:512], onesf[0:1, 0:128],
                                                hrow, start=True, stop=True), "T", 1)
        p.w("vector", "T", tbx)
        p.do("vector", lambda e: e.tensor_copy(scb[:, :], ps[2][:, 0:512]))
        vv = p.do("vector", lambda e: e.tensor_copy(dqb[:, :], ps[3][:, 0:512]), "V", 1)
        p.w("vector", "V", vv)
        for c in range(8):
            p.do("vector", lambda e, c=c: e.tensor_tensor(
                xq[:, c * 512:(c + 1) * 512].bitcast(F32)[:, 0:256] if False else
                m1[:, 0:512], xblk[:, c * 512:(c + 1) * 512], scb[:, :], ALU.mult))
            vv = p.do("vector", lambda e, c=c: e.tensor_scalar(
                xq[:, c * 512:(c + 1) * 512], m1[:, 0:512], MAGIC, -MAGIC,
                ALU.add, ALU.add), "V", 1)
        if tb == 0 and "dbg_q1" in io:
            p.w("sync", "V", vv)
            p.do("sync", lambda e: e.dma_start(out=io["dbg_xq"][:, :], in_=xq), "DS", 16)
            p.w("gpsimd", "DS", p.cnt["DS"])
        for js in range(4):
            p.w("gpsimd", "T", tcnt)
            p.w("gpsimd", "V", vv)
            for c in range(8):
                ld = p.do("gpsimd", lambda e, c=c, js=js: e.dma_start(
                    out=wsl[:, c * 1024:(c + 1) * 1024],
                    in_=io["wq1q"][c * 128:(c + 1) * 128, js * 1024:(js + 1) * 1024]), "DG", 16)
            for jc in range(8):
                pb = ps[jc % 2]
                p.w("tensor", "DG", ld)
                p.w("tensor", "V", vv)
                p.w("tensor", "V", lastdq[jc % 2])
                for c in range(8):
                    tcnt = p.do("tensor", lambda e, pb=pb, c=c, jc=jc: e.matmul(
                        pb[:, :], wsl[:, c * 1024 + jc * 128: c * 1024 + jc * 128 + 128],
                        xq[:, c * 512:(c + 1) * 512], start=(c == 0), stop=(c == 7)),
                        "T" if c == 7 else None, 1 if c == 7 else 0)
                p.w("vector", "T", tcnt)
                p.w("vector", "DS", dh)
                p.do("vector", lambda e, pb=pb: e.tensor_tensor(
                    ev[:, 0:512], pb[:, :], dqb[:, :], ALU.mult))
                evacV = p.do("vector", lambda e: e.engine_nop(), "V", 1)
                lastdq[jc % 2] = evacV
                p.w("sync", "V", evacV)
                dh = p.do("sync", lambda e, js=js, jc=jc, tb=tb: e.dma_start(
                    out=io["hA"][js * 1024 + jc * 128: js * 1024 + jc * 128 + 128,
                                 tb * 512:(tb + 1) * 512], in_=ev[:, 0:512]), "DS", 16)

    hc = AR2[:, 0:16384]
    xq2 = AR2[:, 16384:32768]
    whalf = AR1[:, 8192:24576]
    halo = AR3[:, 9216:9216 + 516]
    x3res = AR3[:, 4096:8192]
    stile = AR3[:, 0:4096]
    lnf = ARF[:, 2560:3072]
    sqf = ARF[:, 3072:3584]
    outb = ARF[:, 3584:4096]
    r_mu, r_m2, r_rs = sb["r_mu"][0:1, 0:512], sb["r_m2"][0:1, 0:512], sb["r_rs"][0:1, 0:512]
    tcnt = p.cnt["T"]
    evacV = p.cnt["V"]
    ost = p.cnt["DS"]
    prevOvv = p.cnt["V"]
    lnV = p.cnt["V"]
    for tb in range(16):
        seq_edge_l = (tb % 8 == 0)
        seq_edge_r = (tb % 8 == 7)
        p.w("scalar", "DS", cld)
        for jc in range(32):
            p.w("sync", "V", evacV if jc == 0 else p.cnt["V"] - 0)
            lo = tb * 512 - (0 if seq_edge_l else 1)
            hi = tb * 512 + 512 + (0 if seq_edge_r else 1)
            off = 1 if seq_edge_l else 0
            hld = p.do("sync", lambda e, jc=jc, lo=lo, hi=hi, off=off: e.dma_start(
                out=halo[:, off:off + (hi - lo)],
                in_=io["hA"][jc * 128:(jc + 1) * 128, lo:hi]), "DS", 16)
            if seq_edge_l:
                p.do("vector", lambda e: e.memset(halo[:, 0:1], 0.0))
            if seq_edge_r:
                p.do("vector", lambda e: e.memset(halo[:, 513:514], 0.0))
            p.w("vector", "DS", hld)
            p.do("vector", lambda e, jc=jc: e.tensor_scalar(
                m1[:, :], halo[:, 1:513], cw[:, jc * 3 + 1:jc * 3 + 2], None, ALU.mult))
            p.do("vector", lambda e, jc=jc: e.scalar_tensor_tensor(
                m1[:, :], halo[:, 0:512], cw[:, jc * 3 + 0:jc * 3 + 1], m1[:, :],
                ALU.mult, ALU.add))
            cvv = p.do("vector", lambda e, jc=jc: e.scalar_tensor_tensor(
                m1[:, :], halo[:, 2:514], cw[:, jc * 3 + 2:jc * 3 + 3], m1[:, :],
                ALU.mult, ALU.add), "V", 1)
            p.w("scalar", "V", cvv)
            aa = p.do("scalar", lambda e, jc=jc: e.activation(
                hc[:, jc * 512:(jc + 1) * 512], m1[:, :], AF.Silu,
                bias=cb2[:, jc:jc + 1]), "A", 1)
        p.w("vector", "A", aa)
        p.do("vector", lambda e: e.scalar_tensor_tensor(
            m1[:, :], hc[:, 0:512], -1.0, hc[:, 0:512], ALU.mult, ALU.max))
        for c in range(1, 32):
            p.do("vector", lambda e, c=c: e.tensor_max(
                m1[:, :], m1[:, :], hc[:, c * 512:(c + 1) * 512]))
            p.do("vector", lambda e, c=c: e.scalar_tensor_tensor(
                m1[:, :], hc[:, c * 512:(c + 1) * 512], -1.0, m1[:, :],
                ALU.mult, ALU.max))
        vv = p.do("vector", lambda e: e.engine_nop(), "V", 1)
        nh = 64
        while nh >= 1:
            vv = p.do("vector", lambda e, nh=nh: e.tensor_copy(
                scb[0:min(2 * nh, 128), 0:512], m1[0:min(2 * nh, 128), :]), "V", 1)
            p.w("sync", "V", vv)
            sh = p.do("sync", lambda e, nh=nh: e.dma_start(
                out=m2[0:max(nh, 1), :], in_=m1[nh:2 * nh, :]), "DS", 16)
            p.w("vector", "DS", sh)
            vv = p.do("vector", lambda e, nh=nh: e.tensor_max(
                m1[0:nh, :], m1[0:nh, :], m2[0:nh, :]), "V", 1)
            p.w("vector", "V", vv)
            nh //= 2
        p.do("vector", lambda e: e.tensor_scalar(
            m1[0:1, :], m1[0:1, :], 1e-5, None, ALU.max))
        vv = p.do("vector", lambda e: e.engine_nop(), "V", 1)
        p.w("vector", "V", vv)
        rc = p.do("vector", lambda e: e.reciprocal(m2[0:1, :], m1[0:1, :]), "V", 1)
        p.w("vector", "V", rc)
        p.do("vector", lambda e: e.tensor_scalar(m2[0:1, :], m2[0:1, :], 127.0, None, ALU.mult))
        p.do("vector", lambda e: e.tensor_scalar(
            hrow, m1[0:1, :], 1.0 / 127.0, msc[0:1, 4:5], ALU.mult, ALU.mult))
        vv = p.do("vector", lambda e: e.engine_nop(), "V", 1)
        p.w("tensor", "V", vv)
        p.do("tensor", lambda e: e.matmul(ps[2][:, 0:512], onesf[0:1, 0:128],
                                          m2[0:1, :], start=True, stop=True))
        tbx = p.do("tensor", lambda e: e.matmul(ps[3][:, 0:512], onesf[0:1, 0:128],
                                                hrow, start=True, stop=True), "T", 1)
        p.w("vector", "T", tbx)
        p.do("vector", lambda e: e.tensor_copy(scb[:, :], ps[2][:, 0:512]))
        vv = p.do("vector", lambda e: e.tensor_copy(dqb[:, :], ps[3][:, 0:512]), "V", 1)
        p.w("vector", "V", vv)
        for c in range(32):
            p.do("vector", lambda e, c=c: e.tensor_tensor(
                m1[:, 0:512], hc[:, c * 512:(c + 1) * 512], scb[:, :], ALU.mult))
            vv = p.do("vector", lambda e, c=c: e.tensor_scalar(
                xq2[:, c * 512:(c + 1) * 512], m1[:, 0:512], MAGIC, -MAGIC,
                ALU.add, ALU.add), "V", 1)
        p.w("sync", "V", vv)
        for c in range(8):
            rld = p.do("sync", lambda e, c=c, tb=tb: e.dma_start(
                out=x3res[:, c * 512:(c + 1) * 512],
                in_=io["x3T"][c * 128:(c + 1) * 128, tb * 512:(tb + 1) * 512]), "DS", 16)
        lnV = max(lnV if tb > 0 else p.cnt["V"], prevOvv)
        for half in range(2):
            p.w("gpsimd", "T", tcnt)
            for c in range(32):
                ld = p.do("gpsimd", lambda e, c=c, half=half: e.dma_start(
                    out=whalf[:, c * 512:(c + 1) * 512],
                    in_=io["wq2q"][c * 128:(c + 1) * 128,
                                   half * 512:(half + 1) * 512]), "DG", 16)
            for el in range(4):
                ec = half * 4 + el
                pb = ps[el % 2]
                p.w("tensor", "DG", ld)
                p.w("tensor", "V", vv)
                p.w("tensor", "DS", rld)
                p.w("tensor", "V", lnV)
                for c in range(32):
                    tcnt = p.do("tensor", lambda e, pb=pb, c=c, el=el: e.matmul(
                        pb[:, :], whalf[:, c * 512 + el * 128: c * 512 + el * 128 + 128],
                        xq2[:, c * 512:(c + 1) * 512], start=(c == 0), stop=(c == 31)),
                        "T" if c == 31 else None, 1 if c == 31 else 0)
                p.w("vector", "T", tcnt)
                sv = stile[:, ec * 512:(ec + 1) * 512]
                p.do("vector", lambda e, pb=pb, sv=sv: e.tensor_tensor(
                    m1[:, :], pb[:, :], dqb[:, :], ALU.mult))
                lnV = p.do("vector", lambda e, sv=sv, ec=ec: e.tensor_tensor(
                    sv, m1[:, :], x3res[:, ec * 512:(ec + 1) * 512], ALU.add), "V", 1)
        tst = p.cnt["T"]
        for fc in range(8):
            sv = stile[:, fc * 512:(fc + 1) * 512]
            p.w("scalar", "V", lnV)
            p.w("scalar", "T", tst)
            aa = p.do("scalar", lambda e, fc=fc, sv=sv: e.activation(
                sqf[:, 0:512], sv, AF.Square), "A", 1)
            p.w("tensor", "A", aa)
            p.w("tensor", "V", lnV)
            p.do("tensor", lambda e, sv=sv, fc=fc: e.matmul(
                ps[2][0:1, 0:512], onesb[:, 0:1], sv, start=(fc == 0), stop=(fc == 7)))
            tst = p.do("tensor", lambda e, fc=fc: e.matmul(
                ps[3][0:1, 0:512], onesf[:, 0:1], sqf[:, 0:512],
                start=(fc == 0), stop=(fc == 7)), "T", 1)
        p.w("vector", "T", tst)
        p.do("vector", lambda e: e.tensor_scalar(r_mu, ps[2][0:1, 0:512], 1.0 / D, None, ALU.mult))
        p.do("vector", lambda e: e.tensor_scalar(r_m2, ps[3][0:1, 0:512], 1.0 / D, None, ALU.mult))
        p.do("vector", lambda e: e.tensor_tensor(r_rs, r_mu, r_mu, ALU.mult))
        p.do("vector", lambda e: e.tensor_tensor(r_m2, r_m2, r_rs, ALU.subtract))
        vv = p.do("vector", lambda e: e.tensor_scalar(r_m2, r_m2, 1e-5, None, ALU.add), "V", 1)
        p.w("scalar", "V", vv)
        p.do("scalar", lambda e: e.activation(r_rs, r_m2, AF.Ln))
        aa = p.do("scalar", lambda e: e.activation(r_rs, r_rs, AF.Exp, scale=-0.5), "A", 1)
        p.w("tensor", "A", aa)
        p.do("tensor", lambda e: e.matmul(ps[2][:, 0:512], onesf[0:1, 0:128], r_mu, start=True, stop=True))
        tb2 = p.do("tensor", lambda e: e.matmul(ps[3][:, 0:512], onesf[0:1, 0:128], r_rs, start=True, stop=True), "T", 1)
        p.w("vector", "T", tb2)
        for fc in range(8):
            sv = stile[:, fc * 512:(fc + 1) * 512]
            p.w("vector", "DS", ost)
            p.do("vector", lambda e, sv=sv: e.tensor_tensor(lnf[:, :], sv, ps[2][:, 0:512], ALU.subtract))
            lvv = p.do("vector", lambda e: e.tensor_tensor(lnf[:, :], lnf[:, :], ps[3][:, 0:512], ALU.mult), "V", 1)
            p.w("tensor", "V", lvv)
            trcp = [prevOvv, prevOvv]
            for tt in range(4):
                p.w("tensor", "V", trcp[tt % 2])
                tr = p.do("tensor", lambda e, tt=tt: e.transpose(
                    ps[0 if tt % 2 == 0 else 1][:, 0:128],
                    lnf[:, tt * 128:(tt + 1) * 128], cst["identf"][:, :]), "T", 1)
                p.w("vector", "T", tr)
                ovv = p.do("vector", lambda e, tt=tt: e.tensor_copy(
                    outb[:, tt * 128:(tt + 1) * 128], ps[0 if tt % 2 == 0 else 1][:, 0:128]), "V", 1)
                trcp[tt % 2] = ovv
            prevOvv = ovv
            p.w("sync", "V", ovv)
            for tt in range(4):
                ost = p.do("sync", lambda e, fc=fc, tb=tb, tt=tt: e.dma_start(
                    out=io["out"][tb * 512 + tt * 128: tb * 512 + tt * 128 + 128,
                                  fc * 128:(fc + 1) * 128],
                    in_=outb[:, tt * 128:(tt + 1) * 128]), "DS", 16)


def build_program(taps=(), upto="all"):
    nc = bass.Bass("TRN2", target_bir_lowering=False)
    io = {}
    def inp(name, shape, dt):
        io[name] = nc.dram_tensor(name, shape, dt, kind="ExternalInput").ap()
    def internal(name, shape, dt):
        kind = "ExternalOutput" if name in taps else "Internal"
        io[name] = nc.dram_tensor(name, shape, dt, kind=kind).ap()

    inp("x", [T, D], F32)
    for nm in ("Wq", "Wk", "Wv", "Wo"):
        inp("l0_" + nm, [D, D], F32)
        inp("l1_" + nm, [D, D], F32)
    for nm in ("bq", "bk", "bv", "bo"):
        inp(f"l0_{nm}_row", [1, D], F32); inp(f"l0_{nm}_pp", [128, 8], F32)
        inp(f"l1_{nm}_row", [1, D], F32); inp(f"l1_{nm}_pp", [128, 8], F32)
    inp("f0_w1T", [D, DH], F32); inp("f0_w2T", [D, DH], F32)
    inp("f0_h1T", [DH, DH], F32); inp("f0_h2T", [DH, DH], F32)
    inp("f0_w3T", [DH, D], F32)
    for nm, nch in (("w1_s", 32), ("w2_s", 32), ("h1_s", 32), ("h2_s", 32), ("w3_s", 8)):
        inp("f0_" + nm, [128, nch], F32)
    inp("l1_probes", [H, HD, 192], F32)
    inp("f1_w1T", [D, DH], F32); inp("f1_w2T", [DH, D], F32)
    inp("f1_convw", [128, 96], F32); inp("f1_convb", [128, 32], F32)
    inp("ident", [128, 128], BF16); inp("onesb", [128, 512], BF16)
    inp("onesf", [128, 144], F32); inp("zz", [128, 544], BF16)
    inp("lup", [128, 128], BF16); inp("gmats", [64, 192], F32)
    inp("mask1", [128, 192], BF16); inp("mask2", [128, 144], BF16)
    inp("identf", [128, 128], F32)

    if "dbg_at0" in taps:
        internal("dbg_at0", [D, S], BF16)
    if "dbg_otok" in taps:
        internal("dbg_otok", [128, D], BF16)
    if "dbg_qkv" in taps:
        internal("dbg_qkv", [128, 3 * D], BF16)
    if "dbg_sc" in taps:
        internal("dbg_sc", [128, 512], F32)
    if "dbg_acc" in taps:
        internal("dbg_acc", [128, 1040], F32)
    if "dbg_ranks" in taps:
        internal("dbg_ranks", [128, 3072], U32)
    if "dbg_oh" in taps:
        internal("dbg_oh", [128, 6144], BF16)
        internal("dbg_proj", [128, 6144], F32)
    if "dbg_cnt" in taps:
        internal("dbg_cnt", [128, 98], F32)
    if "dbg_qrt" in taps:
        internal("dbg_qrt", [128, 16384], F32)
    if "dbg_qsort" in taps:
        internal("dbg_qsort", [S, 256], BF16)
    if "dbg_win" in taps:
        internal("dbg_win", [65, 512], BF16)
        internal("dbg_exp", [256, 512], BF16)
    if "dbg_acc2" in taps:
        internal("dbg_acc2", [128, 4096], BF16)
    if "dbg_acc3" in taps:
        internal("dbg_acc3", [128, 4096], BF16)
    if "dbg_q1" in taps:
        internal("dbg_q1", [4, 512], F32)
        internal("dbg_xq", [128, 4096], BF16)
        internal("dbg_qkt", [128, 8192], BF16)
        internal("dbg_otok2", [128, 2048], BF16)
    internal("xbf_tok", [T, D], BF16)
    internal("x1T", [D, T], BF16)
    internal("x2T", [D, T], BF16)
    internal("x3T", [D, T], BF16)
    internal("hA", [DH, T], BF16)
    internal("qkvs", [B, H, S, 256], BF16)
    internal("qkv_sorted0", [S, 256], BF16)
    internal("qkv_sorted1", [S, 256], BF16)
    internal("outsorted0", [S, 64], BF16)
    internal("outsorted1", [S, 64], BF16)
    internal("attn1tok", [B, 8, S, 128], BF16)
    internal("wq1q", [D, DH], BF16)
    internal("x2Tf", [D, T], F32)
    internal("wq2q", [DH, D], BF16)
    internal("hB", [DH, T], BF16)
    io["out"] = nc.dram_tensor("out", [T, D], F32, kind="ExternalOutput").ap()

    with ExitStack() as st:
        AR1 = st.enter_context(nc.sbuf_tensor("AR1", [128, 28672], BF16))
        AR2 = st.enter_context(nc.sbuf_tensor("AR2", [128, 32768], BF16))
        AR3 = st.enter_context(nc.sbuf_tensor("AR3", [128, 12288], BF16))
        ARF = st.enter_context(nc.sbuf_tensor("ARF", [128, 6656], F32))
        ARS = st.enter_context(nc.sbuf_tensor("ARS", [128, 1024], F32))
        PRB = st.enter_context(nc.sbuf_tensor("PRB", [128, 1536], F32))
        RNK = st.enter_context(nc.sbuf_tensor("RNK", [128, 3072], U32))
        cst = {}
        for nm, shape, dt in (("ident", [128, 128], BF16), ("onesb", [128, 512], BF16),
                              ("onesfsb", [128, 144], F32), ("zz", [128, 544], BF16),
                              ("lup", [128, 128], BF16), ("gmats", [64, 192], F32),
                              ("mask1", [128, 192], BF16), ("mask2", [128, 144], BF16),
                              ("identf", [128, 128], F32)):
            cst[nm] = st.enter_context(nc.sbuf_tensor("c_" + nm, shape, dt))
        ps = {}
        for i in range(4):
            ps[i] = st.enter_context(nc.psum_tensor(f"ps{i}", [128, 512], F32))
        ps["bfA"] = st.enter_context(nc.psum_tensor("psbfA", [128, 512], BF16))
        ps["bfB"] = st.enter_context(nc.psum_tensor("psbfB", [128, 512], BF16))
        blk = st.enter_context(nc.Block())
        p = P(nc, st)

        sb = dict(
            wq=AR1[:, 0:8192], wk=AR1[:, 8192:16384], wv=AR1[:, 16384:24576],
            wo=AR1[:, 0:16384], brow=AR1[:, 24576:27648],
            at0=AR2[:, 0:32768],
            xsb=AR3[:, 0:4096], qkv=AR3[:, 4096:7168], otok=AR3[:, 7168:8192],
            stile=AR3[:, 8192:10240], x1t=AR3[:, 10240:12288],
            prod=ARF[:, 0:1024], sc16=ARF[:, 1024:1280], att=ARF[:, 1280:1536],
            acc=ARF[:, 1536:2560], zr=ARF[:, 2560:2576], zs=ARF[:, 6160:6176],
            sqf=ARF[:, 2576:4624],
            r_mu=ARF[0:1, 4624:5136], r_m2=ARF[0:1, 5136:5648],
            r_rs=ARF[0:1, 5648:6160],
            bo_pp=ARS[:, 0:8], fscal=ARS[:, 8:144], bqpp1=ARS[:, 144:152],
            bo1pp=ARS[:, 168:176], wabs=ARS[:, 176:178], msc=ARS[:, 178:186],
            lnoutf=ARS[:, 320:832],
            wsc=ARS[:, 186:187], convw=ARS[:, 192:288], convb=ARS[:, 288:320],
            probes=PRB, ranks=RNK,
            wq1=AR1[:, 0:8192], wk1=AR1[:, 8192:16384], wv1=AR1[:, 16384:24576],
            brow1=AR1[:, 24576:27648],
        )

        src_map = {"ident": "ident", "onesb": "onesb", "onesfsb": "onesf",
                   "zz": "zz", "lup": "lup", "gmats": "gmats",
                   "mask1": "mask1", "mask2": "mask2", "identf": "identf"}
        for nm, t in cst.items():
            p.do("sync", lambda e, t=t, s=io[src_map[nm]]:
                 e.dma_start(out=t[:, :], in_=s[:, :]), "DS", 16)
        cload = p.snap()
        for e in ("tensor", "vector", "scalar", "gpsimd"):
            p.join(e, cload, ["DS"])

        emit_ingest(p, io)
        ing = p.snap()
        p.join("scalar", ing, ["DG"])
        p.join("tensor", ing, ["DG"])

        emit_l0(p, io, sb, cst, ps, 0)
        emit_l0(p, io, sb, cst, ps, 1)
        if upto in ("ffn0", "l1", "all"):
            emit_ffn0(p, io, AR1, AR2, AR3, ARF, ARS, sb, cst, ps)
        if upto in ("l1q", "l1a", "l1o", "all"):
            emit_l1qkv(p, io, AR1, AR2, AR3, ARF, ARS, sb, cst, ps, 0)
            emit_l1qkv(p, io, AR1, AR2, AR3, ARF, ARS, sb, cst, ps, 1)
            if "dbg_ranks" in io:
                emit_dump_ranks(p, io, sb)
        if upto in ("l1a", "l1o", "all"):
            emit_l1attn(p, io, AR1, AR2, AR3, ARF, ARS, sb, cst, ps, 0)
            if upto in ("l1o", "all"):
                emit_l1out(p, io, AR1, AR2, AR3, ARF, ARS, sb, cst, ps, 0)
            emit_l1attn(p, io, AR1, AR2, AR3, ARF, ARS, sb, cst, ps, 1)
            if upto in ("l1o", "all"):
                emit_l1out(p, io, AR1, AR2, AR3, ARF, ARS, sb, cst, ps, 1)
        if upto == "all":
            emit_ffn1(p, io, AR1, AR2, AR3, ARF, ARS, sb, cst, ps)
        else:
            fin = p.snap()
            p.join("gpsimd", fin, ["DS", "V", "T"])
            for i in range(8):
                p.do("gpsimd", lambda e, i=i: e.dma_start(
                    out=io["out"].rearrange("(a t) d -> a t d", a=8)[i, :, :],
                    in_=io["x1T"][:, i * 1024:(i + 1) * 1024]), "DG", 16)

        p.w("sync", "DS", p.cnt["DS"])
        p.w("gpsimd", "DG", p.cnt["DG"])
        p.finalize(blk)
    return nc


def kernel(**inputs):
    maps = shard_inputs(inputs)
    nc = build_program()
    res = run_bass_kernel_spmd(nc, maps, core_ids=list(range(NCORES)))
    out = np.concatenate([np.asarray(r["out"], np.float32).reshape(B, S, D)
                          for r in res.results], axis=0)
    return out
